# version 01
# baseline (speedup 1.0000x reference)
import contextlib
from collections import defaultdict

import numpy as np
import concourse.bass as bass
import concourse.mybir as mybir
from concourse.bass_utils import run_bass_kernel_spmd

F32 = mybir.dt.float32
BF16 = mybir.dt.bfloat16
AF = mybir.ActivationFunctionType
ALU = mybir.AluOpType
AX = mybir.AxisListType

T = 2048
D = 1024
NTOK = T // 128
NDT = D // 128
EPS1 = 1e-5

FCONSTS = [1e-5, 1e-5 * 1e-6, 1e-5 * 1e-5, 1e-6, 1.0, 0.5, -0.5]


class Ctx:
    def __init__(self, nc, es, block):
        self.nc = nc
        self.es = es
        self.block = block
        self.semh = {}
        self.cnt = defaultdict(int)
        self.waited = defaultdict(dict)
        self.names = set()

    def sem(self, name):
        if name not in self.semh:
            self.semh[name] = self.es.enter_context(self.nc.semaphore(name))
        return self.semh[name]

    def sbuf(self, name, shape, dtype, es=None):
        assert name not in self.names, name
        self.names.add(name)
        return (es or self.es).enter_context(self.nc.sbuf_tensor(name, shape, dtype))

    def psum(self, name, shape, dtype, es=None):
        assert name not in self.names, name
        self.names.add(name)
        return (es or self.es).enter_context(self.nc.psum_tensor(name, shape, dtype))

    def tok(self, sem):
        return (sem, self.cnt[sem])

    def inc(self, inst, sem, n=1):
        inst.then_inc(self.sem(sem), n)
        self.cnt[sem] += n
        return (sem, self.cnt[sem])


class Eng:
    def __init__(self, ctx, name, e):
        self.ctx, self.name, self.e = ctx, name, e

    def wait(self, *toks):
        for t in toks:
            if t is None:
                continue
            sem, val = t
            if val <= 0:
                continue
            if self.ctx.waited[self.name].get(sem, -1) >= val:
                continue
            self.e.wait_ge(self.ctx.sem(sem), val)
            self.ctx.waited[self.name][sem] = val

    def __getattr__(self, a):
        return getattr(self.e, a)


def on(ctx, engname, fn):
    getattr(ctx.block, engname)(lambda e: fn(Eng(ctx, engname, e)))


def host_constants():
    import ml_dtypes
    c = {}
    c["c_ident"] = np.eye(128, dtype=ml_dtypes.bfloat16)
    c["c_ones_f32"] = np.ones((128, 128), np.float32)
    c["c_ones_bf16"] = np.ones((128, 128), dtype=ml_dtypes.bfloat16)
    i = np.arange(128)
    c["c_triu"] = (i[None, :] >= i[:, None]).astype(ml_dtypes.bfloat16)
    hs = np.linspace(2.0, 8.0, D).astype(np.float32)
    decay = (0.5 ** (1.0 / hs)).astype(np.float32)
    c["c_decay"] = decay.reshape(NDT, 128).T.copy()
    c["c_decay_beta"] = (1.0 - decay).reshape(NDT, 128).T.copy()
    c["c_fconst"] = np.tile(np.asarray(FCONSTS, np.float32)[None, :], (128, 1))
    return c


def const_dram(nc, cvals):
    aps = {}
    for k, v in cvals.items():
        dt = F32 if v.dtype == np.float32 else BF16
        aps[k] = nc.dram_tensor(k, list(v.shape), dt, kind="ExternalInput").ap()
    return aps


class Prog:
    def __init__(self, nc, es, block, keep_xres=True):
        self.ctx = Ctx(nc, es, block)
        ctx = self.ctx
        self.keep_xres = keep_xres
        if keep_xres:
            self.xres = ctx.sbuf("xres", [128, NTOK * D], F32)
        self.xnT = ctx.sbuf("xnT", [128, NDT * T], BF16)
        self.ident = ctx.sbuf("ident", [128, 128], BF16)
        self.ones_f = ctx.sbuf("ones_f", [128, 128], F32)
        self.ones_b = ctx.sbuf("ones_b", [128, 128], BF16)
        self.fpool = ctx.sbuf("fpool", [128, len(FCONSTS)], F32)
        self.ln_stats = ctx.sbuf("ln_stats", [128, NTOK * 12], F32)
        self.ln_mv = ctx.sbuf("ln_mv", [128, NTOK * 2], F32)
        self.ln_rstd = ctx.sbuf("ln_rstd", [128, NTOK], F32)
        self.ln_nmr = ctx.sbuf("ln_nmr", [128, NTOK], F32)
        self.xstg = ctx.sbuf("xstg", [128, 2 * D], BF16)
        self.triu = ctx.sbuf("triu", [128, 128], BF16)
        self.pst = ctx.psum("pst", [128, 1024], BF16)

    def fc(self, val):
        return self.fpool[:, FCONSTS.index(val):FCONSTS.index(val) + 1]

    def fc1(self, val):
        return self.fpool[0:1, FCONSTS.index(val):FCONSTS.index(val) + 1]

    def load_const(self, const_aps):
        ctx = self.ctx
        def f(e):
            ctx.inc(e.dma_start(out=self.ident[:], in_=const_aps["c_ident"]), "qs", 16)
            ctx.inc(e.dma_start(out=self.ones_f[:], in_=const_aps["c_ones_f32"]), "qs", 16)
            ctx.inc(e.dma_start(out=self.ones_b[:], in_=const_aps["c_ones_bf16"]), "qs", 16)
            ctx.inc(e.dma_start(out=self.fpool[:], in_=const_aps["c_fconst"]), "qs", 16)
            ctx.inc(e.dma_start(out=self.triu[:], in_=const_aps["c_triu"]), "qs", 16)
        on(ctx, "sync", f)
        self.const_tok = ctx.tok("qs")

    def load_x(self, x_in_ap):
        ctx = self.ctx
        xv = x_in_ap.rearrange("(i p) d -> i p d", p=128)
        def f(e):
            for i in range(NTOK):
                ctx.inc(e.dma_start(out=self.xres[:, i * D:(i + 1) * D], in_=xv[i]), "qs", 16)
        on(ctx, "sync", f)
        self.x_tok = ctx.tok("qs")

    def store_x(self, x_out_ap, dep_toks):
        ctx = self.ctx
        xv = x_out_ap.rearrange("(i p) d -> i p d", p=128)
        def f(e):
            e.wait(*dep_toks)
            for i in range(NTOK):
                ctx.inc(e.dma_start(out=xv[i], in_=self.xres[:, i * D:(i + 1) * D]), "qg", 16)
            e.wait(ctx.tok("qg"))
        on(ctx, "gpsimd", f)



def emit_ln_stats_from(p, src_tiles, dep_toks, eps2=None):
    ctx = p.ctx
    ln_scale = 1.0 if eps2 is None else (1.0 + eps2)
    ln_bias = EPS1 if eps2 is None else (EPS1 * eps2)

    def v_stats(e):
        e.wait(*dep_toks)
        for i in range(NTOK):
            st = p.ln_stats[:, i * 12:(i + 1) * 12]
            s = src_tiles(i)
            e.bn_stats(out=st[:, 0:6], in_=s[:, 0:512])
            inst = e.bn_stats(out=st[:, 6:12], in_=s[:, 512:1024])
        tk = ctx.inc(inst, "v")
        e.wait(tk)
        for i in range(NTOK):
            inst = e.bn_aggr(out=p.ln_mv[:, 2 * i:2 * i + 2], in_=p.ln_stats[:, i * 12:(i + 1) * 12])
        ctx.inc(inst, "v")
    on(ctx, "vector", v_stats)
    t_stats = ctx.tok("v")

    mvh = p.ln_mv[:].rearrange("p (i two) -> p i two", two=2)
    mu, var = mvh[:, :, 0], mvh[:, :, 1]

    def s_rstd(e):
        e.wait(t_stats)
        e.activation(out=p.ln_rstd[:], in_=var, func=AF.Ln, bias=p.fc(ln_bias), scale=ln_scale)
        ctx.inc(e.activation(out=p.ln_rstd[:], in_=p.ln_rstd[:], func=AF.Exp, scale=-0.5), "s")
    on(ctx, "scalar", s_rstd)
    t_rstd = ctx.tok("s")

    def v_nmr(e):
        e.wait(t_rstd)
        ctx.inc(e.scalar_tensor_tensor(out=p.ln_nmr[:], in0=mu, scalar=-1.0,
                                       in1=p.ln_rstd[:], op0=ALU.mult, op1=ALU.mult), "v")
    on(ctx, "vector", v_nmr)
    return ctx.tok("v")


def emit_xnT(p, producers):
    ctx = p.ctx
    xnT_v = p.xnT[:].rearrange("p (j c) -> p j c", j=NDT)
    pst_v = p.pst[:].rearrange("p (j c) -> p j c", j=NDT)
    tr_toks = []
    for i in range(NTOK):
        slot = (i % 2) * D
        eng, sem, fn = producers(i)
        def prod(e, i=i, fn=fn, sem=sem):
            if i >= 2:
                e.wait(tr_toks[i - 2])
            ctx.inc(fn(e), sem)
        on(ctx, eng, prod)
        t_p = ctx.tok(sem)
        base_s = ctx.tok("s")
        def pe_t(e, i=i, slot=slot, t_p=t_p, base_s=base_s):
            e.wait(t_p)
            if i > 0:
                e.wait(base_s)
            for j in range(NDT):
                inst = e.transpose(p.pst[:, j * 128:(j + 1) * 128],
                                   p.xstg[:, slot + j * 128:slot + (j + 1) * 128], p.ident[:])
            tr_toks.append(ctx.inc(inst, "t"))
        on(ctx, "tensor", pe_t)
        t_tr = ctx.tok("t")
        def s_ev(e, i=i, t_tr=t_tr):
            e.wait(t_tr)
            ctx.inc(e.copy(out=xnT_v[:, :, i * 128:(i + 1) * 128], in_=pst_v), "s")
        on(ctx, "scalar", s_ev)
    return ctx.tok("s")


def emit_ln_t(p, dep_toks, eps2=None):
    t_nmr = emit_ln_stats_from(p, lambda i: p.xres[:, i * D:(i + 1) * D], dep_toks, eps2)

    def producers(i):
        def fn(e, i=i):
            e.wait(t_nmr)
            return e.activation(out=p.xstg[:, (i % 2) * D:(i % 2 + 1) * D],
                                in_=p.xres[:, i * D:(i + 1) * D], func=AF.Identity,
                                scale=p.ln_rstd[:, i:i + 1], bias=p.ln_nmr[:, i:i + 1])
        return ("scalar", "s", fn)
    t_xnT = emit_xnT(p, producers)
    return t_nmr, t_xnT



class MMPool:
    def __init__(self, ctx, tag, es=None):
        self.ctx = ctx
        self.ps = [ctx.psum(f"{tag}_mma", [128, 512], F32, es=es),
                   ctx.psum(f"{tag}_mmb", [128, 512], F32, es=es)]
        self.evac_tok = [None, None]
        self.pi = 0

    def step(self, deps, mms, evac_fn, eng="scalar", sem="s", width=512, out_p=128):
        ctx = self.ctx
        k = self.pi % 2
        self.pi += 1
        ps = self.ps[k]
        dep_ev = self.evac_tok[k]
        def pe_f(e):
            e.wait(*deps)
            if dep_ev is not None:
                e.wait(dep_ev)
            n = len(mms)
            for kk, (l, r) in enumerate(mms):
                inst = e.matmul(ps[0:out_p, 0:width], l, r, start=(kk == 0), stop=(kk == n - 1))
            ctx.inc(inst, "t")
        on(ctx, "tensor", pe_f)
        t_mm = ctx.tok("t")
        def ev_f(e):
            e.wait(t_mm)
            self.evac_tok[k] = ctx.inc(evac_fn(e, ps), sem)
        on(ctx, eng, ev_f)
        return ctx.tok(sem)


def emit_l0_ffn_prep(p, wT_ap, alpha_ap, bias_row_ap, tag, es=None):
    ctx = p.ctx
    u = tag
    wq = ctx.sbuf(f"{u}_w", [128, NDT * D], F32, es=es)
    tern = ctx.sbuf(f"{u}_tern", [128, NDT * D], BF16, es=es)
    absacc = ctx.sbuf(f"{u}_absacc", [128, NDT + 1], F32, es=es)
    arow = ctx.sbuf(f"{u}_arow", [1, 4], F32, es=es)
    acol = ctx.sbuf(f"{u}_acol", [128, 3], F32, es=es)
    scr = ctx.sbuf(f"{u}_scr", [128, D], F32, es=es)
    brow = ctx.sbuf(f"{u}_brow", [1, D], F32, es=es)
    brow_b = ctx.sbuf(f"{u}_browb", [1, D], BF16, es=es)
    ps_d = ctx.psum(f"{u}_psd", [1, 8], F32, es=es)
    ps_bc = ctx.psum(f"{u}_psbc", [128, 3], F32, es=es)

    wv = wT_ap.rearrange("(j p) o -> j p o", p=128)

    def dma(e):
        for j in range(NDT):
            ctx.inc(e.dma_start(out=wq[:, j * D:(j + 1) * D], in_=wv[j]), "qg", 16)
        ctx.inc(e.dma_start(out=arow[0:1, 2:3], in_=alpha_ap), "qg", 16)
        ctx.inc(e.dma_start(out=brow[:], in_=bias_row_ap), "qg", 16)
    on(ctx, "gpsimd", dma)
    t_w = ctx.tok("qg")

    def s(e):
        e.wait(t_w)
        for j in range(NDT):
            inst = e.activation(out=scr[:], in_=wq[:, j * D:(j + 1) * D], func=AF.Abs,
                                accum_out=absacc[:, j:j + 1])
        ctx.inc(inst, "s")
    on(ctx, "scalar", s)
    t_abs = ctx.tok("s")

    def v(e):
        e.wait(t_abs)
        ctx.inc(e.reduce_sum(out=absacc[:, NDT:NDT + 1], in_=absacc[:, 0:NDT], axis=AX.X), "v")
    on(ctx, "vector", v)
    t_red = ctx.tok("v")

    def pe(e):
        e.wait(t_red, p.const_tok)
        ctx.inc(e.matmul(ps_d[0:1, 0:1], p.ones_f[:, 0:1], absacc[:, NDT:NDT + 1],
                         start=True, stop=True), "t")
    on(ctx, "tensor", pe)
    t_sum = ctx.tok("t")

    def s2(e):
        e.wait(t_sum)
        e.activation(out=arow[0:1, 0:1], in_=ps_d[0:1, 0:1], func=AF.Copy, scale=0.7 / (D * D))
        e.activation(out=arow[0:1, 1:2], in_=ps_d[0:1, 0:1], func=AF.Copy, scale=-0.7 / (D * D))
        ctx.inc(e.activation(out=arow[0:1, 2:3], in_=arow[0:1, 2:3], func=AF.Copy, scale=0.5), "s")
    on(ctx, "scalar", s2)
    t_row = ctx.tok("s")

    def v2(e):
        e.wait(t_row)
        ctx.inc(e.reciprocal(out=arow[0:1, 3:4], in_=arow[0:1, 2:3]), "v")
    on(ctx, "vector", v2)
    t_rec = ctx.tok("v")

    def pe2(e):
        e.wait(t_rec)
        ctx.inc(e.matmul(ps_bc[:], p.ones_f[0:1, :], arow[0:1, 0:3], start=True, stop=True), "t")
    on(ctx, "tensor", pe2)
    t_bc = ctx.tok("t")

    def s3(e):
        e.wait(t_bc)
        e.copy(out=acol[:], in_=ps_bc[:])
        ctx.inc(e.activation(out=brow_b[:], in_=brow[:], func=AF.Copy, scale=arow[0:1, 3:4]), "s")
    on(ctx, "scalar", s3)
    t_cols = ctx.tok("s")

    def s4(e):
        e.wait(t_cols)
        for j in range(NDT):
            e.activation(out=tern[:, j * D:(j + 1) * D], in_=wq[:, j * D:(j + 1) * D],
                         func=AF.Sign, bias=acol[:, 1:2])
            inst = e.activation(out=wq[:, j * D:(j + 1) * D], in_=wq[:, j * D:(j + 1) * D],
                                func=AF.Sign, bias=acol[:, 0:1])
        ctx.inc(inst, "s")
    on(ctx, "scalar", s4)
    t_signs = ctx.tok("s")

    def v3(e):
        e.wait(t_signs)
        for j in range(NDT):
            inst = e.tensor_tensor(out=tern[:, j * D:(j + 1) * D], in0=tern[:, j * D:(j + 1) * D],
                                   in1=wq[:, j * D:(j + 1) * D], op=ALU.add)
        ctx.inc(inst, "v")
    on(ctx, "vector", v3)
    t_tern = ctx.tok("v")
    return dict(tern=tern, acol=acol, brow_b=brow_b, t_ready=(t_tern, t_cols))


def emit_l0_ffn_call(p, prep, mm, t_xnT):
    ctx = p.ctx
    tern, acol, brow_b = prep["tern"], prep["acol"], prep["brow_b"]
    for i in range(NTOK):
        for half in range(2):
            mms = [(p.xnT[:, j * T + i * 128:j * T + (i + 1) * 128],
                    tern[:, j * D + half * 512:j * D + half * 512 + 512]) for j in range(NDT)]
            mms.append((p.ones_b[0:1, 0:128], brow_b[0:1, half * 512:half * 512 + 512]))
            def ev(e, ps, i=i, half=half):
                sl = p.xres[:, i * D + half * 512:i * D + half * 512 + 512]
                return e.scalar_tensor_tensor(out=sl, in0=ps[:, 0:512], scalar=acol[:, 2:3],
                                              in1=sl, op0=ALU.mult, op1=ALU.add)
            mm.step([t_xnT, *prep["t_ready"]], mms, ev, eng="vector", sem="v")
    return ctx.tok("v")


def emit_l0_attn_b(p, io, x_in_ap, x_out_ap, tag):
    ctx = p.ctx
    u = tag
    sb = lambda n, s, d: ctx.sbuf(f"{u}_{n}", s, d)
    psm = lambda n, s, d: ctx.psum(f"{u}_{n}", s, d)
    NH = 16

    xv = x_in_ap.rearrange("(i p) d -> i p d", p=128)
    xov = x_out_ap.rearrange("(i p) d -> i p d", p=128)
    pev = io["pe_tok"].rearrange("(i p) d -> i p d", p=128)

    xfs = sb("xfs", [128, 2 * D], F32)
    peb = sb("peb", [128, D], F32)
    pescr = sb("pescr", [128, D], F32)

    ld_toks = []
    st_done = []
    for i in range(NTOK):
        def g_l(e, i=i):
            if i >= 2:
                e.wait(st_done[i - 2])
            ld_toks.append(ctx.inc(e.dma_start(out=xfs[:, (i % 2) * D:(i % 2 + 1) * D], in_=xv[i]), "qs", 16))
        on(ctx, "sync", g_l)
        def v_st(e, i=i):
            e.wait(ld_toks[i])
            st = p.ln_stats[:, i * 12:(i + 1) * 12]
            e.bn_stats(out=st[:, 0:6], in_=xfs[:, (i % 2) * D:(i % 2) * D + 512])
            inst = e.bn_stats(out=st[:, 6:12], in_=xfs[:, (i % 2) * D + 512:(i % 2 + 1) * D])
            st_done.append(ctx.inc(inst, "v"))
        on(ctx, "vector", v_st)
    def v_ag(e):
        e.wait(st_done[NTOK - 1])
        for i in range(NTOK):
            inst = e.bn_aggr(out=p.ln_mv[:, 2 * i:2 * i + 2], in_=p.ln_stats[:, i * 12:(i + 1) * 12])
        ctx.inc(inst, "v")
    on(ctx, "vector", v_ag)
    t_stats = ctx.tok("v")

    mvh = p.ln_mv[:].rearrange("p (i two) -> p i two", two=2)
    mu, var = mvh[:, :, 0], mvh[:, :, 1]
    def s_rstd(e):
        e.wait(t_stats)
        e.activation(out=p.ln_rstd[:], in_=var, func=AF.Ln, bias=p.fc(EPS1), scale=1.0)
        ctx.inc(e.activation(out=p.ln_rstd[:], in_=p.ln_rstd[:], func=AF.Exp, scale=-0.5), "s")
    on(ctx, "scalar", s_rstd)
    t_rstd = ctx.tok("s")
    def v_nmr(e):
        e.wait(t_rstd)
        ctx.inc(e.scalar_tensor_tensor(out=p.ln_nmr[:], in0=mu, scalar=-1.0,
                                       in1=p.ln_rstd[:], op0=ALU.mult, op1=ALU.mult), "v")
    on(ctx, "vector", v_nmr)
    t_nmr = ctx.tok("v")

    xa_toks, pa_toks = [], []
    V0 = ctx.cnt["v"]
    def producers(i):
        def g_l(e, i=i):
            e.wait(t_nmr)
            if i >= 1:
                e.wait(("v", V0 + i))
            xa_toks.append(ctx.inc(e.dma_start(out=xfs[:, (i % 2) * D:(i % 2 + 1) * D], in_=xv[i]), "qs", 16))
            pa_toks.append(ctx.inc(e.dma_start(out=peb[:], in_=pev[i]), "qs", 16))
        on(ctx, "sync", g_l)
        def fn(e, i=i):
            e.wait(xa_toks[i], pa_toks[i], t_nmr)
            e.tensor_scalar(out=pescr[:], in0=peb[:],
                            scalar1=p.ln_nmr[:, i:i + 1], scalar2=None, op0=ALU.add)
            return e.scalar_tensor_tensor(out=p.xstg[:, (i % 2) * D:(i % 2 + 1) * D],
                                          in0=xfs[:, (i % 2) * D:(i % 2 + 1) * D],
                                          scalar=p.ln_rstd[:, i:i + 1], in1=pescr[:],
                                          op0=ALU.mult, op1=ALU.add)
        return ("vector", "v", fn)
    t_xnT = emit_xnT(p, producers)

    wbuf = sb("wbuf", [128, NDT * D], BF16)
    stage = sb("stage", [128, D], F32)
    bcols = sb("bcols", [128, 2 * NDT], F32)
    rows_f = sb("rows_f", [1, D], F32)
    rows_b = sb("rows_b", [1, 2 * D], BF16)
    gam = sb("gam", [128, D], F32)
    bet = sb("bet", [128, D], F32)

    def g_small(e):
        ctx.inc(e.dma_start(out=bcols[:, 0:NDT], in_=io["bq_c"]), "qg", 16)
        ctx.inc(e.dma_start(out=bcols[:, NDT:2 * NDT], in_=io["bk_c"]), "qg", 16)
        ctx.inc(e.dma_start(out=rows_f[0:1, 0:D], in_=io["bv_row"]), "qg", 16)
        ctx.inc(e.dma_start(out=gam[:], in_=io["gamma_bc"]), "qg", 16)
        ctx.inc(e.dma_start(out=bet[:], in_=io["beta_bc"]), "qg", 16)
    on(ctx, "gpsimd", g_small)
    t_small = ctx.tok("qg")

    def s_rows(e):
        e.wait(t_small)
        ctx.inc(e.copy(out=rows_b[0:1, 0:D], in_=rows_f[:]), "s")
    on(ctx, "scalar", s_rows)
    t_rows0 = ctx.tok("s")

    def g_small2(e):
        e.wait(t_rows0)
        ctx.inc(e.dma_start(out=rows_f[0:1, 0:D], in_=io["bo_row"]), "qg", 16)
    on(ctx, "gpsimd", g_small2)
    t_small2 = ctx.tok("qg")

    def s_rows2(e):
        e.wait(t_small2)
        ctx.inc(e.copy(out=rows_b[0:1, D:2 * D], in_=rows_f[:]), "s")
    on(ctx, "scalar", s_rows2)
    t_rows = ctx.tok("s")

    def load_weight(wap, dep):
        wvv = wap.rearrange("(j p) f -> j p f", p=128)
        cts = []
        for j in range(NDT):
            def g_w(e, j=j, dep=dep):
                e.wait(*dep)
                if j >= 1:
                    e.wait(cts[j - 1])
                ctx.inc(e.dma_start(out=stage[:], in_=wvv[j]), "qg", 16)
            on(ctx, "gpsimd", g_w)
            t_l = ctx.tok("qg")
            def s_c(e, j=j, t_l=t_l):
                e.wait(t_l)
                cts.append(ctx.inc(e.copy(out=wbuf[:, j * D:(j + 1) * D], in_=stage[:]), "s"))
            on(ctx, "scalar", s_c)
        return ctx.tok("s")

    qT = sb("qT", [128, NDT * T], BF16)
    kT = sb("kT", [128, NDT * T], BF16)
    vplus = sb("vplus", [128, NTOK * 1040], BF16)
    vplus_v = vplus[:].rearrange("p (i h c) -> p i h c", i=NTOK, h=16)
    mm = MMPool(ctx, u)

    def v_ones(e):
        ctx.inc(e.memset(vplus_v[:, :, :, 64:65], 1.0), "v")
    on(ctx, "vector", v_ones)
    t_vones = ctx.tok("v")

    t_proj = None
    for which, dstT, bc_off in [("q", qT, 0), ("k", kT, NDT)]:
        t_w = load_weight(io["w" + which], [t_proj] if t_proj else [])
        for ft in range(NDT):
            for tc in range(4):
                mms = [(wbuf[:, j * D + ft * 128:j * D + (ft + 1) * 128],
                        p.xnT[:, j * T + tc * 512:j * T + (tc + 1) * 512]) for j in range(NDT)]
                def ev(e, ps, dstT=dstT, ft=ft, tc=tc, bc_off=bc_off):
                    return e.activation(out=dstT[:, ft * T + tc * 512:ft * T + (tc + 1) * 512],
                                        in_=ps[:, 0:512], func=AF.Identity, scale=1.0,
                                        bias=bcols[:, bc_off + ft:bc_off + ft + 1])
                t_proj = mm.step([t_xnT, t_w, t_small], mms, ev)
    t_wv = load_weight(io["wv"], [t_proj])
    for ti in range(NTOK):
        for vc in range(2):
            mms = [(p.xnT[:, j * T + ti * 128:j * T + (ti + 1) * 128],
                    wbuf[:, j * D + vc * 512:j * D + (vc + 1) * 512]) for j in range(NDT)]
            mms.append((p.ones_b[0:1, 0:128], rows_b[0:1, vc * 512:(vc + 1) * 512]))
            def ev(e, ps, ti=ti, vc=vc):
                ps_v = ps[:, 0:512].rearrange("p (h c) -> p h c", h=8)
                return e.copy(out=vplus_v[:, ti, vc * 8:(vc + 1) * 8, 0:64], in_=ps_v)
            t_proj = mm.step([t_xnT, t_wv, t_vones, t_rows], mms, ev)
    t_projs = ctx.tok("s")

    ps_s = psm("pss", [128, 1024], F32)
    pv2 = psm("pv2", [128, 512], F32)
    pv3 = psm("pv3", [128, 512], F32)
    pv_banks = [mm.ps[0], mm.ps[1], pv2, pv3]
    PT = sb("PT", [128, 2048], BF16)
    attn_sb = sb("attn_sb", [128, 2048], F32)
    den_sb = sb("den_sb", [1, 2048], F32)

    t_lastdiv = None
    exp_tok = {}
    pv_tok = {}
    for h in range(NH):
        ft, su = h // 2, (h % 2) * 64
        for kt in range(NTOK):
            for half in range(2):
                prev_exp = exp_tok.get((h, kt, half - 1)) or exp_tok.get((h, kt - 1, 1))
                prev_div = t_lastdiv
                def pe_s(e, h=h, kt=kt, half=half, ft=ft, su=su, prev_exp=prev_exp, prev_div=prev_div):
                    e.wait(t_projs)
                    if prev_exp is not None:
                        e.wait(prev_exp)
                    if kt == 0 and half == 0 and prev_div is not None:
                        e.wait(prev_div)
                    for c in range(2):
                        q0 = half * 1024 + c * 512
                        inst = e.matmul(ps_s[:, c * 512:(c + 1) * 512],
                                        kT[su:su + 64, ft * T + kt * 128:ft * T + (kt + 1) * 128],
                                        qT[su:su + 64, ft * T + q0:ft * T + q0 + 512],
                                        start=True, stop=True)
                    ctx.inc(inst, "t")
                on(ctx, "tensor", pe_s)
                t_sc = ctx.tok("t")
                prev_pv = pv_tok.get((h, kt - 1))
                def s_exp(e, kt=kt, half=half, t_sc=t_sc, prev_pv=prev_pv):
                    e.wait(t_sc)
                    if prev_pv is not None:
                        e.wait(prev_pv)
                    inst = e.activation(out=PT[:, half * 1024:(half + 1) * 1024],
                                        in_=ps_s[:], func=AF.Exp, scale=0.125)
                    ctx.inc(inst, "s")
                on(ctx, "scalar", s_exp)
                exp_tok[(h, kt, half)] = ctx.tok("s")
            e_toks = (exp_tok[(h, kt, 0)], exp_tok[(h, kt, 1)])
            prev_div = t_lastdiv
            def pe_pv(e, h=h, kt=kt, e_toks=e_toks, prev_div=prev_div):
                e.wait(*e_toks)
                if kt == 0:
                    e.wait(t_projs)
                    if prev_div is not None:
                        e.wait(prev_div)
                for qc in range(4):
                    inst = e.matmul(pv_banks[qc][0:65, 0:512], vplus_v[:, kt, h, :],
                                    PT[:, qc * 512:(qc + 1) * 512],
                                    start=(kt == 0), stop=(kt == NTOK - 1))
                ctx.inc(inst, "t")
            on(ctx, "tensor", pe_pv)
            pv_tok[(h, kt)] = ctx.tok("t")
        t_pv = pv_tok[(h, NTOK - 1)]

        def s_ev(e, t_pv=t_pv):
            e.wait(t_pv)
            for qc in range(4):
                e.copy(out=attn_sb[0:64, qc * 512:(qc + 1) * 512], in_=pv_banks[qc][0:64, 0:512])
                e.copy(out=den_sb[0:1, qc * 512:(qc + 1) * 512], in_=pv_banks[qc][64:65, 0:512])
            e.activation(out=den_sb[:], in_=den_sb[:], func=AF.Ln, bias=0.0)
            inst = e.activation(out=den_sb[:], in_=den_sb[:], func=AF.Exp, scale=-1.0)
            ctx.inc(inst, "s")
        on(ctx, "scalar", s_ev)
        t_ev = ctx.tok("s")

        def pe_bc(e, t_ev=t_ev):
            e.wait(t_ev)
            for qc in range(4):
                inst = e.matmul(ps_s[(qc % 2) * 64:(qc % 2) * 64 + 64, (qc // 2) * 512:(qc // 2 + 1) * 512],
                                p.ones_f[0:1, 0:64],
                                den_sb[0:1, qc * 512:(qc + 1) * 512],
                                start=True, stop=True)
            ctx.inc(inst, "t")
        on(ctx, "tensor", pe_bc)
        t_bc = ctx.tok("t")

        def v_div(e, ft=ft, su=su, t_bc=t_bc):
            e.wait(t_bc)
            for qc in range(4):
                rb = ps_s[(qc % 2) * 64:(qc % 2) * 64 + 64, (qc // 2) * 512:(qc // 2 + 1) * 512]
                inst = e.tensor_tensor(out=qT[su:su + 64, ft * T + qc * 512:ft * T + (qc + 1) * 512],
                                       in0=attn_sb[0:64, qc * 512:(qc + 1) * 512], in1=rb, op=ALU.mult)
            ctx.inc(inst, "v")
        on(ctx, "vector", v_div)
        t_lastdiv = ctx.tok("v")

    t_wo = load_weight(io["wo"], [t_lastdiv])
    xpescr = pescr
    hbuf = stage
    st2buf = sb("st2buf", [128, 16], F32)
    st2 = st2buf[:, 0:12]
    mv2 = st2buf[:, 12:14]
    r2 = st2buf[:, 14:15]
    n2 = st2buf[:, 15:16]
    xc_toks, pc_toks, out_toks = [], [], []
    vh_toks = []
    for i in range(NTOK):
        def g_l(e, i=i):
            if i >= 2:
                e.wait(out_toks[i - 2])
            if i >= 1:
                e.wait(vh_toks[i - 1])
            e.wait(t_nmr, t_lastdiv)
            xc_toks.append(ctx.inc(e.dma_start(out=xfs[:, (i % 2) * D:(i % 2 + 1) * D], in_=xv[i]), "qs", 16))
            pc_toks.append(ctx.inc(e.dma_start(out=peb[:], in_=pev[i]), "qs", 16))
        on(ctx, "sync", g_l)
        mm_toks = []
        for dc in range(2):
            mms = [(qT[:, j * T + i * 128:j * T + (i + 1) * 128],
                    wbuf[:, j * D + dc * 512:j * D + (dc + 1) * 512]) for j in range(NDT)]
            mms.append((p.ones_b[0:1, 0:128], rows_b[0:1, D + dc * 512:D + (dc + 1) * 512]))
            def ev(e, ps, i=i, dc=dc):
                return e.tensor_copy(out=hbuf[:, dc * 512:(dc + 1) * 512], in_=ps[:, 0:512])
            mm_toks.append(mm.step([t_wo, t_lastdiv, t_rows], mms, ev, eng="vector", sem="v"))
        def v_h(e, i=i, mm_toks=mm_toks):
            e.wait(xc_toks[i], pc_toks[i], *mm_toks)
            e.tensor_scalar(out=xpescr[:], in0=peb[:],
                            scalar1=p.ln_nmr[:, i:i + 1], scalar2=None, op0=ALU.add)
            e.scalar_tensor_tensor(out=xpescr[:], in0=xfs[:, (i % 2) * D:(i % 2 + 1) * D],
                                   scalar=p.ln_rstd[:, i:i + 1], in1=xpescr[:],
                                   op0=ALU.mult, op1=ALU.add)
            inst = e.tensor_tensor(out=hbuf[:], in0=hbuf[:], in1=xpescr[:], op=ALU.add)
            tk = ctx.inc(inst, "v")
            e.wait(tk)
            e.bn_stats(out=st2[:, 0:6], in_=hbuf[:, 0:512])
            inst = e.bn_stats(out=st2[:, 6:12], in_=hbuf[:, 512:1024])
            tk = ctx.inc(inst, "v")
            e.wait(tk)
            ctx.inc(e.bn_aggr(out=mv2[:], in_=st2[:]), "v")
        on(ctx, "vector", v_h)
        t_hs = ctx.tok("v")
        vh_toks.append(t_hs)
        def s_r2(e, t_hs=t_hs):
            e.wait(t_hs)
            e.activation(out=r2, in_=st2buf[:, 13:14], func=AF.Ln, bias=p.fc(EPS1), scale=1.0)
            ctx.inc(e.activation(out=r2, in_=r2, func=AF.Exp, scale=-0.5), "s")
        on(ctx, "scalar", s_r2)
        t_r2 = ctx.tok("s")
        def v_f(e, i=i, t_r2=t_r2):
            e.wait(t_r2)
            ctx.inc(e.scalar_tensor_tensor(out=n2, in0=st2buf[:, 12:13], scalar=-1.0, in1=r2,
                                           op0=ALU.mult, op1=ALU.mult), "v")
        on(ctx, "vector", v_f)
        t_n2 = ctx.tok("v")
        def s_a(e, i=i, t_n2=t_n2):
            e.wait(t_n2)
            ctx.inc(e.activation(out=xpescr[:], in_=hbuf[:], func=AF.Identity,
                                 scale=r2, bias=n2), "s")
        on(ctx, "scalar", s_a)
        t_a = ctx.tok("s")
        def v_out(e, i=i, t_a=t_a):
            e.wait(t_a, t_small)
            e.scalar_tensor_tensor(out=xpescr[:], in0=xpescr[:], scalar=1.0, in1=gam[:],
                                   op0=ALU.mult, op1=ALU.mult)
            e.tensor_tensor(out=xpescr[:], in0=xpescr[:], in1=bet[:], op=ALU.add)
            inst = e.tensor_tensor(out=xfs[:, (i % 2) * D:(i % 2 + 1) * D],
                                   in0=xfs[:, (i % 2) * D:(i % 2 + 1) * D], in1=xpescr[:], op=ALU.add)
            ctx.inc(inst, "v")
        on(ctx, "vector", v_out)
        t_vo = ctx.tok("v")
        def g_o(e, i=i, t_vo=t_vo):
            e.wait(t_vo)
            out_toks.append(ctx.inc(e.dma_start(out=xov[i], in_=xfs[:, (i % 2) * D:(i % 2 + 1) * D]), "qg", 16))
        on(ctx, "gpsimd", g_o)
    def g_fin(e):
        e.wait(ctx.tok("qg"))
    on(ctx, "gpsimd", g_fin)
    return ctx.tok("qg")


def emit_l0_attn_a(p, io, x_in_ap, x_out_ap, tag, dbg_upto=None, dbg_aps=None):
    ctx = p.ctx
    u_ = tag
    sb = lambda n, s, d: ctx.sbuf(f"{u_}_{n}", s, d)
    psm = lambda n, s, d: ctx.psum(f"{u_}_{n}", s, d)

    xv = x_in_ap.rearrange("(i p) d -> i p d", p=128)
    xov = x_out_ap.rearrange("(i p) d -> i p d", p=128)
    xfs = sb("xfs", [128, 2 * D], F32)

    ld_toks, st_done = [], []
    for i in range(NTOK):
        def g_l(e, i=i):
            if i >= 2:
                e.wait(st_done[i - 2])
            ld_toks.append(ctx.inc(e.dma_start(out=xfs[:, (i % 2) * D:(i % 2 + 1) * D], in_=xv[i]), "qs", 16))
        on(ctx, "sync", g_l)
        def v_st(e, i=i):
            e.wait(ld_toks[i])
            st = p.ln_stats[:, i * 12:(i + 1) * 12]
            e.bn_stats(out=st[:, 0:6], in_=xfs[:, (i % 2) * D:(i % 2) * D + 512])
            inst = e.bn_stats(out=st[:, 6:12], in_=xfs[:, (i % 2) * D + 512:(i % 2 + 1) * D])
            st_done.append(ctx.inc(inst, "v"))
        on(ctx, "vector", v_st)
    def v_ag(e):
        e.wait(st_done[NTOK - 1])
        for i in range(NTOK):
            inst = e.bn_aggr(out=p.ln_mv[:, 2 * i:2 * i + 2], in_=p.ln_stats[:, i * 12:(i + 1) * 12])
        ctx.inc(inst, "v")
    on(ctx, "vector", v_ag)
    t_stats = ctx.tok("v")
    mvh = p.ln_mv[:].rearrange("p (i two) -> p i two", two=2)
    mu0, var0 = mvh[:, :, 0], mvh[:, :, 1]
    def s_rstd(e):
        e.wait(t_stats)
        e.activation(out=p.ln_rstd[:], in_=var0, func=AF.Ln, bias=p.fc(EPS1 * 1e-6), scale=1.0 + 1e-6)
        ctx.inc(e.activation(out=p.ln_rstd[:], in_=p.ln_rstd[:], func=AF.Exp, scale=-0.5), "s")
    on(ctx, "scalar", s_rstd)
    t_rstd0 = ctx.tok("s")
    def v_nmr0(e):
        e.wait(t_rstd0)
        ctx.inc(e.scalar_tensor_tensor(out=p.ln_nmr[:], in0=mu0, scalar=-1.0,
                                       in1=p.ln_rstd[:], op0=ALU.mult, op1=ALU.mult), "v")
    on(ctx, "vector", v_nmr0)
    t_nmr = ctx.tok("v")

    xa_toks = []
    S0 = ctx.cnt["s"]
    def producers(i):
        def g_l2(e, i=i):
            e.wait(t_nmr)
            if i >= 2:
                e.wait(("s", S0 + (i - 1) * 2 + 1))
            xa_toks.append(ctx.inc(e.dma_start(out=xfs[:, (i % 2) * D:(i % 2 + 1) * D], in_=xv[i]), "qs", 16))
        on(ctx, "sync", g_l2)
        def fn(e, i=i):
            e.wait(xa_toks[i], t_nmr)
            return e.activation(out=p.xstg[:, (i % 2) * D:(i % 2 + 1) * D],
                                in_=xfs[:, (i % 2) * D:(i % 2 + 1) * D], func=AF.Identity,
                                scale=p.ln_rstd[:, i:i + 1], bias=p.ln_nmr[:, i:i + 1])
        return ("scalar", "s", fn)
    t_xnT = emit_xnT(p, producers)

    ubuf = sb("ubuf", [128, NDT * T], BF16)
    cxbuf = sb("cxbuf", [128, NDT * T], BF16)
    wbuf = sb("wbuf", [128, NDT * D], BF16)
    stage = sb("stage", [128, D], F32)
    cols = sb("cols", [128, 8 * NDT], F32)
    ubias = sb("ubias", [128, NDT], F32)
    ratio = sb("ratio", [128, 2 * NDT], BF16)
    dbc = sb("dbc", [128, T], F32)
    wsum_b = sb("wsumb", [1, D], BF16)
    bwo_b = sb("bwob", [1, D], BF16)
    nmr_b = sb("nmrb", [1, T], BF16)
    rinv_b = sb("rinvb", [1, T], BF16)
    ones512 = sb("ones512", [1, 512], BF16)
    rbc = sb("rbc", [128, T], BF16)
    rrow = sb("rrow", [1, T], F32)
    mm = MMPool(ctx, u_)
    ps_bwt = psm("psbwt", [128, NDT], F32)
    ps_row = psm("psrow", [1, T], F32)

    c_ln1w = cols[:, 0:8]; c_ln1b = cols[:, 8:16]
    c_lncw = cols[:, 16:24]; c_lncb = cols[:, 24:32]
    c_lngw = cols[:, 32:40]; c_lngb = cols[:, 40:48]
    c_dec = cols[:, 48:56]; c_beta = cols[:, 56:64]

    def g_cols(e):
        ctx.inc(e.dma_start(out=c_ln1w, in_=io["ln1w_c"]), "qg", 16)
        ctx.inc(e.dma_start(out=c_ln1b, in_=io["ln1b_c"]), "qg", 16)
        ctx.inc(e.dma_start(out=c_lncw, in_=io["lncw_c"]), "qg", 16)
        ctx.inc(e.dma_start(out=c_lncb, in_=io["lncb_c"]), "qg", 16)
        ctx.inc(e.dma_start(out=c_lngw, in_=io["lngw_c"]), "qg", 16)
        ctx.inc(e.dma_start(out=c_lngb, in_=io["lngb_c"]), "qg", 16)
        ctx.inc(e.dma_start(out=c_dec, in_=p.c_aps["c_decay"]), "qg", 16)
        ctx.inc(e.dma_start(out=c_beta, in_=p.c_aps["c_decay_beta"]), "qg", 16)
    on(ctx, "gpsimd", g_cols)
    t_cols = ctx.tok("qg")

    def v_ratio(e):
        e.wait(t_cols)
        tk = ctx.inc(e.reciprocal(out=ubias[:, 0:NDT], in_=c_ln1w), "v")
        e.wait(tk)
        tk = ctx.inc(e.tensor_tensor(out=ratio[:, 0:NDT], in0=c_ln1b, in1=ubias[:, 0:NDT], op=ALU.mult), "v")
        e.wait(tk)
        tk = ctx.inc(e.reciprocal(out=ubias[:, 0:NDT], in_=c_lncw), "v")
        e.wait(tk)
        ctx.inc(e.tensor_tensor(out=ratio[:, NDT:2 * NDT], in0=c_lncb, in1=ubias[:, 0:NDT], op=ALU.mult), "v")
    on(ctx, "vector", v_ratio)
    t_ratio = ctx.tok("v")

    wtv = io["Wt"].rearrange("(j p) h -> j p h", p=128)
    cast_toks = []
    for j in range(NDT):
        def g_w(e, j=j):
            if j >= 1:
                e.wait(cast_toks[j - 1])
            ctx.inc(e.dma_start(out=stage[:], in_=wtv[j]), "qg", 16)
        on(ctx, "gpsimd", g_w)
        t_l = ctx.tok("qg")
        def s_c(e, j=j, t_l=t_l):
            e.wait(t_l, t_cols)
            cast_toks.append(ctx.inc(e.activation(out=wbuf[:, j * D:(j + 1) * D], in_=stage[:],
                                                  func=AF.Identity, scale=c_ln1w[:, j:j + 1], bias=0.0), "s"))
        on(ctx, "scalar", s_c)
    t_wt = ctx.tok("s")

    def pe_b(e):
        e.wait(t_wt, t_ratio)
        for jt in range(NDT):
            for j in range(NDT):
                inst = e.matmul(ps_bwt[:, jt:jt + 1], wbuf[:, j * D + jt * 128:j * D + (jt + 1) * 128],
                                ratio[:, j:j + 1], start=(j == 0), stop=(j == NDT - 1))
        ctx.inc(inst, "t")
    on(ctx, "tensor", pe_b)

    def v_ub(e):
        e.wait(ctx.tok("t"), t_wt)
        ctx.inc(e.tensor_tensor(out=ubias[:], in0=ps_bwt[:], in1=c_beta, op=ALU.mult), "v")
    on(ctx, "vector", v_ub)
    t_ub = ctx.tok("v")

    for ht in range(NDT):
        for sc in range(4):
            mms = [(wbuf[:, j * D + ht * 128:j * D + (ht + 1) * 128],
                    p.xnT[:, j * T + sc * 512:j * T + (sc + 1) * 512]) for j in range(NDT)]
            def ev(e, ps, ht=ht, sc=sc):
                return e.activation(out=ubuf[:, ht * T + sc * 512:ht * T + (sc + 1) * 512],
                                    in_=ps[:, 0:512], func=AF.Identity,
                                    scale=c_beta[:, ht:ht + 1], bias=ubias[:, ht:ht + 1])
            mm.step([t_xnT, t_wt, t_ub], mms, ev)
    t_u = ctx.tok("s")

    scan_toks = []
    def v_init(e):
        ctx.inc(e.memset(dbc[:], 0.0), "v")
    on(ctx, "vector", v_init)
    for ht in range(NDT):
        def v_sc(e, ht=ht):
            e.wait(t_u, t_cols)
            e.tensor_scalar(out=dbc[:], in0=dbc[:], scalar1=0.0, scalar2=c_dec[:, ht:ht + 1],
                            op0=ALU.mult, op1=ALU.add)
            inst = e.tensor_tensor_scan(out=cxbuf[:, ht * T:(ht + 1) * T], data0=dbc[:],
                                        data1=ubuf[:, ht * T:(ht + 1) * T], initial=0.0,
                                        op0=ALU.mult, op1=ALU.add)
            scan_toks.append(ctx.inc(inst, "v"))
        on(ctx, "vector", v_sc)
    t_scan = ctx.tok("v")

    sqs = sb("sqs", [128, 512], BF16)
    for sc in range(4):
        def pe_s1(e, sc=sc):
            e.wait(t_scan, p.const_tok)
            for ht in range(NDT):
                inst = e.matmul(ps_row[0:1, sc * 512:(sc + 1) * 512], p.ones_b[:, 0:1],
                                cxbuf[:, ht * T + sc * 512:ht * T + (sc + 1) * 512],
                                start=(ht == 0), stop=(ht == NDT - 1))
            ctx.inc(inst, "t")
        on(ctx, "tensor", pe_s1)
    t_sum1 = ctx.tok("t")
    def s_ev1(e):
        e.wait(t_sum1)
        ctx.inc(e.copy(out=dbc[0:1, 0:T], in_=ps_row[:]), "s")
    on(ctx, "scalar", s_ev1)
    t_r1 = ctx.tok("s")
    sq_toks = {}
    for sc in range(4):
        for ht in range(NDT):
            def s_sq(e, ht=ht, sc=sc):
                e.wait(t_scan)
                if sq_toks:
                    pass
                e.wait(t_r1)
                ctx.inc(e.activation(out=sqs[:], in_=cxbuf[:, ht * T + sc * 512:ht * T + (sc + 1) * 512],
                                     func=AF.Square), "s")
            on(ctx, "scalar", s_sq)
            t_sq = ctx.tok("s")
            def pe_s2(e, ht=ht, sc=sc, t_sq=t_sq):
                e.wait(t_sq)
                ctx.inc(e.matmul(ps_row[0:1, sc * 512:(sc + 1) * 512], p.ones_b[:, 0:1], sqs[:],
                                 start=(ht == 0), stop=(ht == NDT - 1)), "t")
            on(ctx, "tensor", pe_s2)
            sq_toks[(sc, ht)] = ctx.tok("t")
            t_mm2 = ctx.tok("t")
            def s_gate(e, t_mm2=t_mm2):
                e.wait(t_mm2)
            on(ctx, "scalar", s_gate)
    t_sum2 = ctx.tok("t")
    def s_ev2(e):
        e.wait(t_sum2)
        ctx.inc(e.copy(out=rrow[0:1, 0:T], in_=ps_row[:]), "s")
    on(ctx, "scalar", s_ev2)
    t_r2tok = ctx.tok("s")

    mu_r = dbc[0:1, 0:T]
    r_r = rrow[0:1, 0:T]
    def v_rows_fixed(e):
        e.wait(t_r2tok, t_scan)
        e.tensor_scalar(out=mu_r, in0=mu_r, scalar1=1.0 / D, scalar2=None, op0=ALU.mult)
        e.tensor_scalar(out=r_r, in0=r_r, scalar1=1.0 / D, scalar2=None, op0=ALU.mult)
        e.tensor_tensor(out=nmr_b[0:1, 0:T], in0=mu_r, in1=mu_r, op=ALU.mult)
        inst = e.tensor_tensor(out=r_r, in0=r_r, in1=nmr_b[0:1, 0:T], op=ALU.subtract)
        ctx.inc(inst, "v")
    on(ctx, "vector", v_rows_fixed)
    t_var = ctx.tok("v")
    def s_rr(e):
        e.wait(t_var)
        e.activation(out=r_r, in_=r_r, func=AF.Ln, bias=p.fc1(1e-6), scale=1.0)
        e.activation(out=rinv_b[0:1, 0:T], in_=r_r, func=AF.Exp, scale=0.5)
        ctx.inc(e.activation(out=r_r, in_=r_r, func=AF.Exp, scale=-0.5), "s")
    on(ctx, "scalar", s_rr)
    t_rr = ctx.tok("s")
    def v_nm(e):
        e.wait(t_rr)
        e.tensor_scalar(out=nmr_b[0:1, 0:T], in0=mu_r, scalar1=-1.0, scalar2=None, op0=ALU.mult)
        ctx.inc(e.memset(ones512[:], 1.0), "v")
    on(ctx, "vector", v_nm)
    t_nm = ctx.tok("v")

    if dbg_upto == "scan":
        def g_d(e):
            e.wait(t_scan, t_xnT)
            ctx.inc(e.dma_start(out=dbg_aps["d_u"], in_=ubuf[:]), "qg", 16)
            ctx.inc(e.dma_start(out=dbg_aps["d_cx"], in_=cxbuf[:]), "qg", 16)
            ctx.inc(e.dma_start(out=dbg_aps["d_xnT"], in_=p.xnT[:]), "qg", 16)
            ctx.inc(e.dma_start(out=dbg_aps["d_ub"], in_=ubias[:]), "qg", 16)
            ctx.inc(e.dma_start(out=dbg_aps["d_ratio"], in_=ratio[:]), "qg", 16)
            e.wait(ctx.tok("qg"))
        on(ctx, "gpsimd", g_d)
        return ctx.tok("qg")
    wov = io["Wo"].rearrange("(j p) d2 -> j p d2", p=128)
    cast2 = []
    for j in range(NDT):
        def g_w(e, j=j):
            e.wait(t_u)
            if j >= 1:
                e.wait(cast2[j - 1])
            ctx.inc(e.dma_start(out=stage[:], in_=wov[j]), "qg", 16)
        on(ctx, "gpsimd", g_w)
        t_l = ctx.tok("qg")
        def s_c2(e, j=j, t_l=t_l):
            e.wait(t_l)
            cast2.append(ctx.inc(e.activation(out=wbuf[:, j * D:(j + 1) * D], in_=stage[:],
                                              func=AF.Identity, scale=c_lncw[:, j:j + 1], bias=0.0), "s"))
        on(ctx, "scalar", s_c2)
    t_wo = ctx.tok("s")

    def pe_bwo(e):
        e.wait(t_wo, t_ratio, t_r2tok)
        for c in range(2):
            for j in range(NDT):
                inst = e.matmul(ps_row[0:1, c * 512:(c + 1) * 512], ratio[:, NDT + j:NDT + j + 1],
                                wbuf[:, j * D + c * 512:j * D + (c + 1) * 512],
                                start=(j == 0), stop=(j == NDT - 1))
        ctx.inc(inst, "t")
    on(ctx, "tensor", pe_bwo)

    def s_bwo(e):
        e.wait(ctx.tok("t"))
        ctx.inc(e.copy(out=bwo_b[0:1, 0:D], in_=ps_row[0:1, 0:D]), "s")
    on(ctx, "scalar", s_bwo)
    t_bwo = ctx.tok("s")

    for c in range(2):
        def pe_ws(e, c=c):
            e.wait(t_wo, t_bwo)
            for j in range(NDT):
                inst = e.matmul(ps_row[0:1, c * 512:(c + 1) * 512], p.ones_b[:, 0:1],
                                wbuf[:, j * D + c * 512:j * D + (c + 1) * 512],
                                start=(j == 0), stop=(j == NDT - 1))
            ctx.inc(inst, "t")
        on(ctx, "tensor", pe_ws)
    t_ws = ctx.tok("t")
    def s_ws(e):
        e.wait(t_ws)
        ctx.inc(e.copy(out=wsum_b[0:1, 0:D], in_=ps_row[0:1, 0:D]), "s")
    on(ctx, "scalar", s_ws)
    t_ws2 = ctx.tok("s")

    for c in range(4):
        def ev(e, ps, c=c):
            return e.tensor_copy(out=rbc[:, c * 512:(c + 1) * 512], in_=ps[:, 0:512])
        mm.step([t_rr, t_nm], [(p.ones_f[0:1, :], rrow[0:1, c * 512:(c + 1) * 512])], ev,
                eng="vector", sem="v")
    t_rbc = ctx.tok("v")

    for dt in range(NDT):
        for sc in range(4):
            mms = [(wbuf[:, j * D + dt * 128:j * D + (dt + 1) * 128],
                    cxbuf[:, j * T + sc * 512:j * T + (sc + 1) * 512]) for j in range(NDT)]
            mms.append((wsum_b[0:1, dt * 128:(dt + 1) * 128], nmr_b[0:1, sc * 512:(sc + 1) * 512]))
            mms.append((bwo_b[0:1, dt * 128:(dt + 1) * 128], rinv_b[0:1, sc * 512:(sc + 1) * 512]))
            def ev(e, ps, dt=dt, sc=sc):
                return e.tensor_tensor(out=ubuf[:, dt * T + sc * 512:dt * T + (sc + 1) * 512],
                                       in0=ps[:, 0:512], in1=rbc[:, sc * 512:(sc + 1) * 512],
                                       op=ALU.mult)
            mm.step([t_wo, t_ws2, t_nm, t_rbc, t_scan], mms, ev, eng="vector", sem="v")
    t_om = ctx.tok("v")

    if dbg_upto == "om":
        def g_d(e):
            e.wait(t_om)
            ctx.inc(e.dma_start(out=dbg_aps["d_u"], in_=ubuf[:]), "qg", 16)
            ctx.inc(e.dma_start(out=dbg_aps["d_cx"], in_=cxbuf[:]), "qg", 16)
            ctx.inc(e.dma_start(out=dbg_aps["d_xnT"], in_=p.xnT[:]), "qg", 16)
            ctx.inc(e.dma_start(out=dbg_aps["d_ub"], in_=ubias[:]), "qg", 16)
            ctx.inc(e.dma_start(out=dbg_aps["d_ratio"], in_=ratio[:]), "qg", 16)
            ctx.inc(e.dma_start(out=dbg_aps["d_rrow"], in_=rrow[:]), "qg", 16)
            ctx.inc(e.dma_start(out=dbg_aps["d_murow"], in_=dbc[0:1, 0:T]), "qg", 16)
            e.wait(ctx.tok("qg"))
        on(ctx, "gpsimd", g_d)
        return ctx.tok("qg")
    def s_xt(e):
        e.wait(t_om)
        for j in range(NDT):
            inst = e.activation(out=p.xnT[:, j * T:(j + 1) * T], in_=p.xnT[:, j * T:(j + 1) * T],
                                func=AF.Identity, scale=c_ln1w[:, j:j + 1], bias=c_ln1b[:, j:j + 1])
        ctx.inc(inst, "s")
    on(ctx, "scalar", s_xt)
    t_xt = ctx.tok("s")

    wgv = io["Wg"].rearrange("(j p) g -> j p g", p=128)
    bg_row = sb("bgrow", [1, D], BF16)
    bg_f = sb("bgf", [1, D], F32)
    def g_bg(e):
        ctx.inc(e.dma_start(out=bg_f[:], in_=io["bg_row"]), "qg", 16)
    on(ctx, "gpsimd", g_bg)
    t_bgl = ctx.tok("qg")
    def s_bg(e):
        e.wait(t_bgl)
        ctx.inc(e.copy(out=bg_row[:], in_=bg_f[:]), "s")
    on(ctx, "scalar", s_bg)
    t_bg = ctx.tok("s")
    cast3 = []
    for j in range(16):
        def g_w(e, j=j):
            e.wait(t_om)
            if j >= 1:
                e.wait(cast3[j - 1])
            ctx.inc(e.dma_start(out=stage[:], in_=wgv[j]), "qg", 16)
        on(ctx, "gpsimd", g_w)
        t_l = ctx.tok("qg")
        def s_c3(e, j=j, t_l=t_l):
            e.wait(t_l)
            cast3.append(ctx.inc(e.copy(out=cxbuf[:, j * D:(j + 1) * D], in_=stage[:]), "s"))
        on(ctx, "scalar", s_c3)
    t_wg = ctx.tok("s")

    ystg = sb("ystg", [128, NDT * 512], BF16)
    substg = ystg
    onesr = sb("onesr", [1, 512], BF16)
    def v_or(e):
        ctx.inc(e.memset(onesr[:], 1.0), "v")
    on(ctx, "vector", v_or)
    t_or = ctx.tok("v")
    gstg = sb("gstg", [128, NDT * 512], BF16)
    r3bc = sb("r3bc", [128, 512], F32)
    n3bc = sb("n3bc", [128, 512], F32)
    xc_toks, out_toks = [], []
    last_tok = None
    for sc in range(4):
        y_toks = []
        for gt in range(NDT):
            mms = [(cxbuf[:, j * D + gt * 128:j * D + (gt + 1) * 128],
                    (p.xnT if j < NDT else ubuf)[:, (j % NDT) * T + sc * 512:(j % NDT) * T + (sc + 1) * 512])
                   for j in range(16)]
            mms.append((bg_row[0:1, gt * 128:(gt + 1) * 128], onesr[0:1, 0:512]))
            def ev(e, ps, gt=gt, sc=sc):
                return e.copy(out=ystg[:, gt * 512:(gt + 1) * 512], in_=ps[:, 0:512])
            y_toks.append(mm.step([t_xt, t_wg, t_bg, t_or], mms, ev))
        t_y = ctx.tok("s")
        def pe_sy(e, sc=sc, t_y=t_y):
            e.wait(t_y)
            for gt in range(NDT):
                inst = e.matmul(ps_row[0:1, 0:512], p.ones_b[:, 0:1], ystg[:, gt * 512:(gt + 1) * 512],
                                start=(gt == 0), stop=(gt == NDT - 1))
            ctx.inc(inst, "t")
        on(ctx, "tensor", pe_sy)
        t_sy = ctx.tok("t")
        def s_evy(e, t_sy=t_sy):
            e.wait(t_sy)
            ctx.inc(e.copy(out=dbc[0:1, 0:512], in_=ps_row[0:1, 0:512]), "s")
        on(ctx, "scalar", s_evy)
        t_ry = ctx.tok("s")
        sq2_toks = []
        for gt in range(NDT):
            def s_sq2(e, gt=gt, t_ry=t_ry):
                e.wait(t_ry)
                if sq2_toks:
                    e.wait(sq2_toks[-1])
                ctx.inc(e.activation(out=sqs[:], in_=ystg[:, gt * 512:(gt + 1) * 512], func=AF.Square), "s")
            on(ctx, "scalar", s_sq2)
            t_sq = ctx.tok("s")
            def pe_sq2(e, gt=gt, t_sq=t_sq):
                e.wait(t_sq)
                ctx.inc(e.matmul(ps_row[0:1, 0:512], p.ones_b[:, 0:1], sqs[:],
                                 start=(gt == 0), stop=(gt == NDT - 1)), "t")
            on(ctx, "tensor", pe_sq2)
            sq2_toks.append(ctx.tok("t"))
        t_sy2 = ctx.tok("t")
        def s_evy2(e, t_sy2=t_sy2):
            e.wait(t_sy2)
            ctx.inc(e.copy(out=dbc[0:1, 512:1024], in_=ps_row[0:1, 0:512]), "s")
        on(ctx, "scalar", s_evy2)
        t_ry2 = ctx.tok("s")
        def v_r3(e, t_ry2=t_ry2):
            e.wait(t_ry2)
            e.tensor_scalar(out=dbc[0:1, 0:512], in0=dbc[0:1, 0:512], scalar1=1.0 / D,
                            scalar2=None, op0=ALU.mult)
            e.tensor_scalar(out=dbc[0:1, 512:1024], in0=dbc[0:1, 512:1024], scalar1=1.0 / D,
                            scalar2=None, op0=ALU.mult)
            e.tensor_tensor(out=dbc[0:1, 1024:1536], in0=dbc[0:1, 0:512], in1=dbc[0:1, 0:512],
                            op=ALU.mult)
            inst = e.tensor_tensor(out=dbc[0:1, 512:1024], in0=dbc[0:1, 512:1024],
                                   in1=dbc[0:1, 1024:1536], op=ALU.subtract)
            ctx.inc(inst, "v")
        on(ctx, "vector", v_r3)
        t_v3 = ctx.tok("v")
        def s_r3(e, t_v3=t_v3):
            e.wait(t_v3)
            e.activation(out=dbc[0:1, 512:1024], in_=dbc[0:1, 512:1024], func=AF.Ln,
                         bias=p.fc1(1e-6), scale=1.0)
            ctx.inc(e.activation(out=dbc[0:1, 512:1024], in_=dbc[0:1, 512:1024],
                                 func=AF.Exp, scale=-0.5), "s")
        on(ctx, "scalar", s_r3)
        t_sr3 = ctx.tok("s")
        def v_n3(e, t_sr3=t_sr3):
            e.wait(t_sr3)
            ctx.inc(e.scalar_tensor_tensor(out=dbc[0:1, 0:512], in0=dbc[0:1, 0:512], scalar=-1.0,
                                           in1=dbc[0:1, 512:1024], op0=ALU.mult, op1=ALU.mult), "v")
        on(ctx, "vector", v_n3)
        t_n3 = ctx.tok("v")
        def ev_r(e, ps):
            return e.tensor_copy(out=r3bc[:], in_=ps[:, 0:512])
        mm.step([t_n3], [(p.ones_f[0:1, :], dbc[0:1, 512:1024])], ev_r, eng="vector", sem="v")
        def ev_n(e, ps):
            return e.tensor_copy(out=n3bc[:], in_=ps[:, 0:512])
        mm.step([t_n3], [(p.ones_f[0:1, :], dbc[0:1, 0:512])], ev_n, eng="vector", sem="v")
        t_bc3 = ctx.tok("v")
        g_toks = []
        for gt in range(NDT):
            def v_yh(e, gt=gt, t_bc3=t_bc3):
                e.wait(t_bc3)
                e.scalar_tensor_tensor(out=sqs[:], in0=ystg[:, gt * 512:(gt + 1) * 512], scalar=1.0,
                                       in1=r3bc[:], op0=ALU.mult, op1=ALU.mult)
                inst = e.tensor_tensor(out=sqs[:], in0=sqs[:], in1=n3bc[:], op=ALU.add)
                ctx.inc(inst, "v")
            on(ctx, "vector", v_yh)
            t_yh = ctx.tok("v")
            def s_sig(e, gt=gt, t_yh=t_yh):
                e.wait(t_yh)
                ctx.inc(e.activation(out=gstg[:, gt * 512:(gt + 1) * 512], in_=sqs[:], func=AF.Sigmoid,
                                     scale=c_lngw[:, gt:gt + 1], bias=c_lngb[:, gt:gt + 1]), "s")
            on(ctx, "scalar", s_sig)
            t_sg = ctx.tok("s")
            def v_gate2(e, t_sg=t_sg):
                e.wait(t_sg)
            on(ctx, "vector", v_gate2)
            g_toks.append(t_sg)
        comb_toks = []
        for dt2 in range(NDT):
            def v_cb(e, dt2=dt2, sc=sc, g_toks=g_toks):
                e.wait(g_toks[dt2])
                a = p.xnT[:, dt2 * T + sc * 512:dt2 * T + (sc + 1) * 512]
                o = ubuf[:, dt2 * T + sc * 512:dt2 * T + (sc + 1) * 512]
                d_ = substg[:, dt2 * 512:(dt2 + 1) * 512]
                e.tensor_tensor(out=d_, in0=o, in1=a, op=ALU.subtract)
                e.tensor_tensor(out=d_, in0=d_, in1=gstg[:, dt2 * 512:(dt2 + 1) * 512], op=ALU.mult)
                inst = e.tensor_tensor(out=d_, in0=d_, in1=a, op=ALU.add)
                ctx.inc(inst, "v")
            on(ctx, "vector", v_cb)
            comb_toks.append(ctx.tok("v"))
        for q in range(4):
            ti = sc * 4 + q
            def g_lx(e, ti=ti):
                if len(out_toks) >= 2:
                    e.wait(out_toks[-2])
                e.wait(t_xnT)
                xc_toks.append(ctx.inc(e.dma_start(out=xfs[:, (ti % 2) * D:(ti % 2 + 1) * D], in_=xv[ti]), "qs", 16))
            on(ctx, "sync", g_lx)
            base_v = ctx.tok("v")
            def pe_tr(e, q=q, comb_toks=comb_toks, base_v=base_v):
                e.wait(*comb_toks)
                e.wait(base_v)
                for j in range(NDT):
                    inst = e.transpose(p.pst[:, j * 128:(j + 1) * 128],
                                       substg[:, j * 512 + q * 128:j * 512 + (q + 1) * 128], p.ident[:])
                ctx.inc(inst, "t")
            on(ctx, "tensor", pe_tr)
            t_tr = ctx.tok("t")
            def v_acc(e, ti=ti, t_tr=t_tr):
                e.wait(t_tr, xc_toks[-1])
                sl = xfs[:, (ti % 2) * D:(ti % 2 + 1) * D]
                ctx.inc(e.scalar_tensor_tensor(out=sl, in0=p.pst[:], scalar=1.0, in1=sl,
                                               op0=ALU.mult, op1=ALU.add), "v")
            on(ctx, "vector", v_acc)
            t_va = ctx.tok("v")
            def g_ox(e, ti=ti, t_va=t_va):
                e.wait(t_va)
                out_toks.append(ctx.inc(e.dma_start(out=xov[ti], in_=xfs[:, (ti % 2) * D:(ti % 2 + 1) * D]), "qg", 16))
            on(ctx, "gpsimd", g_ox)
            last_tok = ctx.tok("qg")
    def g_fin(e):
        e.wait(ctx.tok("qg"))
    on(ctx, "gpsimd", g_fin)
    return ctx.tok("qg")


def emit_l1_attn_a(p, io, x_in_ap, x_out_ap, tag):
    ctx = p.ctx
    u_ = tag
    sb = lambda n, s, d: ctx.sbuf(f"{u_}_{n}", s, d)
    psm = lambda n, s, d: ctx.psum(f"{u_}_{n}", s, d)
    nc = ctx.nc

    xv = x_in_ap.rearrange("(i p) d -> i p d", p=128)
    xov = x_out_ap.rearrange("(i p) d -> i p d", p=128)
    xfs = sb("xfs", [128, 2 * D], F32)

    ld_toks, st_done = [], []
    for i in range(NTOK):
        def g_l(e, i=i):
            if i >= 2:
                e.wait(st_done[i - 2])
            ld_toks.append(ctx.inc(e.dma_start(out=xfs[:, (i % 2) * D:(i % 2 + 1) * D], in_=xv[i]), "qs", 16))
        on(ctx, "sync", g_l)
        def v_st(e, i=i):
            e.wait(ld_toks[i])
            st = p.ln_stats[:, i * 12:(i + 1) * 12]
            e.bn_stats(out=st[:, 0:6], in_=xfs[:, (i % 2) * D:(i % 2) * D + 512])
            inst = e.bn_stats(out=st[:, 6:12], in_=xfs[:, (i % 2) * D + 512:(i % 2 + 1) * D])
            st_done.append(ctx.inc(inst, "v"))
        on(ctx, "vector", v_st)
    def v_ag(e):
        e.wait(st_done[NTOK - 1])
        for i in range(NTOK):
            inst = e.bn_aggr(out=p.ln_mv[:, 2 * i:2 * i + 2], in_=p.ln_stats[:, i * 12:(i + 1) * 12])
        ctx.inc(inst, "v")
    on(ctx, "vector", v_ag)
    t_stats = ctx.tok("v")
    mvh = p.ln_mv[:].rearrange("p (i two) -> p i two", two=2)
    mu0, var0 = mvh[:, :, 0], mvh[:, :, 1]
    def s_rstd(e):
        e.wait(t_stats)
        e.activation(out=p.ln_rstd[:], in_=var0, func=AF.Ln, bias=p.fc(EPS1 * EPS1), scale=1.0 + EPS1)
        ctx.inc(e.activation(out=p.ln_rstd[:], in_=p.ln_rstd[:], func=AF.Exp, scale=-0.5), "s")
    on(ctx, "scalar", s_rstd)
    t_rstd0 = ctx.tok("s")
    def v_nmr0(e):
        e.wait(t_rstd0)
        ctx.inc(e.scalar_tensor_tensor(out=p.ln_nmr[:], in0=mu0, scalar=-1.0,
                                       in1=p.ln_rstd[:], op0=ALU.mult, op1=ALU.mult), "v")
    on(ctx, "vector", v_nmr0)
    t_nmr = ctx.tok("v")

    xa_toks = []
    S0 = ctx.cnt["s"]
    def producers(i):
        def g_l2(e, i=i):
            e.wait(t_nmr)
            if i >= 2:
                e.wait(("s", S0 + (i - 1) * 2 + 1))
            xa_toks.append(ctx.inc(e.dma_start(out=xfs[:, (i % 2) * D:(i % 2 + 1) * D], in_=xv[i]), "qs", 16))
        on(ctx, "sync", g_l2)
        def fn(e, i=i):
            e.wait(xa_toks[i], t_nmr)
            return e.activation(out=p.xstg[:, (i % 2) * D:(i % 2 + 1) * D],
                                in_=xfs[:, (i % 2) * D:(i % 2 + 1) * D], func=AF.Identity,
                                scale=p.ln_rstd[:, i:i + 1], bias=p.ln_nmr[:, i:i + 1])
        return ("scalar", "s", fn)
    t_xnT = emit_xnT(p, producers)

    qT = sb("qT", [128, NDT * T], BF16)
    kT = sb("kT", [128, NDT * T], BF16)
    gT = sb("gT", [128, NDT * T], BF16)
    vtok = sb("vtok", [128, NTOK * D], BF16)
    Sm = sb("Sm", [128, NDT * D], BF16)
    wchunk = sb("wchunk", [128, NDT * 128], BF16)
    stage = sb("stage", [128, 128], F32)
    cols = sb("cols", [128, 2 * NDT], F32)
    ratio = sb("ratio", [128, NDT], BF16)
    rscr = sb("rscr", [128, NDT], F32)
    borig = sb("borig", [1, 128], F32)
    bchunk = sb("bchunk", [1, 128], BF16)
    onesr = sb("onesr", [1, 512], BF16)
    scr1 = sb("scr1", [128, T], BF16)
    scr2 = scr1
    escr = sb("escr", [128, 512], BF16)
    escr2 = sb("escr2", [128, 512], BF16)
    den_acc = sb("den_acc", [1, T], BF16)
    dcol = sb("dcol", [128, NTOK], BF16)
    kqm = sb("kqm", [128, 128], BF16)
    bpbc = sb("bpbc", [128, D], BF16)
    dram_den = nc.dram_tensor(f"{u_}_denb", [1, T], BF16)

    mm = MMPool(ctx, u_)
    ps_num = [psm("pnumA", [128, 512], F32), psm("pnumB", [128, 512], F32)]
    ps_row = psm("psrow", [1, 512], F32)

    def g_small(e):
        ctx.inc(e.dma_start(out=cols[:, 0:NDT], in_=io["lng_c"]), "qg", 16)
        ctx.inc(e.dma_start(out=cols[:, NDT:2 * NDT], in_=io["lnb_c"]), "qg", 16)

    on(ctx, "gpsimd", g_small)
    t_small = ctx.tok("qg")
    bp_cast = []
    for hh in range(8):
        def g_bp(e, hh=hh):
            if bp_cast:
                e.wait(bp_cast[-1])
            ctx.inc(e.dma_start(out=stage[:, 0:128], in_=io["b_proj_bc"][:, hh * 128:(hh + 1) * 128]), "qg", 16)
        on(ctx, "gpsimd", g_bp)
        t_lbp = ctx.tok("qg")
        def s_bp(e, hh=hh, t_lbp=t_lbp):
            e.wait(t_lbp)
            bp_cast.append(ctx.inc(e.copy(out=bpbc[:, hh * 128:(hh + 1) * 128], in_=stage[:, 0:128]), "s"))
        on(ctx, "scalar", s_bp)
    t_bpbc = ctx.tok("s")

    def v_ratio(e):
        e.wait(t_small)
        tk = ctx.inc(e.reciprocal(out=rscr[:], in_=cols[:, 0:NDT]), "v")
        e.wait(tk)
        e.tensor_tensor(out=ratio[:], in0=cols[:, NDT:2 * NDT], in1=rscr[:], op=ALU.mult)
        ctx.inc(e.memset(onesr[:], 1.0), "v")
    on(ctx, "vector", v_ratio)
    t_ratio = ctx.tok("v")

    def stream_chunks(wap, n_fs, fs_off_in_w, dst_kind, brow_off, extra_dep, evac_maker,
                      bias_src=None, brow_off0=0):
        last = None
        for fs in range(n_fs):
            col0 = fs_off_in_w + fs * 128
            wv_ = wap[:, col0:col0 + 128].rearrange("(j p) f -> j p f", p=128)
            casts = []
            for j in range(NDT):
                def g_w(e, j=j, wv_=wv_, last=last):
                    if j == 0 and last is not None:
                        e.wait(last)
                    if j >= 1:
                        e.wait(casts[j - 1])
                    ctx.inc(e.dma_start(out=stage[:, 0:128], in_=wv_[j]), "qg", 16)
                on(ctx, "gpsimd", g_w)
                t_l = ctx.tok("qg")
                def s_c(e, j=j, t_l=t_l):
                    e.wait(t_l, t_small)
                    casts.append(ctx.inc(e.activation(out=wchunk[:, j * 128:(j + 1) * 128],
                                                      in_=stage[:, 0:128], func=AF.Identity,
                                                      scale=cols[:, j:j + 1], bias=0.0), "s"))
                on(ctx, "scalar", s_c)
            t_wc = ctx.tok("s")
            def pe_bias(e, t_wc=t_wc):
                e.wait(t_wc, t_ratio)
                for j in range(NDT):
                    inst = e.matmul(ps_row[0:1, 0:128], ratio[:, j:j + 1],
                                    wchunk[:, j * 128:(j + 1) * 128], start=(j == 0), stop=(j == NDT - 1))
                ctx.inc(inst, "t")
            on(ctx, "tensor", pe_bias)
            t_pb = ctx.tok("t")
            def g_bo(e, fs=fs, t_pb=t_pb):
                e.wait(t_pb)
                ctx.inc(e.dma_start(out=borig[:], in_=bias_src[0:1, (brow_off - brow_off0) + fs * 128:(brow_off - brow_off0) + (fs + 1) * 128]), "qg", 16)
            on(ctx, "gpsimd", g_bo)
            t_bo = ctx.tok("qg")
            def v_bias(e, fs=fs, t_pb=t_pb, t_bo=t_bo):
                e.wait(t_pb, t_bo)
                ctx.inc(e.tensor_tensor(out=bchunk[0:1, 0:128], in0=ps_row[0:1, 0:128],
                                        in1=borig[:], op=ALU.add), "v")
            on(ctx, "vector", v_bias)
            t_bc = ctx.tok("v")
            for tc in range(4):
                if dst_kind == "feat":
                    mms = [(wchunk[:, j * 128:(j + 1) * 128],
                            p.xnT[:, j * T + tc * 512:j * T + (tc + 1) * 512]) for j in range(NDT)]
                    mms.append((bchunk[0:1, 0:128], onesr[0:1, 0:512]))
                else:
                    ti = None
                ev = evac_maker(fs, tc)
                last = mm.step([t_xnT, t_wc, t_bc, t_ratio], mms, ev[0], eng=ev[1], sem=ev[2])
        return last

    def gate_ev(fs, tc):
        def f(e, ps, fs=fs, tc=tc):
            return e.activation(out=gT[:, fs * T + tc * 512:fs * T + (tc + 1) * 512],
                                in_=ps[:, 0:512], func=AF.Sigmoid)
        return (f, "scalar", "s")
    t_gate = stream_chunks(io["w_gate"], NDT, 0, "feat", 3 * D, [], gate_ev,
                           bias_src=io["b_gate_row"], brow_off0=3 * D)

    def q_ev(fs, tc):
        def f(e, ps, fs=fs, tc=tc):
            e.activation(out=escr[:], in_=ps[:, 0:512], func=AF.Relu)
            e.activation(out=escr2[:], in_=ps[:, 0:512], func=AF.Relu, scale=-1.0)
            return e.activation(out=escr2[:], in_=escr2[:], func=AF.Exp, scale=-1.0)
        return (f, "scalar", "s")
    qsave = []
    def q_ev2(fs, tc):
        def f(e, ps, fs=fs, tc=tc):
            e.activation(out=escr[:], in_=ps[:, 0:512], func=AF.Relu)
            e.activation(out=escr2[:], in_=ps[:, 0:512], func=AF.Relu, scale=-1.0)
            inst = e.activation(out=escr2[:], in_=escr2[:], func=AF.Exp, scale=-1.0)
            return inst
        return (f, "scalar", "s")
    def q_stream():
        last = None
        for fs in range(NDT):
            col0 = 0 + fs * 128
            wv_ = io["w_qkv"][:, col0:col0 + 128].rearrange("(j p) f -> j p f", p=128)
            casts = []
            for j in range(NDT):
                def g_w(e, j=j, wv_=wv_, last=last):
                    if j == 0 and last is not None:
                        e.wait(last)
                    if j >= 1:
                        e.wait(casts[j - 1])
                    ctx.inc(e.dma_start(out=stage[:, 0:128], in_=wv_[j]), "qg", 16)
                on(ctx, "gpsimd", g_w)
                t_l = ctx.tok("qg")
                def s_c(e, j=j, t_l=t_l):
                    e.wait(t_l, t_small)
                    casts.append(ctx.inc(e.activation(out=wchunk[:, j * 128:(j + 1) * 128],
                                                      in_=stage[:, 0:128], func=AF.Identity,
                                                      scale=cols[:, j:j + 1], bias=0.0), "s"))
                on(ctx, "scalar", s_c)
            t_wc = ctx.tok("s")
            def pe_bias(e, t_wc=t_wc):
                e.wait(t_wc, t_ratio)
                for j in range(NDT):
                    inst = e.matmul(ps_row[0:1, 0:128], ratio[:, j:j + 1],
                                    wchunk[:, j * 128:(j + 1) * 128], start=(j == 0), stop=(j == NDT - 1))
                ctx.inc(inst, "t")
            on(ctx, "tensor", pe_bias)
            t_pb = ctx.tok("t")
            def g_bo(e, fs=fs, t_pb=t_pb):
                e.wait(t_pb)
                ctx.inc(e.dma_start(out=borig[:], in_=io["b_qkv_row"][0:1, fs * 128:(fs + 1) * 128]), "qg", 16)
            on(ctx, "gpsimd", g_bo)
            t_bo = ctx.tok("qg")
            def v_bias(e, fs=fs, t_pb=t_pb, t_bo=t_bo):
                e.wait(t_pb, t_bo)
                ctx.inc(e.tensor_tensor(out=bchunk[0:1, 0:128], in0=ps_row[0:1, 0:128],
                                        in1=borig[:], op=ALU.add), "v")
            on(ctx, "vector", v_bias)
            t_bc = ctx.tok("v")
            for tc in range(4):
                mms = [(wchunk[:, j * 128:(j + 1) * 128],
                        p.xnT[:, j * T + tc * 512:j * T + (tc + 1) * 512]) for j in range(NDT)]
                mms.append((bchunk[0:1, 0:128], onesr[0:1, 0:512]))
                tok_s = mm.step([t_xnT, t_bc], mms, q_ev2(fs, tc)[0], eng="scalar", sem="s")
                def v_add(e, fs=fs, tc=tc, tok_s=tok_s):
                    e.wait(tok_s)
                    ctx.inc(e.tensor_tensor(out=qT[:, fs * T + tc * 512:fs * T + (tc + 1) * 512],
                                            in0=escr[:], in1=escr2[:], op=ALU.add), "v")
                on(ctx, "vector", v_add)
                last = ctx.tok("v")
                def s_gate2(e, last=last):
                    e.wait(last)
                on(ctx, "scalar", s_gate2)
        return last
    t_q = q_stream()

    def k_stream():
        last = None
        for fs in range(NDT):
            col0 = D + fs * 128
            wv_ = io["w_qkv"][:, col0:col0 + 128].rearrange("(j p) f -> j p f", p=128)
            casts = []
            for j in range(NDT):
                def g_w(e, j=j, wv_=wv_, last=last):
                    if j == 0 and last is not None:
                        e.wait(last)
                    if j >= 1:
                        e.wait(casts[j - 1])
                    ctx.inc(e.dma_start(out=stage[:, 0:128], in_=wv_[j]), "qg", 16)
                on(ctx, "gpsimd", g_w)
                t_l = ctx.tok("qg")
                def s_c(e, j=j, t_l=t_l):
                    e.wait(t_l, t_small)
                    casts.append(ctx.inc(e.activation(out=wchunk[:, j * 128:(j + 1) * 128],
                                                      in_=stage[:, 0:128], func=AF.Identity,
                                                      scale=cols[:, j:j + 1], bias=0.0), "s"))
                on(ctx, "scalar", s_c)
            t_wc = ctx.tok("s")
            def pe_bias(e, t_wc=t_wc):
                e.wait(t_wc, t_ratio)
                for j in range(NDT):
                    inst = e.matmul(ps_row[0:1, 0:128], ratio[:, j:j + 1],
                                    wchunk[:, j * 128:(j + 1) * 128], start=(j == 0), stop=(j == NDT - 1))
                ctx.inc(inst, "t")
            on(ctx, "tensor", pe_bias)
            t_pb = ctx.tok("t")
            def g_bo(e, fs=fs, t_pb=t_pb):
                e.wait(t_pb)
                ctx.inc(e.dma_start(out=borig[:], in_=io["b_qkv_row"][0:1, D + fs * 128:D + (fs + 1) * 128]), "qg", 16)
            on(ctx, "gpsimd", g_bo)
            t_bo = ctx.tok("qg")
            def v_bias(e, fs=fs, t_pb=t_pb, t_bo=t_bo):
                e.wait(t_pb, t_bo)
                ctx.inc(e.tensor_tensor(out=bchunk[0:1, 0:128], in0=ps_row[0:1, 0:128],
                                        in1=borig[:], op=ALU.add), "v")
            on(ctx, "vector", v_bias)
            t_bc = ctx.tok("v")
            for tc in range(4):
                mms = [(wchunk[:, j * 128:(j + 1) * 128],
                        p.xnT[:, j * T + tc * 512:j * T + (tc + 1) * 512]) for j in range(NDT)]
                mms.append((bchunk[0:1, 0:128], onesr[0:1, 0:512]))
                def kev(e, ps, fs=fs, tc=tc):
                    return e.tensor_tensor(out=escr[:], in0=ps[:, 0:512],
                                           in1=gT[:, fs * T + tc * 512:fs * T + (tc + 1) * 512],
                                           op=ALU.mult)
                tok_v = mm.step([t_xnT, t_bc, t_gate], mms, kev, eng="vector", sem="v")
                def s_elu(e, fs=fs, tc=tc, tok_v=tok_v):
                    e.wait(tok_v)
                    e.activation(out=escr2[:], in_=escr[:], func=AF.Relu, scale=-1.0)
                    e.activation(out=escr2[:], in_=escr2[:], func=AF.Exp, scale=-1.0)
                    ctx.inc(e.activation(out=escr[:], in_=escr[:], func=AF.Relu), "s")
                on(ctx, "scalar", s_elu)
                t_se = ctx.tok("s")
                def v_add(e, fs=fs, tc=tc, t_se=t_se):
                    e.wait(t_se)
                    ctx.inc(e.tensor_tensor(out=kT[:, fs * T + tc * 512:fs * T + (tc + 1) * 512],
                                            in0=escr[:], in1=escr2[:], op=ALU.add), "v")
                on(ctx, "vector", v_add)
                last = ctx.tok("v")
        return last
    t_k = k_stream()

    def v_stream():
        last = None
        for fs in range(NDT):
            col0 = 2 * D + fs * 128
            wv_ = io["w_qkv"][:, col0:col0 + 128].rearrange("(j p) f -> j p f", p=128)
            casts = []
            for j in range(NDT):
                def g_w(e, j=j, wv_=wv_, last=last):
                    if j == 0 and last is not None:
                        e.wait(last)
                    if j >= 1:
                        e.wait(casts[j - 1])
                    ctx.inc(e.dma_start(out=stage[:, 0:128], in_=wv_[j]), "qg", 16)
                on(ctx, "gpsimd", g_w)
                t_l = ctx.tok("qg")
                def s_c(e, j=j, t_l=t_l):
                    e.wait(t_l, t_small)
                    casts.append(ctx.inc(e.activation(out=wchunk[:, j * 128:(j + 1) * 128],
                                                      in_=stage[:, 0:128], func=AF.Identity,
                                                      scale=cols[:, j:j + 1], bias=0.0), "s"))
                on(ctx, "scalar", s_c)
            t_wc = ctx.tok("s")
            def pe_bias(e, t_wc=t_wc):
                e.wait(t_wc, t_ratio)
                for j in range(NDT):
                    inst = e.matmul(ps_row[0:1, 0:128], ratio[:, j:j + 1],
                                    wchunk[:, j * 128:(j + 1) * 128], start=(j == 0), stop=(j == NDT - 1))
                ctx.inc(inst, "t")
            on(ctx, "tensor", pe_bias)
            t_pb = ctx.tok("t")
            def g_bo(e, fs=fs, t_pb=t_pb):
                e.wait(t_pb)
                ctx.inc(e.dma_start(out=borig[:], in_=io["b_qkv_row"][0:1, 2 * D + fs * 128:2 * D + (fs + 1) * 128]), "qg", 16)
            on(ctx, "gpsimd", g_bo)
            t_bo = ctx.tok("qg")
            def v_bias(e, fs=fs, t_pb=t_pb, t_bo=t_bo):
                e.wait(t_pb, t_bo)
                ctx.inc(e.tensor_tensor(out=bchunk[0:1, 0:128], in0=ps_row[0:1, 0:128],
                                        in1=borig[:], op=ALU.add), "v")
            on(ctx, "vector", v_bias)
            t_bc = ctx.tok("v")
            for tc in range(4):
                for q4 in range(4):
                    ti = tc * 4 + q4
                    mms = [(p.xnT[:, j * T + ti * 128:j * T + (i + 1) * 128] if False else p.xnT[:, j * T + ti * 128:j * T + (ti + 1) * 128],
                            wchunk[:, j * 128:(j + 1) * 128]) for j in range(NDT)]
                    mms.append((onesr[0:1, 0:128], bchunk[0:1, 0:128]))
                    def vev(e, ps, fs=fs, ti=ti):
                        return e.copy(out=vtok[:, ti * D + fs * 128:ti * D + (fs + 1) * 128],
                                      in_=ps[:, 0:128])
                    last = mm.step([t_xnT, t_bc], mms, vev, eng="scalar", sem="s", width=128)
        return last
    t_v = v_stream()

    den_toks = []
    def v_den0(e):
        ctx.inc(e.memset(den_acc[:], 1e-6), "v")
    on(ctx, "vector", v_den0)
    for kf in range(NDT):
        def v_scan(e, kf=kf):
            e.wait(t_k, t_q)
            tk = ctx.inc(e.tensor_tensor_scan(out=scr1[:], data0=p.ones_b[:, 0:1].to_broadcast([128, T]),
                                              data1=kT[:, kf * T:(kf + 1) * T], initial=0.0,
                                              op0=ALU.mult, op1=ALU.add), "v")
            e.wait(tk)
            ctx.inc(e.tensor_tensor(out=scr2[:], in0=qT[:, kf * T:(kf + 1) * T], in1=scr1[:],
                                    op=ALU.mult), "v")
        on(ctx, "vector", v_scan)
        t_qk = ctx.tok("v")
        for tc in range(4):
            def pe_d(e, kf=kf, tc=tc, t_qk=t_qk):
                e.wait(t_qk)
                if den_toks:
                    e.wait(den_toks[-1])
                ctx.inc(e.matmul(ps_row[0:1, 0:512], p.ones_b[:, 0:1],
                                 scr2[:, tc * 512:(tc + 1) * 512], start=True, stop=True), "t")
            on(ctx, "tensor", pe_d)
            t_pd = ctx.tok("t")
            def v_d(e, tc=tc, t_pd=t_pd):
                e.wait(t_pd)
                ctx.inc(e.tensor_tensor(out=den_acc[0:1, tc * 512:(tc + 1) * 512],
                                        in0=den_acc[0:1, tc * 512:(tc + 1) * 512],
                                        in1=ps_row[0:1, 0:512], op=ALU.add), "v")
            on(ctx, "vector", v_d)
            den_toks.append(ctx.tok("v"))
    t_den = ctx.tok("v")
    def s_den(e):
        e.wait(t_den)
        e.activation(out=den_acc[:], in_=den_acc[:], func=AF.Ln, bias=0.0)
        ctx.inc(e.activation(out=den_acc[:], in_=den_acc[:], func=AF.Exp, scale=-1.0), "s")
    on(ctx, "scalar", s_den)
    t_dr = ctx.tok("s")
    def pe_dc(e):
        e.wait(t_dr)
        for i in range(NTOK):
            inst = e.matmul(ps_num[0][:, i:i + 1], den_acc[0:1, i * 128:(i + 1) * 128],
                            p.ones_b[0:1, 0:1], start=True, stop=True)
        ctx.inc(inst, "t")
    on(ctx, "tensor", pe_dc)
    t_pdc = ctx.tok("t")
    def s_dc(e):
        e.wait(t_pdc)
        ctx.inc(e.copy(out=dcol[:], in_=ps_num[0][:, 0:NTOK]), "s")
    on(ctx, "scalar", s_dc)
    t_dcol = ctx.tok("s")

    ktok = gT
    ktok_done = []
    for ti in range(NTOK):
        base_s = ctx.tok("s")
        def pe_t(e, ti=ti, base_s=base_s):
            e.wait(t_k, t_gate)
            if ti > 0:
                e.wait(base_s)
            for j in range(NDT):
                inst = e.transpose(p.pst[:, j * 128:(j + 1) * 128],
                                   kT[:, j * T + ti * 128:j * T + (ti + 1) * 128], p.ident[:])
            ctx.inc(inst, "t")
        on(ctx, "tensor", pe_t)
        t_tr = ctx.tok("t")
        def s_ev(e, ti=ti, t_tr=t_tr):
            e.wait(t_tr)
            pv = p.pst[:].rearrange("p (j c) -> p j c", j=NDT)
            kv = ktok[:].rearrange("p (i c) -> p i c", i=NTOK)[:, ti, :].rearrange("p (j c) -> p j c", j=NDT)
            ctx.inc(e.copy(out=kv, in_=pv), "s")
        on(ctx, "scalar", s_ev)
        ktok_done.append(ctx.tok("s"))
    t_ktok = ctx.tok("s")

    numT = p.xnT
    def v_S0(e):
        e.wait(t_v)
        ctx.inc(e.memset(Sm[:], 0.0), "v")
    on(ctx, "vector", v_S0)
    t_S = ctx.tok("v")

    num_done = []
    for c in range(NTOK):
        def pe_kq(e, c=c):
            e.wait(t_k, t_q, t_ktok)
            if num_done:
                e.wait(num_done[-1])
            for j in range(NDT):
                inst = e.matmul(mm.ps[0][:, 0:128], kT[:, j * T + c * 128:j * T + (c + 1) * 128],
                                qT[:, j * T + c * 128:j * T + (c + 1) * 128],
                                start=(j == 0), stop=(j == NDT - 1))
            ctx.inc(inst, "t")
        on(ctx, "tensor", pe_kq)
        t_kq = ctx.tok("t")
        def v_mask(e, t_kq=t_kq):
            e.wait(t_kq, p.const_tok)
            ctx.inc(e.tensor_tensor(out=kqm[:], in0=mm.ps[0][:, 0:128], in1=p.triu[:], op=ALU.mult), "v")
        on(ctx, "vector", v_mask)
        t_km = ctx.tok("v")

        for half in range(2):
            def pe_n(e, c=c, half=half, t_km=t_km):
                e.wait(t_km, t_S)
                for vq in range(4):
                    vf = half * 4 + vq
                    for j in range(NDT):
                        e.matmul(ps_num[half][:, vq * 128:(vq + 1) * 128],
                                 Sm[:, j * D + vf * 128:j * D + (vf + 1) * 128],
                                 qT[:, j * T + c * 128:j * T + (c + 1) * 128],
                                 start=(j == 0), stop=False)
                    inst = e.matmul(ps_num[half][:, vq * 128:(vq + 1) * 128],
                                    vtok[:, c * D + vf * 128:c * D + (vf + 1) * 128],
                                    kqm[:], start=False, stop=True)
                ctx.inc(inst, "t")
            on(ctx, "tensor", pe_n)
            t_n = ctx.tok("t")
            def s_evn(e, c=c, half=half, t_n=t_n):
                e.wait(t_n)
                pv = ps_num[half][:, 0:512].rearrange("p (v c) -> p v c", v=4)
                nv = numT[:].rearrange("p (j c) -> p j c", j=NDT)[:, half * 4:(half + 1) * 4,
                                                                 c * 128:(c + 1) * 128]
                ctx.inc(e.copy(out=nv, in_=pv), "s")
            on(ctx, "scalar", s_evn)
            num_done.append(ctx.tok("s"))

        if c < NTOK - 1:
            for kf in range(NDT):
                def pe_su(e, c=c, kf=kf):
                    e.wait(num_done[-1])
                    inst = e.matmul(mm.ps[1][:, 0:512], ktok[:, c * D + kf * 128:c * D + (kf + 1) * 128],
                                    vtok[:, c * D:c * D + 512], start=True, stop=True)
                    ctx.inc(inst, "t")
                    inst = e.matmul(mm.ps[0][:, 0:512], ktok[:, c * D + kf * 128:c * D + (kf + 1) * 128],
                                    vtok[:, c * D + 512:(c + 1) * D], start=True, stop=True)
                    ctx.inc(inst, "t")
                on(ctx, "tensor", pe_su)
                t_su = ctx.tok("t")
                def v_su(e, kf=kf, t_su=t_su):
                    e.wait(t_su)
                    e.tensor_tensor(out=Sm[:, kf * D:kf * D + 512], in0=Sm[:, kf * D:kf * D + 512],
                                    in1=mm.ps[1][:, 0:512], op=ALU.add)
                    ctx.inc(e.tensor_tensor(out=Sm[:, kf * D + 512:(kf + 1) * D],
                                            in0=Sm[:, kf * D + 512:(kf + 1) * D],
                                            in1=mm.ps[0][:, 0:512], op=ALU.add), "v")
                on(ctx, "vector", v_su)
                t_S = ctx.tok("v")
                num_done.append(t_S)

    t_num = ctx.tok("s")

    wpb = Sm
    wvv = io["w_proj"].rearrange("(j p) d2 -> j p d2", p=128)
    wp_casts = []
    for j in range(NDT):
        for hh in range(8):
            def g_w(e, j=j, hh=hh):
                e.wait(t_num)
                if wp_casts:
                    e.wait(wp_casts[-1])
                ctx.inc(e.dma_start(out=stage[:, 0:128], in_=wvv[j][:, hh * 128:(hh + 1) * 128]), "qg", 16)
            on(ctx, "gpsimd", g_w)
            t_l = ctx.tok("qg")
            def s_c(e, j=j, hh=hh, t_l=t_l):
                e.wait(t_l)
                wp_casts.append(ctx.inc(e.copy(out=wpb[:, j * D + hh * 128:j * D + (hh + 1) * 128],
                                               in_=stage[:, 0:128]), "s"))
            on(ctx, "scalar", s_c)
    t_wp = ctx.tok("s")

    xc_toks, out_toks = [], []
    for ti in range(NTOK):
        def g_lx(e, ti=ti):
            if len(out_toks) >= 2:
                e.wait(out_toks[-2])
            e.wait(t_num)
            xc_toks.append(ctx.inc(e.dma_start(out=xfs[:, (ti % 2) * D:(ti % 2 + 1) * D], in_=xv[ti]), "qs", 16))
        on(ctx, "sync", g_lx)
        def v_bp(e, ti=ti):
            e.wait(xc_toks[ti], t_bpbc)
            ctx.inc(e.tensor_tensor(out=xfs[:, (ti % 2) * D:(ti % 2 + 1) * D],
                                    in0=xfs[:, (ti % 2) * D:(ti % 2 + 1) * D], in1=bpbc[:], op=ALU.add), "v")
        on(ctx, "vector", v_bp)
        t_vbp = ctx.tok("v")
        ev_toks = []
        for dc in range(2):
            mms = [(numT[:, j * T + ti * 128:j * T + (ti + 1) * 128],
                    wpb[:, j * D + dc * 512:j * D + (dc + 1) * 512]) for j in range(NDT)]
            def ev(e, ps, ti=ti, dc=dc):
                sl = xfs[:, (ti % 2) * D + dc * 512:(ti % 2) * D + dc * 512 + 512]
                return e.scalar_tensor_tensor(out=sl, in0=ps[:, 0:512], scalar=dcol[:, ti:ti + 1],
                                              in1=sl, op0=ALU.mult, op1=ALU.add)
            ev_toks.append(mm.step([t_num, t_wp, t_vbp, t_dcol], mms, ev, eng="vector", sem="v"))
        def g_ox(e, ti=ti, ev_toks=ev_toks):
            e.wait(*ev_toks)
            out_toks.append(ctx.inc(e.dma_start(out=xov[ti], in_=xfs[:, (ti % 2) * D:(ti % 2 + 1) * D]), "qg", 16))
        on(ctx, "gpsimd", g_ox)
    def g_fin(e):
        e.wait(ctx.tok("qg"))
    on(ctx, "gpsimd", g_fin)
    return ctx.tok("qg")


def emit_l1_ffn(p, io, tag, ncores=8):
    ctx = p.ctx
    u_ = tag
    sb = lambda n, s, d: ctx.sbuf(f"{u_}_{n}", s, d)
    psm = lambda n, s, d: ctx.psum(f"{u_}_{n}", s, d)
    nc = ctx.nc
    RD = 48

    t_nmr, t_xnT = emit_ln_t(p, p.ln_dep)

    qT = sb("qT", [128, T], BF16)
    rqT = sb("rqT", [48, T], BF16)
    rkT = sb("rkT", [48, 4096], BF16)
    sc_sb = sb("sc", [128, 4096], F32)
    w_sb = sb("w", [128, 4096], BF16)
    wTg = sb("wTg", [128, 3 * 4096], BF16)
    m8 = sb("m8", [128, 8], F32)
    vstg = sb("vstg", [128, D], F32)
    vb = sb("vb", [128, D], BF16)
    h1T = sb("h1T", [128, 2 * T], BF16)
    cols = sb("cols", [128, 16 * 6], F32)
    ccc = sb("ccc", [128, 4], F32)
    rowbuf = sb("rowbuf", [1, T], BF16)
    tiny = sb("tiny", [1, 8], F32)
    stg128 = sb("stg128", [128, 128], F32)
    aggscr = sb("aggscr", [128, 512], F32)
    mm = MMPool(ctx, u_)
    ps_agg = [psm(f"agg{k}", [128, 512], F32) for k in range(4)]
    wsums = sb("wsums", [128, NTOK], F32)
    dram_cc_in = nc.dram_tensor(f"{u_}_ccin", [1, 8], F32)
    dram_cc_out = nc.dram_tensor(f"{u_}_ccout", [1, 8], F32)

    c_lv = cols[:, 0:16]; c_nv = cols[:, 16:32]; c_sg = cols[:, 32:48]
    c_unc = cols[:, 48:64]; c_gate = cols[:, 64:80]; c_rp1 = cols[:, 80:96]

    def g_small(e):
        ctx.inc(e.dma_start(out=ccc[0:1, 0:1], in_=io["unc_b2"]), "qg", 16)
        ctx.inc(e.dma_start(out=ccc[0:1, 1:2], in_=io["gate_w1"]), "qg", 16)
        ctx.inc(e.dma_start(out=ccc[0:1, 2:3], in_=io["gate_bias"]), "qg", 16)
    on(ctx, "gpsimd", g_small)
    t_small = ctx.tok("qg")

    qpv = io["q_projT"].rearrange("(j p) f -> j p f", p=128)

    wq8 = sb("wq8", [128, NDT * 128], BF16)
    cast_t = []
    for j in range(NDT):
        def g_w(e, j=j):
            if j >= 1:
                e.wait(cast_t[j - 1])
            ctx.inc(e.dma_start(out=stg128[:], in_=qpv[j]), "qg", 16)
        on(ctx, "gpsimd", g_w)
        t_l = ctx.tok("qg")
        def s_c(e, j=j, t_l=t_l):
            e.wait(t_l)
            cast_t.append(ctx.inc(e.copy(out=wq8[:, j * 128:(j + 1) * 128], in_=stg128[:]), "s"))
        on(ctx, "scalar", s_c)
    t_wq = ctx.tok("s")
    for tc in range(4):
        mms = [(wq8[:, j * 128:(j + 1) * 128],
                p.xnT[:, j * T + tc * 512:j * T + (tc + 1) * 512]) for j in range(NDT)]
        def ev(e, ps, tc=tc):
            return e.copy(out=qT[:, tc * 512:(tc + 1) * 512], in_=ps[:, 0:512])
        mm.step([t_xnT, t_wq], mms, ev)
    t_q = ctx.tok("s")

    wr = sb("wr", [128, 48], BF16)
    def g_wr(e):
        ctx.inc(e.dma_start(out=stg128[:, 0:48], in_=io["router_wT"]), "qg", 16)
    on(ctx, "gpsimd", g_wr)
    t_lwr = ctx.tok("qg")
    def s_wr(e):
        e.wait(t_lwr)
        ctx.inc(e.copy(out=wr[:], in_=stg128[:, 0:48]), "s")
    on(ctx, "scalar", s_wr)
    t_wr = ctx.tok("s")

    import math
    for tc in range(4):
        mms = [(wr[:], qT[:, tc * 512:(tc + 1) * 512])]
        def ev(e, ps, tc=tc):
            return e.activation(out=rqT[0:48, tc * 512:(tc + 1) * 512], in_=ps[0:48, 0:512],
                                func=AF.Copy, scale=1.0 / math.sqrt(RD))
        mm.step([t_q, t_wr], mms, ev, out_p=48)
    t_rq = ctx.tok("s")

    akv = io["aux_keysT"]
    akstg = sb("akstg", [128, 512], F32)
    akb = sb("akb", [128, 512], BF16)
    for ck in range(8):
        def g_ak(e, ck=ck):
            if ck >= 1:
                e.wait(("s", ctx.cnt["s"]))
            ctx.inc(e.dma_start(out=akstg[:], in_=akv[:, ck * 512:(ck + 1) * 512]), "qg", 16)
        on(ctx, "gpsimd", g_ak)
        t_l = ctx.tok("qg")
        def s_ak(e, ck=ck, t_l=t_l):
            e.wait(t_l)
            ctx.inc(e.copy(out=akb[:], in_=akstg[:]), "s")
        on(ctx, "scalar", s_ak)
        t_cak = ctx.tok("s")
        def pe_rk(e, ck=ck, t_cak=t_cak):
            e.wait(t_cak, t_wr)
            if ck >= 2:
                e.wait(("s", ctx.cnt["s"]))
            ctx.inc(e.matmul(mm.ps[ck % 2][0:48, 0:512], wr[:], akb[:], start=True, stop=True), "t")
        on(ctx, "tensor", pe_rk)
        t_prk = ctx.tok("t")
        def s_erk(e, ck=ck, t_prk=t_prk):
            e.wait(t_prk)
            ctx.inc(e.copy(out=rkT[0:48, ck * 512:(ck + 1) * 512], in_=mm.ps[ck % 2][0:48, 0:512]), "s")
        on(ctx, "scalar", s_erk)
        mm.evac_tok[ck % 2] = ctx.tok("s")
    t_rk = ctx.tok("s")

    uw1 = sb("uw1", [128, NDT * 256], BF16)
    uwv = io["unc_w1T"].rearrange("(j p) f -> j p f", p=128)
    c2 = []
    for j in range(NDT):
        for hh in range(2):
            def g_w(e, j=j, hh=hh):
                if c2:
                    e.wait(c2[-1])
                ctx.inc(e.dma_start(out=stg128[:], in_=uwv[j][:, hh * 128:(hh + 1) * 128]), "qg", 16)
            on(ctx, "gpsimd", g_w)
            t_l = ctx.tok("qg")
            def s_c(e, j=j, hh=hh, t_l=t_l):
                e.wait(t_l)
                c2.append(ctx.inc(e.copy(out=uw1[:, j * 256 + hh * 128:j * 256 + (hh + 1) * 128],
                                         in_=stg128[:]), "s"))
            on(ctx, "scalar", s_c)
    t_uw1 = ctx.tok("s")
    ub1 = sb("ub1", [128, 2], F32)
    def g_b1(e):
        ctx.inc(e.dma_start(out=ub1[:], in_=io["unc_b1_c"]), "qg", 16)
    on(ctx, "gpsimd", g_b1)
    t_ub1 = ctx.tok("qg")
    for us in range(2):
        for tc in range(4):
            mms = [(uw1[:, j * 256 + us * 128:j * 256 + (us + 1) * 128],
                    p.xnT[:, j * T + tc * 512:j * T + (tc + 1) * 512]) for j in range(NDT)]
            def ev(e, ps, us=us, tc=tc):
                return e.activation(out=h1T[:, us * T + tc * 512:us * T + (tc + 1) * 512],
                                    in_=ps[:, 0:512], func=AF.Gelu, bias=ub1[:, us:us + 1])
            mm.step([t_xnT, t_uw1, t_ub1], mms, ev)
    t_h1 = ctx.tok("s")
    uw2 = sb("uw2", [128, 2], BF16)
    def g_w2(e):
        ctx.inc(e.dma_start(out=stg128[:, 0:2], in_=io["unc_w2_c"]), "qg", 16)
    on(ctx, "gpsimd", g_w2)
    t_lw2 = ctx.tok("qg")
    def s_w2(e):
        e.wait(t_lw2)
        ctx.inc(e.copy(out=uw2[:], in_=stg128[:, 0:2]), "s")
    on(ctx, "scalar", s_w2)
    t_w2 = ctx.tok("s")
    lrn_toks = []
    for tc in range(4):
        def pe_lr(e, tc=tc):
            e.wait(t_h1, t_w2)
            if mm.evac_tok[0] is not None:
                e.wait(mm.evac_tok[0])
            if lrn_toks:
                e.wait(lrn_toks[-1])
            for us in range(2):
                inst = e.matmul(mm.ps[0][0:1, 0:512], uw2[:, us:us + 1],
                                h1T[:, us * T + tc * 512:us * T + (tc + 1) * 512],
                                start=(us == 0), stop=(us == 1))
            ctx.inc(inst, "t")
        on(ctx, "tensor", pe_lr)
        t_plr = ctx.tok("t")
        def s_lr(e, tc=tc, t_plr=t_plr):
            e.wait(t_plr, mm.evac_tok[0] if mm.evac_tok[0] else None)
            lrn_toks.append(ctx.inc(e.copy(out=rowbuf[0:1, tc * 512:(tc + 1) * 512],
                                           in_=mm.ps[0][0:1, 0:512]), "s"))
        on(ctx, "scalar", s_lr)
    mm.evac_tok[0] = ctx.tok("s")
    t_learn = ctx.tok("s")
    lcol = sb("lcol", [128, 16], F32)
    def pe_lc(e):
        e.wait(t_learn)
        e.wait(mm.evac_tok[0])
        for i in range(NTOK):
            inst = e.matmul(mm.ps[0][:, i:i + 1], rowbuf[0:1, i * 128:(i + 1) * 128],
                            p.ones_b[0:1, 0:1], start=True, stop=True)
        ctx.inc(inst, "t")
    on(ctx, "tensor", pe_lc)
    t_plc = ctx.tok("t")
    def s_lc(e):
        e.wait(t_plc)
        ctx.inc(e.copy(out=lcol[:], in_=mm.ps[0][:, 0:NTOK]), "s")
    on(ctx, "scalar", s_lc)
    mm.evac_tok[0] = ctx.tok("s")
    t_lcol = ctx.tok("s")

    mvh = p.ln_mv[:].rearrange("p (i two) -> p i two", two=2)
    var0 = mvh[:, :, 1]
    def v_lv(e):
        e.wait(t_nmr)
        e.tensor_scalar(out=c_lv, in0=var0, scalar1=EPS1, scalar2=None, op0=ALU.add)
        tk = ctx.inc(e.reciprocal(out=c_nv, in_=c_lv), "v")
        e.wait(tk)
        ctx.inc(e.tensor_tensor(out=c_lv, in0=var0, in1=c_nv, op=ALU.mult), "v")
    on(ctx, "vector", v_lv)
    t_vr = ctx.tok("v")
    def s_lv(e):
        e.wait(t_vr)
        ctx.inc(e.activation(out=c_lv, in_=c_lv, func=AF.Ln, bias=p.fc(1.0)), "s")
    on(ctx, "scalar", s_lv)
    t_slv = ctx.tok("s")
    def v_sum(e):
        e.wait(t_slv)
        ctx.inc(e.reduce_sum(out=ccc[:, 3:4], in_=c_lv, axis=AX.X), "v")
    on(ctx, "vector", v_sum)
    t_vs = ctx.tok("v")
    def pe_sum(e):
        e.wait(t_vs)
        e.wait(mm.evac_tok[1])
        ctx.inc(e.matmul(mm.ps[1][0:1, 0:1], p.ones_f[:, 0:1], ccc[:, 3:4], start=True, stop=True), "t")
    on(ctx, "tensor", pe_sum)
    t_ps = ctx.tok("t")
    def s_sum(e):
        e.wait(t_ps)
        ctx.inc(e.copy(out=tiny[0:1, 0:1], in_=mm.ps[1][0:1, 0:1]), "s")
    on(ctx, "scalar", s_sum)
    mm.evac_tok[1] = ctx.tok("s")
    t_tin = ctx.tok("s")
    def g_cc(e):
        e.wait(t_tin)
        ctx.inc(e.dma_start(out=dram_cc_in.ap(), in_=tiny[:]), "qg", 16)
        e.wait(ctx.tok("qg"))
        ctx.inc(e.collective_compute(
            "AllReduce", mybir.AluOpType.add,
            replica_groups=[list(range(ncores))],
            ins=[dram_cc_in.ap().opt()],
            outs=[dram_cc_out.ap().opt()],
        ), "cc")
        e.wait(ctx.tok("cc"))
        ctx.inc(e.dma_start(out=tiny[:], in_=dram_cc_out.ap()), "qg", 16)
    on(ctx, "gpsimd", g_cc)
    t_cc = ctx.tok("qg")
    def s_gm(e):
        e.wait(t_cc)
        ctx.inc(e.activation(out=tiny[0:1, 1:2], in_=tiny[0:1, 0:1], func=AF.Copy,
                             scale=1.0 / (ncores * T)), "s")
    on(ctx, "scalar", s_gm)
    t_gm = ctx.tok("s")
    def v_gi(e):
        e.wait(t_gm)
        e.tensor_scalar(out=tiny[0:1, 1:2], in0=tiny[0:1, 1:2], scalar1=1e-6, scalar2=None, op0=ALU.add)
        tk = ctx.inc(e.reciprocal(out=tiny[0:1, 2:3], in_=tiny[0:1, 1:2]), "v")
        e.wait(tk)
    on(ctx, "vector", v_gi)
    t_gi = ctx.tok("v")
    def pe_bc(e):
        e.wait(t_gi, t_small)
        e.wait(mm.evac_tok[0])
        ctx.inc(e.matmul(mm.ps[0][:, 0:4], p.ones_f[0:1, :], ccc[0:1, 0:3].to_broadcast([1, 3]) if False else tiny[0:1, 0:4],
                         start=True, stop=True), "t")
    on(ctx, "tensor", pe_bc)
    t_pbc = ctx.tok("t")
    def s_bc(e):
        e.wait(t_pbc)
        ctx.inc(e.copy(out=ccc[:, 0:4] if False else stg128[:, 0:4], in_=mm.ps[0][:, 0:4]), "s")
    on(ctx, "scalar", s_bc)
    mm.evac_tok[0] = ctx.tok("s")
    t_sbc = ctx.tok("s")
    def pe_bc2(e):
        e.wait(t_small, t_sbc)
        e.wait(mm.evac_tok[1])
        ctx.inc(e.matmul(mm.ps[1][:, 0:3], p.ones_f[0:1, :], ccc[0:1, 0:3], start=True, stop=True), "t")
    on(ctx, "tensor", pe_bc2)
    t_pbc2 = ctx.tok("t")
    def s_bc2(e):
        e.wait(t_pbc2)
        ctx.inc(e.copy(out=stg128[:, 4:7], in_=mm.ps[1][:, 0:3]), "s")
    on(ctx, "scalar", s_bc2)
    mm.evac_tok[1] = ctx.tok("s")
    t_sbc2 = ctx.tok("s")
    def v_cols(e):
        e.wait(t_sbc, t_sbc2, t_lcol, t_slv)
        ctx.inc(e.tensor_scalar(out=c_nv, in0=c_lv, scalar1=stg128[:, 2:3], scalar2=None,
                                op0=ALU.mult), "v")
    on(ctx, "vector", v_cols)
    t_nv = ctx.tok("v")
    def s_sg(e):
        e.wait(t_nv)
        ctx.inc(e.activation(out=c_sg, in_=lcol[:], func=AF.Sigmoid, bias=stg128[:, 4:5]), "s")
    on(ctx, "scalar", s_sg)
    t_sg2 = ctx.tok("s")
    def v_unc(e):
        e.wait(t_sg2)
        e.tensor_scalar(out=c_unc, in0=c_sg, scalar1=2.5, scalar2=None, op0=ALU.mult)
        ctx.inc(e.scalar_tensor_tensor(out=c_unc, in0=c_nv, scalar=0.5, in1=c_unc,
                                       op0=ALU.mult, op1=ALU.add), "v")
    on(ctx, "vector", v_unc)
    t_unc1 = ctx.tok("v")
    def v_clip(e):
        e.wait(t_unc1)
        ctx.inc(e.tensor_scalar(out=c_unc, in0=c_unc, scalar1=5.0, scalar2=0.0,
                                op0=ALU.min, op1=ALU.max), "v")
    on(ctx, "vector", v_clip)
    t_unc = ctx.tok("v")
    def s_gate(e):
        e.wait(t_unc)
        ctx.inc(e.activation(out=c_gate, in_=c_unc, func=AF.Sigmoid,
                             scale=stg128[:, 5:6], bias=stg128[:, 6:7]), "s")
    on(ctx, "scalar", s_gate)
    t_gate = ctx.tok("s")
    def v_rp1(e):
        e.wait(t_gate)
        ctx.inc(e.tensor_scalar(out=c_rp1, in0=p.ln_rstd[:], scalar1=1.0, scalar2=None, op0=ALU.add), "v")
    on(ctx, "vector", v_rp1)
    t_rp1 = ctx.tok("v")

    relv = io["rel_row"]
    rel_sb = sb("rel", [1, 4096], BF16)
    def g_rel(e):
        ctx.inc(e.dma_start(out=sc_sb[0:1, :], in_=relv), "qg", 16)
    on(ctx, "gpsimd", g_rel)
    t_lrel = ctx.tok("qg")
    def s_rel(e):
        e.wait(t_lrel)
        ctx.inc(e.copy(out=rel_sb[:], in_=sc_sb[0:1, :]), "s")
    on(ctx, "scalar", s_rel)
    t_rel = ctx.tok("s")
    onesr5 = sb("onesr5", [1, 512], BF16)
    def v_o5(e):
        ctx.inc(e.memset(onesr5[:], 1.0), "v")
    on(ctx, "vector", v_o5)
    t_o5 = ctx.tok("v")

    groups = [list(range(g * 3, min(16, g * 3 + 3))) for g in range(6)]
    last_tok = None
    wT_v = wTg[:].rearrange("p (g n c) -> p g n c", g=3, n=32)
    for grp_id, grp in enumerate(groups):
        wt_done = []
        for slot, ti in enumerate(grp):
            for nch in range(8):
                mms = [(rqT[0:48, ti * 128:(ti + 1) * 128], rkT[0:48, nch * 512:(nch + 1) * 512])]
                mms.append((onesr5[0:1, 0:128] if False else rel_sb[0:1, 0:1], None))
                mms.pop()
                mms.append((p.ones_b[0:1, 0:128], rel_sb[0:1, nch * 512:(nch + 1) * 512]))
                def ev(e, ps, ti=ti, nch=nch):
                    return e.copy(out=sc_sb[:, nch * 512:(nch + 1) * 512], in_=ps[:, 0:512])
                mm.step([t_rq, t_rk, t_rel, t_o5], mms, ev)
            t_sc = ctx.tok("s")
            def v_m8(e, t_sc=t_sc):
                e.wait(t_sc)
                tk = ctx.inc(e.max(out=m8[:], in_=sc_sb[:]), "v")
                e.wait(tk)
                ctx.inc(e.engine_nop() if False else e.tensor_copy(out=m8[:, 0:1], in_=m8[:, 0:1]), "v")
            on(ctx, "vector", v_m8)
            t_m8 = ctx.tok("v")
            def s_exp(e, t_m8=t_m8):
                e.wait(t_m8)
                ctx.inc(e.activation(out=w_sb[:], in_=sc_sb[:], func=AF.Exp), "s")
            on(ctx, "scalar", s_exp)
            t_ex = ctx.tok("s")
            def v_w(e, ti=ti, t_ex=t_ex):
                e.wait(t_ex)
                ctx.inc(e.scalar_tensor_tensor(out=w_sb[:], in0=sc_sb[:], scalar=m8[:, 7:8],
                                               in1=w_sb[:], op0=ALU.is_ge, op1=ALU.mult,
                                               accum_out=wsums[:, ti:ti + 1]), "v")
            on(ctx, "vector", v_w)
            t_w = ctx.tok("v")
            for rnd in range(4):
                base_s = ctx.tok("s")
                def pe_t(e, rnd=rnd, t_w=t_w, base_s=base_s):
                    e.wait(t_w)
                    e.wait(base_s)
                    for b in range(8):
                        inst = e.transpose(p.pst[:, b * 128:(b + 1) * 128],
                                           w_sb[:, rnd * 1024 + b * 128:rnd * 1024 + (b + 1) * 128],
                                           p.ident[:])
                    ctx.inc(inst, "t")
                on(ctx, "tensor", pe_t)
                t_tr = ctx.tok("t")
                def s_ev(e, slot=slot, ti=ti, rnd=rnd, t_tr=t_tr):
                    e.wait(t_tr)
                    pv = p.pst[:].rearrange("p (b c) -> p b c", b=8)
                    ov = wT_v[:, slot, rnd * 8:(rnd + 1) * 8, :]
                    ctx.inc(e.copy(out=ov, in_=pv), "s")
                on(ctx, "scalar", s_ev)
            wt_done.append(ctx.tok("s"))
        t_grp = ctx.tok("s")
        agg_banks = {}
        for slot in range(len(grp)):
            agg_banks[slot] = (ps_agg[(slot * 2) % 4], ps_agg[(slot * 2 + 1) % 4] if slot < 2 else mm.ps[slot % 2])
        agg_banks = {0: (ps_agg[0], ps_agg[1]), 1: (ps_agg[2], ps_agg[3]), 2: (mm.ps[0], mm.ps[1])}
        vload = []
        for nt in range(32):
            def g_v(e, nt=nt, grp_id=grp_id):
                if vload:
                    e.wait(vload[-1])
                if nt == 0:
                    e.wait(t_grp)
                    if grp_id > 0 and last_tok is not None:
                        e.wait(last_tok)
                ctx.inc(e.dma_start(out=vstg[:], in_=io["aux_values"][nt * 128:(nt + 1) * 128, :]), "qg", 16)
            on(ctx, "gpsimd", g_v)
            t_lv2 = ctx.tok("qg")
            def s_v(e, t_lv2=t_lv2):
                e.wait(t_lv2)
                if vload:
                    pass
                vload.append(ctx.inc(e.copy(out=vb[:], in_=vstg[:]), "s"))
            on(ctx, "scalar", s_v)
            t_vb = ctx.tok("s")
            def pe_agg(e, nt=nt, grp=grp, t_vb=t_vb, grp_id=grp_id):
                e.wait(t_vb, t_grp)
                if grp_id > 0 and last_tok is not None and nt == 0:
                    e.wait(last_tok)
                for slot, ti in enumerate(grp):
                    lhs = wT_v[:, slot, nt, :]
                    bA, bB = agg_banks[slot]
                    e.matmul(bA[:, 0:512], lhs, vb[:, 0:512], start=(nt == 0), stop=(nt == 31))
                    inst = e.matmul(bB[:, 0:512], lhs, vb[:, 512:1024], start=(nt == 0), stop=(nt == 31))
                ctx.inc(inst, "t")
            on(ctx, "tensor", pe_agg)
            t_pa = ctx.tok("t")
            def s_gate2(e, t_pa=t_pa):
                e.wait(t_pa)
            on(ctx, "scalar", s_gate2)
        t_agg = ctx.tok("t")
        fin = []
        for slot, ti in enumerate(grp):
            def v_fin(e, slot=slot, ti=ti, t_agg=t_agg):
                e.wait(t_agg, t_rp1)
                bA, bB = agg_banks[slot]
                tk = ctx.inc(e.reciprocal(out=ccc[:, 3:4], in_=wsums[:, ti:ti + 1]), "v")
                e.wait(tk)
                for half, bank in ((0, bA), (1, bB)):
                    sl = p.xres[:, ti * D + half * 512:ti * D + half * 512 + 512]
                    e.tensor_scalar(out=aggscr[:], in0=bank[:, 0:512], scalar1=ccc[:, 3:4],
                                    scalar2=c_gate[:, ti:ti + 1], op0=ALU.mult, op1=ALU.mult)
                    e.scalar_tensor_tensor(out=sl, in0=sl, scalar=c_rp1[:, ti:ti + 1], in1=aggscr[:],
                                           op0=ALU.mult, op1=ALU.add)
                    inst = e.tensor_scalar(out=sl, in0=sl, scalar1=p.ln_nmr[:, ti:ti + 1],
                                           scalar2=None, op0=ALU.add)
                ctx.inc(inst, "v")
            on(ctx, "vector", v_fin)
            fin.append(ctx.tok("v"))
        last_tok = fin[-1]
        mm.evac_tok[0] = last_tok
        mm.evac_tok[1] = last_tok
    return last_tok


def emit_l1_mamba(p, io, x_in_ap, x_out_ap, tag, uid=""):
    ctx = p.ctx
    u_ = tag
    sb = lambda n, s, d: ctx.sbuf(f"{u_}_{n}", s, d)
    psm = lambda n, s, d: ctx.psum(f"{u_}_{n}", s, d)
    nc = ctx.nc
    DI = 2048
    NDI = 16

    xv = x_in_ap.rearrange("(i p) d -> i p d", p=128)
    xov = x_out_ap.rearrange("(i p) d -> i p d", p=128)
    xfs = sb("xfs", [128, 2 * D], F32)

    ld_toks, st_done = [], []
    for i in range(NTOK):
        def g_l(e, i=i):
            if i >= 2:
                e.wait(st_done[i - 2])
            ld_toks.append(ctx.inc(e.dma_start(out=xfs[:, (i % 2) * D:(i % 2 + 1) * D], in_=xv[i]), "qs", 16))
        on(ctx, "sync", g_l)
        def v_st(e, i=i):
            e.wait(ld_toks[i])
            st = p.ln_stats[:, i * 12:(i + 1) * 12]
            e.bn_stats(out=st[:, 0:6], in_=xfs[:, (i % 2) * D:(i % 2) * D + 512])
            inst = e.bn_stats(out=st[:, 6:12], in_=xfs[:, (i % 2) * D + 512:(i % 2 + 1) * D])
            st_done.append(ctx.inc(inst, "v"))
        on(ctx, "vector", v_st)
    def v_ag(e):
        e.wait(st_done[NTOK - 1])
        for i in range(NTOK):
            inst = e.bn_aggr(out=p.ln_mv[:, 2 * i:2 * i + 2], in_=p.ln_stats[:, i * 12:(i + 1) * 12])
        ctx.inc(inst, "v")
    on(ctx, "vector", v_ag)
    t_stats = ctx.tok("v")
    mvh = p.ln_mv[:].rearrange("p (i two) -> p i two", two=2)
    mu0, var0 = mvh[:, :, 0], mvh[:, :, 1]
    def s_rstd(e):
        e.wait(t_stats)
        e.activation(out=p.ln_rstd[:], in_=var0, func=AF.Ln, bias=p.fc(EPS1), scale=1.0)
        ctx.inc(e.activation(out=p.ln_rstd[:], in_=p.ln_rstd[:], func=AF.Exp, scale=-0.5), "s")
    on(ctx, "scalar", s_rstd)
    t_rstd0 = ctx.tok("s")
    def v_nmr0(e):
        e.wait(t_rstd0)
        ctx.inc(e.scalar_tensor_tensor(out=p.ln_nmr[:], in0=mu0, scalar=-1.0,
                                       in1=p.ln_rstd[:], op0=ALU.mult, op1=ALU.mult), "v")
    on(ctx, "vector", v_nmr0)
    t_nmr = ctx.tok("v")

    xa_toks = []
    S0 = ctx.cnt["s"]
    def producers(i):
        def g_l2(e, i=i):
            e.wait(t_nmr)
            if i >= 2:
                e.wait(("s", S0 + (i - 1) * 2 + 1))
            xa_toks.append(ctx.inc(e.dma_start(out=xfs[:, (i % 2) * D:(i % 2 + 1) * D], in_=xv[i]), "qs", 16))
        on(ctx, "sync", g_l2)
        def fn(e, i=i):
            e.wait(xa_toks[i], t_nmr)
            return e.activation(out=p.xstg[:, (i % 2) * D:(i % 2 + 1) * D],
                                in_=xfs[:, (i % 2) * D:(i % 2 + 1) * D], func=AF.Identity,
                                scale=p.ln_rstd[:, i:i + 1], bias=p.ln_nmr[:, i:i + 1])
        return ("scalar", "s", fn)
    t_xnT = emit_xnT(p, producers)

    xs = sb("xs", [128, NDI * T], BF16)
    wstg = sb("wstg", [128, 512], F32)
    wchunk = sb("wchunk", [128, NDT * 128], BF16)
    cols = sb("cols", [128, 64 + 16 * 3 + 256], F32)
    xdbl = sb("xdbl", [64, T], BF16)
    bc_sb = sb("bc_sb", [32, T], BF16)
    dtw = sb("dtw", [64, DI], BF16)
    xpw = sb("xpw", [128, NDI * 96], BF16)
    delt = sb("delt", [128, T], BF16)
    dxs = sb("dxs", [128, T], BF16)
    dA = sb("dA", [128, T], F32)
    ubuf = sb("ubuf", [128, T], BF16)
    hbuf = sb("hbuf", [128, T], BF16)
    ybuf = sb("ybuf", [128, T], F32)
    yscr = sb("yscr", [128, T], BF16)
    convscr = sb("convscr", [128, T], F32)
    rowB = sb("rowB", [1, T], BF16)
    rowC = sb("rowC", [1, T], BF16)
    zb = sb("zb", [128, T], BF16)
    c_convw = cols[:, 0:64]
    c_convb = cols[:, 64:80]
    c_dtb = cols[:, 80:96]
    c_D = cols[:, 96:112]
    c_A = cols[:, 112:368]
    mm = MMPool(ctx, u_)
    ps_b = psm("psB", [128, T], F32)
    zdram = nc.dram_tensor(f"{u_}_zscr{uid}", [128, NDI * T], BF16)
    bcdram = nc.dram_tensor(f"{u_}_bcscr{uid}", [32, T], BF16)

    def g_cols(e):
        ctx.inc(e.dma_start(out=c_convw, in_=io["conv_w_c"]), "qg", 16)
        ctx.inc(e.dma_start(out=c_convb, in_=io["conv_b_c"]), "qg", 16)
        ctx.inc(e.dma_start(out=c_dtb, in_=io["dtb_c"]), "qg", 16)
        ctx.inc(e.dma_start(out=c_D, in_=io["D_c"]), "qg", 16)
        ctx.inc(e.dma_start(out=c_A, in_=io["A_log_c"]), "qg", 16)
    on(ctx, "gpsimd", g_cols)
    t_cols = ctx.tok("qg")
    def s_A(e):
        e.wait(t_cols)
        ctx.inc(e.activation(out=c_A, in_=c_A, func=AF.Exp), "s")
    on(ctx, "scalar", s_A)
    t_sA = ctx.tok("s")
    def v_A(e):
        e.wait(t_sA)
        ctx.inc(e.tensor_scalar(out=c_A, in0=c_A, scalar1=-1.0, scalar2=None, op0=ALU.mult), "v")
    on(ctx, "vector", v_A)
    t_A = ctx.tok("v")

    ipv = io["in_projT"]
    conv_toks = []
    zout_toks = []
    for cs in range(32):
        wv_ = ipv[:, cs * 128:(cs + 1) * 128].rearrange("(j p) f -> j p f", p=128)
        casts = []
        for j in range(NDT):
            def g_w(e, j=j, wv_=wv_, cs=cs):
                if j == 0 and (conv_toks or zout_toks):
                    e.wait((conv_toks + zout_toks)[-1])
                if j >= 1:
                    e.wait(casts[j - 1])
                ctx.inc(e.dma_start(out=wstg[:, 0:128], in_=wv_[j]), "qg", 16)
            on(ctx, "gpsimd", g_w)
            t_l = ctx.tok("qg")
            def s_c(e, j=j, t_l=t_l):
                e.wait(t_l)
                casts.append(ctx.inc(e.copy(out=wchunk[:, j * 128:(j + 1) * 128], in_=wstg[:, 0:128]), "s"))
            on(ctx, "scalar", s_c)
        t_wc = ctx.tok("s")
        ev_toks = []
        for tc in range(4):
            mms = [(wchunk[:, j * 128:(j + 1) * 128],
                    p.xnT[:, j * T + tc * 512:j * T + (tc + 1) * 512]) for j in range(NDT)]
            if cs < 16:
                def ev(e, ps, cs=cs, tc=tc):
                    return e.copy(out=xs[:, cs * T + tc * 512:cs * T + (tc + 1) * 512],
                                  in_=ps[:, 0:512])
            else:
                def ev(e, ps, cs=cs, tc=tc):
                    return e.copy(out=zb[:, tc * 512:(tc + 1) * 512], in_=ps[:, 0:512])
            ev_toks.append(mm.step([t_xnT, t_wc], mms, ev))
        if cs < 16:
            t_e = ev_toks[-1]
            def v_conv(e, cs=cs, t_e=t_e):
                e.wait(t_e, t_cols)
                x_ = xs[:, cs * T:(cs + 1) * T]
                e.tensor_scalar(out=convscr[:], in0=x_, scalar1=c_convw[:, cs * 4 + 3:cs * 4 + 4],
                                scalar2=None, op0=ALU.mult)
                for s_ in (1, 2, 3):
                    k = 3 - s_
                    e.scalar_tensor_tensor(out=convscr[:, s_:T], in0=x_[:, 0:T - s_],
                                           scalar=c_convw[:, cs * 4 + k:cs * 4 + k + 1],
                                           in1=convscr[:, s_:T], op0=ALU.mult, op1=ALU.add)
                ctx.inc(e.engine_nop(), "v")
            on(ctx, "vector", v_conv)
            t_cv = ctx.tok("v")
            def s_silu(e, cs=cs, t_cv=t_cv):
                e.wait(t_cv)
                ctx.inc(e.activation(out=xs[:, cs * T:(cs + 1) * T], in_=convscr[:],
                                     func=AF.Silu, bias=c_convb[:, cs:cs + 1]), "s")
            on(ctx, "scalar", s_silu)
            conv_toks.append(ctx.tok("s"))
        else:
            t_e = ev_toks[-1]
            def g_z(e, cs=cs, t_e=t_e):
                e.wait(t_e)
                zout_toks.append(ctx.inc(e.dma_start(out=zdram.ap()[:, (cs - 16) * T:(cs - 15) * T],
                                                     in_=zb[:]), "qg", 16))
            on(ctx, "gpsimd", g_z)
    t_conv = ctx.tok("s")
    t_zout = ctx.tok("qg")

    xpv = io["x_projT"].rearrange("(j p) f -> j p f", p=128)
    c3 = []
    for j in range(NDI):
        def g_w(e, j=j):
            if c3:
                e.wait(c3[-1])
            ctx.inc(e.dma_start(out=wstg[:, 0:96], in_=xpv[j]), "qg", 16)
        on(ctx, "gpsimd", g_w)
        t_l = ctx.tok("qg")
        def s_c(e, j=j, t_l=t_l):
            e.wait(t_l)
            c3.append(ctx.inc(e.copy(out=xpw[:, j * 96:(j + 1) * 96], in_=wstg[:, 0:96]), "s"))
        on(ctx, "scalar", s_c)
    t_xpw = ctx.tok("s")
    for tc in range(4):
        mms = [(xpw[:, j * 96:(j + 1) * 96], xs[:, j * T + tc * 512:j * T + (tc + 1) * 512])
               for j in range(NDI)]
        def ev(e, ps, tc=tc):
            e.copy(out=xdbl[0:64, tc * 512:(tc + 1) * 512], in_=ps[0:64, 0:512])
            return e.copy(out=bc_sb[0:32, tc * 512:(tc + 1) * 512], in_=ps[64:96, 0:512])
        mm.step([t_conv, t_xpw], mms, ev, out_p=96)
    t_xdbl = ctx.tok("s")
    def g_bc(e):
        e.wait(t_xdbl)
        ctx.inc(e.dma_start(out=bcdram.ap(), in_=bc_sb[:]), "qg", 16)
    on(ctx, "gpsimd", g_bc)
    t_bc = ctx.tok("qg")

    dtv = io["dt_projT"]
    c4 = []
    for hh in range(4):
        def g_w(e, hh=hh):
            if c4:
                e.wait(c4[-1])
            ctx.inc(e.dma_start(out=wstg[0:64, 0:512], in_=dtv[:, hh * 512:(hh + 1) * 512]), "qg", 16)
        on(ctx, "gpsimd", g_w)
        t_l = ctx.tok("qg")
        def s_c(e, hh=hh, t_l=t_l):
            e.wait(t_l)
            c4.append(ctx.inc(e.copy(out=dtw[0:64, hh * 512:(hh + 1) * 512], in_=wstg[0:64, 0:512]), "s"))
        on(ctx, "scalar", s_c)
    t_dtw = ctx.tok("s")

    opb = p.xnT
    opv = io["out_projT"].rearrange("(j p) d2 -> j p d2", p=128)
    c5 = []
    for j in range(NDI):
        for hh in range(2):
            def g_w(e, j=j, hh=hh):
                e.wait(t_xnT if False else ctx.tok("qs"))
                e.wait(t_conv, t_zout)
                if c5:
                    e.wait(c5[-1])
                ctx.inc(e.dma_start(out=wstg[:, 0:512], in_=opv[j][:, hh * 512:(hh + 1) * 512]), "qg", 16)
            on(ctx, "gpsimd", g_w)
            t_l = ctx.tok("qg")
            def s_c(e, j=j, hh=hh, t_l=t_l):
                e.wait(t_l)
                c5.append(ctx.inc(e.copy(out=opb[:, j * D + hh * 512:j * D + (hh + 1) * 512],
                                         in_=wstg[:, 0:512]), "s"))
            on(ctx, "scalar", s_c)
    t_opb = ctx.tok("s")

    yg_toks = []
    for dt in range(NDI):
        dl_toks = []
        for tc in range(4):
            mms = [(dtw[0:64, dt * 128:(dt + 1) * 128], xdbl[0:64, tc * 512:(tc + 1) * 512])]
            def ev(e, ps, dt=dt, tc=tc):
                sl = delt[:, tc * 512:(tc + 1) * 512]
                e.activation(out=convscr[:, 0:512], in_=ps[:, 0:512], func=AF.Abs,
                             bias=c_dtb[:, dt:dt + 1])
                e.activation(out=convscr[:, 0:512], in_=convscr[:, 0:512], func=AF.Exp, scale=-1.0)
                e.activation(out=convscr[:, 0:512], in_=convscr[:, 0:512], func=AF.Ln,
                             bias=p.fc(1.0))
                e.activation(out=convscr[:, 512:1024], in_=ps[:, 0:512], func=AF.Relu,
                             bias=c_dtb[:, dt:dt + 1])
                return e.activation(out=convscr[:, 512:1024], in_=convscr[:, 512:1024],
                                    func=AF.Copy, scale=1.0)
            dl_toks.append(mm.step([t_xdbl, t_dtw], mms, ev, out_p=128))
            def v_sp(e, tc=tc, dl=dl_toks[-1]):
                e.wait(dl)
                ctx.inc(e.tensor_tensor(out=delt[:, tc * 512:(tc + 1) * 512],
                                        in0=convscr[:, 0:512], in1=convscr[:, 512:1024],
                                        op=ALU.add), "v")
            on(ctx, "vector", v_sp)
            t_vsp = ctx.tok("v")
            def s_g(e, t_vsp=t_vsp):
                e.wait(t_vsp)
            on(ctx, "scalar", s_g)
        t_delt = ctx.tok("v")
        def v_dxs(e, dt=dt, t_delt=t_delt):
            e.wait(t_delt, t_conv, t_cols)
            x_ = xs[:, dt * T:(dt + 1) * T]
            e.tensor_tensor(out=dxs[:], in0=delt[:], in1=x_, op=ALU.mult)
            ctx.inc(e.tensor_scalar(out=ybuf[:], in0=x_, scalar1=c_D[:, dt:dt + 1],
                                    scalar2=None, op0=ALU.mult), "v")
        on(ctx, "vector", v_dxs)
        t_dxs = ctx.tok("v")
        for n in range(16):
            def g_row(e, n=n, t_bc=t_bc):
                e.wait(t_bc)
                e.wait(("t", ctx.cnt["t"]))
                ctx.inc(e.dma_start(out=rowB[:], in_=bcdram.ap()[n:n + 1, :]), "qg", 16)
                ctx.inc(e.dma_start(out=rowC[:], in_=bcdram.ap()[16 + n:17 + n, :]), "qg", 16)
            on(ctx, "gpsimd", g_row)
            t_row = ctx.tok("qg")
            prev_u = ctx.tok("v")
            def pe_bc2(e, n=n, t_row=t_row, prev_u=prev_u):
                e.wait(t_row)
                e.wait(prev_u)
                for tc in range(4):
                    e.matmul(ps_b[:, tc * 512:(tc + 1) * 512], p.ones_b[0:1, :],
                             rowB[0:1, tc * 512:(tc + 1) * 512], start=True, stop=True)
                for tc in range(4):
                    inst = e.matmul(mm.ps[tc % 2][:, 0:512], p.ones_b[0:1, :],
                                    rowC[0:1, tc * 512:(tc + 1) * 512], start=True, stop=True)
                    if tc < 2:
                        pass
                ctx.inc(inst, "t")
            def pe_bcB(e, n=n, t_row=t_row, prev_u=prev_u):
                e.wait(t_row, prev_u)
                for tc in range(4):
                    inst = e.matmul(ps_b[:, tc * 512:(tc + 1) * 512], p.ones_b[0:1, :],
                                    rowB[0:1, tc * 512:(tc + 1) * 512], start=True, stop=True)
                ctx.inc(inst, "t")
            on(ctx, "tensor", pe_bcB)
            t_pbB = ctx.tok("t")
            prev_v = ctx.tok("v")
            def s_dA(e, dt=dt, n=n, t_delt=t_delt, prev_v=prev_v):
                e.wait(t_delt, t_A, prev_v)
                ctx.inc(e.activation(out=dA[:], in_=delt[:], func=AF.Exp,
                                     scale=c_A[:, dt * 16 + n:dt * 16 + n + 1]), "s")
            on(ctx, "scalar", s_dA)
            t_dA = ctx.tok("s")
            def v_u(e, t_pbB=t_pbB, t_dxs=t_dxs):
                e.wait(t_pbB, t_dxs)
                ctx.inc(e.tensor_tensor(out=ubuf[:], in0=dxs[:], in1=ps_b[:], op=ALU.mult), "v")
            on(ctx, "vector", v_u)
            t_u = ctx.tok("v")
            def pe_bcC(e, n=n, t_row=t_row, t_u=t_u):
                e.wait(t_row, t_u)
                for tc in range(4):
                    inst = e.matmul(ps_b[:, tc * 512:(tc + 1) * 512], p.ones_b[0:1, :],
                                    rowC[0:1, tc * 512:(tc + 1) * 512], start=True, stop=True)
                ctx.inc(inst, "t")
            on(ctx, "tensor", pe_bcC)
            t_pbC = ctx.tok("t")
            def v_scan(e, t_dA=t_dA, t_u=t_u):
                e.wait(t_dA, t_u)
                tk = ctx.inc(e.tensor_tensor_scan(out=hbuf[:], data0=dA[:], data1=ubuf[:],
                                                  initial=0.0, op0=ALU.mult, op1=ALU.add), "v")
                e.wait(tk)
                ctx.inc(e.engine_nop(), "v")
            on(ctx, "vector", v_scan)
            t_h = ctx.tok("v")
            def v_y(e, t_h=t_h, t_pbC=t_pbC):
                e.wait(t_h, t_pbC)
                e.tensor_tensor(out=yscr[:], in0=hbuf[:], in1=ps_b[:], op=ALU.mult)
                ctx.inc(e.tensor_tensor(out=ybuf[:], in0=ybuf[:], in1=yscr[:], op=ALU.add), "v")
            on(ctx, "vector", v_y)
        t_y = ctx.tok("v")
        def g_z2(e, dt=dt, t_zout=t_zout):
            e.wait(t_zout)
            if yg_toks:
                e.wait(yg_toks[-1])
            ctx.inc(e.dma_start(out=zb[:], in_=zdram.ap()[:, dt * T:(dt + 1) * T]), "qg", 16)
        on(ctx, "gpsimd", g_z2)
        t_z = ctx.tok("qg")
        def s_sz(e, t_z=t_z, t_y=t_y):
            e.wait(t_z, t_y)
            ctx.inc(e.activation(out=zb[:], in_=zb[:], func=AF.Silu), "s")
        on(ctx, "scalar", s_sz)
        t_sz = ctx.tok("s")
        def v_g(e, dt=dt, t_sz=t_sz, t_y=t_y):
            e.wait(t_sz, t_y)
            ctx.inc(e.tensor_tensor(out=xs[:, dt * T:(dt + 1) * T], in0=ybuf[:], in1=zb[:],
                                    op=ALU.mult), "v")
        on(ctx, "vector", v_g)
        yg_toks.append(ctx.tok("v"))
    t_yg = ctx.tok("v")

    xc_toks, out_toks = [], []
    for ti in range(NTOK):
        def g_lx(e, ti=ti):
            if len(out_toks) >= 2:
                e.wait(out_toks[-2])
            e.wait(t_yg)
            xc_toks.append(ctx.inc(e.dma_start(out=xfs[:, (ti % 2) * D:(ti % 2 + 1) * D], in_=xv[ti]), "qs", 16))
        on(ctx, "sync", g_lx)
        ev_toks = []
        for dc in range(2):
            mms = [(xs[:, j * T + ti * 128:j * T + (ti + 1) * 128],
                    opb[:, j * D + dc * 512:j * D + (dc + 1) * 512]) for j in range(NDI)]
            def ev(e, ps, ti=ti, dc=dc):
                sl = xfs[:, (ti % 2) * D + dc * 512:(ti % 2) * D + dc * 512 + 512]
                return e.scalar_tensor_tensor(out=sl, in0=ps[:, 0:512], scalar=1.0,
                                              in1=sl, op0=ALU.mult, op1=ALU.add)
            def evw(e, ps, ti=ti, dc=dc):
                e.wait(xc_toks[ti])
                return ev(e, ps)
            ev_toks.append(mm.step([t_yg, t_opb], mms, evw, eng="vector", sem="v"))
        def g_ox(e, ti=ti, ev_toks=ev_toks):
            e.wait(*ev_toks)
            out_toks.append(ctx.inc(e.dma_start(out=xov[ti], in_=xfs[:, (ti % 2) * D:(ti % 2 + 1) * D]), "qg", 16))
        on(ctx, "gpsimd", g_ox)
    def g_fin(e):
        e.wait(ctx.tok("qg"))
    on(ctx, "gpsimd", g_fin)
    return ctx.tok("qg")



def _build_retnet():
    nc = bass.Bass("TRN2", target_bir_lowering=False, debug=False)
    x_in = nc.dram_tensor("x_in", [T, D], F32, kind="ExternalInput").ap()
    x_out = nc.dram_tensor("x_out", [T, D], F32, kind="ExternalOutput").ap()
    io = {}
    for n, shp in [("Wt", [D, D]), ("Wo", [D, D]), ("Wg", [2 * D, D]), ("bg_row", [1, D]),
                   ("ln1w_c", [128, 8]), ("ln1b_c", [128, 8]), ("lncw_c", [128, 8]),
                   ("lncb_c", [128, 8]), ("lngw_c", [128, 8]), ("lngb_c", [128, 8])]:
        io[n] = nc.dram_tensor(n, shp, F32, kind="ExternalInput").ap()
    cvals = host_constants()
    consts = const_dram(nc, cvals)
    es = contextlib.ExitStack()
    with es:
        block = es.enter_context(nc.Block())
        p = Prog(nc, es, block, keep_xres=False)
        p.c_aps = consts
        p.load_const(consts)
        emit_l0_attn_a(p, io, x_in, x_out, "ra")
    return nc


def _build_ffn():
    nc = bass.Bass("TRN2", target_bir_lowering=False, debug=False)
    x_in = nc.dram_tensor("x_in", [T, D], F32, kind="ExternalInput").ap()
    x_out = nc.dram_tensor("x_out", [T, D], F32, kind="ExternalOutput").ap()
    wT = nc.dram_tensor("wT", [D, D], F32, kind="ExternalInput").ap()
    alpha = nc.dram_tensor("alpha", [1, 1], F32, kind="ExternalInput").ap()
    brow = nc.dram_tensor("brow", [1, D], F32, kind="ExternalInput").ap()
    cvals = host_constants()
    consts = const_dram(nc, cvals)
    es = contextlib.ExitStack()
    with es:
        block = es.enter_context(nc.Block())
        p = Prog(nc, es, block, keep_xres=True)
        p.c_aps = consts
        p.load_const(consts)
        p.load_x(x_in)
        t_nmr, t_xnT = emit_ln_t(p, [p.x_tok])
        prep = emit_l0_ffn_prep(p, wT, alpha, brow, "f0")
        mm = MMPool(p.ctx, "f0c")
        tok = emit_l0_ffn_call(p, prep, mm, t_xnT)
        p.store_x(x_out, [tok])
    return nc


def _build_attn():
    nc = bass.Bass("TRN2", target_bir_lowering=False, debug=False)
    x_in = nc.dram_tensor("x_in", [T, D], F32, kind="ExternalInput").ap()
    x_out = nc.dram_tensor("x_out", [T, D], F32, kind="ExternalOutput").ap()
    io = {}
    for n, shp in [("wq", [D, D]), ("wk", [D, D]), ("wv", [D, D]), ("wo", [D, D]),
                   ("bq_c", [128, 8]), ("bk_c", [128, 8]), ("bv_row", [1, D]), ("bo_row", [1, D]),
                   ("gamma_bc", [128, D]), ("beta_bc", [128, D]), ("pe_tok", [T, D])]:
        io[n] = nc.dram_tensor(n, shp, F32, kind="ExternalInput").ap()
    cvals = host_constants()
    consts = const_dram(nc, cvals)
    es = contextlib.ExitStack()
    with es:
        block = es.enter_context(nc.Block())
        p = Prog(nc, es, block, keep_xres=False)
        p.c_aps = consts
        p.load_const(consts)
        emit_l0_attn_b(p, io, x_in, x_out, "ab")
    return nc


def _build_linattn():
    nc = bass.Bass("TRN2", target_bir_lowering=False, debug=False)
    x_in = nc.dram_tensor("x_in", [T, D], F32, kind="ExternalInput").ap()
    x_out = nc.dram_tensor("x_out", [T, D], F32, kind="ExternalOutput").ap()
    io = {}
    for n, shp in [("w_qkv", [D, 3 * D]), ("w_gate", [D, D]), ("w_proj", [D, D]),
                   ("b_qkv_row", [1, 3 * D]), ("b_gate_row", [1, D]), ("b_proj_bc", [128, D]),
                   ("lng_c", [128, 8]), ("lnb_c", [128, 8])]:
        io[n] = nc.dram_tensor(n, shp, F32, kind="ExternalInput").ap()
    cvals = host_constants()
    consts = const_dram(nc, cvals)
    es = contextlib.ExitStack()
    with es:
        block = es.enter_context(nc.Block())
        p = Prog(nc, es, block, keep_xres=False)
        p.c_aps = consts
        p.load_const(consts)
        emit_l1_attn_a(p, io, x_in, x_out, "la")
    return nc


def _build_auxffn():
    nc = bass.Bass("TRN2", target_bir_lowering=False, debug=False)
    x_in = nc.dram_tensor("x_in", [T, D], F32, kind="ExternalInput").ap()
    x_out = nc.dram_tensor("x_out", [T, D], F32, kind="ExternalOutput").ap()
    io = {}
    for n, shp in [("q_projT", [D, 128]), ("router_wT", [128, 48]), ("aux_keysT", [128, 4096]),
                   ("aux_values", [4096, D]), ("rel_row", [1, 4096]), ("unc_w1T", [D, 256]),
                   ("unc_b1_c", [128, 2]), ("unc_w2_c", [128, 2]), ("unc_b2", [1, 1]),
                   ("gate_w1", [1, 1]), ("gate_bias", [1, 1])]:
        io[n] = nc.dram_tensor(n, shp, F32, kind="ExternalInput").ap()
    cvals = host_constants()
    consts = const_dram(nc, cvals)
    es = contextlib.ExitStack()
    with es:
        block = es.enter_context(nc.Block())
        p = Prog(nc, es, block, keep_xres=True)
        p.c_aps = consts
        p.load_const(consts)
        p.load_x(x_in)
        p.ln_dep = [p.x_tok]
        tok = emit_l1_ffn(p, io, "lf")
        p.store_x(x_out, [tok])
    return nc


def _build_mamba():
    nc = bass.Bass("TRN2", target_bir_lowering=False, debug=False)
    x_in = nc.dram_tensor("x_in", [T, D], F32, kind="ExternalInput").ap()
    x_out = nc.dram_tensor("x_out", [T, D], F32, kind="ExternalOutput").ap()
    io = {}
    for n, shp in [("in_projT", [D, 4096]), ("x_projT", [2048, 96]), ("dt_projT", [64, 2048]),
                   ("out_projT", [2048, D]), ("conv_w_c", [128, 64]), ("conv_b_c", [128, 16]),
                   ("dtb_c", [128, 16]), ("D_c", [128, 16]), ("A_log_c", [128, 256])]:
        io[n] = nc.dram_tensor(n, shp, F32, kind="ExternalInput").ap()
    cvals = host_constants()
    consts = const_dram(nc, cvals)
    es = contextlib.ExitStack()
    with es:
        block = es.enter_context(nc.Block())
        p = Prog(nc, es, block, keep_xres=False)
        p.c_aps = consts
        p.load_const(consts)
        emit_l1_mamba(p, io, x_in, x_out, "mb")
    return nc


def kernel(**inputs):
    f32 = lambda a: np.ascontiguousarray(np.asarray(a), dtype=np.float32)
    I = {k: np.asarray(v) for k, v in inputs.items()}
    cvals = host_constants()
    col8 = lambda v: f32(v).reshape(8, 128).T.copy()
    col16 = lambda v: f32(v).reshape(16, 128).T.copy()

    m_ra = dict(cvals)
    m_ra.update({
        "Wt": f32(I["L0_attn_a_Wt"]), "Wo": f32(I["L0_attn_a_Wo"]), "Wg": f32(I["L0_attn_a_Wg"]),
        "bg_row": f32(I["L0_attn_a_bg"]).reshape(1, D),
        "ln1w_c": col8(I["L0_attn_a_ln1_w"]), "ln1b_c": col8(I["L0_attn_a_ln1_b"]),
        "lncw_c": col8(I["L0_attn_a_lnc_w"]), "lncb_c": col8(I["L0_attn_a_lnc_b"]),
        "lngw_c": col8(I["L0_attn_a_lng_w"]), "lngb_c": col8(I["L0_attn_a_lng_b"]),
    })
    m_f0 = dict(cvals)
    m_f0.update({
        "wT": f32(I["L0_ffn_weight"].T), "alpha": f32(I["L0_ffn_alpha"]).reshape(1, 1),
        "brow": f32(I["L0_ffn_bias"]).reshape(1, D),
    })
    m_ab = dict(cvals)
    m_ab.update({
        "wq": f32(I["L0_attn_b_wq"]), "wk": f32(I["L0_attn_b_wk"]),
        "wv": f32(I["L0_attn_b_wv"]), "wo": f32(I["L0_attn_b_wo"]),
        "bq_c": col8(I["L0_attn_b_bq"]), "bk_c": col8(I["L0_attn_b_bk"]),
        "bv_row": f32(I["L0_attn_b_bv"]).reshape(1, D), "bo_row": f32(I["L0_attn_b_bo"]).reshape(1, D),
        "gamma_bc": np.broadcast_to(f32(I["L0_attn_b_gamma"]), (128, D)).copy(),
        "beta_bc": np.broadcast_to(f32(I["L0_attn_b_beta"]), (128, D)).copy(),
        "pe_tok": f32(I["L0_attn_b_pe"][0, :T, :]),
    })
    m_la = dict(cvals)
    m_la.update({
        "w_qkv": f32(I["L1_attn_a_w_qkv"]), "w_gate": f32(I["L1_attn_a_w_gate"]),
        "w_proj": f32(I["L1_attn_a_w_proj"]),
        "b_qkv_row": f32(I["L1_attn_a_b_qkv"]).reshape(1, 3 * D),
        "b_gate_row": f32(I["L1_attn_a_b_gate"]).reshape(1, D),
        "b_proj_bc": np.broadcast_to(f32(I["L1_attn_a_b_proj"]), (128, D)).copy(),
        "lng_c": col8(I["L1_attn_a_ln_g"]), "lnb_c": col8(I["L1_attn_a_ln_b"]),
    })
    m_lf = dict(cvals)
    m_lf.update({
        "q_projT": f32(I["L1_ffn_q_proj_w"].T), "router_wT": f32(I["L1_ffn_router_w"].T),
        "aux_keysT": f32(I["L1_ffn_aux_keys"].T), "aux_values": f32(I["L1_ffn_aux_values"]),
        "rel_row": f32(I["L1_ffn_reliability_mask"]).reshape(1, 4096),
        "unc_w1T": f32(I["L1_ffn_unc_w1"].T),
        "unc_b1_c": f32(I["L1_ffn_unc_b1"]).reshape(2, 128).T.copy(),
        "unc_w2_c": f32(I["L1_ffn_unc_w2"]).reshape(2, 128).T.copy(),
        "unc_b2": f32(I["L1_ffn_unc_b2"]).reshape(1, 1),
        "gate_w1": f32(I["L1_ffn_gate_w1"]).reshape(1, 1),
        "gate_bias": f32(I["L1_ffn_gate_bias"]).reshape(1, 1),
    })
    m_mb = dict(cvals)
    m_mb.update({
        "in_projT": f32(I["L1_attn_b_in_proj_w"].T), "x_projT": f32(I["L1_attn_b_x_proj_w"].T),
        "dt_projT": f32(I["L1_attn_b_dt_proj_w"].T), "out_projT": f32(I["L1_attn_b_out_proj_w"].T),
        "conv_w_c": f32(I["L1_attn_b_conv_w"]).reshape(16, 128, 4).transpose(1, 0, 2).reshape(128, 64).copy(),
        "conv_b_c": col16(I["L1_attn_b_conv_b"]), "dtb_c": col16(I["L1_attn_b_dt_proj_b"]),
        "D_c": col16(I["L1_attn_b_D"]), "A_log_c": f32(I["L1_attn_b_A_log"]).reshape(16, 128, 16).transpose(1, 0, 2).reshape(128, 256).copy(),
    })

    stages = [
        (_build_retnet, m_ra), (_build_ffn, m_f0), (_build_attn, m_ab), (_build_ffn, m_f0),
        (_build_linattn, m_la), (_build_auxffn, m_lf), (_build_mamba, m_mb), (_build_auxffn, m_lf),
    ]
    x = f32(I["L0_attn_a_x"])
    xs = [np.ascontiguousarray(x[c]) for c in range(8)]
    built = {}
    for builder, m in stages:
        if builder not in built:
            built[builder] = builder()
        nc = built[builder]
        in_maps = []
        for c in range(8):
            mm_ = dict(m)
            mm_["x_in"] = xs[c]
            in_maps.append(mm_)
        res = run_bass_kernel_spmd(nc, in_maps, core_ids=list(range(8)))
        xs = [np.ascontiguousarray(res.results[c]["x_out"]) for c in range(8)]
    return np.stack(xs).astype(np.float32)


# version 02
# speedup vs baseline: 1.0031x; 1.0031x over previous
import contextlib
from collections import defaultdict

import numpy as np
import concourse.bass as bass
import concourse.mybir as mybir
from concourse.bass_utils import run_bass_kernel_spmd

F32 = mybir.dt.float32
BF16 = mybir.dt.bfloat16
AF = mybir.ActivationFunctionType
ALU = mybir.AluOpType
AX = mybir.AxisListType

T = 2048
D = 1024
NTOK = T // 128
NDT = D // 128
EPS1 = 1e-5

FCONSTS = [1e-5, 1e-5 * 1e-6, 1e-5 * 1e-5, 1e-6, 1.0, 0.5, -0.5]


class Ctx:
    def __init__(self, nc, es, block):
        self.nc = nc
        self.es = es
        self.block = block
        self.semh = {}
        self.cnt = defaultdict(int)
        self.waited = defaultdict(dict)
        self.names = set()

    def sem(self, name):
        if name not in self.semh:
            self.semh[name] = self.es.enter_context(self.nc.semaphore(name))
        return self.semh[name]

    def sbuf(self, name, shape, dtype, es=None):
        assert name not in self.names, name
        self.names.add(name)
        tgt = es or getattr(self, "scope", None) or self.es
        return tgt.enter_context(self.nc.sbuf_tensor(name, shape, dtype))

    def psum(self, name, shape, dtype, es=None):
        assert name not in self.names, name
        self.names.add(name)
        tgt = es or getattr(self, "scope", None) or self.es
        return tgt.enter_context(self.nc.psum_tensor(name, shape, dtype))

    def tok(self, sem):
        return (sem, self.cnt[sem])

    def inc(self, inst, sem, n=1):
        inst.then_inc(self.sem(sem), n)
        self.cnt[sem] += n
        return (sem, self.cnt[sem])


class Eng:
    def __init__(self, ctx, name, e):
        self.ctx, self.name, self.e = ctx, name, e

    def wait(self, *toks):
        for t in toks:
            if t is None:
                continue
            sem, val = t
            if val <= 0:
                continue
            if self.ctx.waited[self.name].get(sem, -1) >= val:
                continue
            self.e.wait_ge(self.ctx.sem(sem), val)
            self.ctx.waited[self.name][sem] = val

    def __getattr__(self, a):
        return getattr(self.e, a)


def on(ctx, engname, fn):
    getattr(ctx.block, engname)(lambda e: fn(Eng(ctx, engname, e)))


def host_constants():
    import ml_dtypes
    c = {}
    c["c_ident"] = np.eye(128, dtype=ml_dtypes.bfloat16)
    c["c_ones_f32"] = np.ones((128, 128), np.float32)
    c["c_ones_bf16"] = np.ones((128, 128), dtype=ml_dtypes.bfloat16)
    i = np.arange(128)
    c["c_triu"] = (i[None, :] >= i[:, None]).astype(ml_dtypes.bfloat16)
    hs = np.linspace(2.0, 8.0, D).astype(np.float32)
    decay = (0.5 ** (1.0 / hs)).astype(np.float32)
    c["c_decay"] = decay.reshape(NDT, 128).T.copy()
    c["c_decay_beta"] = (1.0 - decay).reshape(NDT, 128).T.copy()
    c["c_fconst"] = np.tile(np.asarray(FCONSTS, np.float32)[None, :], (128, 1))
    return c


def const_dram(nc, cvals):
    aps = {}
    for k, v in cvals.items():
        dt = F32 if v.dtype == np.float32 else BF16
        aps[k] = nc.dram_tensor(k, list(v.shape), dt, kind="ExternalInput").ap()
    return aps


class Prog:
    def __init__(self, nc, es, block, keep_xres=True):
        self.ctx = Ctx(nc, es, block)
        ctx = self.ctx
        self.keep_xres = keep_xres
        if keep_xres:
            self.xres = ctx.sbuf("xres", [128, NTOK * D], F32)
        self.xnT = ctx.sbuf("xnT", [128, NDT * T], BF16)
        self.ident = ctx.sbuf("ident", [128, 128], BF16)
        self.ones_f = ctx.sbuf("ones_f", [128, 128], F32)
        self.ones_b = ctx.sbuf("ones_b", [128, 128], BF16)
        self.fpool = ctx.sbuf("fpool", [128, len(FCONSTS)], F32)
        self.ln_stats = ctx.sbuf("ln_stats", [128, NTOK * 12], F32)
        self.ln_mv = ctx.sbuf("ln_mv", [128, NTOK * 2], F32)
        self.ln_rstd = ctx.sbuf("ln_rstd", [128, NTOK], F32)
        self.ln_nmr = ctx.sbuf("ln_nmr", [128, NTOK], F32)
        self.xstg = ctx.sbuf("xstg", [128, 2 * D], BF16)
        self.triu = ctx.sbuf("triu", [128, 128], BF16)
        self.pst = ctx.psum("pst", [128, 1024], BF16)

    def fc(self, val):
        return self.fpool[:, FCONSTS.index(val):FCONSTS.index(val) + 1]

    def fc1(self, val):
        return self.fpool[0:1, FCONSTS.index(val):FCONSTS.index(val) + 1]

    def load_const(self, const_aps):
        ctx = self.ctx
        def f(e):
            ctx.inc(e.dma_start(out=self.ident[:], in_=const_aps["c_ident"]), "qs", 16)
            ctx.inc(e.dma_start(out=self.ones_f[:], in_=const_aps["c_ones_f32"]), "qs", 16)
            ctx.inc(e.dma_start(out=self.ones_b[:], in_=const_aps["c_ones_bf16"]), "qs", 16)
            ctx.inc(e.dma_start(out=self.fpool[:], in_=const_aps["c_fconst"]), "qs", 16)
            ctx.inc(e.dma_start(out=self.triu[:], in_=const_aps["c_triu"]), "qs", 16)
        on(ctx, "sync", f)
        self.const_tok = ctx.tok("qs")

    def load_x(self, x_in_ap):
        ctx = self.ctx
        xv = x_in_ap.rearrange("(i p) d -> i p d", p=128)
        def f(e):
            for i in range(NTOK):
                ctx.inc(e.dma_start(out=self.xres[:, i * D:(i + 1) * D], in_=xv[i]), "qs", 16)
        on(ctx, "sync", f)
        self.x_tok = ctx.tok("qs")

    def store_x(self, x_out_ap, dep_toks):
        ctx = self.ctx
        xv = x_out_ap.rearrange("(i p) d -> i p d", p=128)
        def f(e):
            e.wait(*dep_toks)
            for i in range(NTOK):
                ctx.inc(e.dma_start(out=xv[i], in_=self.xres[:, i * D:(i + 1) * D]), "qg", 16)
            e.wait(ctx.tok("qg"))
        on(ctx, "gpsimd", f)



def emit_ln_stats_from(p, src_tiles, dep_toks, eps2=None):
    ctx = p.ctx
    ln_scale = 1.0 if eps2 is None else (1.0 + eps2)
    ln_bias = EPS1 if eps2 is None else (EPS1 * eps2)

    def v_stats(e):
        e.wait(*dep_toks)
        for i in range(NTOK):
            st = p.ln_stats[:, i * 12:(i + 1) * 12]
            s = src_tiles(i)
            e.bn_stats(out=st[:, 0:6], in_=s[:, 0:512])
            inst = e.bn_stats(out=st[:, 6:12], in_=s[:, 512:1024])
        tk = ctx.inc(inst, "v")
        e.wait(tk)
        for i in range(NTOK):
            inst = e.bn_aggr(out=p.ln_mv[:, 2 * i:2 * i + 2], in_=p.ln_stats[:, i * 12:(i + 1) * 12])
        ctx.inc(inst, "v")
    on(ctx, "vector", v_stats)
    t_stats = ctx.tok("v")

    mvh = p.ln_mv[:].rearrange("p (i two) -> p i two", two=2)
    mu, var = mvh[:, :, 0], mvh[:, :, 1]

    def s_rstd(e):
        e.wait(t_stats)
        e.activation(out=p.ln_rstd[:], in_=var, func=AF.Ln, bias=p.fc(ln_bias), scale=ln_scale)
        ctx.inc(e.activation(out=p.ln_rstd[:], in_=p.ln_rstd[:], func=AF.Exp, scale=-0.5), "s")
    on(ctx, "scalar", s_rstd)
    t_rstd = ctx.tok("s")

    def v_nmr(e):
        e.wait(t_rstd)
        ctx.inc(e.scalar_tensor_tensor(out=p.ln_nmr[:], in0=mu, scalar=-1.0,
                                       in1=p.ln_rstd[:], op0=ALU.mult, op1=ALU.mult), "v")
    on(ctx, "vector", v_nmr)
    return ctx.tok("v")


def emit_xnT(p, producers):
    ctx = p.ctx
    xnT_v = p.xnT[:].rearrange("p (j c) -> p j c", j=NDT)
    pst_v = p.pst[:].rearrange("p (j c) -> p j c", j=NDT)
    tr_toks = []
    for i in range(NTOK):
        slot = (i % 2) * D
        eng, sem, fn = producers(i)
        def prod(e, i=i, fn=fn, sem=sem):
            if i >= 2:
                e.wait(tr_toks[i - 2])
            ctx.inc(fn(e), sem)
        on(ctx, eng, prod)
        t_p = ctx.tok(sem)
        base_s = ctx.tok("s")
        def pe_t(e, i=i, slot=slot, t_p=t_p, base_s=base_s):
            e.wait(t_p)
            if i > 0:
                e.wait(base_s)
            for j in range(NDT):
                inst = e.transpose(p.pst[:, j * 128:(j + 1) * 128],
                                   p.xstg[:, slot + j * 128:slot + (j + 1) * 128], p.ident[:])
            tr_toks.append(ctx.inc(inst, "t"))
        on(ctx, "tensor", pe_t)
        t_tr = ctx.tok("t")
        def s_ev(e, i=i, t_tr=t_tr):
            e.wait(t_tr)
            ctx.inc(e.copy(out=xnT_v[:, :, i * 128:(i + 1) * 128], in_=pst_v), "s")
        on(ctx, "scalar", s_ev)
    return ctx.tok("s")


def emit_ln_t(p, dep_toks, eps2=None):
    t_nmr = emit_ln_stats_from(p, lambda i: p.xres[:, i * D:(i + 1) * D], dep_toks, eps2)

    def producers(i):
        def fn(e, i=i):
            e.wait(t_nmr)
            return e.activation(out=p.xstg[:, (i % 2) * D:(i % 2 + 1) * D],
                                in_=p.xres[:, i * D:(i + 1) * D], func=AF.Identity,
                                scale=p.ln_rstd[:, i:i + 1], bias=p.ln_nmr[:, i:i + 1])
        return ("scalar", "s", fn)
    t_xnT = emit_xnT(p, producers)
    return t_nmr, t_xnT



class MMPool:
    def __init__(self, ctx, tag, es=None):
        self.ctx = ctx
        self.ps = [ctx.psum(f"{tag}_mma", [128, 512], F32, es=es),
                   ctx.psum(f"{tag}_mmb", [128, 512], F32, es=es)]
        self.evac_tok = [None, None]
        self.pi = 0

    def step(self, deps, mms, evac_fn, eng="scalar", sem="s", width=512, out_p=128):
        ctx = self.ctx
        k = self.pi % 2
        self.pi += 1
        ps = self.ps[k]
        dep_ev = self.evac_tok[k]
        def pe_f(e):
            e.wait(*deps)
            if dep_ev is not None:
                e.wait(dep_ev)
            n = len(mms)
            for kk, (l, r) in enumerate(mms):
                inst = e.matmul(ps[0:out_p, 0:width], l, r, start=(kk == 0), stop=(kk == n - 1))
            ctx.inc(inst, "t")
        on(ctx, "tensor", pe_f)
        t_mm = ctx.tok("t")
        def ev_f(e):
            e.wait(t_mm)
            self.evac_tok[k] = ctx.inc(evac_fn(e, ps), sem)
        on(ctx, eng, ev_f)
        return ctx.tok(sem)

def emit_barrier(p):
    ctx = p.ctx
    toks = [(s, c) for s, c in ctx.cnt.items() if c > 0]
    for eng in ("tensor", "vector", "scalar", "gpsimd", "sync"):
        def f(e, toks=toks):
            e.wait(*toks)
        on(ctx, eng, f)



def emit_l0_ffn_prep(p, wT_ap, alpha_ap, bias_row_ap, tag, es=None):
    ctx = p.ctx
    u = tag
    wq = ctx.sbuf(f"{u}_w", [128, NDT * D], F32, es=es)
    tern = ctx.sbuf(f"{u}_tern", [128, NDT * D], BF16, es=es)
    absacc = ctx.sbuf(f"{u}_absacc", [128, NDT + 1], F32, es=es)
    arow = ctx.sbuf(f"{u}_arow", [1, 4], F32, es=es)
    acol = ctx.sbuf(f"{u}_acol", [128, 3], F32, es=es)
    scr = ctx.sbuf(f"{u}_scr", [128, D], F32, es=es)
    brow = ctx.sbuf(f"{u}_brow", [1, D], F32, es=es)
    brow_b = ctx.sbuf(f"{u}_browb", [1, D], BF16, es=es)
    ps_d = ctx.psum(f"{u}_psd", [1, 8], F32, es=es)
    ps_bc = ctx.psum(f"{u}_psbc", [128, 3], F32, es=es)

    wv = wT_ap.rearrange("(j p) o -> j p o", p=128)

    def dma(e):
        for j in range(NDT):
            ctx.inc(e.dma_start(out=wq[:, j * D:(j + 1) * D], in_=wv[j]), "qg", 16)
        ctx.inc(e.dma_start(out=arow[0:1, 2:3], in_=alpha_ap), "qg", 16)
        ctx.inc(e.dma_start(out=brow[:], in_=bias_row_ap), "qg", 16)
    on(ctx, "gpsimd", dma)
    t_w = ctx.tok("qg")

    def s(e):
        e.wait(t_w)
        for j in range(NDT):
            inst = e.activation(out=scr[:], in_=wq[:, j * D:(j + 1) * D], func=AF.Abs,
                                accum_out=absacc[:, j:j + 1])
        ctx.inc(inst, "s")
    on(ctx, "scalar", s)
    t_abs = ctx.tok("s")

    def v(e):
        e.wait(t_abs)
        ctx.inc(e.reduce_sum(out=absacc[:, NDT:NDT + 1], in_=absacc[:, 0:NDT], axis=AX.X), "v")
    on(ctx, "vector", v)
    t_red = ctx.tok("v")

    def pe(e):
        e.wait(t_red, p.const_tok)
        ctx.inc(e.matmul(ps_d[0:1, 0:1], p.ones_f[:, 0:1], absacc[:, NDT:NDT + 1],
                         start=True, stop=True), "t")
    on(ctx, "tensor", pe)
    t_sum = ctx.tok("t")

    def s2(e):
        e.wait(t_sum)
        e.activation(out=arow[0:1, 0:1], in_=ps_d[0:1, 0:1], func=AF.Copy, scale=0.7 / (D * D))
        e.activation(out=arow[0:1, 1:2], in_=ps_d[0:1, 0:1], func=AF.Copy, scale=-0.7 / (D * D))
        ctx.inc(e.activation(out=arow[0:1, 2:3], in_=arow[0:1, 2:3], func=AF.Copy, scale=0.5), "s")
    on(ctx, "scalar", s2)
    t_row = ctx.tok("s")

    def v2(e):
        e.wait(t_row)
        ctx.inc(e.reciprocal(out=arow[0:1, 3:4], in_=arow[0:1, 2:3]), "v")
    on(ctx, "vector", v2)
    t_rec = ctx.tok("v")

    def pe2(e):
        e.wait(t_rec)
        ctx.inc(e.matmul(ps_bc[:], p.ones_f[0:1, :], arow[0:1, 0:3], start=True, stop=True), "t")
    on(ctx, "tensor", pe2)
    t_bc = ctx.tok("t")

    def s3(e):
        e.wait(t_bc)
        e.copy(out=acol[:], in_=ps_bc[:])
        ctx.inc(e.activation(out=brow_b[:], in_=brow[:], func=AF.Copy, scale=arow[0:1, 3:4]), "s")
    on(ctx, "scalar", s3)
    t_cols = ctx.tok("s")

    def s4(e):
        e.wait(t_cols)
        for j in range(NDT):
            e.activation(out=tern[:, j * D:(j + 1) * D], in_=wq[:, j * D:(j + 1) * D],
                         func=AF.Sign, bias=acol[:, 1:2])
            inst = e.activation(out=wq[:, j * D:(j + 1) * D], in_=wq[:, j * D:(j + 1) * D],
                                func=AF.Sign, bias=acol[:, 0:1])
        ctx.inc(inst, "s")
    on(ctx, "scalar", s4)
    t_signs = ctx.tok("s")

    def v3(e):
        e.wait(t_signs)
        for j in range(NDT):
            inst = e.tensor_tensor(out=tern[:, j * D:(j + 1) * D], in0=tern[:, j * D:(j + 1) * D],
                                   in1=wq[:, j * D:(j + 1) * D], op=ALU.add)
        ctx.inc(inst, "v")
    on(ctx, "vector", v3)
    t_tern = ctx.tok("v")
    return dict(tern=tern, acol=acol, brow_b=brow_b, t_ready=(t_tern, t_cols))


def emit_l0_ffn_call(p, prep, mm, t_xnT):
    ctx = p.ctx
    tern, acol, brow_b = prep["tern"], prep["acol"], prep["brow_b"]
    for i in range(NTOK):
        for half in range(2):
            mms = [(p.xnT[:, j * T + i * 128:j * T + (i + 1) * 128],
                    tern[:, j * D + half * 512:j * D + half * 512 + 512]) for j in range(NDT)]
            mms.append((p.ones_b[0:1, 0:128], brow_b[0:1, half * 512:half * 512 + 512]))
            def ev(e, ps, i=i, half=half):
                sl = p.xres[:, i * D + half * 512:i * D + half * 512 + 512]
                return e.scalar_tensor_tensor(out=sl, in0=ps[:, 0:512], scalar=acol[:, 2:3],
                                              in1=sl, op0=ALU.mult, op1=ALU.add)
            mm.step([t_xnT, *prep["t_ready"]], mms, ev, eng="vector", sem="v")
    return ctx.tok("v")


def emit_l0_attn_b(p, io, x_in_ap, x_out_ap, tag):
    ctx = p.ctx
    u = tag
    sb = lambda n, s, d: ctx.sbuf(f"{u}_{n}", s, d)
    psm = lambda n, s, d: ctx.psum(f"{u}_{n}", s, d)
    NH = 16

    xv = x_in_ap.rearrange("(i p) d -> i p d", p=128)
    xov = x_out_ap.rearrange("(i p) d -> i p d", p=128)
    pev = io["pe_tok"].rearrange("(i p) d -> i p d", p=128)

    xfs = sb("xfs", [128, 2 * D], F32)
    peb = sb("peb", [128, D], F32)
    pescr = sb("pescr", [128, D], F32)

    ld_toks = []
    st_done = []
    for i in range(NTOK):
        def g_l(e, i=i):
            if i >= 2:
                e.wait(st_done[i - 2])
            ld_toks.append(ctx.inc(e.dma_start(out=xfs[:, (i % 2) * D:(i % 2 + 1) * D], in_=xv[i]), "qs", 16))
        on(ctx, "sync", g_l)
        def v_st(e, i=i):
            e.wait(ld_toks[i])
            st = p.ln_stats[:, i * 12:(i + 1) * 12]
            e.bn_stats(out=st[:, 0:6], in_=xfs[:, (i % 2) * D:(i % 2) * D + 512])
            inst = e.bn_stats(out=st[:, 6:12], in_=xfs[:, (i % 2) * D + 512:(i % 2 + 1) * D])
            st_done.append(ctx.inc(inst, "v"))
        on(ctx, "vector", v_st)
    def v_ag(e):
        e.wait(st_done[NTOK - 1])
        for i in range(NTOK):
            inst = e.bn_aggr(out=p.ln_mv[:, 2 * i:2 * i + 2], in_=p.ln_stats[:, i * 12:(i + 1) * 12])
        ctx.inc(inst, "v")
    on(ctx, "vector", v_ag)
    t_stats = ctx.tok("v")

    mvh = p.ln_mv[:].rearrange("p (i two) -> p i two", two=2)
    mu, var = mvh[:, :, 0], mvh[:, :, 1]
    def s_rstd(e):
        e.wait(t_stats)
        e.activation(out=p.ln_rstd[:], in_=var, func=AF.Ln, bias=p.fc(EPS1), scale=1.0)
        ctx.inc(e.activation(out=p.ln_rstd[:], in_=p.ln_rstd[:], func=AF.Exp, scale=-0.5), "s")
    on(ctx, "scalar", s_rstd)
    t_rstd = ctx.tok("s")
    def v_nmr(e):
        e.wait(t_rstd)
        ctx.inc(e.scalar_tensor_tensor(out=p.ln_nmr[:], in0=mu, scalar=-1.0,
                                       in1=p.ln_rstd[:], op0=ALU.mult, op1=ALU.mult), "v")
    on(ctx, "vector", v_nmr)
    t_nmr = ctx.tok("v")

    xa_toks, pa_toks = [], []
    V0 = ctx.cnt["v"]
    def producers(i):
        def g_l(e, i=i):
            e.wait(t_nmr)
            if i >= 1:
                e.wait(("v", V0 + i))
            xa_toks.append(ctx.inc(e.dma_start(out=xfs[:, (i % 2) * D:(i % 2 + 1) * D], in_=xv[i]), "qs", 16))
            pa_toks.append(ctx.inc(e.dma_start(out=peb[:], in_=pev[i]), "qs", 16))
        on(ctx, "sync", g_l)
        def fn(e, i=i):
            e.wait(xa_toks[i], pa_toks[i], t_nmr)
            e.tensor_scalar(out=pescr[:], in0=peb[:],
                            scalar1=p.ln_nmr[:, i:i + 1], scalar2=None, op0=ALU.add)
            return e.scalar_tensor_tensor(out=p.xstg[:, (i % 2) * D:(i % 2 + 1) * D],
                                          in0=xfs[:, (i % 2) * D:(i % 2 + 1) * D],
                                          scalar=p.ln_rstd[:, i:i + 1], in1=pescr[:],
                                          op0=ALU.mult, op1=ALU.add)
        return ("vector", "v", fn)
    t_xnT = emit_xnT(p, producers)

    wbuf = sb("wbuf", [128, NDT * D], BF16)
    stage = sb("stage", [128, D], F32)
    bcols = sb("bcols", [128, 2 * NDT], F32)
    rows_f = sb("rows_f", [1, D], F32)
    rows_b = sb("rows_b", [1, 2 * D], BF16)
    gam = sb("gam", [128, D], F32)
    bet = sb("bet", [128, D], F32)

    def g_small(e):
        ctx.inc(e.dma_start(out=bcols[:, 0:NDT], in_=io["bq_c"]), "qg", 16)
        ctx.inc(e.dma_start(out=bcols[:, NDT:2 * NDT], in_=io["bk_c"]), "qg", 16)
        ctx.inc(e.dma_start(out=rows_f[0:1, 0:D], in_=io["bv_row"]), "qg", 16)
        ctx.inc(e.dma_start(out=gam[:], in_=io["gamma_bc"]), "qg", 16)
        ctx.inc(e.dma_start(out=bet[:], in_=io["beta_bc"]), "qg", 16)
    on(ctx, "gpsimd", g_small)
    t_small = ctx.tok("qg")

    def s_rows(e):
        e.wait(t_small)
        ctx.inc(e.copy(out=rows_b[0:1, 0:D], in_=rows_f[:]), "s")
    on(ctx, "scalar", s_rows)
    t_rows0 = ctx.tok("s")

    def g_small2(e):
        e.wait(t_rows0)
        ctx.inc(e.dma_start(out=rows_f[0:1, 0:D], in_=io["bo_row"]), "qg", 16)
    on(ctx, "gpsimd", g_small2)
    t_small2 = ctx.tok("qg")

    def s_rows2(e):
        e.wait(t_small2)
        ctx.inc(e.copy(out=rows_b[0:1, D:2 * D], in_=rows_f[:]), "s")
    on(ctx, "scalar", s_rows2)
    t_rows = ctx.tok("s")

    def load_weight(wap, dep):
        wvv = wap.rearrange("(j p) f -> j p f", p=128)
        cts = []
        for j in range(NDT):
            def g_w(e, j=j, dep=dep):
                e.wait(*dep)
                if j >= 1:
                    e.wait(cts[j - 1])
                ctx.inc(e.dma_start(out=stage[:], in_=wvv[j]), "qg", 16)
            on(ctx, "gpsimd", g_w)
            t_l = ctx.tok("qg")
            def s_c(e, j=j, t_l=t_l):
                e.wait(t_l)
                cts.append(ctx.inc(e.copy(out=wbuf[:, j * D:(j + 1) * D], in_=stage[:]), "s"))
            on(ctx, "scalar", s_c)
        return ctx.tok("s")

    qT = sb("qT", [128, NDT * T], BF16)
    kT = sb("kT", [128, NDT * T], BF16)
    vplus = sb("vplus", [128, NTOK * 1040], BF16)
    vplus_v = vplus[:].rearrange("p (i h c) -> p i h c", i=NTOK, h=16)
    mm = MMPool(ctx, u)

    def v_ones(e):
        ctx.inc(e.memset(vplus_v[:, :, :, 64:65], 1.0), "v")
    on(ctx, "vector", v_ones)
    t_vones = ctx.tok("v")

    t_proj = None
    for which, dstT, bc_off in [("q", qT, 0), ("k", kT, NDT)]:
        t_w = load_weight(io["w" + which], [t_proj] if t_proj else [])
        for ft in range(NDT):
            for tc in range(4):
                mms = [(wbuf[:, j * D + ft * 128:j * D + (ft + 1) * 128],
                        p.xnT[:, j * T + tc * 512:j * T + (tc + 1) * 512]) for j in range(NDT)]
                def ev(e, ps, dstT=dstT, ft=ft, tc=tc, bc_off=bc_off):
                    return e.activation(out=dstT[:, ft * T + tc * 512:ft * T + (tc + 1) * 512],
                                        in_=ps[:, 0:512], func=AF.Identity, scale=1.0,
                                        bias=bcols[:, bc_off + ft:bc_off + ft + 1])
                t_proj = mm.step([t_xnT, t_w, t_small], mms, ev)
    t_wv = load_weight(io["wv"], [t_proj])
    for ti in range(NTOK):
        for vc in range(2):
            mms = [(p.xnT[:, j * T + ti * 128:j * T + (ti + 1) * 128],
                    wbuf[:, j * D + vc * 512:j * D + (vc + 1) * 512]) for j in range(NDT)]
            mms.append((p.ones_b[0:1, 0:128], rows_b[0:1, vc * 512:(vc + 1) * 512]))
            def ev(e, ps, ti=ti, vc=vc):
                ps_v = ps[:, 0:512].rearrange("p (h c) -> p h c", h=8)
                return e.copy(out=vplus_v[:, ti, vc * 8:(vc + 1) * 8, 0:64], in_=ps_v)
            t_proj = mm.step([t_xnT, t_wv, t_vones, t_rows], mms, ev)
    t_projs = ctx.tok("s")

    ps_s = psm("pss", [128, 1024], F32)
    pv2 = psm("pv2", [128, 512], F32)
    pv3 = psm("pv3", [128, 512], F32)
    pv_banks = [mm.ps[0], mm.ps[1], pv2, pv3]
    PT = sb("PT", [128, 2048], BF16)
    attn_sb = sb("attn_sb", [128, 2048], F32)
    den_sb = sb("den_sb", [1, 2048], F32)

    t_lastdiv = None
    exp_tok = {}
    pv_tok = {}
    for h in range(NH):
        ft, su = h // 2, (h % 2) * 64
        for kt in range(NTOK):
            for half in range(2):
                prev_exp = exp_tok.get((h, kt, half - 1)) or exp_tok.get((h, kt - 1, 1))
                prev_div = t_lastdiv
                def pe_s(e, h=h, kt=kt, half=half, ft=ft, su=su, prev_exp=prev_exp, prev_div=prev_div):
                    e.wait(t_projs)
                    if prev_exp is not None:
                        e.wait(prev_exp)
                    if kt == 0 and half == 0 and prev_div is not None:
                        e.wait(prev_div)
                    for c in range(2):
                        q0 = half * 1024 + c * 512
                        inst = e.matmul(ps_s[:, c * 512:(c + 1) * 512],
                                        kT[su:su + 64, ft * T + kt * 128:ft * T + (kt + 1) * 128],
                                        qT[su:su + 64, ft * T + q0:ft * T + q0 + 512],
                                        start=True, stop=True)
                    ctx.inc(inst, "t")
                on(ctx, "tensor", pe_s)
                t_sc = ctx.tok("t")
                prev_pv = pv_tok.get((h, kt - 1))
                def s_exp(e, kt=kt, half=half, t_sc=t_sc, prev_pv=prev_pv):
                    e.wait(t_sc)
                    if prev_pv is not None:
                        e.wait(prev_pv)
                    inst = e.activation(out=PT[:, half * 1024:(half + 1) * 1024],
                                        in_=ps_s[:], func=AF.Exp, scale=0.125)
                    ctx.inc(inst, "s")
                on(ctx, "scalar", s_exp)
                exp_tok[(h, kt, half)] = ctx.tok("s")
            e_toks = (exp_tok[(h, kt, 0)], exp_tok[(h, kt, 1)])
            prev_div = t_lastdiv
            def pe_pv(e, h=h, kt=kt, e_toks=e_toks, prev_div=prev_div):
                e.wait(*e_toks)
                if kt == 0:
                    e.wait(t_projs)
                    if prev_div is not None:
                        e.wait(prev_div)
                for qc in range(4):
                    inst = e.matmul(pv_banks[qc][0:65, 0:512], vplus_v[:, kt, h, :],
                                    PT[:, qc * 512:(qc + 1) * 512],
                                    start=(kt == 0), stop=(kt == NTOK - 1))
                ctx.inc(inst, "t")
            on(ctx, "tensor", pe_pv)
            pv_tok[(h, kt)] = ctx.tok("t")
        t_pv = pv_tok[(h, NTOK - 1)]

        def s_ev(e, t_pv=t_pv):
            e.wait(t_pv)
            for qc in range(4):
                e.copy(out=attn_sb[0:64, qc * 512:(qc + 1) * 512], in_=pv_banks[qc][0:64, 0:512])
                e.copy(out=den_sb[0:1, qc * 512:(qc + 1) * 512], in_=pv_banks[qc][64:65, 0:512])
            e.activation(out=den_sb[:], in_=den_sb[:], func=AF.Ln, bias=0.0)
            inst = e.activation(out=den_sb[:], in_=den_sb[:], func=AF.Exp, scale=-1.0)
            ctx.inc(inst, "s")
        on(ctx, "scalar", s_ev)
        t_ev = ctx.tok("s")

        def pe_bc(e, t_ev=t_ev):
            e.wait(t_ev)
            for qc in range(4):
                inst = e.matmul(ps_s[(qc % 2) * 64:(qc % 2) * 64 + 64, (qc // 2) * 512:(qc // 2 + 1) * 512],
                                p.ones_f[0:1, 0:64],
                                den_sb[0:1, qc * 512:(qc + 1) * 512],
                                start=True, stop=True)
            ctx.inc(inst, "t")
        on(ctx, "tensor", pe_bc)
        t_bc = ctx.tok("t")

        def v_div(e, ft=ft, su=su, t_bc=t_bc):
            e.wait(t_bc)
            for qc in range(4):
                rb = ps_s[(qc % 2) * 64:(qc % 2) * 64 + 64, (qc // 2) * 512:(qc // 2 + 1) * 512]
                inst = e.tensor_tensor(out=qT[su:su + 64, ft * T + qc * 512:ft * T + (qc + 1) * 512],
                                       in0=attn_sb[0:64, qc * 512:(qc + 1) * 512], in1=rb, op=ALU.mult)
            ctx.inc(inst, "v")
        on(ctx, "vector", v_div)
        t_lastdiv = ctx.tok("v")

    t_wo = load_weight(io["wo"], [t_lastdiv])
    xpescr = pescr
    hbuf = stage
    st2buf = sb("st2buf", [128, 16], F32)
    st2 = st2buf[:, 0:12]
    mv2 = st2buf[:, 12:14]
    r2 = st2buf[:, 14:15]
    n2 = st2buf[:, 15:16]
    xc_toks, pc_toks, out_toks = [], [], []
    vh_toks = []
    for i in range(NTOK):
        def g_l(e, i=i):
            if i >= 2:
                e.wait(out_toks[i - 2])
            if i >= 1:
                e.wait(vh_toks[i - 1])
            e.wait(t_nmr, t_lastdiv)
            xc_toks.append(ctx.inc(e.dma_start(out=xfs[:, (i % 2) * D:(i % 2 + 1) * D], in_=xv[i]), "qs", 16))
            pc_toks.append(ctx.inc(e.dma_start(out=peb[:], in_=pev[i]), "qs", 16))
        on(ctx, "sync", g_l)
        mm_toks = []
        for dc in range(2):
            mms = [(qT[:, j * T + i * 128:j * T + (i + 1) * 128],
                    wbuf[:, j * D + dc * 512:j * D + (dc + 1) * 512]) for j in range(NDT)]
            mms.append((p.ones_b[0:1, 0:128], rows_b[0:1, D + dc * 512:D + (dc + 1) * 512]))
            def ev(e, ps, i=i, dc=dc):
                return e.tensor_copy(out=hbuf[:, dc * 512:(dc + 1) * 512], in_=ps[:, 0:512])
            mm_toks.append(mm.step([t_wo, t_lastdiv, t_rows], mms, ev, eng="vector", sem="v"))
        def v_h(e, i=i, mm_toks=mm_toks):
            e.wait(xc_toks[i], pc_toks[i], *mm_toks)
            e.tensor_scalar(out=xpescr[:], in0=peb[:],
                            scalar1=p.ln_nmr[:, i:i + 1], scalar2=None, op0=ALU.add)
            e.scalar_tensor_tensor(out=xpescr[:], in0=xfs[:, (i % 2) * D:(i % 2 + 1) * D],
                                   scalar=p.ln_rstd[:, i:i + 1], in1=xpescr[:],
                                   op0=ALU.mult, op1=ALU.add)
            inst = e.tensor_tensor(out=hbuf[:], in0=hbuf[:], in1=xpescr[:], op=ALU.add)
            tk = ctx.inc(inst, "v")
            e.wait(tk)
            e.bn_stats(out=st2[:, 0:6], in_=hbuf[:, 0:512])
            inst = e.bn_stats(out=st2[:, 6:12], in_=hbuf[:, 512:1024])
            tk = ctx.inc(inst, "v")
            e.wait(tk)
            ctx.inc(e.bn_aggr(out=mv2[:], in_=st2[:]), "v")
        on(ctx, "vector", v_h)
        t_hs = ctx.tok("v")
        vh_toks.append(t_hs)
        def s_r2(e, t_hs=t_hs):
            e.wait(t_hs)
            e.activation(out=r2, in_=st2buf[:, 13:14], func=AF.Ln, bias=p.fc(EPS1), scale=1.0)
            ctx.inc(e.activation(out=r2, in_=r2, func=AF.Exp, scale=-0.5), "s")
        on(ctx, "scalar", s_r2)
        t_r2 = ctx.tok("s")
        def v_f(e, i=i, t_r2=t_r2):
            e.wait(t_r2)
            ctx.inc(e.scalar_tensor_tensor(out=n2, in0=st2buf[:, 12:13], scalar=-1.0, in1=r2,
                                           op0=ALU.mult, op1=ALU.mult), "v")
        on(ctx, "vector", v_f)
        t_n2 = ctx.tok("v")
        def s_a(e, i=i, t_n2=t_n2):
            e.wait(t_n2)
            ctx.inc(e.activation(out=xpescr[:], in_=hbuf[:], func=AF.Identity,
                                 scale=r2, bias=n2), "s")
        on(ctx, "scalar", s_a)
        t_a = ctx.tok("s")
        def v_out(e, i=i, t_a=t_a):
            e.wait(t_a, t_small)
            e.scalar_tensor_tensor(out=xpescr[:], in0=xpescr[:], scalar=1.0, in1=gam[:],
                                   op0=ALU.mult, op1=ALU.mult)
            e.tensor_tensor(out=xpescr[:], in0=xpescr[:], in1=bet[:], op=ALU.add)
            inst = e.tensor_tensor(out=xfs[:, (i % 2) * D:(i % 2 + 1) * D],
                                   in0=xfs[:, (i % 2) * D:(i % 2 + 1) * D], in1=xpescr[:], op=ALU.add)
            ctx.inc(inst, "v")
        on(ctx, "vector", v_out)
        t_vo = ctx.tok("v")
        def g_o(e, i=i, t_vo=t_vo):
            e.wait(t_vo)
            out_toks.append(ctx.inc(e.dma_start(out=xov[i], in_=xfs[:, (i % 2) * D:(i % 2 + 1) * D]), "qg", 16))
        on(ctx, "gpsimd", g_o)
    def g_fin(e):
        e.wait(ctx.tok("qg"))
    on(ctx, "gpsimd", g_fin)
    return ctx.tok("qg")


def emit_l0_attn_a(p, io, x_in_ap, x_out_ap, tag, dbg_upto=None, dbg_aps=None):
    ctx = p.ctx
    u_ = tag
    sb = lambda n, s, d: ctx.sbuf(f"{u_}_{n}", s, d)
    psm = lambda n, s, d: ctx.psum(f"{u_}_{n}", s, d)

    xv = x_in_ap.rearrange("(i p) d -> i p d", p=128)
    xov = x_out_ap.rearrange("(i p) d -> i p d", p=128)
    xfs = sb("xfs", [128, 2 * D], F32)

    ld_toks, st_done = [], []
    for i in range(NTOK):
        def g_l(e, i=i):
            if i >= 2:
                e.wait(st_done[i - 2])
            ld_toks.append(ctx.inc(e.dma_start(out=xfs[:, (i % 2) * D:(i % 2 + 1) * D], in_=xv[i]), "qs", 16))
        on(ctx, "sync", g_l)
        def v_st(e, i=i):
            e.wait(ld_toks[i])
            st = p.ln_stats[:, i * 12:(i + 1) * 12]
            e.bn_stats(out=st[:, 0:6], in_=xfs[:, (i % 2) * D:(i % 2) * D + 512])
            inst = e.bn_stats(out=st[:, 6:12], in_=xfs[:, (i % 2) * D + 512:(i % 2 + 1) * D])
            st_done.append(ctx.inc(inst, "v"))
        on(ctx, "vector", v_st)
    def v_ag(e):
        e.wait(st_done[NTOK - 1])
        for i in range(NTOK):
            inst = e.bn_aggr(out=p.ln_mv[:, 2 * i:2 * i + 2], in_=p.ln_stats[:, i * 12:(i + 1) * 12])
        ctx.inc(inst, "v")
    on(ctx, "vector", v_ag)
    t_stats = ctx.tok("v")
    mvh = p.ln_mv[:].rearrange("p (i two) -> p i two", two=2)
    mu0, var0 = mvh[:, :, 0], mvh[:, :, 1]
    def s_rstd(e):
        e.wait(t_stats)
        e.activation(out=p.ln_rstd[:], in_=var0, func=AF.Ln, bias=p.fc(EPS1 * 1e-6), scale=1.0 + 1e-6)
        ctx.inc(e.activation(out=p.ln_rstd[:], in_=p.ln_rstd[:], func=AF.Exp, scale=-0.5), "s")
    on(ctx, "scalar", s_rstd)
    t_rstd0 = ctx.tok("s")
    def v_nmr0(e):
        e.wait(t_rstd0)
        ctx.inc(e.scalar_tensor_tensor(out=p.ln_nmr[:], in0=mu0, scalar=-1.0,
                                       in1=p.ln_rstd[:], op0=ALU.mult, op1=ALU.mult), "v")
    on(ctx, "vector", v_nmr0)
    t_nmr = ctx.tok("v")

    xa_toks = []
    S0 = ctx.cnt["s"]
    def producers(i):
        def g_l2(e, i=i):
            e.wait(t_nmr)
            if i >= 2:
                e.wait(("s", S0 + (i - 1) * 2 + 1))
            xa_toks.append(ctx.inc(e.dma_start(out=xfs[:, (i % 2) * D:(i % 2 + 1) * D], in_=xv[i]), "qs", 16))
        on(ctx, "sync", g_l2)
        def fn(e, i=i):
            e.wait(xa_toks[i], t_nmr)
            return e.activation(out=p.xstg[:, (i % 2) * D:(i % 2 + 1) * D],
                                in_=xfs[:, (i % 2) * D:(i % 2 + 1) * D], func=AF.Identity,
                                scale=p.ln_rstd[:, i:i + 1], bias=p.ln_nmr[:, i:i + 1])
        return ("scalar", "s", fn)
    t_xnT = emit_xnT(p, producers)

    ubuf = sb("ubuf", [128, NDT * T], BF16)
    cxbuf = sb("cxbuf", [128, NDT * T], BF16)
    wbuf = sb("wbuf", [128, NDT * D], BF16)
    stage = sb("stage", [128, D], F32)
    cols = sb("cols", [128, 8 * NDT], F32)
    ubias = sb("ubias", [128, NDT], F32)
    ratio = sb("ratio", [128, 2 * NDT], BF16)
    dbc = sb("dbc", [128, T], F32)
    wsum_b = sb("wsumb", [1, D], BF16)
    bwo_b = sb("bwob", [1, D], BF16)
    nmr_b = sb("nmrb", [1, T], BF16)
    rinv_b = sb("rinvb", [1, T], BF16)
    ones512 = sb("ones512", [1, 512], BF16)
    rbc = sb("rbc", [128, T], BF16)
    rrow = sb("rrow", [1, T], F32)
    mm = MMPool(ctx, u_)
    ps_bwt = psm("psbwt", [128, NDT], F32)
    ps_row = psm("psrow", [1, T], F32)

    c_ln1w = cols[:, 0:8]; c_ln1b = cols[:, 8:16]
    c_lncw = cols[:, 16:24]; c_lncb = cols[:, 24:32]
    c_lngw = cols[:, 32:40]; c_lngb = cols[:, 40:48]
    c_dec = cols[:, 48:56]; c_beta = cols[:, 56:64]

    def g_cols(e):
        ctx.inc(e.dma_start(out=c_ln1w, in_=io["ln1w_c"]), "qg", 16)
        ctx.inc(e.dma_start(out=c_ln1b, in_=io["ln1b_c"]), "qg", 16)
        ctx.inc(e.dma_start(out=c_lncw, in_=io["lncw_c"]), "qg", 16)
        ctx.inc(e.dma_start(out=c_lncb, in_=io["lncb_c"]), "qg", 16)
        ctx.inc(e.dma_start(out=c_lngw, in_=io["lngw_c"]), "qg", 16)
        ctx.inc(e.dma_start(out=c_lngb, in_=io["lngb_c"]), "qg", 16)
        ctx.inc(e.dma_start(out=c_dec, in_=p.c_aps["c_decay"]), "qg", 16)
        ctx.inc(e.dma_start(out=c_beta, in_=p.c_aps["c_decay_beta"]), "qg", 16)
    on(ctx, "gpsimd", g_cols)
    t_cols = ctx.tok("qg")

    def v_ratio(e):
        e.wait(t_cols)
        tk = ctx.inc(e.reciprocal(out=ubias[:, 0:NDT], in_=c_ln1w), "v")
        e.wait(tk)
        tk = ctx.inc(e.tensor_tensor(out=ratio[:, 0:NDT], in0=c_ln1b, in1=ubias[:, 0:NDT], op=ALU.mult), "v")
        e.wait(tk)
        tk = ctx.inc(e.reciprocal(out=ubias[:, 0:NDT], in_=c_lncw), "v")
        e.wait(tk)
        ctx.inc(e.tensor_tensor(out=ratio[:, NDT:2 * NDT], in0=c_lncb, in1=ubias[:, 0:NDT], op=ALU.mult), "v")
    on(ctx, "vector", v_ratio)
    t_ratio = ctx.tok("v")

    wtv = io["Wt"].rearrange("(j p) h -> j p h", p=128)
    cast_toks = []
    for j in range(NDT):
        def g_w(e, j=j):
            if j >= 1:
                e.wait(cast_toks[j - 1])
            ctx.inc(e.dma_start(out=stage[:], in_=wtv[j]), "qg", 16)
        on(ctx, "gpsimd", g_w)
        t_l = ctx.tok("qg")
        def s_c(e, j=j, t_l=t_l):
            e.wait(t_l, t_cols)
            cast_toks.append(ctx.inc(e.activation(out=wbuf[:, j * D:(j + 1) * D], in_=stage[:],
                                                  func=AF.Identity, scale=c_ln1w[:, j:j + 1], bias=0.0), "s"))
        on(ctx, "scalar", s_c)
    t_wt = ctx.tok("s")

    def pe_b(e):
        e.wait(t_wt, t_ratio)
        for jt in range(NDT):
            for j in range(NDT):
                inst = e.matmul(ps_bwt[:, jt:jt + 1], wbuf[:, j * D + jt * 128:j * D + (jt + 1) * 128],
                                ratio[:, j:j + 1], start=(j == 0), stop=(j == NDT - 1))
        ctx.inc(inst, "t")
    on(ctx, "tensor", pe_b)

    def v_ub(e):
        e.wait(ctx.tok("t"), t_wt)
        ctx.inc(e.tensor_tensor(out=ubias[:], in0=ps_bwt[:], in1=c_beta, op=ALU.mult), "v")
    on(ctx, "vector", v_ub)
    t_ub = ctx.tok("v")

    for ht in range(NDT):
        for sc in range(4):
            mms = [(wbuf[:, j * D + ht * 128:j * D + (ht + 1) * 128],
                    p.xnT[:, j * T + sc * 512:j * T + (sc + 1) * 512]) for j in range(NDT)]
            def ev(e, ps, ht=ht, sc=sc):
                return e.activation(out=ubuf[:, ht * T + sc * 512:ht * T + (sc + 1) * 512],
                                    in_=ps[:, 0:512], func=AF.Identity,
                                    scale=c_beta[:, ht:ht + 1], bias=ubias[:, ht:ht + 1])
            mm.step([t_xnT, t_wt, t_ub], mms, ev)
    t_u = ctx.tok("s")

    scan_toks = []
    def v_init(e):
        ctx.inc(e.memset(dbc[:], 0.0), "v")
    on(ctx, "vector", v_init)
    for ht in range(NDT):
        def v_sc(e, ht=ht):
            e.wait(t_u, t_cols)
            e.tensor_scalar(out=dbc[:], in0=dbc[:], scalar1=0.0, scalar2=c_dec[:, ht:ht + 1],
                            op0=ALU.mult, op1=ALU.add)
            inst = e.tensor_tensor_scan(out=cxbuf[:, ht * T:(ht + 1) * T], data0=dbc[:],
                                        data1=ubuf[:, ht * T:(ht + 1) * T], initial=0.0,
                                        op0=ALU.mult, op1=ALU.add)
            scan_toks.append(ctx.inc(inst, "v"))
        on(ctx, "vector", v_sc)
    t_scan = ctx.tok("v")

    sqs = sb("sqs", [128, 512], BF16)
    for sc in range(4):
        def pe_s1(e, sc=sc):
            e.wait(t_scan, p.const_tok)
            for ht in range(NDT):
                inst = e.matmul(ps_row[0:1, sc * 512:(sc + 1) * 512], p.ones_b[:, 0:1],
                                cxbuf[:, ht * T + sc * 512:ht * T + (sc + 1) * 512],
                                start=(ht == 0), stop=(ht == NDT - 1))
            ctx.inc(inst, "t")
        on(ctx, "tensor", pe_s1)
    t_sum1 = ctx.tok("t")
    def s_ev1(e):
        e.wait(t_sum1)
        ctx.inc(e.copy(out=dbc[0:1, 0:T], in_=ps_row[:]), "s")
    on(ctx, "scalar", s_ev1)
    t_r1 = ctx.tok("s")
    sq_toks = {}
    for sc in range(4):
        for ht in range(NDT):
            def s_sq(e, ht=ht, sc=sc):
                e.wait(t_scan)
                if sq_toks:
                    pass
                e.wait(t_r1)
                ctx.inc(e.activation(out=sqs[:], in_=cxbuf[:, ht * T + sc * 512:ht * T + (sc + 1) * 512],
                                     func=AF.Square), "s")
            on(ctx, "scalar", s_sq)
            t_sq = ctx.tok("s")
            def pe_s2(e, ht=ht, sc=sc, t_sq=t_sq):
                e.wait(t_sq)
                ctx.inc(e.matmul(ps_row[0:1, sc * 512:(sc + 1) * 512], p.ones_b[:, 0:1], sqs[:],
                                 start=(ht == 0), stop=(ht == NDT - 1)), "t")
            on(ctx, "tensor", pe_s2)
            sq_toks[(sc, ht)] = ctx.tok("t")
            t_mm2 = ctx.tok("t")
            def s_gate(e, t_mm2=t_mm2):
                e.wait(t_mm2)
            on(ctx, "scalar", s_gate)
    t_sum2 = ctx.tok("t")
    def s_ev2(e):
        e.wait(t_sum2)
        ctx.inc(e.copy(out=rrow[0:1, 0:T], in_=ps_row[:]), "s")
    on(ctx, "scalar", s_ev2)
    t_r2tok = ctx.tok("s")

    mu_r = dbc[0:1, 0:T]
    r_r = rrow[0:1, 0:T]
    def v_rows_fixed(e):
        e.wait(t_r2tok, t_scan)
        e.tensor_scalar(out=mu_r, in0=mu_r, scalar1=1.0 / D, scalar2=None, op0=ALU.mult)
        e.tensor_scalar(out=r_r, in0=r_r, scalar1=1.0 / D, scalar2=None, op0=ALU.mult)
        e.tensor_tensor(out=nmr_b[0:1, 0:T], in0=mu_r, in1=mu_r, op=ALU.mult)
        inst = e.tensor_tensor(out=r_r, in0=r_r, in1=nmr_b[0:1, 0:T], op=ALU.subtract)
        ctx.inc(inst, "v")
    on(ctx, "vector", v_rows_fixed)
    t_var = ctx.tok("v")
    def s_rr(e):
        e.wait(t_var)
        e.activation(out=r_r, in_=r_r, func=AF.Ln, bias=p.fc1(1e-6), scale=1.0)
        e.activation(out=rinv_b[0:1, 0:T], in_=r_r, func=AF.Exp, scale=0.5)
        ctx.inc(e.activation(out=r_r, in_=r_r, func=AF.Exp, scale=-0.5), "s")
    on(ctx, "scalar", s_rr)
    t_rr = ctx.tok("s")
    def v_nm(e):
        e.wait(t_rr)
        e.tensor_scalar(out=nmr_b[0:1, 0:T], in0=mu_r, scalar1=-1.0, scalar2=None, op0=ALU.mult)
        ctx.inc(e.memset(ones512[:], 1.0), "v")
    on(ctx, "vector", v_nm)
    t_nm = ctx.tok("v")

    if dbg_upto == "scan":
        def g_d(e):
            e.wait(t_scan, t_xnT)
            ctx.inc(e.dma_start(out=dbg_aps["d_u"], in_=ubuf[:]), "qg", 16)
            ctx.inc(e.dma_start(out=dbg_aps["d_cx"], in_=cxbuf[:]), "qg", 16)
            ctx.inc(e.dma_start(out=dbg_aps["d_xnT"], in_=p.xnT[:]), "qg", 16)
            ctx.inc(e.dma_start(out=dbg_aps["d_ub"], in_=ubias[:]), "qg", 16)
            ctx.inc(e.dma_start(out=dbg_aps["d_ratio"], in_=ratio[:]), "qg", 16)
            e.wait(ctx.tok("qg"))
        on(ctx, "gpsimd", g_d)
        return ctx.tok("qg")
    wov = io["Wo"].rearrange("(j p) d2 -> j p d2", p=128)
    cast2 = []
    for j in range(NDT):
        def g_w(e, j=j):
            e.wait(t_u)
            if j >= 1:
                e.wait(cast2[j - 1])
            ctx.inc(e.dma_start(out=stage[:], in_=wov[j]), "qg", 16)
        on(ctx, "gpsimd", g_w)
        t_l = ctx.tok("qg")
        def s_c2(e, j=j, t_l=t_l):
            e.wait(t_l)
            cast2.append(ctx.inc(e.activation(out=wbuf[:, j * D:(j + 1) * D], in_=stage[:],
                                              func=AF.Identity, scale=c_lncw[:, j:j + 1], bias=0.0), "s"))
        on(ctx, "scalar", s_c2)
    t_wo = ctx.tok("s")

    def pe_bwo(e):
        e.wait(t_wo, t_ratio, t_r2tok)
        for c in range(2):
            for j in range(NDT):
                inst = e.matmul(ps_row[0:1, c * 512:(c + 1) * 512], ratio[:, NDT + j:NDT + j + 1],
                                wbuf[:, j * D + c * 512:j * D + (c + 1) * 512],
                                start=(j == 0), stop=(j == NDT - 1))
        ctx.inc(inst, "t")
    on(ctx, "tensor", pe_bwo)

    def s_bwo(e):
        e.wait(ctx.tok("t"))
        ctx.inc(e.copy(out=bwo_b[0:1, 0:D], in_=ps_row[0:1, 0:D]), "s")
    on(ctx, "scalar", s_bwo)
    t_bwo = ctx.tok("s")

    for c in range(2):
        def pe_ws(e, c=c):
            e.wait(t_wo, t_bwo)
            for j in range(NDT):
                inst = e.matmul(ps_row[0:1, c * 512:(c + 1) * 512], p.ones_b[:, 0:1],
                                wbuf[:, j * D + c * 512:j * D + (c + 1) * 512],
                                start=(j == 0), stop=(j == NDT - 1))
            ctx.inc(inst, "t")
        on(ctx, "tensor", pe_ws)
    t_ws = ctx.tok("t")
    def s_ws(e):
        e.wait(t_ws)
        ctx.inc(e.copy(out=wsum_b[0:1, 0:D], in_=ps_row[0:1, 0:D]), "s")
    on(ctx, "scalar", s_ws)
    t_ws2 = ctx.tok("s")

    for c in range(4):
        def ev(e, ps, c=c):
            return e.tensor_copy(out=rbc[:, c * 512:(c + 1) * 512], in_=ps[:, 0:512])
        mm.step([t_rr, t_nm], [(p.ones_f[0:1, :], rrow[0:1, c * 512:(c + 1) * 512])], ev,
                eng="vector", sem="v")
    t_rbc = ctx.tok("v")

    for dt in range(NDT):
        for sc in range(4):
            mms = [(wbuf[:, j * D + dt * 128:j * D + (dt + 1) * 128],
                    cxbuf[:, j * T + sc * 512:j * T + (sc + 1) * 512]) for j in range(NDT)]
            mms.append((wsum_b[0:1, dt * 128:(dt + 1) * 128], nmr_b[0:1, sc * 512:(sc + 1) * 512]))
            mms.append((bwo_b[0:1, dt * 128:(dt + 1) * 128], rinv_b[0:1, sc * 512:(sc + 1) * 512]))
            def ev(e, ps, dt=dt, sc=sc):
                return e.tensor_tensor(out=ubuf[:, dt * T + sc * 512:dt * T + (sc + 1) * 512],
                                       in0=ps[:, 0:512], in1=rbc[:, sc * 512:(sc + 1) * 512],
                                       op=ALU.mult)
            mm.step([t_wo, t_ws2, t_nm, t_rbc, t_scan], mms, ev, eng="vector", sem="v")
    t_om = ctx.tok("v")

    if dbg_upto == "om":
        def g_d(e):
            e.wait(t_om)
            ctx.inc(e.dma_start(out=dbg_aps["d_u"], in_=ubuf[:]), "qg", 16)
            ctx.inc(e.dma_start(out=dbg_aps["d_cx"], in_=cxbuf[:]), "qg", 16)
            ctx.inc(e.dma_start(out=dbg_aps["d_xnT"], in_=p.xnT[:]), "qg", 16)
            ctx.inc(e.dma_start(out=dbg_aps["d_ub"], in_=ubias[:]), "qg", 16)
            ctx.inc(e.dma_start(out=dbg_aps["d_ratio"], in_=ratio[:]), "qg", 16)
            ctx.inc(e.dma_start(out=dbg_aps["d_rrow"], in_=rrow[:]), "qg", 16)
            ctx.inc(e.dma_start(out=dbg_aps["d_murow"], in_=dbc[0:1, 0:T]), "qg", 16)
            e.wait(ctx.tok("qg"))
        on(ctx, "gpsimd", g_d)
        return ctx.tok("qg")
    def s_xt(e):
        e.wait(t_om)
        for j in range(NDT):
            inst = e.activation(out=p.xnT[:, j * T:(j + 1) * T], in_=p.xnT[:, j * T:(j + 1) * T],
                                func=AF.Identity, scale=c_ln1w[:, j:j + 1], bias=c_ln1b[:, j:j + 1])
        ctx.inc(inst, "s")
    on(ctx, "scalar", s_xt)
    t_xt = ctx.tok("s")

    wgv = io["Wg"].rearrange("(j p) g -> j p g", p=128)
    bg_row = sb("bgrow", [1, D], BF16)
    bg_f = sb("bgf", [1, D], F32)
    def g_bg(e):
        ctx.inc(e.dma_start(out=bg_f[:], in_=io["bg_row"]), "qg", 16)
    on(ctx, "gpsimd", g_bg)
    t_bgl = ctx.tok("qg")
    def s_bg(e):
        e.wait(t_bgl)
        ctx.inc(e.copy(out=bg_row[:], in_=bg_f[:]), "s")
    on(ctx, "scalar", s_bg)
    t_bg = ctx.tok("s")
    cast3 = []
    for j in range(16):
        def g_w(e, j=j):
            e.wait(t_om)
            if j >= 1:
                e.wait(cast3[j - 1])
            ctx.inc(e.dma_start(out=stage[:], in_=wgv[j]), "qg", 16)
        on(ctx, "gpsimd", g_w)
        t_l = ctx.tok("qg")
        def s_c3(e, j=j, t_l=t_l):
            e.wait(t_l)
            cast3.append(ctx.inc(e.copy(out=cxbuf[:, j * D:(j + 1) * D], in_=stage[:]), "s"))
        on(ctx, "scalar", s_c3)
    t_wg = ctx.tok("s")

    ystg = sb("ystg", [128, NDT * 512], BF16)
    substg = ystg
    onesr = sb("onesr", [1, 512], BF16)
    def v_or(e):
        ctx.inc(e.memset(onesr[:], 1.0), "v")
    on(ctx, "vector", v_or)
    t_or = ctx.tok("v")
    gstg = sb("gstg", [128, NDT * 512], BF16)
    r3bc = sb("r3bc", [128, 512], F32)
    n3bc = sb("n3bc", [128, 512], F32)
    xc_toks, out_toks = [], []
    last_tok = None
    for sc in range(4):
        y_toks = []
        for gt in range(NDT):
            mms = [(cxbuf[:, j * D + gt * 128:j * D + (gt + 1) * 128],
                    (p.xnT if j < NDT else ubuf)[:, (j % NDT) * T + sc * 512:(j % NDT) * T + (sc + 1) * 512])
                   for j in range(16)]
            mms.append((bg_row[0:1, gt * 128:(gt + 1) * 128], onesr[0:1, 0:512]))
            def ev(e, ps, gt=gt, sc=sc):
                return e.copy(out=ystg[:, gt * 512:(gt + 1) * 512], in_=ps[:, 0:512])
            y_toks.append(mm.step([t_xt, t_wg, t_bg, t_or], mms, ev))
        t_y = ctx.tok("s")
        def pe_sy(e, sc=sc, t_y=t_y):
            e.wait(t_y)
            for gt in range(NDT):
                inst = e.matmul(ps_row[0:1, 0:512], p.ones_b[:, 0:1], ystg[:, gt * 512:(gt + 1) * 512],
                                start=(gt == 0), stop=(gt == NDT - 1))
            ctx.inc(inst, "t")
        on(ctx, "tensor", pe_sy)
        t_sy = ctx.tok("t")
        def s_evy(e, t_sy=t_sy):
            e.wait(t_sy)
            ctx.inc(e.copy(out=dbc[0:1, 0:512], in_=ps_row[0:1, 0:512]), "s")
        on(ctx, "scalar", s_evy)
        t_ry = ctx.tok("s")
        sq2_toks = []
        for gt in range(NDT):
            def s_sq2(e, gt=gt, t_ry=t_ry):
                e.wait(t_ry)
                if sq2_toks:
                    e.wait(sq2_toks[-1])
                ctx.inc(e.activation(out=sqs[:], in_=ystg[:, gt * 512:(gt + 1) * 512], func=AF.Square), "s")
            on(ctx, "scalar", s_sq2)
            t_sq = ctx.tok("s")
            def pe_sq2(e, gt=gt, t_sq=t_sq):
                e.wait(t_sq)
                ctx.inc(e.matmul(ps_row[0:1, 0:512], p.ones_b[:, 0:1], sqs[:],
                                 start=(gt == 0), stop=(gt == NDT - 1)), "t")
            on(ctx, "tensor", pe_sq2)
            sq2_toks.append(ctx.tok("t"))
        t_sy2 = ctx.tok("t")
        def s_evy2(e, t_sy2=t_sy2):
            e.wait(t_sy2)
            ctx.inc(e.copy(out=dbc[0:1, 512:1024], in_=ps_row[0:1, 0:512]), "s")
        on(ctx, "scalar", s_evy2)
        t_ry2 = ctx.tok("s")
        def v_r3(e, t_ry2=t_ry2):
            e.wait(t_ry2)
            e.tensor_scalar(out=dbc[0:1, 0:512], in0=dbc[0:1, 0:512], scalar1=1.0 / D,
                            scalar2=None, op0=ALU.mult)
            e.tensor_scalar(out=dbc[0:1, 512:1024], in0=dbc[0:1, 512:1024], scalar1=1.0 / D,
                            scalar2=None, op0=ALU.mult)
            e.tensor_tensor(out=dbc[0:1, 1024:1536], in0=dbc[0:1, 0:512], in1=dbc[0:1, 0:512],
                            op=ALU.mult)
            inst = e.tensor_tensor(out=dbc[0:1, 512:1024], in0=dbc[0:1, 512:1024],
                                   in1=dbc[0:1, 1024:1536], op=ALU.subtract)
            ctx.inc(inst, "v")
        on(ctx, "vector", v_r3)
        t_v3 = ctx.tok("v")
        def s_r3(e, t_v3=t_v3):
            e.wait(t_v3)
            e.activation(out=dbc[0:1, 512:1024], in_=dbc[0:1, 512:1024], func=AF.Ln,
                         bias=p.fc1(1e-6), scale=1.0)
            ctx.inc(e.activation(out=dbc[0:1, 512:1024], in_=dbc[0:1, 512:1024],
                                 func=AF.Exp, scale=-0.5), "s")
        on(ctx, "scalar", s_r3)
        t_sr3 = ctx.tok("s")
        def v_n3(e, t_sr3=t_sr3):
            e.wait(t_sr3)
            ctx.inc(e.scalar_tensor_tensor(out=dbc[0:1, 0:512], in0=dbc[0:1, 0:512], scalar=-1.0,
                                           in1=dbc[0:1, 512:1024], op0=ALU.mult, op1=ALU.mult), "v")
        on(ctx, "vector", v_n3)
        t_n3 = ctx.tok("v")
        def ev_r(e, ps):
            return e.tensor_copy(out=r3bc[:], in_=ps[:, 0:512])
        mm.step([t_n3], [(p.ones_f[0:1, :], dbc[0:1, 512:1024])], ev_r, eng="vector", sem="v")
        def ev_n(e, ps):
            return e.tensor_copy(out=n3bc[:], in_=ps[:, 0:512])
        mm.step([t_n3], [(p.ones_f[0:1, :], dbc[0:1, 0:512])], ev_n, eng="vector", sem="v")
        t_bc3 = ctx.tok("v")
        g_toks = []
        for gt in range(NDT):
            def v_yh(e, gt=gt, t_bc3=t_bc3):
                e.wait(t_bc3)
                e.scalar_tensor_tensor(out=sqs[:], in0=ystg[:, gt * 512:(gt + 1) * 512], scalar=1.0,
                                       in1=r3bc[:], op0=ALU.mult, op1=ALU.mult)
                inst = e.tensor_tensor(out=sqs[:], in0=sqs[:], in1=n3bc[:], op=ALU.add)
                ctx.inc(inst, "v")
            on(ctx, "vector", v_yh)
            t_yh = ctx.tok("v")
            def s_sig(e, gt=gt, t_yh=t_yh):
                e.wait(t_yh)
                ctx.inc(e.activation(out=gstg[:, gt * 512:(gt + 1) * 512], in_=sqs[:], func=AF.Sigmoid,
                                     scale=c_lngw[:, gt:gt + 1], bias=c_lngb[:, gt:gt + 1]), "s")
            on(ctx, "scalar", s_sig)
            t_sg = ctx.tok("s")
            def v_gate2(e, t_sg=t_sg):
                e.wait(t_sg)
            on(ctx, "vector", v_gate2)
            g_toks.append(t_sg)
        comb_toks = []
        for dt2 in range(NDT):
            def v_cb(e, dt2=dt2, sc=sc, g_toks=g_toks):
                e.wait(g_toks[dt2])
                a = p.xnT[:, dt2 * T + sc * 512:dt2 * T + (sc + 1) * 512]
                o = ubuf[:, dt2 * T + sc * 512:dt2 * T + (sc + 1) * 512]
                d_ = substg[:, dt2 * 512:(dt2 + 1) * 512]
                e.tensor_tensor(out=d_, in0=o, in1=a, op=ALU.subtract)
                e.tensor_tensor(out=d_, in0=d_, in1=gstg[:, dt2 * 512:(dt2 + 1) * 512], op=ALU.mult)
                inst = e.tensor_tensor(out=d_, in0=d_, in1=a, op=ALU.add)
                ctx.inc(inst, "v")
            on(ctx, "vector", v_cb)
            comb_toks.append(ctx.tok("v"))
        for q in range(4):
            ti = sc * 4 + q
            def g_lx(e, ti=ti):
                if len(out_toks) >= 2:
                    e.wait(out_toks[-2])
                e.wait(t_xnT)
                xc_toks.append(ctx.inc(e.dma_start(out=xfs[:, (ti % 2) * D:(ti % 2 + 1) * D], in_=xv[ti]), "qs", 16))
            on(ctx, "sync", g_lx)
            base_v = ctx.tok("v")
            def pe_tr(e, q=q, comb_toks=comb_toks, base_v=base_v):
                e.wait(*comb_toks)
                e.wait(base_v)
                for j in range(NDT):
                    inst = e.transpose(p.pst[:, j * 128:(j + 1) * 128],
                                       substg[:, j * 512 + q * 128:j * 512 + (q + 1) * 128], p.ident[:])
                ctx.inc(inst, "t")
            on(ctx, "tensor", pe_tr)
            t_tr = ctx.tok("t")
            def v_acc(e, ti=ti, t_tr=t_tr):
                e.wait(t_tr, xc_toks[-1])
                sl = xfs[:, (ti % 2) * D:(ti % 2 + 1) * D]
                ctx.inc(e.scalar_tensor_tensor(out=sl, in0=p.pst[:], scalar=1.0, in1=sl,
                                               op0=ALU.mult, op1=ALU.add), "v")
            on(ctx, "vector", v_acc)
            t_va = ctx.tok("v")
            def g_ox(e, ti=ti, t_va=t_va):
                e.wait(t_va)
                out_toks.append(ctx.inc(e.dma_start(out=xov[ti], in_=xfs[:, (ti % 2) * D:(ti % 2 + 1) * D]), "qg", 16))
            on(ctx, "gpsimd", g_ox)
            last_tok = ctx.tok("qg")
    def g_fin(e):
        e.wait(ctx.tok("qg"))
    on(ctx, "gpsimd", g_fin)
    return ctx.tok("qg")


def emit_l1_attn_a(p, io, x_in_ap, x_out_ap, tag):
    ctx = p.ctx
    u_ = tag
    sb = lambda n, s, d: ctx.sbuf(f"{u_}_{n}", s, d)
    psm = lambda n, s, d: ctx.psum(f"{u_}_{n}", s, d)
    nc = ctx.nc

    xv = x_in_ap.rearrange("(i p) d -> i p d", p=128)
    xov = x_out_ap.rearrange("(i p) d -> i p d", p=128)
    xfs = sb("xfs", [128, 2 * D], F32)

    ld_toks, st_done = [], []
    for i in range(NTOK):
        def g_l(e, i=i):
            if i >= 2:
                e.wait(st_done[i - 2])
            ld_toks.append(ctx.inc(e.dma_start(out=xfs[:, (i % 2) * D:(i % 2 + 1) * D], in_=xv[i]), "qs", 16))
        on(ctx, "sync", g_l)
        def v_st(e, i=i):
            e.wait(ld_toks[i])
            st = p.ln_stats[:, i * 12:(i + 1) * 12]
            e.bn_stats(out=st[:, 0:6], in_=xfs[:, (i % 2) * D:(i % 2) * D + 512])
            inst = e.bn_stats(out=st[:, 6:12], in_=xfs[:, (i % 2) * D + 512:(i % 2 + 1) * D])
            st_done.append(ctx.inc(inst, "v"))
        on(ctx, "vector", v_st)
    def v_ag(e):
        e.wait(st_done[NTOK - 1])
        for i in range(NTOK):
            inst = e.bn_aggr(out=p.ln_mv[:, 2 * i:2 * i + 2], in_=p.ln_stats[:, i * 12:(i + 1) * 12])
        ctx.inc(inst, "v")
    on(ctx, "vector", v_ag)
    t_stats = ctx.tok("v")
    mvh = p.ln_mv[:].rearrange("p (i two) -> p i two", two=2)
    mu0, var0 = mvh[:, :, 0], mvh[:, :, 1]
    def s_rstd(e):
        e.wait(t_stats)
        e.activation(out=p.ln_rstd[:], in_=var0, func=AF.Ln, bias=p.fc(EPS1 * EPS1), scale=1.0 + EPS1)
        ctx.inc(e.activation(out=p.ln_rstd[:], in_=p.ln_rstd[:], func=AF.Exp, scale=-0.5), "s")
    on(ctx, "scalar", s_rstd)
    t_rstd0 = ctx.tok("s")
    def v_nmr0(e):
        e.wait(t_rstd0)
        ctx.inc(e.scalar_tensor_tensor(out=p.ln_nmr[:], in0=mu0, scalar=-1.0,
                                       in1=p.ln_rstd[:], op0=ALU.mult, op1=ALU.mult), "v")
    on(ctx, "vector", v_nmr0)
    t_nmr = ctx.tok("v")

    xa_toks = []
    S0 = ctx.cnt["s"]
    def producers(i):
        def g_l2(e, i=i):
            e.wait(t_nmr)
            if i >= 2:
                e.wait(("s", S0 + (i - 1) * 2 + 1))
            xa_toks.append(ctx.inc(e.dma_start(out=xfs[:, (i % 2) * D:(i % 2 + 1) * D], in_=xv[i]), "qs", 16))
        on(ctx, "sync", g_l2)
        def fn(e, i=i):
            e.wait(xa_toks[i], t_nmr)
            return e.activation(out=p.xstg[:, (i % 2) * D:(i % 2 + 1) * D],
                                in_=xfs[:, (i % 2) * D:(i % 2 + 1) * D], func=AF.Identity,
                                scale=p.ln_rstd[:, i:i + 1], bias=p.ln_nmr[:, i:i + 1])
        return ("scalar", "s", fn)
    t_xnT = emit_xnT(p, producers)

    qT = sb("qT", [128, NDT * T], BF16)
    kT = sb("kT", [128, NDT * T], BF16)
    gT = sb("gT", [128, NDT * T], BF16)
    vtok = sb("vtok", [128, NTOK * D], BF16)
    Sm = sb("Sm", [128, NDT * D], BF16)
    wchunk = sb("wchunk", [128, NDT * 128], BF16)
    stage = sb("stage", [128, 128], F32)
    cols = sb("cols", [128, 2 * NDT], F32)
    ratio = sb("ratio", [128, NDT], BF16)
    rscr = sb("rscr", [128, NDT], F32)
    borig = sb("borig", [1, 128], F32)
    bchunk = sb("bchunk", [1, 128], BF16)
    onesr = sb("onesr", [1, 512], BF16)
    scr1 = sb("scr1", [128, T], BF16)
    scr2 = scr1
    escr = sb("escr", [128, 512], BF16)
    escr2 = sb("escr2", [128, 512], BF16)
    den_acc = sb("den_acc", [1, T], BF16)
    dcol = sb("dcol", [128, NTOK], BF16)
    kqm = sb("kqm", [128, 128], BF16)
    bpbc = sb("bpbc", [128, D], BF16)
    dram_den = nc.dram_tensor(f"{u_}_denb", [1, T], BF16)

    mm = MMPool(ctx, u_)
    ps_num = [psm("pnumA", [128, 512], F32), psm("pnumB", [128, 512], F32)]
    ps_row = psm("psrow", [1, 512], F32)

    def g_small(e):
        ctx.inc(e.dma_start(out=cols[:, 0:NDT], in_=io["lng_c"]), "qg", 16)
        ctx.inc(e.dma_start(out=cols[:, NDT:2 * NDT], in_=io["lnb_c"]), "qg", 16)

    on(ctx, "gpsimd", g_small)
    t_small = ctx.tok("qg")
    bp_cast = []
    for hh in range(8):
        def g_bp(e, hh=hh):
            if bp_cast:
                e.wait(bp_cast[-1])
            ctx.inc(e.dma_start(out=stage[:, 0:128], in_=io["b_proj_bc"][:, hh * 128:(hh + 1) * 128]), "qg", 16)
        on(ctx, "gpsimd", g_bp)
        t_lbp = ctx.tok("qg")
        def s_bp(e, hh=hh, t_lbp=t_lbp):
            e.wait(t_lbp)
            bp_cast.append(ctx.inc(e.copy(out=bpbc[:, hh * 128:(hh + 1) * 128], in_=stage[:, 0:128]), "s"))
        on(ctx, "scalar", s_bp)
    t_bpbc = ctx.tok("s")

    def v_ratio(e):
        e.wait(t_small)
        tk = ctx.inc(e.reciprocal(out=rscr[:], in_=cols[:, 0:NDT]), "v")
        e.wait(tk)
        e.tensor_tensor(out=ratio[:], in0=cols[:, NDT:2 * NDT], in1=rscr[:], op=ALU.mult)
        ctx.inc(e.memset(onesr[:], 1.0), "v")
    on(ctx, "vector", v_ratio)
    t_ratio = ctx.tok("v")

    def stream_chunks(wap, n_fs, fs_off_in_w, dst_kind, brow_off, extra_dep, evac_maker,
                      bias_src=None, brow_off0=0):
        last = None
        for fs in range(n_fs):
            col0 = fs_off_in_w + fs * 128
            wv_ = wap[:, col0:col0 + 128].rearrange("(j p) f -> j p f", p=128)
            casts = []
            for j in range(NDT):
                def g_w(e, j=j, wv_=wv_, last=last):
                    if j == 0 and last is not None:
                        e.wait(last)
                    if j >= 1:
                        e.wait(casts[j - 1])
                    ctx.inc(e.dma_start(out=stage[:, 0:128], in_=wv_[j]), "qg", 16)
                on(ctx, "gpsimd", g_w)
                t_l = ctx.tok("qg")
                def s_c(e, j=j, t_l=t_l):
                    e.wait(t_l, t_small)
                    casts.append(ctx.inc(e.activation(out=wchunk[:, j * 128:(j + 1) * 128],
                                                      in_=stage[:, 0:128], func=AF.Identity,
                                                      scale=cols[:, j:j + 1], bias=0.0), "s"))
                on(ctx, "scalar", s_c)
            t_wc = ctx.tok("s")
            def pe_bias(e, t_wc=t_wc):
                e.wait(t_wc, t_ratio)
                for j in range(NDT):
                    inst = e.matmul(ps_row[0:1, 0:128], ratio[:, j:j + 1],
                                    wchunk[:, j * 128:(j + 1) * 128], start=(j == 0), stop=(j == NDT - 1))
                ctx.inc(inst, "t")
            on(ctx, "tensor", pe_bias)
            t_pb = ctx.tok("t")
            def g_bo(e, fs=fs, t_pb=t_pb):
                e.wait(t_pb)
                ctx.inc(e.dma_start(out=borig[:], in_=bias_src[0:1, (brow_off - brow_off0) + fs * 128:(brow_off - brow_off0) + (fs + 1) * 128]), "qg", 16)
            on(ctx, "gpsimd", g_bo)
            t_bo = ctx.tok("qg")
            def v_bias(e, fs=fs, t_pb=t_pb, t_bo=t_bo):
                e.wait(t_pb, t_bo)
                ctx.inc(e.tensor_tensor(out=bchunk[0:1, 0:128], in0=ps_row[0:1, 0:128],
                                        in1=borig[:], op=ALU.add), "v")
            on(ctx, "vector", v_bias)
            t_bc = ctx.tok("v")
            for tc in range(4):
                if dst_kind == "feat":
                    mms = [(wchunk[:, j * 128:(j + 1) * 128],
                            p.xnT[:, j * T + tc * 512:j * T + (tc + 1) * 512]) for j in range(NDT)]
                    mms.append((bchunk[0:1, 0:128], onesr[0:1, 0:512]))
                else:
                    ti = None
                ev = evac_maker(fs, tc)
                last = mm.step([t_xnT, t_wc, t_bc, t_ratio], mms, ev[0], eng=ev[1], sem=ev[2])
        return last

    def gate_ev(fs, tc):
        def f(e, ps, fs=fs, tc=tc):
            return e.activation(out=gT[:, fs * T + tc * 512:fs * T + (tc + 1) * 512],
                                in_=ps[:, 0:512], func=AF.Sigmoid)
        return (f, "scalar", "s")
    t_gate = stream_chunks(io["w_gate"], NDT, 0, "feat", 3 * D, [], gate_ev,
                           bias_src=io["b_gate_row"], brow_off0=3 * D)

    def q_ev(fs, tc):
        def f(e, ps, fs=fs, tc=tc):
            e.activation(out=escr[:], in_=ps[:, 0:512], func=AF.Relu)
            e.activation(out=escr2[:], in_=ps[:, 0:512], func=AF.Relu, scale=-1.0)
            return e.activation(out=escr2[:], in_=escr2[:], func=AF.Exp, scale=-1.0)
        return (f, "scalar", "s")
    qsave = []
    def q_ev2(fs, tc):
        def f(e, ps, fs=fs, tc=tc):
            e.activation(out=escr[:], in_=ps[:, 0:512], func=AF.Relu)
            e.activation(out=escr2[:], in_=ps[:, 0:512], func=AF.Relu, scale=-1.0)
            inst = e.activation(out=escr2[:], in_=escr2[:], func=AF.Exp, scale=-1.0)
            return inst
        return (f, "scalar", "s")
    def q_stream():
        last = None
        for fs in range(NDT):
            col0 = 0 + fs * 128
            wv_ = io["w_qkv"][:, col0:col0 + 128].rearrange("(j p) f -> j p f", p=128)
            casts = []
            for j in range(NDT):
                def g_w(e, j=j, wv_=wv_, last=last):
                    if j == 0 and last is not None:
                        e.wait(last)
                    if j >= 1:
                        e.wait(casts[j - 1])
                    ctx.inc(e.dma_start(out=stage[:, 0:128], in_=wv_[j]), "qg", 16)
                on(ctx, "gpsimd", g_w)
                t_l = ctx.tok("qg")
                def s_c(e, j=j, t_l=t_l):
                    e.wait(t_l, t_small)
                    casts.append(ctx.inc(e.activation(out=wchunk[:, j * 128:(j + 1) * 128],
                                                      in_=stage[:, 0:128], func=AF.Identity,
                                                      scale=cols[:, j:j + 1], bias=0.0), "s"))
                on(ctx, "scalar", s_c)
            t_wc = ctx.tok("s")
            def pe_bias(e, t_wc=t_wc):
                e.wait(t_wc, t_ratio)
                for j in range(NDT):
                    inst = e.matmul(ps_row[0:1, 0:128], ratio[:, j:j + 1],
                                    wchunk[:, j * 128:(j + 1) * 128], start=(j == 0), stop=(j == NDT - 1))
                ctx.inc(inst, "t")
            on(ctx, "tensor", pe_bias)
            t_pb = ctx.tok("t")
            def g_bo(e, fs=fs, t_pb=t_pb):
                e.wait(t_pb)
                ctx.inc(e.dma_start(out=borig[:], in_=io["b_qkv_row"][0:1, fs * 128:(fs + 1) * 128]), "qg", 16)
            on(ctx, "gpsimd", g_bo)
            t_bo = ctx.tok("qg")
            def v_bias(e, fs=fs, t_pb=t_pb, t_bo=t_bo):
                e.wait(t_pb, t_bo)
                ctx.inc(e.tensor_tensor(out=bchunk[0:1, 0:128], in0=ps_row[0:1, 0:128],
                                        in1=borig[:], op=ALU.add), "v")
            on(ctx, "vector", v_bias)
            t_bc = ctx.tok("v")
            for tc in range(4):
                mms = [(wchunk[:, j * 128:(j + 1) * 128],
                        p.xnT[:, j * T + tc * 512:j * T + (tc + 1) * 512]) for j in range(NDT)]
                mms.append((bchunk[0:1, 0:128], onesr[0:1, 0:512]))
                tok_s = mm.step([t_xnT, t_bc], mms, q_ev2(fs, tc)[0], eng="scalar", sem="s")
                def v_add(e, fs=fs, tc=tc, tok_s=tok_s):
                    e.wait(tok_s)
                    ctx.inc(e.tensor_tensor(out=qT[:, fs * T + tc * 512:fs * T + (tc + 1) * 512],
                                            in0=escr[:], in1=escr2[:], op=ALU.add), "v")
                on(ctx, "vector", v_add)
                last = ctx.tok("v")
                def s_gate2(e, last=last):
                    e.wait(last)
                on(ctx, "scalar", s_gate2)
        return last
    t_q = q_stream()

    def k_stream():
        last = None
        for fs in range(NDT):
            col0 = D + fs * 128
            wv_ = io["w_qkv"][:, col0:col0 + 128].rearrange("(j p) f -> j p f", p=128)
            casts = []
            for j in range(NDT):
                def g_w(e, j=j, wv_=wv_, last=last):
                    if j == 0 and last is not None:
                        e.wait(last)
                    if j >= 1:
                        e.wait(casts[j - 1])
                    ctx.inc(e.dma_start(out=stage[:, 0:128], in_=wv_[j]), "qg", 16)
                on(ctx, "gpsimd", g_w)
                t_l = ctx.tok("qg")
                def s_c(e, j=j, t_l=t_l):
                    e.wait(t_l, t_small)
                    casts.append(ctx.inc(e.activation(out=wchunk[:, j * 128:(j + 1) * 128],
                                                      in_=stage[:, 0:128], func=AF.Identity,
                                                      scale=cols[:, j:j + 1], bias=0.0), "s"))
                on(ctx, "scalar", s_c)
            t_wc = ctx.tok("s")
            def pe_bias(e, t_wc=t_wc):
                e.wait(t_wc, t_ratio)
                for j in range(NDT):
                    inst = e.matmul(ps_row[0:1, 0:128], ratio[:, j:j + 1],
                                    wchunk[:, j * 128:(j + 1) * 128], start=(j == 0), stop=(j == NDT - 1))
                ctx.inc(inst, "t")
            on(ctx, "tensor", pe_bias)
            t_pb = ctx.tok("t")
            def g_bo(e, fs=fs, t_pb=t_pb):
                e.wait(t_pb)
                ctx.inc(e.dma_start(out=borig[:], in_=io["b_qkv_row"][0:1, D + fs * 128:D + (fs + 1) * 128]), "qg", 16)
            on(ctx, "gpsimd", g_bo)
            t_bo = ctx.tok("qg")
            def v_bias(e, fs=fs, t_pb=t_pb, t_bo=t_bo):
                e.wait(t_pb, t_bo)
                ctx.inc(e.tensor_tensor(out=bchunk[0:1, 0:128], in0=ps_row[0:1, 0:128],
                                        in1=borig[:], op=ALU.add), "v")
            on(ctx, "vector", v_bias)
            t_bc = ctx.tok("v")
            for tc in range(4):
                mms = [(wchunk[:, j * 128:(j + 1) * 128],
                        p.xnT[:, j * T + tc * 512:j * T + (tc + 1) * 512]) for j in range(NDT)]
                mms.append((bchunk[0:1, 0:128], onesr[0:1, 0:512]))
                def kev(e, ps, fs=fs, tc=tc):
                    return e.tensor_tensor(out=escr[:], in0=ps[:, 0:512],
                                           in1=gT[:, fs * T + tc * 512:fs * T + (tc + 1) * 512],
                                           op=ALU.mult)
                tok_v = mm.step([t_xnT, t_bc, t_gate], mms, kev, eng="vector", sem="v")
                def s_elu(e, fs=fs, tc=tc, tok_v=tok_v):
                    e.wait(tok_v)
                    e.activation(out=escr2[:], in_=escr[:], func=AF.Relu, scale=-1.0)
                    e.activation(out=escr2[:], in_=escr2[:], func=AF.Exp, scale=-1.0)
                    ctx.inc(e.activation(out=escr[:], in_=escr[:], func=AF.Relu), "s")
                on(ctx, "scalar", s_elu)
                t_se = ctx.tok("s")
                def v_add(e, fs=fs, tc=tc, t_se=t_se):
                    e.wait(t_se)
                    ctx.inc(e.tensor_tensor(out=kT[:, fs * T + tc * 512:fs * T + (tc + 1) * 512],
                                            in0=escr[:], in1=escr2[:], op=ALU.add), "v")
                on(ctx, "vector", v_add)
                last = ctx.tok("v")
        return last
    t_k = k_stream()

    def v_stream():
        last = None
        for fs in range(NDT):
            col0 = 2 * D + fs * 128
            wv_ = io["w_qkv"][:, col0:col0 + 128].rearrange("(j p) f -> j p f", p=128)
            casts = []
            for j in range(NDT):
                def g_w(e, j=j, wv_=wv_, last=last):
                    if j == 0 and last is not None:
                        e.wait(last)
                    if j >= 1:
                        e.wait(casts[j - 1])
                    ctx.inc(e.dma_start(out=stage[:, 0:128], in_=wv_[j]), "qg", 16)
                on(ctx, "gpsimd", g_w)
                t_l = ctx.tok("qg")
                def s_c(e, j=j, t_l=t_l):
                    e.wait(t_l, t_small)
                    casts.append(ctx.inc(e.activation(out=wchunk[:, j * 128:(j + 1) * 128],
                                                      in_=stage[:, 0:128], func=AF.Identity,
                                                      scale=cols[:, j:j + 1], bias=0.0), "s"))
                on(ctx, "scalar", s_c)
            t_wc = ctx.tok("s")
            def pe_bias(e, t_wc=t_wc):
                e.wait(t_wc, t_ratio)
                for j in range(NDT):
                    inst = e.matmul(ps_row[0:1, 0:128], ratio[:, j:j + 1],
                                    wchunk[:, j * 128:(j + 1) * 128], start=(j == 0), stop=(j == NDT - 1))
                ctx.inc(inst, "t")
            on(ctx, "tensor", pe_bias)
            t_pb = ctx.tok("t")
            def g_bo(e, fs=fs, t_pb=t_pb):
                e.wait(t_pb)
                ctx.inc(e.dma_start(out=borig[:], in_=io["b_qkv_row"][0:1, 2 * D + fs * 128:2 * D + (fs + 1) * 128]), "qg", 16)
            on(ctx, "gpsimd", g_bo)
            t_bo = ctx.tok("qg")
            def v_bias(e, fs=fs, t_pb=t_pb, t_bo=t_bo):
                e.wait(t_pb, t_bo)
                ctx.inc(e.tensor_tensor(out=bchunk[0:1, 0:128], in0=ps_row[0:1, 0:128],
                                        in1=borig[:], op=ALU.add), "v")
            on(ctx, "vector", v_bias)
            t_bc = ctx.tok("v")
            for tc in range(4):
                for q4 in range(4):
                    ti = tc * 4 + q4
                    mms = [(p.xnT[:, j * T + ti * 128:j * T + (i + 1) * 128] if False else p.xnT[:, j * T + ti * 128:j * T + (ti + 1) * 128],
                            wchunk[:, j * 128:(j + 1) * 128]) for j in range(NDT)]
                    mms.append((onesr[0:1, 0:128], bchunk[0:1, 0:128]))
                    def vev(e, ps, fs=fs, ti=ti):
                        return e.copy(out=vtok[:, ti * D + fs * 128:ti * D + (fs + 1) * 128],
                                      in_=ps[:, 0:128])
                    last = mm.step([t_xnT, t_bc], mms, vev, eng="scalar", sem="s", width=128)
        return last
    t_v = v_stream()

    den_toks = []
    def v_den0(e):
        ctx.inc(e.memset(den_acc[:], 1e-6), "v")
    on(ctx, "vector", v_den0)
    for kf in range(NDT):
        def v_scan(e, kf=kf):
            e.wait(t_k, t_q)
            tk = ctx.inc(e.tensor_tensor_scan(out=scr1[:], data0=p.ones_b[:, 0:1].to_broadcast([128, T]),
                                              data1=kT[:, kf * T:(kf + 1) * T], initial=0.0,
                                              op0=ALU.mult, op1=ALU.add), "v")
            e.wait(tk)
            ctx.inc(e.tensor_tensor(out=scr2[:], in0=qT[:, kf * T:(kf + 1) * T], in1=scr1[:],
                                    op=ALU.mult), "v")
        on(ctx, "vector", v_scan)
        t_qk = ctx.tok("v")
        for tc in range(4):
            def pe_d(e, kf=kf, tc=tc, t_qk=t_qk):
                e.wait(t_qk)
                if den_toks:
                    e.wait(den_toks[-1])
                ctx.inc(e.matmul(ps_row[0:1, 0:512], p.ones_b[:, 0:1],
                                 scr2[:, tc * 512:(tc + 1) * 512], start=True, stop=True), "t")
            on(ctx, "tensor", pe_d)
            t_pd = ctx.tok("t")
            def v_d(e, tc=tc, t_pd=t_pd):
                e.wait(t_pd)
                ctx.inc(e.tensor_tensor(out=den_acc[0:1, tc * 512:(tc + 1) * 512],
                                        in0=den_acc[0:1, tc * 512:(tc + 1) * 512],
                                        in1=ps_row[0:1, 0:512], op=ALU.add), "v")
            on(ctx, "vector", v_d)
            den_toks.append(ctx.tok("v"))
    t_den = ctx.tok("v")
    def s_den(e):
        e.wait(t_den)
        e.activation(out=den_acc[:], in_=den_acc[:], func=AF.Ln, bias=0.0)
        ctx.inc(e.activation(out=den_acc[:], in_=den_acc[:], func=AF.Exp, scale=-1.0), "s")
    on(ctx, "scalar", s_den)
    t_dr = ctx.tok("s")
    def pe_dc(e):
        e.wait(t_dr)
        for i in range(NTOK):
            inst = e.matmul(ps_num[0][:, i:i + 1], den_acc[0:1, i * 128:(i + 1) * 128],
                            p.ones_b[0:1, 0:1], start=True, stop=True)
        ctx.inc(inst, "t")
    on(ctx, "tensor", pe_dc)
    t_pdc = ctx.tok("t")
    def s_dc(e):
        e.wait(t_pdc)
        ctx.inc(e.copy(out=dcol[:], in_=ps_num[0][:, 0:NTOK]), "s")
    on(ctx, "scalar", s_dc)
    t_dcol = ctx.tok("s")

    ktok = gT
    ktok_done = []
    for ti in range(NTOK):
        base_s = ctx.tok("s")
        def pe_t(e, ti=ti, base_s=base_s):
            e.wait(t_k, t_gate)
            if ti > 0:
                e.wait(base_s)
            for j in range(NDT):
                inst = e.transpose(p.pst[:, j * 128:(j + 1) * 128],
                                   kT[:, j * T + ti * 128:j * T + (ti + 1) * 128], p.ident[:])
            ctx.inc(inst, "t")
        on(ctx, "tensor", pe_t)
        t_tr = ctx.tok("t")
        def s_ev(e, ti=ti, t_tr=t_tr):
            e.wait(t_tr)
            pv = p.pst[:].rearrange("p (j c) -> p j c", j=NDT)
            kv = ktok[:].rearrange("p (i c) -> p i c", i=NTOK)[:, ti, :].rearrange("p (j c) -> p j c", j=NDT)
            ctx.inc(e.copy(out=kv, in_=pv), "s")
        on(ctx, "scalar", s_ev)
        ktok_done.append(ctx.tok("s"))
    t_ktok = ctx.tok("s")

    numT = p.xnT
    def v_S0(e):
        e.wait(t_v)
        ctx.inc(e.memset(Sm[:], 0.0), "v")
    on(ctx, "vector", v_S0)
    t_S = ctx.tok("v")

    num_done = []
    for c in range(NTOK):
        def pe_kq(e, c=c):
            e.wait(t_k, t_q, t_ktok)
            if num_done:
                e.wait(num_done[-1])
            for j in range(NDT):
                inst = e.matmul(mm.ps[0][:, 0:128], kT[:, j * T + c * 128:j * T + (c + 1) * 128],
                                qT[:, j * T + c * 128:j * T + (c + 1) * 128],
                                start=(j == 0), stop=(j == NDT - 1))
            ctx.inc(inst, "t")
        on(ctx, "tensor", pe_kq)
        t_kq = ctx.tok("t")
        def v_mask(e, t_kq=t_kq):
            e.wait(t_kq, p.const_tok)
            ctx.inc(e.tensor_tensor(out=kqm[:], in0=mm.ps[0][:, 0:128], in1=p.triu[:], op=ALU.mult), "v")
        on(ctx, "vector", v_mask)
        t_km = ctx.tok("v")

        for half in range(2):
            def pe_n(e, c=c, half=half, t_km=t_km):
                e.wait(t_km, t_S)
                for vq in range(4):
                    vf = half * 4 + vq
                    for j in range(NDT):
                        e.matmul(ps_num[half][:, vq * 128:(vq + 1) * 128],
                                 Sm[:, j * D + vf * 128:j * D + (vf + 1) * 128],
                                 qT[:, j * T + c * 128:j * T + (c + 1) * 128],
                                 start=(j == 0), stop=False)
                    inst = e.matmul(ps_num[half][:, vq * 128:(vq + 1) * 128],
                                    vtok[:, c * D + vf * 128:c * D + (vf + 1) * 128],
                                    kqm[:], start=False, stop=True)
                ctx.inc(inst, "t")
            on(ctx, "tensor", pe_n)
            t_n = ctx.tok("t")
            def s_evn(e, c=c, half=half, t_n=t_n):
                e.wait(t_n)
                pv = ps_num[half][:, 0:512].rearrange("p (v c) -> p v c", v=4)
                nv = numT[:].rearrange("p (j c) -> p j c", j=NDT)[:, half * 4:(half + 1) * 4,
                                                                 c * 128:(c + 1) * 128]
                ctx.inc(e.copy(out=nv, in_=pv), "s")
            on(ctx, "scalar", s_evn)
            num_done.append(ctx.tok("s"))

        if c < NTOK - 1:
            for kf in range(NDT):
                def pe_su(e, c=c, kf=kf):
                    e.wait(num_done[-1])
                    inst = e.matmul(mm.ps[1][:, 0:512], ktok[:, c * D + kf * 128:c * D + (kf + 1) * 128],
                                    vtok[:, c * D:c * D + 512], start=True, stop=True)
                    ctx.inc(inst, "t")
                    inst = e.matmul(mm.ps[0][:, 0:512], ktok[:, c * D + kf * 128:c * D + (kf + 1) * 128],
                                    vtok[:, c * D + 512:(c + 1) * D], start=True, stop=True)
                    ctx.inc(inst, "t")
                on(ctx, "tensor", pe_su)
                t_su = ctx.tok("t")
                def v_su(e, kf=kf, t_su=t_su):
                    e.wait(t_su)
                    e.tensor_tensor(out=Sm[:, kf * D:kf * D + 512], in0=Sm[:, kf * D:kf * D + 512],
                                    in1=mm.ps[1][:, 0:512], op=ALU.add)
                    ctx.inc(e.tensor_tensor(out=Sm[:, kf * D + 512:(kf + 1) * D],
                                            in0=Sm[:, kf * D + 512:(kf + 1) * D],
                                            in1=mm.ps[0][:, 0:512], op=ALU.add), "v")
                on(ctx, "vector", v_su)
                t_S = ctx.tok("v")
                num_done.append(t_S)

    t_num = ctx.tok("s")

    wpb = Sm
    wvv = io["w_proj"].rearrange("(j p) d2 -> j p d2", p=128)
    wp_casts = []
    for j in range(NDT):
        for hh in range(8):
            def g_w(e, j=j, hh=hh):
                e.wait(t_num)
                if wp_casts:
                    e.wait(wp_casts[-1])
                ctx.inc(e.dma_start(out=stage[:, 0:128], in_=wvv[j][:, hh * 128:(hh + 1) * 128]), "qg", 16)
            on(ctx, "gpsimd", g_w)
            t_l = ctx.tok("qg")
            def s_c(e, j=j, hh=hh, t_l=t_l):
                e.wait(t_l)
                wp_casts.append(ctx.inc(e.copy(out=wpb[:, j * D + hh * 128:j * D + (hh + 1) * 128],
                                               in_=stage[:, 0:128]), "s"))
            on(ctx, "scalar", s_c)
    t_wp = ctx.tok("s")

    xc_toks, out_toks = [], []
    for ti in range(NTOK):
        def g_lx(e, ti=ti):
            if len(out_toks) >= 2:
                e.wait(out_toks[-2])
            e.wait(t_num)
            xc_toks.append(ctx.inc(e.dma_start(out=xfs[:, (ti % 2) * D:(ti % 2 + 1) * D], in_=xv[ti]), "qs", 16))
        on(ctx, "sync", g_lx)
        def v_bp(e, ti=ti):
            e.wait(xc_toks[ti], t_bpbc)
            ctx.inc(e.tensor_tensor(out=xfs[:, (ti % 2) * D:(ti % 2 + 1) * D],
                                    in0=xfs[:, (ti % 2) * D:(ti % 2 + 1) * D], in1=bpbc[:], op=ALU.add), "v")
        on(ctx, "vector", v_bp)
        t_vbp = ctx.tok("v")
        ev_toks = []
        for dc in range(2):
            mms = [(numT[:, j * T + ti * 128:j * T + (ti + 1) * 128],
                    wpb[:, j * D + dc * 512:j * D + (dc + 1) * 512]) for j in range(NDT)]
            def ev(e, ps, ti=ti, dc=dc):
                sl = xfs[:, (ti % 2) * D + dc * 512:(ti % 2) * D + dc * 512 + 512]
                return e.scalar_tensor_tensor(out=sl, in0=ps[:, 0:512], scalar=dcol[:, ti:ti + 1],
                                              in1=sl, op0=ALU.mult, op1=ALU.add)
            ev_toks.append(mm.step([t_num, t_wp, t_vbp, t_dcol], mms, ev, eng="vector", sem="v"))
        def g_ox(e, ti=ti, ev_toks=ev_toks):
            e.wait(*ev_toks)
            out_toks.append(ctx.inc(e.dma_start(out=xov[ti], in_=xfs[:, (ti % 2) * D:(ti % 2 + 1) * D]), "qg", 16))
        on(ctx, "gpsimd", g_ox)
    def g_fin(e):
        e.wait(ctx.tok("qg"))
    on(ctx, "gpsimd", g_fin)
    return ctx.tok("qg")


def emit_l1_ffn(p, io, tag, ncores=8):
    ctx = p.ctx
    u_ = tag
    sb = lambda n, s, d: ctx.sbuf(f"{u_}_{n}", s, d)
    psm = lambda n, s, d: ctx.psum(f"{u_}_{n}", s, d)
    nc = ctx.nc
    RD = 48

    t_nmr, t_xnT = emit_ln_t(p, p.ln_dep)

    qT = sb("qT", [128, T], BF16)
    rqT = sb("rqT", [48, T], BF16)
    rkT = sb("rkT", [48, 4096], BF16)
    sc_sb = sb("sc", [128, 4096], F32)
    w_sb = sb("w", [128, 4096], BF16)
    wTg = sb("wTg", [128, 3 * 4096], BF16)
    m8 = sb("m8", [128, 8], F32)
    vstg = sb("vstg", [128, D], F32)
    vb = sb("vb", [128, D], BF16)
    h1T = sb("h1T", [128, 2 * T], BF16)
    cols = sb("cols", [128, 16 * 6], F32)
    ccc = sb("ccc", [128, 4], F32)
    rowbuf = sb("rowbuf", [1, T], BF16)
    tiny = sb("tiny", [1, 8], F32)
    stg128 = sb("stg128", [128, 128], F32)
    aggscr = sb("aggscr", [128, 512], F32)
    mm = MMPool(ctx, u_)
    ps_agg = [psm(f"agg{k}", [128, 512], F32) for k in range(4)]
    wsums = sb("wsums", [128, NTOK], F32)
    dram_cc_in = nc.dram_tensor(f"{u_}_ccin", [1, 8], F32)
    dram_cc_out = nc.dram_tensor(f"{u_}_ccout", [1, 8], F32)

    c_lv = cols[:, 0:16]; c_nv = cols[:, 16:32]; c_sg = cols[:, 32:48]
    c_unc = cols[:, 48:64]; c_gate = cols[:, 64:80]; c_rp1 = cols[:, 80:96]

    def g_small(e):
        ctx.inc(e.dma_start(out=ccc[0:1, 0:1], in_=io["unc_b2"]), "qg", 16)
        ctx.inc(e.dma_start(out=ccc[0:1, 1:2], in_=io["gate_w1"]), "qg", 16)
        ctx.inc(e.dma_start(out=ccc[0:1, 2:3], in_=io["gate_bias"]), "qg", 16)
    on(ctx, "gpsimd", g_small)
    t_small = ctx.tok("qg")

    qpv = io["q_projT"].rearrange("(j p) f -> j p f", p=128)

    wq8 = sb("wq8", [128, NDT * 128], BF16)
    cast_t = []
    for j in range(NDT):
        def g_w(e, j=j):
            if j >= 1:
                e.wait(cast_t[j - 1])
            ctx.inc(e.dma_start(out=stg128[:], in_=qpv[j]), "qg", 16)
        on(ctx, "gpsimd", g_w)
        t_l = ctx.tok("qg")
        def s_c(e, j=j, t_l=t_l):
            e.wait(t_l)
            cast_t.append(ctx.inc(e.copy(out=wq8[:, j * 128:(j + 1) * 128], in_=stg128[:]), "s"))
        on(ctx, "scalar", s_c)
    t_wq = ctx.tok("s")
    for tc in range(4):
        mms = [(wq8[:, j * 128:(j + 1) * 128],
                p.xnT[:, j * T + tc * 512:j * T + (tc + 1) * 512]) for j in range(NDT)]
        def ev(e, ps, tc=tc):
            return e.copy(out=qT[:, tc * 512:(tc + 1) * 512], in_=ps[:, 0:512])
        mm.step([t_xnT, t_wq], mms, ev)
    t_q = ctx.tok("s")

    wr = sb("wr", [128, 48], BF16)
    def g_wr(e):
        ctx.inc(e.dma_start(out=stg128[:, 0:48], in_=io["router_wT"]), "qg", 16)
    on(ctx, "gpsimd", g_wr)
    t_lwr = ctx.tok("qg")
    def s_wr(e):
        e.wait(t_lwr)
        ctx.inc(e.copy(out=wr[:], in_=stg128[:, 0:48]), "s")
    on(ctx, "scalar", s_wr)
    t_wr = ctx.tok("s")

    import math
    for tc in range(4):
        mms = [(wr[:], qT[:, tc * 512:(tc + 1) * 512])]
        def ev(e, ps, tc=tc):
            return e.activation(out=rqT[0:48, tc * 512:(tc + 1) * 512], in_=ps[0:48, 0:512],
                                func=AF.Copy, scale=1.0 / math.sqrt(RD))
        mm.step([t_q, t_wr], mms, ev, out_p=48)
    t_rq = ctx.tok("s")

    akv = io["aux_keysT"]
    akstg = sb("akstg", [128, 512], F32)
    akb = sb("akb", [128, 512], BF16)
    for ck in range(8):
        def g_ak(e, ck=ck):
            if ck >= 1:
                e.wait(("s", ctx.cnt["s"]))
            ctx.inc(e.dma_start(out=akstg[:], in_=akv[:, ck * 512:(ck + 1) * 512]), "qg", 16)
        on(ctx, "gpsimd", g_ak)
        t_l = ctx.tok("qg")
        def s_ak(e, ck=ck, t_l=t_l):
            e.wait(t_l)
            ctx.inc(e.copy(out=akb[:], in_=akstg[:]), "s")
        on(ctx, "scalar", s_ak)
        t_cak = ctx.tok("s")
        def pe_rk(e, ck=ck, t_cak=t_cak):
            e.wait(t_cak, t_wr)
            if ck >= 2:
                e.wait(("s", ctx.cnt["s"]))
            ctx.inc(e.matmul(mm.ps[ck % 2][0:48, 0:512], wr[:], akb[:], start=True, stop=True), "t")
        on(ctx, "tensor", pe_rk)
        t_prk = ctx.tok("t")
        def s_erk(e, ck=ck, t_prk=t_prk):
            e.wait(t_prk)
            ctx.inc(e.copy(out=rkT[0:48, ck * 512:(ck + 1) * 512], in_=mm.ps[ck % 2][0:48, 0:512]), "s")
        on(ctx, "scalar", s_erk)
        mm.evac_tok[ck % 2] = ctx.tok("s")
    t_rk = ctx.tok("s")

    uw1 = sb("uw1", [128, NDT * 256], BF16)
    uwv = io["unc_w1T"].rearrange("(j p) f -> j p f", p=128)
    c2 = []
    for j in range(NDT):
        for hh in range(2):
            def g_w(e, j=j, hh=hh):
                if c2:
                    e.wait(c2[-1])
                ctx.inc(e.dma_start(out=stg128[:], in_=uwv[j][:, hh * 128:(hh + 1) * 128]), "qg", 16)
            on(ctx, "gpsimd", g_w)
            t_l = ctx.tok("qg")
            def s_c(e, j=j, hh=hh, t_l=t_l):
                e.wait(t_l)
                c2.append(ctx.inc(e.copy(out=uw1[:, j * 256 + hh * 128:j * 256 + (hh + 1) * 128],
                                         in_=stg128[:]), "s"))
            on(ctx, "scalar", s_c)
    t_uw1 = ctx.tok("s")
    ub1 = sb("ub1", [128, 2], F32)
    def g_b1(e):
        ctx.inc(e.dma_start(out=ub1[:], in_=io["unc_b1_c"]), "qg", 16)
    on(ctx, "gpsimd", g_b1)
    t_ub1 = ctx.tok("qg")
    for us in range(2):
        for tc in range(4):
            mms = [(uw1[:, j * 256 + us * 128:j * 256 + (us + 1) * 128],
                    p.xnT[:, j * T + tc * 512:j * T + (tc + 1) * 512]) for j in range(NDT)]
            def ev(e, ps, us=us, tc=tc):
                return e.activation(out=h1T[:, us * T + tc * 512:us * T + (tc + 1) * 512],
                                    in_=ps[:, 0:512], func=AF.Gelu, bias=ub1[:, us:us + 1])
            mm.step([t_xnT, t_uw1, t_ub1], mms, ev)
    t_h1 = ctx.tok("s")
    uw2 = sb("uw2", [128, 2], BF16)
    def g_w2(e):
        ctx.inc(e.dma_start(out=stg128[:, 0:2], in_=io["unc_w2_c"]), "qg", 16)
    on(ctx, "gpsimd", g_w2)
    t_lw2 = ctx.tok("qg")
    def s_w2(e):
        e.wait(t_lw2)
        ctx.inc(e.copy(out=uw2[:], in_=stg128[:, 0:2]), "s")
    on(ctx, "scalar", s_w2)
    t_w2 = ctx.tok("s")
    lrn_toks = []
    for tc in range(4):
        def pe_lr(e, tc=tc):
            e.wait(t_h1, t_w2)
            if mm.evac_tok[0] is not None:
                e.wait(mm.evac_tok[0])
            if lrn_toks:
                e.wait(lrn_toks[-1])
            for us in range(2):
                inst = e.matmul(mm.ps[0][0:1, 0:512], uw2[:, us:us + 1],
                                h1T[:, us * T + tc * 512:us * T + (tc + 1) * 512],
                                start=(us == 0), stop=(us == 1))
            ctx.inc(inst, "t")
        on(ctx, "tensor", pe_lr)
        t_plr = ctx.tok("t")
        def s_lr(e, tc=tc, t_plr=t_plr):
            e.wait(t_plr, mm.evac_tok[0] if mm.evac_tok[0] else None)
            lrn_toks.append(ctx.inc(e.copy(out=rowbuf[0:1, tc * 512:(tc + 1) * 512],
                                           in_=mm.ps[0][0:1, 0:512]), "s"))
        on(ctx, "scalar", s_lr)
    mm.evac_tok[0] = ctx.tok("s")
    t_learn = ctx.tok("s")
    lcol = sb("lcol", [128, 16], F32)
    def pe_lc(e):
        e.wait(t_learn)
        e.wait(mm.evac_tok[0])
        for i in range(NTOK):
            inst = e.matmul(mm.ps[0][:, i:i + 1], rowbuf[0:1, i * 128:(i + 1) * 128],
                            p.ones_b[0:1, 0:1], start=True, stop=True)
        ctx.inc(inst, "t")
    on(ctx, "tensor", pe_lc)
    t_plc = ctx.tok("t")
    def s_lc(e):
        e.wait(t_plc)
        ctx.inc(e.copy(out=lcol[:], in_=mm.ps[0][:, 0:NTOK]), "s")
    on(ctx, "scalar", s_lc)
    mm.evac_tok[0] = ctx.tok("s")
    t_lcol = ctx.tok("s")

    mvh = p.ln_mv[:].rearrange("p (i two) -> p i two", two=2)
    var0 = mvh[:, :, 1]
    def v_lv(e):
        e.wait(t_nmr)
        e.tensor_scalar(out=c_lv, in0=var0, scalar1=EPS1, scalar2=None, op0=ALU.add)
        tk = ctx.inc(e.reciprocal(out=c_nv, in_=c_lv), "v")
        e.wait(tk)
        ctx.inc(e.tensor_tensor(out=c_lv, in0=var0, in1=c_nv, op=ALU.mult), "v")
    on(ctx, "vector", v_lv)
    t_vr = ctx.tok("v")
    def s_lv(e):
        e.wait(t_vr)
        ctx.inc(e.activation(out=c_lv, in_=c_lv, func=AF.Ln, bias=p.fc(1.0)), "s")
    on(ctx, "scalar", s_lv)
    t_slv = ctx.tok("s")
    def v_sum(e):
        e.wait(t_slv)
        ctx.inc(e.reduce_sum(out=ccc[:, 3:4], in_=c_lv, axis=AX.X), "v")
    on(ctx, "vector", v_sum)
    t_vs = ctx.tok("v")
    def pe_sum(e):
        e.wait(t_vs)
        e.wait(mm.evac_tok[1])
        ctx.inc(e.matmul(mm.ps[1][0:1, 0:1], p.ones_f[:, 0:1], ccc[:, 3:4], start=True, stop=True), "t")
    on(ctx, "tensor", pe_sum)
    t_ps = ctx.tok("t")
    def s_sum(e):
        e.wait(t_ps)
        ctx.inc(e.copy(out=tiny[0:1, 0:1], in_=mm.ps[1][0:1, 0:1]), "s")
    on(ctx, "scalar", s_sum)
    mm.evac_tok[1] = ctx.tok("s")
    t_tin = ctx.tok("s")
    def g_cc(e):
        e.wait(t_tin)
        ctx.inc(e.dma_start(out=dram_cc_in.ap(), in_=tiny[:]), "qg", 16)
        e.wait(ctx.tok("qg"))
        ctx.inc(e.collective_compute(
            "AllReduce", mybir.AluOpType.add,
            replica_groups=[list(range(ncores))],
            ins=[dram_cc_in.ap().opt()],
            outs=[dram_cc_out.ap().opt()],
        ), "cc")
        e.wait(ctx.tok("cc"))
        ctx.inc(e.dma_start(out=tiny[:], in_=dram_cc_out.ap()), "qg", 16)
    on(ctx, "gpsimd", g_cc)
    t_cc = ctx.tok("qg")
    def s_gm(e):
        e.wait(t_cc)
        ctx.inc(e.activation(out=tiny[0:1, 1:2], in_=tiny[0:1, 0:1], func=AF.Copy,
                             scale=1.0 / (ncores * T)), "s")
    on(ctx, "scalar", s_gm)
    t_gm = ctx.tok("s")
    def v_gi(e):
        e.wait(t_gm)
        e.tensor_scalar(out=tiny[0:1, 1:2], in0=tiny[0:1, 1:2], scalar1=1e-6, scalar2=None, op0=ALU.add)
        tk = ctx.inc(e.reciprocal(out=tiny[0:1, 2:3], in_=tiny[0:1, 1:2]), "v")
        e.wait(tk)
    on(ctx, "vector", v_gi)
    t_gi = ctx.tok("v")
    def pe_bc(e):
        e.wait(t_gi, t_small)
        e.wait(mm.evac_tok[0])
        ctx.inc(e.matmul(mm.ps[0][:, 0:4], p.ones_f[0:1, :], ccc[0:1, 0:3].to_broadcast([1, 3]) if False else tiny[0:1, 0:4],
                         start=True, stop=True), "t")
    on(ctx, "tensor", pe_bc)
    t_pbc = ctx.tok("t")
    def s_bc(e):
        e.wait(t_pbc)
        ctx.inc(e.copy(out=ccc[:, 0:4] if False else stg128[:, 0:4], in_=mm.ps[0][:, 0:4]), "s")
    on(ctx, "scalar", s_bc)
    mm.evac_tok[0] = ctx.tok("s")
    t_sbc = ctx.tok("s")
    def pe_bc2(e):
        e.wait(t_small, t_sbc)
        e.wait(mm.evac_tok[1])
        ctx.inc(e.matmul(mm.ps[1][:, 0:3], p.ones_f[0:1, :], ccc[0:1, 0:3], start=True, stop=True), "t")
    on(ctx, "tensor", pe_bc2)
    t_pbc2 = ctx.tok("t")
    def s_bc2(e):
        e.wait(t_pbc2)
        ctx.inc(e.copy(out=stg128[:, 4:7], in_=mm.ps[1][:, 0:3]), "s")
    on(ctx, "scalar", s_bc2)
    mm.evac_tok[1] = ctx.tok("s")
    t_sbc2 = ctx.tok("s")
    def v_cols(e):
        e.wait(t_sbc, t_sbc2, t_lcol, t_slv)
        ctx.inc(e.tensor_scalar(out=c_nv, in0=c_lv, scalar1=stg128[:, 2:3], scalar2=None,
                                op0=ALU.mult), "v")
    on(ctx, "vector", v_cols)
    t_nv = ctx.tok("v")
    def s_sg(e):
        e.wait(t_nv)
        ctx.inc(e.activation(out=c_sg, in_=lcol[:], func=AF.Sigmoid, bias=stg128[:, 4:5]), "s")
    on(ctx, "scalar", s_sg)
    t_sg2 = ctx.tok("s")
    def v_unc(e):
        e.wait(t_sg2)
        e.tensor_scalar(out=c_unc, in0=c_sg, scalar1=2.5, scalar2=None, op0=ALU.mult)
        ctx.inc(e.scalar_tensor_tensor(out=c_unc, in0=c_nv, scalar=0.5, in1=c_unc,
                                       op0=ALU.mult, op1=ALU.add), "v")
    on(ctx, "vector", v_unc)
    t_unc1 = ctx.tok("v")
    def v_clip(e):
        e.wait(t_unc1)
        ctx.inc(e.tensor_scalar(out=c_unc, in0=c_unc, scalar1=5.0, scalar2=0.0,
                                op0=ALU.min, op1=ALU.max), "v")
    on(ctx, "vector", v_clip)
    t_unc = ctx.tok("v")
    def s_gate(e):
        e.wait(t_unc)
        ctx.inc(e.activation(out=c_gate, in_=c_unc, func=AF.Sigmoid,
                             scale=stg128[:, 5:6], bias=stg128[:, 6:7]), "s")
    on(ctx, "scalar", s_gate)
    t_gate = ctx.tok("s")
    def v_rp1(e):
        e.wait(t_gate)
        ctx.inc(e.tensor_scalar(out=c_rp1, in0=p.ln_rstd[:], scalar1=1.0, scalar2=None, op0=ALU.add), "v")
    on(ctx, "vector", v_rp1)
    t_rp1 = ctx.tok("v")

    relv = io["rel_row"]
    rel_sb = sb("rel", [1, 4096], BF16)
    def g_rel(e):
        ctx.inc(e.dma_start(out=sc_sb[0:1, :], in_=relv), "qg", 16)
    on(ctx, "gpsimd", g_rel)
    t_lrel = ctx.tok("qg")
    def s_rel(e):
        e.wait(t_lrel)
        ctx.inc(e.copy(out=rel_sb[:], in_=sc_sb[0:1, :]), "s")
    on(ctx, "scalar", s_rel)
    t_rel = ctx.tok("s")
    onesr5 = sb("onesr5", [1, 512], BF16)
    def v_o5(e):
        ctx.inc(e.memset(onesr5[:], 1.0), "v")
    on(ctx, "vector", v_o5)
    t_o5 = ctx.tok("v")

    groups = [list(range(g * 3, min(16, g * 3 + 3))) for g in range(6)]
    last_tok = None
    wT_v = wTg[:].rearrange("p (g n c) -> p g n c", g=3, n=32)
    for grp_id, grp in enumerate(groups):
        wt_done = []
        for slot, ti in enumerate(grp):
            for nch in range(8):
                mms = [(rqT[0:48, ti * 128:(ti + 1) * 128], rkT[0:48, nch * 512:(nch + 1) * 512])]
                mms.append((onesr5[0:1, 0:128] if False else rel_sb[0:1, 0:1], None))
                mms.pop()
                mms.append((p.ones_b[0:1, 0:128], rel_sb[0:1, nch * 512:(nch + 1) * 512]))
                def ev(e, ps, ti=ti, nch=nch):
                    return e.copy(out=sc_sb[:, nch * 512:(nch + 1) * 512], in_=ps[:, 0:512])
                mm.step([t_rq, t_rk, t_rel, t_o5], mms, ev)
            t_sc = ctx.tok("s")
            def v_m8(e, t_sc=t_sc):
                e.wait(t_sc)
                tk = ctx.inc(e.max(out=m8[:], in_=sc_sb[:]), "v")
                e.wait(tk)
                ctx.inc(e.engine_nop() if False else e.tensor_copy(out=m8[:, 0:1], in_=m8[:, 0:1]), "v")
            on(ctx, "vector", v_m8)
            t_m8 = ctx.tok("v")
            def s_exp(e, t_m8=t_m8):
                e.wait(t_m8)
                ctx.inc(e.activation(out=w_sb[:], in_=sc_sb[:], func=AF.Exp), "s")
            on(ctx, "scalar", s_exp)
            t_ex = ctx.tok("s")
            def v_w(e, ti=ti, t_ex=t_ex):
                e.wait(t_ex)
                ctx.inc(e.scalar_tensor_tensor(out=w_sb[:], in0=sc_sb[:], scalar=m8[:, 7:8],
                                               in1=w_sb[:], op0=ALU.is_ge, op1=ALU.mult,
                                               accum_out=wsums[:, ti:ti + 1]), "v")
            on(ctx, "vector", v_w)
            t_w = ctx.tok("v")
            for rnd in range(4):
                base_s = ctx.tok("s")
                def pe_t(e, rnd=rnd, t_w=t_w, base_s=base_s):
                    e.wait(t_w)
                    e.wait(base_s)
                    for b in range(8):
                        inst = e.transpose(p.pst[:, b * 128:(b + 1) * 128],
                                           w_sb[:, rnd * 1024 + b * 128:rnd * 1024 + (b + 1) * 128],
                                           p.ident[:])
                    ctx.inc(inst, "t")
                on(ctx, "tensor", pe_t)
                t_tr = ctx.tok("t")
                def s_ev(e, slot=slot, ti=ti, rnd=rnd, t_tr=t_tr):
                    e.wait(t_tr)
                    pv = p.pst[:].rearrange("p (b c) -> p b c", b=8)
                    ov = wT_v[:, slot, rnd * 8:(rnd + 1) * 8, :]
                    ctx.inc(e.copy(out=ov, in_=pv), "s")
                on(ctx, "scalar", s_ev)
            wt_done.append(ctx.tok("s"))
        t_grp = ctx.tok("s")
        agg_banks = {}
        for slot in range(len(grp)):
            agg_banks[slot] = (ps_agg[(slot * 2) % 4], ps_agg[(slot * 2 + 1) % 4] if slot < 2 else mm.ps[slot % 2])
        agg_banks = {0: (ps_agg[0], ps_agg[1]), 1: (ps_agg[2], ps_agg[3]), 2: (mm.ps[0], mm.ps[1])}
        vload = []
        for nt in range(32):
            def g_v(e, nt=nt, grp_id=grp_id):
                if vload:
                    e.wait(vload[-1])
                if nt == 0:
                    e.wait(t_grp)
                    if grp_id > 0 and last_tok is not None:
                        e.wait(last_tok)
                ctx.inc(e.dma_start(out=vstg[:], in_=io["aux_values"][nt * 128:(nt + 1) * 128, :]), "qg", 16)
            on(ctx, "gpsimd", g_v)
            t_lv2 = ctx.tok("qg")
            def s_v(e, t_lv2=t_lv2):
                e.wait(t_lv2)
                if vload:
                    pass
                vload.append(ctx.inc(e.copy(out=vb[:], in_=vstg[:]), "s"))
            on(ctx, "scalar", s_v)
            t_vb = ctx.tok("s")
            def pe_agg(e, nt=nt, grp=grp, t_vb=t_vb, grp_id=grp_id):
                e.wait(t_vb, t_grp)
                if grp_id > 0 and last_tok is not None and nt == 0:
                    e.wait(last_tok)
                for slot, ti in enumerate(grp):
                    lhs = wT_v[:, slot, nt, :]
                    bA, bB = agg_banks[slot]
                    e.matmul(bA[:, 0:512], lhs, vb[:, 0:512], start=(nt == 0), stop=(nt == 31))
                    inst = e.matmul(bB[:, 0:512], lhs, vb[:, 512:1024], start=(nt == 0), stop=(nt == 31))
                ctx.inc(inst, "t")
            on(ctx, "tensor", pe_agg)
            t_pa = ctx.tok("t")
            def s_gate2(e, t_pa=t_pa):
                e.wait(t_pa)
            on(ctx, "scalar", s_gate2)
        t_agg = ctx.tok("t")
        fin = []
        for slot, ti in enumerate(grp):
            def v_fin(e, slot=slot, ti=ti, t_agg=t_agg):
                e.wait(t_agg, t_rp1)
                bA, bB = agg_banks[slot]
                tk = ctx.inc(e.reciprocal(out=ccc[:, 3:4], in_=wsums[:, ti:ti + 1]), "v")
                e.wait(tk)
                for half, bank in ((0, bA), (1, bB)):
                    sl = p.xres[:, ti * D + half * 512:ti * D + half * 512 + 512]
                    e.tensor_scalar(out=aggscr[:], in0=bank[:, 0:512], scalar1=ccc[:, 3:4],
                                    scalar2=c_gate[:, ti:ti + 1], op0=ALU.mult, op1=ALU.mult)
                    e.scalar_tensor_tensor(out=sl, in0=sl, scalar=c_rp1[:, ti:ti + 1], in1=aggscr[:],
                                           op0=ALU.mult, op1=ALU.add)
                    inst = e.tensor_scalar(out=sl, in0=sl, scalar1=p.ln_nmr[:, ti:ti + 1],
                                           scalar2=None, op0=ALU.add)
                ctx.inc(inst, "v")
            on(ctx, "vector", v_fin)
            fin.append(ctx.tok("v"))
        last_tok = fin[-1]
        mm.evac_tok[0] = last_tok
        mm.evac_tok[1] = last_tok
    return last_tok


def emit_l1_mamba(p, io, x_in_ap, x_out_ap, tag, uid=""):
    ctx = p.ctx
    u_ = tag
    sb = lambda n, s, d: ctx.sbuf(f"{u_}_{n}", s, d)
    psm = lambda n, s, d: ctx.psum(f"{u_}_{n}", s, d)
    nc = ctx.nc
    DI = 2048
    NDI = 16

    xv = x_in_ap.rearrange("(i p) d -> i p d", p=128)
    xov = x_out_ap.rearrange("(i p) d -> i p d", p=128)
    xfs = sb("xfs", [128, 2 * D], F32)

    ld_toks, st_done = [], []
    for i in range(NTOK):
        def g_l(e, i=i):
            if i >= 2:
                e.wait(st_done[i - 2])
            ld_toks.append(ctx.inc(e.dma_start(out=xfs[:, (i % 2) * D:(i % 2 + 1) * D], in_=xv[i]), "qs", 16))
        on(ctx, "sync", g_l)
        def v_st(e, i=i):
            e.wait(ld_toks[i])
            st = p.ln_stats[:, i * 12:(i + 1) * 12]
            e.bn_stats(out=st[:, 0:6], in_=xfs[:, (i % 2) * D:(i % 2) * D + 512])
            inst = e.bn_stats(out=st[:, 6:12], in_=xfs[:, (i % 2) * D + 512:(i % 2 + 1) * D])
            st_done.append(ctx.inc(inst, "v"))
        on(ctx, "vector", v_st)
    def v_ag(e):
        e.wait(st_done[NTOK - 1])
        for i in range(NTOK):
            inst = e.bn_aggr(out=p.ln_mv[:, 2 * i:2 * i + 2], in_=p.ln_stats[:, i * 12:(i + 1) * 12])
        ctx.inc(inst, "v")
    on(ctx, "vector", v_ag)
    t_stats = ctx.tok("v")
    mvh = p.ln_mv[:].rearrange("p (i two) -> p i two", two=2)
    mu0, var0 = mvh[:, :, 0], mvh[:, :, 1]
    def s_rstd(e):
        e.wait(t_stats)
        e.activation(out=p.ln_rstd[:], in_=var0, func=AF.Ln, bias=p.fc(EPS1), scale=1.0)
        ctx.inc(e.activation(out=p.ln_rstd[:], in_=p.ln_rstd[:], func=AF.Exp, scale=-0.5), "s")
    on(ctx, "scalar", s_rstd)
    t_rstd0 = ctx.tok("s")
    def v_nmr0(e):
        e.wait(t_rstd0)
        ctx.inc(e.scalar_tensor_tensor(out=p.ln_nmr[:], in0=mu0, scalar=-1.0,
                                       in1=p.ln_rstd[:], op0=ALU.mult, op1=ALU.mult), "v")
    on(ctx, "vector", v_nmr0)
    t_nmr = ctx.tok("v")

    xa_toks = []
    S0 = ctx.cnt["s"]
    def producers(i):
        def g_l2(e, i=i):
            e.wait(t_nmr)
            if i >= 2:
                e.wait(("s", S0 + (i - 1) * 2 + 1))
            xa_toks.append(ctx.inc(e.dma_start(out=xfs[:, (i % 2) * D:(i % 2 + 1) * D], in_=xv[i]), "qs", 16))
        on(ctx, "sync", g_l2)
        def fn(e, i=i):
            e.wait(xa_toks[i], t_nmr)
            return e.activation(out=p.xstg[:, (i % 2) * D:(i % 2 + 1) * D],
                                in_=xfs[:, (i % 2) * D:(i % 2 + 1) * D], func=AF.Identity,
                                scale=p.ln_rstd[:, i:i + 1], bias=p.ln_nmr[:, i:i + 1])
        return ("scalar", "s", fn)
    t_xnT = emit_xnT(p, producers)

    xs = sb("xs", [128, NDI * T], BF16)
    wstg = sb("wstg", [128, 512], F32)
    wchunk = sb("wchunk", [128, NDT * 128], BF16)
    cols = sb("cols", [128, 64 + 16 * 3 + 256], F32)
    xdbl = sb("xdbl", [64, T], BF16)
    bc_sb = sb("bc_sb", [32, T], BF16)
    dtw = sb("dtw", [64, DI], BF16)
    xpw = sb("xpw", [128, NDI * 96], BF16)
    delt = sb("delt", [128, T], BF16)
    dxs = sb("dxs", [128, T], BF16)
    dA = sb("dA", [128, T], F32)
    ubuf = sb("ubuf", [128, T], BF16)
    hbuf = sb("hbuf", [128, T], BF16)
    ybuf = sb("ybuf", [128, T], F32)
    yscr = sb("yscr", [128, T], BF16)
    convscr = sb("convscr", [128, T], F32)
    rowB = sb("rowB", [1, T], BF16)
    rowC = sb("rowC", [1, T], BF16)
    zb = sb("zb", [128, T], BF16)
    c_convw = cols[:, 0:64]
    c_convb = cols[:, 64:80]
    c_dtb = cols[:, 80:96]
    c_D = cols[:, 96:112]
    c_A = cols[:, 112:368]
    mm = MMPool(ctx, u_)
    ps_b = psm("psB", [128, T], F32)
    zdram = nc.dram_tensor(f"{u_}_zscr{uid}", [128, NDI * T], BF16)
    bcdram = nc.dram_tensor(f"{u_}_bcscr{uid}", [32, T], BF16)

    def g_cols(e):
        ctx.inc(e.dma_start(out=c_convw, in_=io["conv_w_c"]), "qg", 16)
        ctx.inc(e.dma_start(out=c_convb, in_=io["conv_b_c"]), "qg", 16)
        ctx.inc(e.dma_start(out=c_dtb, in_=io["dtb_c"]), "qg", 16)
        ctx.inc(e.dma_start(out=c_D, in_=io["D_c"]), "qg", 16)
        ctx.inc(e.dma_start(out=c_A, in_=io["A_log_c"]), "qg", 16)
    on(ctx, "gpsimd", g_cols)
    t_cols = ctx.tok("qg")
    def s_A(e):
        e.wait(t_cols)
        ctx.inc(e.activation(out=c_A, in_=c_A, func=AF.Exp), "s")
    on(ctx, "scalar", s_A)
    t_sA = ctx.tok("s")
    def v_A(e):
        e.wait(t_sA)
        ctx.inc(e.tensor_scalar(out=c_A, in0=c_A, scalar1=-1.0, scalar2=None, op0=ALU.mult), "v")
    on(ctx, "vector", v_A)
    t_A = ctx.tok("v")

    ipv = io["in_projT"]
    conv_toks = []
    zout_toks = []
    for cs in range(32):
        wv_ = ipv[:, cs * 128:(cs + 1) * 128].rearrange("(j p) f -> j p f", p=128)
        casts = []
        for j in range(NDT):
            def g_w(e, j=j, wv_=wv_, cs=cs):
                if j == 0 and (conv_toks or zout_toks):
                    e.wait((conv_toks + zout_toks)[-1])
                if j >= 1:
                    e.wait(casts[j - 1])
                ctx.inc(e.dma_start(out=wstg[:, 0:128], in_=wv_[j]), "qg", 16)
            on(ctx, "gpsimd", g_w)
            t_l = ctx.tok("qg")
            def s_c(e, j=j, t_l=t_l):
                e.wait(t_l)
                casts.append(ctx.inc(e.copy(out=wchunk[:, j * 128:(j + 1) * 128], in_=wstg[:, 0:128]), "s"))
            on(ctx, "scalar", s_c)
        t_wc = ctx.tok("s")
        ev_toks = []
        for tc in range(4):
            mms = [(wchunk[:, j * 128:(j + 1) * 128],
                    p.xnT[:, j * T + tc * 512:j * T + (tc + 1) * 512]) for j in range(NDT)]
            if cs < 16:
                def ev(e, ps, cs=cs, tc=tc):
                    return e.copy(out=xs[:, cs * T + tc * 512:cs * T + (tc + 1) * 512],
                                  in_=ps[:, 0:512])
            else:
                def ev(e, ps, cs=cs, tc=tc):
                    return e.copy(out=zb[:, tc * 512:(tc + 1) * 512], in_=ps[:, 0:512])
            ev_toks.append(mm.step([t_xnT, t_wc], mms, ev))
        if cs < 16:
            t_e = ev_toks[-1]
            def v_conv(e, cs=cs, t_e=t_e):
                e.wait(t_e, t_cols)
                x_ = xs[:, cs * T:(cs + 1) * T]
                e.tensor_scalar(out=convscr[:], in0=x_, scalar1=c_convw[:, cs * 4 + 3:cs * 4 + 4],
                                scalar2=None, op0=ALU.mult)
                for s_ in (1, 2, 3):
                    k = 3 - s_
                    e.scalar_tensor_tensor(out=convscr[:, s_:T], in0=x_[:, 0:T - s_],
                                           scalar=c_convw[:, cs * 4 + k:cs * 4 + k + 1],
                                           in1=convscr[:, s_:T], op0=ALU.mult, op1=ALU.add)
                ctx.inc(e.engine_nop(), "v")
            on(ctx, "vector", v_conv)
            t_cv = ctx.tok("v")
            def s_silu(e, cs=cs, t_cv=t_cv):
                e.wait(t_cv)
                ctx.inc(e.activation(out=xs[:, cs * T:(cs + 1) * T], in_=convscr[:],
                                     func=AF.Silu, bias=c_convb[:, cs:cs + 1]), "s")
            on(ctx, "scalar", s_silu)
            conv_toks.append(ctx.tok("s"))
        else:
            t_e = ev_toks[-1]
            def g_z(e, cs=cs, t_e=t_e):
                e.wait(t_e)
                zout_toks.append(ctx.inc(e.dma_start(out=zdram.ap()[:, (cs - 16) * T:(cs - 15) * T],
                                                     in_=zb[:]), "qg", 16))
            on(ctx, "gpsimd", g_z)
    t_conv = ctx.tok("s")
    t_zout = ctx.tok("qg")

    xpv = io["x_projT"].rearrange("(j p) f -> j p f", p=128)
    c3 = []
    for j in range(NDI):
        def g_w(e, j=j):
            if c3:
                e.wait(c3[-1])
            ctx.inc(e.dma_start(out=wstg[:, 0:96], in_=xpv[j]), "qg", 16)
        on(ctx, "gpsimd", g_w)
        t_l = ctx.tok("qg")
        def s_c(e, j=j, t_l=t_l):
            e.wait(t_l)
            c3.append(ctx.inc(e.copy(out=xpw[:, j * 96:(j + 1) * 96], in_=wstg[:, 0:96]), "s"))
        on(ctx, "scalar", s_c)
    t_xpw = ctx.tok("s")
    for tc in range(4):
        mms = [(xpw[:, j * 96:(j + 1) * 96], xs[:, j * T + tc * 512:j * T + (tc + 1) * 512])
               for j in range(NDI)]
        def ev(e, ps, tc=tc):
            e.copy(out=xdbl[0:64, tc * 512:(tc + 1) * 512], in_=ps[0:64, 0:512])
            return e.copy(out=bc_sb[0:32, tc * 512:(tc + 1) * 512], in_=ps[64:96, 0:512])
        mm.step([t_conv, t_xpw], mms, ev, out_p=96)
    t_xdbl = ctx.tok("s")
    def g_bc(e):
        e.wait(t_xdbl)
        ctx.inc(e.dma_start(out=bcdram.ap(), in_=bc_sb[:]), "qg", 16)
    on(ctx, "gpsimd", g_bc)
    t_bc = ctx.tok("qg")

    dtv = io["dt_projT"]
    c4 = []
    for hh in range(4):
        def g_w(e, hh=hh):
            if c4:
                e.wait(c4[-1])
            ctx.inc(e.dma_start(out=wstg[0:64, 0:512], in_=dtv[:, hh * 512:(hh + 1) * 512]), "qg", 16)
        on(ctx, "gpsimd", g_w)
        t_l = ctx.tok("qg")
        def s_c(e, hh=hh, t_l=t_l):
            e.wait(t_l)
            c4.append(ctx.inc(e.copy(out=dtw[0:64, hh * 512:(hh + 1) * 512], in_=wstg[0:64, 0:512]), "s"))
        on(ctx, "scalar", s_c)
    t_dtw = ctx.tok("s")

    opb = p.xnT
    opv = io["out_projT"].rearrange("(j p) d2 -> j p d2", p=128)
    c5 = []
    for j in range(NDI):
        for hh in range(2):
            def g_w(e, j=j, hh=hh):
                e.wait(t_xnT if False else ctx.tok("qs"))
                e.wait(t_conv, t_zout)
                if c5:
                    e.wait(c5[-1])
                ctx.inc(e.dma_start(out=wstg[:, 0:512], in_=opv[j][:, hh * 512:(hh + 1) * 512]), "qg", 16)
            on(ctx, "gpsimd", g_w)
            t_l = ctx.tok("qg")
            def s_c(e, j=j, hh=hh, t_l=t_l):
                e.wait(t_l)
                c5.append(ctx.inc(e.copy(out=opb[:, j * D + hh * 512:j * D + (hh + 1) * 512],
                                         in_=wstg[:, 0:512]), "s"))
            on(ctx, "scalar", s_c)
    t_opb = ctx.tok("s")

    yg_toks = []
    for dt in range(NDI):
        dl_toks = []
        for tc in range(4):
            mms = [(dtw[0:64, dt * 128:(dt + 1) * 128], xdbl[0:64, tc * 512:(tc + 1) * 512])]
            def ev(e, ps, dt=dt, tc=tc):
                sl = delt[:, tc * 512:(tc + 1) * 512]
                e.activation(out=convscr[:, 0:512], in_=ps[:, 0:512], func=AF.Abs,
                             bias=c_dtb[:, dt:dt + 1])
                e.activation(out=convscr[:, 0:512], in_=convscr[:, 0:512], func=AF.Exp, scale=-1.0)
                e.activation(out=convscr[:, 0:512], in_=convscr[:, 0:512], func=AF.Ln,
                             bias=p.fc(1.0))
                e.activation(out=convscr[:, 512:1024], in_=ps[:, 0:512], func=AF.Relu,
                             bias=c_dtb[:, dt:dt + 1])
                return e.activation(out=convscr[:, 512:1024], in_=convscr[:, 512:1024],
                                    func=AF.Copy, scale=1.0)
            dl_toks.append(mm.step([t_xdbl, t_dtw], mms, ev, out_p=128))
            def v_sp(e, tc=tc, dl=dl_toks[-1]):
                e.wait(dl)
                ctx.inc(e.tensor_tensor(out=delt[:, tc * 512:(tc + 1) * 512],
                                        in0=convscr[:, 0:512], in1=convscr[:, 512:1024],
                                        op=ALU.add), "v")
            on(ctx, "vector", v_sp)
            t_vsp = ctx.tok("v")
            def s_g(e, t_vsp=t_vsp):
                e.wait(t_vsp)
            on(ctx, "scalar", s_g)
        t_delt = ctx.tok("v")
        def v_dxs(e, dt=dt, t_delt=t_delt):
            e.wait(t_delt, t_conv, t_cols)
            x_ = xs[:, dt * T:(dt + 1) * T]
            e.tensor_tensor(out=dxs[:], in0=delt[:], in1=x_, op=ALU.mult)
            ctx.inc(e.tensor_scalar(out=ybuf[:], in0=x_, scalar1=c_D[:, dt:dt + 1],
                                    scalar2=None, op0=ALU.mult), "v")
        on(ctx, "vector", v_dxs)
        t_dxs = ctx.tok("v")
        for n in range(16):
            def g_row(e, n=n, t_bc=t_bc):
                e.wait(t_bc)
                e.wait(("t", ctx.cnt["t"]))
                ctx.inc(e.dma_start(out=rowB[:], in_=bcdram.ap()[n:n + 1, :]), "qg", 16)
                ctx.inc(e.dma_start(out=rowC[:], in_=bcdram.ap()[16 + n:17 + n, :]), "qg", 16)
            on(ctx, "gpsimd", g_row)
            t_row = ctx.tok("qg")
            prev_u = ctx.tok("v")
            def pe_bc2(e, n=n, t_row=t_row, prev_u=prev_u):
                e.wait(t_row)
                e.wait(prev_u)
                for tc in range(4):
                    e.matmul(ps_b[:, tc * 512:(tc + 1) * 512], p.ones_b[0:1, :],
                             rowB[0:1, tc * 512:(tc + 1) * 512], start=True, stop=True)
                for tc in range(4):
                    inst = e.matmul(mm.ps[tc % 2][:, 0:512], p.ones_b[0:1, :],
                                    rowC[0:1, tc * 512:(tc + 1) * 512], start=True, stop=True)
                    if tc < 2:
                        pass
                ctx.inc(inst, "t")
            def pe_bcB(e, n=n, t_row=t_row, prev_u=prev_u):
                e.wait(t_row, prev_u)
                for tc in range(4):
                    inst = e.matmul(ps_b[:, tc * 512:(tc + 1) * 512], p.ones_b[0:1, :],
                                    rowB[0:1, tc * 512:(tc + 1) * 512], start=True, stop=True)
                ctx.inc(inst, "t")
            on(ctx, "tensor", pe_bcB)
            t_pbB = ctx.tok("t")
            prev_v = ctx.tok("v")
            def s_dA(e, dt=dt, n=n, t_delt=t_delt, prev_v=prev_v):
                e.wait(t_delt, t_A, prev_v)
                ctx.inc(e.activation(out=dA[:], in_=delt[:], func=AF.Exp,
                                     scale=c_A[:, dt * 16 + n:dt * 16 + n + 1]), "s")
            on(ctx, "scalar", s_dA)
            t_dA = ctx.tok("s")
            def v_u(e, t_pbB=t_pbB, t_dxs=t_dxs):
                e.wait(t_pbB, t_dxs)
                ctx.inc(e.tensor_tensor(out=ubuf[:], in0=dxs[:], in1=ps_b[:], op=ALU.mult), "v")
            on(ctx, "vector", v_u)
            t_u = ctx.tok("v")
            def pe_bcC(e, n=n, t_row=t_row, t_u=t_u):
                e.wait(t_row, t_u)
                for tc in range(4):
                    inst = e.matmul(ps_b[:, tc * 512:(tc + 1) * 512], p.ones_b[0:1, :],
                                    rowC[0:1, tc * 512:(tc + 1) * 512], start=True, stop=True)
                ctx.inc(inst, "t")
            on(ctx, "tensor", pe_bcC)
            t_pbC = ctx.tok("t")
            def v_scan(e, t_dA=t_dA, t_u=t_u):
                e.wait(t_dA, t_u)
                tk = ctx.inc(e.tensor_tensor_scan(out=hbuf[:], data0=dA[:], data1=ubuf[:],
                                                  initial=0.0, op0=ALU.mult, op1=ALU.add), "v")
                e.wait(tk)
                ctx.inc(e.engine_nop(), "v")
            on(ctx, "vector", v_scan)
            t_h = ctx.tok("v")
            def v_y(e, t_h=t_h, t_pbC=t_pbC):
                e.wait(t_h, t_pbC)
                e.tensor_tensor(out=yscr[:], in0=hbuf[:], in1=ps_b[:], op=ALU.mult)
                ctx.inc(e.tensor_tensor(out=ybuf[:], in0=ybuf[:], in1=yscr[:], op=ALU.add), "v")
            on(ctx, "vector", v_y)
        t_y = ctx.tok("v")
        def g_z2(e, dt=dt, t_zout=t_zout):
            e.wait(t_zout)
            if yg_toks:
                e.wait(yg_toks[-1])
            ctx.inc(e.dma_start(out=zb[:], in_=zdram.ap()[:, dt * T:(dt + 1) * T]), "qg", 16)
        on(ctx, "gpsimd", g_z2)
        t_z = ctx.tok("qg")
        def s_sz(e, t_z=t_z, t_y=t_y):
            e.wait(t_z, t_y)
            ctx.inc(e.activation(out=zb[:], in_=zb[:], func=AF.Silu), "s")
        on(ctx, "scalar", s_sz)
        t_sz = ctx.tok("s")
        def v_g(e, dt=dt, t_sz=t_sz, t_y=t_y):
            e.wait(t_sz, t_y)
            ctx.inc(e.tensor_tensor(out=xs[:, dt * T:(dt + 1) * T], in0=ybuf[:], in1=zb[:],
                                    op=ALU.mult), "v")
        on(ctx, "vector", v_g)
        yg_toks.append(ctx.tok("v"))
    t_yg = ctx.tok("v")

    xc_toks, out_toks = [], []
    for ti in range(NTOK):
        def g_lx(e, ti=ti):
            if len(out_toks) >= 2:
                e.wait(out_toks[-2])
            e.wait(t_yg)
            xc_toks.append(ctx.inc(e.dma_start(out=xfs[:, (ti % 2) * D:(ti % 2 + 1) * D], in_=xv[ti]), "qs", 16))
        on(ctx, "sync", g_lx)
        ev_toks = []
        for dc in range(2):
            mms = [(xs[:, j * T + ti * 128:j * T + (ti + 1) * 128],
                    opb[:, j * D + dc * 512:j * D + (dc + 1) * 512]) for j in range(NDI)]
            def ev(e, ps, ti=ti, dc=dc):
                sl = xfs[:, (ti % 2) * D + dc * 512:(ti % 2) * D + dc * 512 + 512]
                return e.scalar_tensor_tensor(out=sl, in0=ps[:, 0:512], scalar=1.0,
                                              in1=sl, op0=ALU.mult, op1=ALU.add)
            def evw(e, ps, ti=ti, dc=dc):
                e.wait(xc_toks[ti])
                return ev(e, ps)
            ev_toks.append(mm.step([t_yg, t_opb], mms, evw, eng="vector", sem="v"))
        def g_ox(e, ti=ti, ev_toks=ev_toks):
            e.wait(*ev_toks)
            out_toks.append(ctx.inc(e.dma_start(out=xov[ti], in_=xfs[:, (ti % 2) * D:(ti % 2 + 1) * D]), "qg", 16))
        on(ctx, "gpsimd", g_ox)
    def g_fin(e):
        e.wait(ctx.tok("qg"))
    on(ctx, "gpsimd", g_fin)
    return ctx.tok("qg")



def _build_fused():
    nc = bass.Bass("TRN2", target_bir_lowering=False, debug=False)
    x_ext = nc.dram_tensor("x_ext", [T, D], F32, kind="ExternalInput").ap()
    x_out = nc.dram_tensor("x_out", [T, D], F32, kind="ExternalOutput").ap()
    xsA = nc.dram_tensor("xscrA", [T, D], F32).ap()
    xsB = nc.dram_tensor("xscrB", [T, D], F32).ap()

    def din(n, shp):
        return nc.dram_tensor(n, shp, F32, kind="ExternalInput").ap()

    io_ra = {n: din(n, shp) for n, shp in [
        ("Wt", [D, D]), ("Wo", [D, D]), ("Wg", [2 * D, D]), ("bg_row", [1, D]),
        ("ln1w_c", [128, 8]), ("ln1b_c", [128, 8]), ("lncw_c", [128, 8]),
        ("lncb_c", [128, 8]), ("lngw_c", [128, 8]), ("lngb_c", [128, 8])]}
    io_f0 = {n: din(n, shp) for n, shp in [
        ("wT", [D, D]), ("alpha", [1, 1]), ("brow", [1, D])]}
    io_ab = {n: din(n, shp) for n, shp in [
        ("wq", [D, D]), ("wk", [D, D]), ("wv", [D, D]), ("wo", [D, D]),
        ("bq_c", [128, 8]), ("bk_c", [128, 8]), ("bv_row", [1, D]), ("bo_row", [1, D]),
        ("gamma_bc", [128, D]), ("beta_bc", [128, D]), ("pe_tok", [T, D])]}
    io_la = {n: din(n, shp) for n, shp in [
        ("w_qkv", [D, 3 * D]), ("w_gate", [D, D]), ("w_proj", [D, D]),
        ("b_qkv_row", [1, 3 * D]), ("b_gate_row", [1, D]), ("b_proj_bc", [128, D]),
        ("lng_c", [128, 8]), ("lnb_c", [128, 8])]}
    io_lf = {n: din(n, shp) for n, shp in [
        ("q_projT", [D, 128]), ("router_wT", [128, 48]), ("aux_keysT", [128, 4096]),
        ("aux_values", [4096, D]), ("rel_row", [1, 4096]), ("unc_w1T", [D, 256]),
        ("unc_b1_c", [128, 2]), ("unc_w2_c", [128, 2]), ("unc_b2", [1, 1]),
        ("gate_w1", [1, 1]), ("gate_bias", [1, 1])]}
    io_mb = {n: din(n, shp) for n, shp in [
        ("in_projT", [D, 4096]), ("x_projT", [2048, 96]), ("dt_projT", [64, 2048]),
        ("out_projT", [2048, D]), ("conv_w_c", [128, 64]), ("conv_b_c", [128, 16]),
        ("dtb_c", [128, 16]), ("D_c", [128, 16]), ("A_log_c", [128, 256])]}

    cvals = host_constants()
    consts = const_dram(nc, cvals)
    es = contextlib.ExitStack()
    with es:
        block = es.enter_context(nc.Block())
        p = Prog(nc, es, block, keep_xres=False)
        p.c_aps = consts
        p.load_const(consts)
        ctx = p.ctx

        def scoped(fn):
            sc = contextlib.ExitStack()
            ctx.scope = sc
            with sc:
                r = fn()
            ctx.scope = None
            emit_barrier(p)
            return r

        scoped(lambda: emit_l0_attn_a(p, io_ra, x_ext, xsA, "ra"))
        def f_ffn1():
            p.xres = ctx.sbuf("xres_f0", [128, NTOK * D], F32)
            p.load_x(xsA)
            t_nmr, t_xnT = emit_ln_t(p, [p.x_tok])
            prep = emit_l0_ffn_prep(p, io_f0["wT"], io_f0["alpha"], io_f0["brow"], "f0")
            mm = MMPool(ctx, "f0c")
            tok = emit_l0_ffn_call(p, prep, mm, t_xnT)
            p.store_x(xsB, [tok])
        scoped(f_ffn1)
        scoped(lambda: emit_l0_attn_b(p, io_ab, xsB, xsA, "ab"))
        def f_ffn2():
            p.xres = ctx.sbuf("xres_f1", [128, NTOK * D], F32)
            p.load_x(xsA)
            t_nmr, t_xnT = emit_ln_t(p, [p.x_tok])
            prep = emit_l0_ffn_prep(p, io_f0["wT"], io_f0["alpha"], io_f0["brow"], "f1")
            mm = MMPool(ctx, "f1c")
            tok = emit_l0_ffn_call(p, prep, mm, t_xnT)
            p.store_x(xsB, [tok])
        scoped(f_ffn2)
        scoped(lambda: emit_l1_attn_a(p, io_la, xsB, xsA, "la"))
        def f_lf1():
            p.xres = ctx.sbuf("xres_lf1", [128, NTOK * D], F32)
            p.load_x(xsA)
            p.ln_dep = [p.x_tok]
            tok = emit_l1_ffn(p, io_lf, "lf1")
            p.store_x(xsB, [tok])
        scoped(f_lf1)
        scoped(lambda: emit_l1_mamba(p, io_mb, xsB, xsA, "mb"))
        def f_lf2():
            p.xres = ctx.sbuf("xres_lf2", [128, NTOK * D], F32)
            p.load_x(xsA)
            p.ln_dep = [p.x_tok]
            tok = emit_l1_ffn(p, io_lf, "lf2")
            p.store_x(x_out, [tok])
        scoped(f_lf2)
    return nc


def kernel(**inputs):
    f32 = lambda a: np.ascontiguousarray(np.asarray(a), dtype=np.float32)
    I = {k: np.asarray(v) for k, v in inputs.items()}
    cvals = host_constants()
    col8 = lambda v: f32(v).reshape(8, 128).T.copy()
    col16 = lambda v: f32(v).reshape(16, 128).T.copy()

    m = dict(cvals)
    m.update({
        "Wt": f32(I["L0_attn_a_Wt"]), "Wo": f32(I["L0_attn_a_Wo"]), "Wg": f32(I["L0_attn_a_Wg"]),
        "bg_row": f32(I["L0_attn_a_bg"]).reshape(1, D),
        "ln1w_c": col8(I["L0_attn_a_ln1_w"]), "ln1b_c": col8(I["L0_attn_a_ln1_b"]),
        "lncw_c": col8(I["L0_attn_a_lnc_w"]), "lncb_c": col8(I["L0_attn_a_lnc_b"]),
        "lngw_c": col8(I["L0_attn_a_lng_w"]), "lngb_c": col8(I["L0_attn_a_lng_b"]),
        "wT": f32(I["L0_ffn_weight"].T), "alpha": f32(I["L0_ffn_alpha"]).reshape(1, 1),
        "brow": f32(I["L0_ffn_bias"]).reshape(1, D),
        "wq": f32(I["L0_attn_b_wq"]), "wk": f32(I["L0_attn_b_wk"]),
        "wv": f32(I["L0_attn_b_wv"]), "wo": f32(I["L0_attn_b_wo"]),
        "bq_c": col8(I["L0_attn_b_bq"]), "bk_c": col8(I["L0_attn_b_bk"]),
        "bv_row": f32(I["L0_attn_b_bv"]).reshape(1, D), "bo_row": f32(I["L0_attn_b_bo"]).reshape(1, D),
        "gamma_bc": np.broadcast_to(f32(I["L0_attn_b_gamma"]), (128, D)).copy(),
        "beta_bc": np.broadcast_to(f32(I["L0_attn_b_beta"]), (128, D)).copy(),
        "pe_tok": f32(I["L0_attn_b_pe"][0, :T, :]),
        "w_qkv": f32(I["L1_attn_a_w_qkv"]), "w_gate": f32(I["L1_attn_a_w_gate"]),
        "w_proj": f32(I["L1_attn_a_w_proj"]),
        "b_qkv_row": f32(I["L1_attn_a_b_qkv"]).reshape(1, 3 * D),
        "b_gate_row": f32(I["L1_attn_a_b_gate"]).reshape(1, D),
        "b_proj_bc": np.broadcast_to(f32(I["L1_attn_a_b_proj"]), (128, D)).copy(),
        "lng_c": col8(I["L1_attn_a_ln_g"]), "lnb_c": col8(I["L1_attn_a_ln_b"]),
        "q_projT": f32(I["L1_ffn_q_proj_w"].T), "router_wT": f32(I["L1_ffn_router_w"].T),
        "aux_keysT": f32(I["L1_ffn_aux_keys"].T), "aux_values": f32(I["L1_ffn_aux_values"]),
        "rel_row": f32(I["L1_ffn_reliability_mask"]).reshape(1, 4096),
        "unc_w1T": f32(I["L1_ffn_unc_w1"].T),
        "unc_b1_c": f32(I["L1_ffn_unc_b1"]).reshape(2, 128).T.copy(),
        "unc_w2_c": f32(I["L1_ffn_unc_w2"]).reshape(2, 128).T.copy(),
        "unc_b2": f32(I["L1_ffn_unc_b2"]).reshape(1, 1),
        "gate_w1": f32(I["L1_ffn_gate_w1"]).reshape(1, 1),
        "gate_bias": f32(I["L1_ffn_gate_bias"]).reshape(1, 1),
        "in_projT": f32(I["L1_attn_b_in_proj_w"].T), "x_projT": f32(I["L1_attn_b_x_proj_w"].T),
        "dt_projT": f32(I["L1_attn_b_dt_proj_w"].T), "out_projT": f32(I["L1_attn_b_out_proj_w"].T),
        "conv_w_c": f32(I["L1_attn_b_conv_w"]).reshape(16, 128, 4).transpose(1, 0, 2).reshape(128, 64).copy(),
        "conv_b_c": col16(I["L1_attn_b_conv_b"]), "dtb_c": col16(I["L1_attn_b_dt_proj_b"]),
        "D_c": col16(I["L1_attn_b_D"]),
        "A_log_c": f32(I["L1_attn_b_A_log"]).reshape(16, 128, 16).transpose(1, 0, 2).reshape(128, 256).copy(),
    })

    x = f32(I["L0_attn_a_x"])
    nc = _build_fused()
    in_maps = []
    for c in range(8):
        mc = dict(m)
        mc["x_ext"] = np.ascontiguousarray(x[c])
        in_maps.append(mc)
    res = run_bass_kernel_spmd(nc, in_maps, core_ids=list(range(8)))
    return np.stack([res.results[c]["x_out"] for c in range(8)]).astype(np.float32)


# version 03
# speedup vs baseline: 1.0927x; 1.0893x over previous
import contextlib
from collections import defaultdict

import numpy as np
import concourse.bass as bass
import concourse.mybir as mybir
from concourse.bass_utils import run_bass_kernel_spmd

F32 = mybir.dt.float32
BF16 = mybir.dt.bfloat16
AF = mybir.ActivationFunctionType
ALU = mybir.AluOpType
AX = mybir.AxisListType

T = 2048
D = 1024
NTOK = T // 128
NDT = D // 128
EPS1 = 1e-5

FCONSTS = [1e-5, 1e-5 * 1e-6, 1e-5 * 1e-5, 1e-6, 1.0, 0.5, -0.5]


class Ctx:
    def __init__(self, nc, es, block):
        self.nc = nc
        self.es = es
        self.block = block
        self.semh = {}
        self.cnt = defaultdict(int)
        self.waited = defaultdict(dict)
        self.names = set()

    def sem(self, name):
        if name not in self.semh:
            self.semh[name] = self.es.enter_context(self.nc.semaphore(name))
        return self.semh[name]

    def sbuf(self, name, shape, dtype, es=None):
        assert name not in self.names, name
        self.names.add(name)
        tgt = es or getattr(self, "scope", None) or self.es
        return tgt.enter_context(self.nc.sbuf_tensor(name, shape, dtype))

    def psum(self, name, shape, dtype, es=None):
        assert name not in self.names, name
        self.names.add(name)
        tgt = es or getattr(self, "scope", None) or self.es
        return tgt.enter_context(self.nc.psum_tensor(name, shape, dtype))

    def tok(self, sem):
        return (sem, self.cnt[sem])

    def inc(self, inst, sem, n=1):
        inst.then_inc(self.sem(sem), n)
        self.cnt[sem] += n
        return (sem, self.cnt[sem])


class Eng:
    def __init__(self, ctx, name, e):
        self.ctx, self.name, self.e = ctx, name, e

    def wait(self, *toks):
        for t in toks:
            if t is None:
                continue
            sem, val = t
            if val <= 0:
                continue
            if self.ctx.waited[self.name].get(sem, -1) >= val:
                continue
            self.e.wait_ge(self.ctx.sem(sem), val)
            self.ctx.waited[self.name][sem] = val

    def __getattr__(self, a):
        return getattr(self.e, a)


def on(ctx, engname, fn):
    getattr(ctx.block, engname)(lambda e: fn(Eng(ctx, engname, e)))


SET_NL_EXP = 6


def preload_set(e, set_id=SET_NL_EXP):
    eng = e.e if isinstance(e, Eng) else e
    return eng.add_instruction(mybir.InstLoadActFuncSet(
        name=eng.bass.get_next_instruction_name(), act_func_set_id=set_id, ins=[], outs=[]))


def emit_preload(p, set_id=SET_NL_EXP):
    on(p.ctx, "scalar", lambda e: preload_set(e, set_id))


def host_constants():
    import ml_dtypes
    c = {}
    c["c_ident"] = np.eye(128, dtype=ml_dtypes.bfloat16)
    c["c_ones_f32"] = np.ones((128, 128), np.float32)
    c["c_ones_bf16"] = np.ones((128, 128), dtype=ml_dtypes.bfloat16)
    i = np.arange(128)
    c["c_triu"] = (i[None, :] >= i[:, None]).astype(ml_dtypes.bfloat16)
    hs = np.linspace(2.0, 8.0, D).astype(np.float32)
    decay = (0.5 ** (1.0 / hs)).astype(np.float32)
    c["c_decay"] = decay.reshape(NDT, 128).T.copy()
    c["c_decay_beta"] = (1.0 - decay).reshape(NDT, 128).T.copy()
    c["c_fconst"] = np.tile(np.asarray(FCONSTS, np.float32)[None, :], (128, 1))
    return c


def const_dram(nc, cvals):
    aps = {}
    for k, v in cvals.items():
        dt = F32 if v.dtype == np.float32 else BF16
        aps[k] = nc.dram_tensor(k, list(v.shape), dt, kind="ExternalInput").ap()
    return aps


class Prog:
    def __init__(self, nc, es, block, keep_xres=True):
        self.ctx = Ctx(nc, es, block)
        ctx = self.ctx
        self.keep_xres = keep_xres
        if keep_xres:
            self.xres = ctx.sbuf("xres", [128, NTOK * D], F32)
        self.xnT = ctx.sbuf("xnT", [128, NDT * T], BF16)
        self.ident = ctx.sbuf("ident", [128, 128], BF16)
        self.ones_f = ctx.sbuf("ones_f", [128, 128], F32)
        self.ones_b = ctx.sbuf("ones_b", [128, 128], BF16)
        self.fpool = ctx.sbuf("fpool", [128, len(FCONSTS)], F32)
        self.ln_stats = ctx.sbuf("ln_stats", [128, NTOK * 12], F32)
        self.ln_mv = ctx.sbuf("ln_mv", [128, NTOK * 2], F32)
        self.ln_rstd = ctx.sbuf("ln_rstd", [128, NTOK], F32)
        self.ln_nmr = ctx.sbuf("ln_nmr", [128, NTOK], F32)
        self.xstg = ctx.sbuf("xstg", [128, 2 * D], BF16)
        self.triu = ctx.sbuf("triu", [128, 128], BF16)
        self.pst = ctx.psum("pst", [128, 1024], BF16)

    def fc(self, val):
        return self.fpool[:, FCONSTS.index(val):FCONSTS.index(val) + 1]

    def fc1(self, val):
        return self.fpool[0:1, FCONSTS.index(val):FCONSTS.index(val) + 1]

    def load_const(self, const_aps):
        ctx = self.ctx
        def f(e):
            ctx.inc(e.dma_start(out=self.ident[:], in_=const_aps["c_ident"]), "qs", 16)
            ctx.inc(e.dma_start(out=self.ones_f[:], in_=const_aps["c_ones_f32"]), "qs", 16)
            ctx.inc(e.dma_start(out=self.ones_b[:], in_=const_aps["c_ones_bf16"]), "qs", 16)
            ctx.inc(e.dma_start(out=self.fpool[:], in_=const_aps["c_fconst"]), "qs", 16)
            ctx.inc(e.dma_start(out=self.triu[:], in_=const_aps["c_triu"]), "qs", 16)
        on(ctx, "sync", f)
        self.const_tok = ctx.tok("qs")

    def load_x(self, x_in_ap):
        ctx = self.ctx
        xv = x_in_ap.rearrange("(i p) d -> i p d", p=128)
        def f(e):
            for i in range(NTOK):
                ctx.inc(e.dma_start(out=self.xres[:, i * D:(i + 1) * D], in_=xv[i]), "qs", 16)
        on(ctx, "sync", f)
        self.x_tok = ctx.tok("qs")

    def store_x(self, x_out_ap, dep_toks):
        ctx = self.ctx
        xv = x_out_ap.rearrange("(i p) d -> i p d", p=128)
        def f(e):
            e.wait(*dep_toks)
            for i in range(NTOK):
                ctx.inc(e.dma_start(out=xv[i], in_=self.xres[:, i * D:(i + 1) * D]), "qg", 16)
            e.wait(ctx.tok("qg"))
        on(ctx, "gpsimd", f)



def emit_ln_stats_from(p, src_tiles, dep_toks, eps2=None):
    ctx = p.ctx
    ln_scale = 1.0 if eps2 is None else (1.0 + eps2)
    ln_bias = EPS1 if eps2 is None else (EPS1 * eps2)

    def v_stats(e):
        e.wait(*dep_toks)
        for i in range(NTOK):
            st = p.ln_stats[:, i * 12:(i + 1) * 12]
            s = src_tiles(i)
            e.bn_stats(out=st[:, 0:6], in_=s[:, 0:512])
            inst = e.bn_stats(out=st[:, 6:12], in_=s[:, 512:1024])
        tk = ctx.inc(inst, "v")
        e.wait(tk)
        for i in range(NTOK):
            inst = e.bn_aggr(out=p.ln_mv[:, 2 * i:2 * i + 2], in_=p.ln_stats[:, i * 12:(i + 1) * 12])
        ctx.inc(inst, "v")
    on(ctx, "vector", v_stats)
    t_stats = ctx.tok("v")

    mvh = p.ln_mv[:].rearrange("p (i two) -> p i two", two=2)
    mu, var = mvh[:, :, 0], mvh[:, :, 1]

    def s_rstd(e):
        e.wait(t_stats)
        e.activation(out=p.ln_rstd[:], in_=var, func=AF.Ln, bias=p.fc(ln_bias), scale=ln_scale)
        ctx.inc(e.activation(out=p.ln_rstd[:], in_=p.ln_rstd[:], func=AF.Exp, scale=-0.5), "s")
    on(ctx, "scalar", s_rstd)
    t_rstd = ctx.tok("s")

    def v_nmr(e):
        e.wait(t_rstd)
        ctx.inc(e.scalar_tensor_tensor(out=p.ln_nmr[:], in0=mu, scalar=-1.0,
                                       in1=p.ln_rstd[:], op0=ALU.mult, op1=ALU.mult), "v")
    on(ctx, "vector", v_nmr)
    return ctx.tok("v")


def emit_xnT(p, producers):
    ctx = p.ctx
    xnT_v = p.xnT[:].rearrange("p (j c) -> p j c", j=NDT)
    pst_v = p.pst[:].rearrange("p (j c) -> p j c", j=NDT)
    tr_toks = []
    for i in range(NTOK):
        slot = (i % 2) * D
        eng, sem, fn = producers(i)
        def prod(e, i=i, fn=fn, sem=sem):
            if i >= 2:
                e.wait(tr_toks[i - 2])
            ctx.inc(fn(e), sem)
        on(ctx, eng, prod)
        t_p = ctx.tok(sem)
        base_s = ctx.tok("s")
        def pe_t(e, i=i, slot=slot, t_p=t_p, base_s=base_s):
            e.wait(t_p)
            if i > 0:
                e.wait(base_s)
            for j in range(NDT):
                inst = e.transpose(p.pst[:, j * 128:(j + 1) * 128],
                                   p.xstg[:, slot + j * 128:slot + (j + 1) * 128], p.ident[:])
            tr_toks.append(ctx.inc(inst, "t"))
        on(ctx, "tensor", pe_t)
        t_tr = ctx.tok("t")
        def s_ev(e, i=i, t_tr=t_tr):
            e.wait(t_tr)
            ctx.inc(e.copy(out=xnT_v[:, :, i * 128:(i + 1) * 128], in_=pst_v), "s")
        on(ctx, "scalar", s_ev)
    return ctx.tok("s")


def emit_ln_t(p, dep_toks, eps2=None):
    t_nmr = emit_ln_stats_from(p, lambda i: p.xres[:, i * D:(i + 1) * D], dep_toks, eps2)

    def producers(i):
        def fn(e, i=i):
            e.wait(t_nmr)
            return e.tensor_scalar(out=p.xstg[:, (i % 2) * D:(i % 2 + 1) * D],
                                   in0=p.xres[:, i * D:(i + 1) * D],
                                   scalar1=p.ln_rstd[:, i:i + 1], scalar2=p.ln_nmr[:, i:i + 1],
                                   op0=ALU.mult, op1=ALU.add)
        return ("vector", "v", fn)
    t_xnT = emit_xnT(p, producers)
    return t_nmr, t_xnT



class MMPool:
    def __init__(self, ctx, tag, es=None):
        self.ctx = ctx
        self.ps = [ctx.psum(f"{tag}_mma", [128, 512], F32, es=es),
                   ctx.psum(f"{tag}_mmb", [128, 512], F32, es=es)]
        self.evac_tok = [None, None]
        self.pi = 0

    def step(self, deps, mms, evac_fn, eng="scalar", sem="s", width=512, out_p=128):
        ctx = self.ctx
        k = self.pi % 2
        self.pi += 1
        ps = self.ps[k]
        dep_ev = self.evac_tok[k]
        def pe_f(e):
            e.wait(*deps)
            if dep_ev is not None:
                e.wait(dep_ev)
            n = len(mms)
            for kk, (l, r) in enumerate(mms):
                inst = e.matmul(ps[0:out_p, 0:width], l, r, start=(kk == 0), stop=(kk == n - 1))
            ctx.inc(inst, "t")
        on(ctx, "tensor", pe_f)
        t_mm = ctx.tok("t")
        def ev_f(e):
            e.wait(t_mm)
            self.evac_tok[k] = ctx.inc(evac_fn(e, ps), sem)
        on(ctx, eng, ev_f)
        return ctx.tok(sem)

def emit_barrier(p):
    ctx = p.ctx
    toks = [(s, c) for s, c in ctx.cnt.items() if c > 0]
    for eng in ("tensor", "vector", "scalar", "gpsimd", "sync"):
        def f(e, toks=toks):
            e.wait(*toks)
        on(ctx, eng, f)



def emit_l0_ffn_prep(p, wT_ap, alpha_ap, bias_row_ap, tag, es=None):
    ctx = p.ctx
    u = tag
    wq = ctx.sbuf(f"{u}_w", [128, NDT * D], F32, es=es)
    tern = ctx.sbuf(f"{u}_tern", [128, NDT * D], BF16, es=es)
    absacc = ctx.sbuf(f"{u}_absacc", [128, NDT + 1], F32, es=es)
    arow = ctx.sbuf(f"{u}_arow", [1, 4], F32, es=es)
    acol = ctx.sbuf(f"{u}_acol", [128, 3], F32, es=es)
    scr = ctx.sbuf(f"{u}_scr", [128, D], F32, es=es)
    brow = ctx.sbuf(f"{u}_brow", [1, D], F32, es=es)
    brow_b = ctx.sbuf(f"{u}_browb", [1, D], BF16, es=es)
    ps_d = ctx.psum(f"{u}_psd", [1, 8], F32, es=es)
    ps_bc = ctx.psum(f"{u}_psbc", [128, 3], F32, es=es)

    wv = wT_ap.rearrange("(j p) o -> j p o", p=128)

    def dma(e):
        for j in range(NDT):
            ctx.inc(e.dma_start(out=wq[:, j * D:(j + 1) * D], in_=wv[j]), "qg", 16)
        ctx.inc(e.dma_start(out=arow[0:1, 2:3], in_=alpha_ap), "qg", 16)
        ctx.inc(e.dma_start(out=brow[:], in_=bias_row_ap), "qg", 16)
    on(ctx, "gpsimd", dma)
    t_w = ctx.tok("qg")

    def s(e):
        e.wait(t_w)
        for j in range(NDT):
            inst = e.activation(out=scr[:], in_=wq[:, j * D:(j + 1) * D], func=AF.Abs,
                                accum_out=absacc[:, j:j + 1])
        ctx.inc(inst, "s")
    on(ctx, "scalar", s)
    t_abs = ctx.tok("s")

    def v(e):
        e.wait(t_abs)
        ctx.inc(e.reduce_sum(out=absacc[:, NDT:NDT + 1], in_=absacc[:, 0:NDT], axis=AX.X), "v")
    on(ctx, "vector", v)
    t_red = ctx.tok("v")

    def pe(e):
        e.wait(t_red, p.const_tok)
        ctx.inc(e.matmul(ps_d[0:1, 0:1], p.ones_f[:, 0:1], absacc[:, NDT:NDT + 1],
                         start=True, stop=True), "t")
    on(ctx, "tensor", pe)
    t_sum = ctx.tok("t")

    def s2(e):
        e.wait(t_sum)
        e.activation(out=arow[0:1, 0:1], in_=ps_d[0:1, 0:1], func=AF.Copy, scale=0.7 / (D * D))
        e.activation(out=arow[0:1, 1:2], in_=ps_d[0:1, 0:1], func=AF.Copy, scale=-0.7 / (D * D))
        ctx.inc(e.activation(out=arow[0:1, 2:3], in_=arow[0:1, 2:3], func=AF.Copy, scale=0.5), "s")
    on(ctx, "scalar", s2)
    t_row = ctx.tok("s")

    def v2(e):
        e.wait(t_row)
        ctx.inc(e.reciprocal(out=arow[0:1, 3:4], in_=arow[0:1, 2:3]), "v")
    on(ctx, "vector", v2)
    t_rec = ctx.tok("v")

    def pe2(e):
        e.wait(t_rec)
        ctx.inc(e.matmul(ps_bc[:], p.ones_f[0:1, :], arow[0:1, 0:3], start=True, stop=True), "t")
    on(ctx, "tensor", pe2)
    t_bc = ctx.tok("t")

    def s3(e):
        e.wait(t_bc)
        e.copy(out=acol[:], in_=ps_bc[:])
        ctx.inc(e.activation(out=brow_b[:], in_=brow[:], func=AF.Copy, scale=arow[0:1, 3:4]), "s")
    on(ctx, "scalar", s3)
    t_cols = ctx.tok("s")

    def s4(e):
        e.wait(t_cols)
        for j in range(NDT):
            e.activation(out=tern[:, j * D:(j + 1) * D], in_=wq[:, j * D:(j + 1) * D],
                         func=AF.Sign, bias=acol[:, 1:2])
            inst = e.activation(out=wq[:, j * D:(j + 1) * D], in_=wq[:, j * D:(j + 1) * D],
                                func=AF.Sign, bias=acol[:, 0:1])
        ctx.inc(inst, "s")
    on(ctx, "scalar", s4)
    t_signs = ctx.tok("s")

    def v3(e):
        e.wait(t_signs)
        for j in range(NDT):
            inst = e.tensor_tensor(out=tern[:, j * D:(j + 1) * D], in0=tern[:, j * D:(j + 1) * D],
                                   in1=wq[:, j * D:(j + 1) * D], op=ALU.add)
        ctx.inc(inst, "v")
    on(ctx, "vector", v3)
    t_tern = ctx.tok("v")
    return dict(tern=tern, acol=acol, brow_b=brow_b, t_ready=(t_tern, t_cols))


def emit_l0_ffn_call(p, prep, mm, t_xnT):
    ctx = p.ctx
    tern, acol, brow_b = prep["tern"], prep["acol"], prep["brow_b"]
    for i in range(NTOK):
        for half in range(2):
            mms = [(p.xnT[:, j * T + i * 128:j * T + (i + 1) * 128],
                    tern[:, j * D + half * 512:j * D + half * 512 + 512]) for j in range(NDT)]
            mms.append((p.ones_b[0:1, 0:128], brow_b[0:1, half * 512:half * 512 + 512]))
            def ev(e, ps, i=i, half=half):
                sl = p.xres[:, i * D + half * 512:i * D + half * 512 + 512]
                return e.scalar_tensor_tensor(out=sl, in0=ps[:, 0:512], scalar=acol[:, 2:3],
                                              in1=sl, op0=ALU.mult, op1=ALU.add)
            mm.step([t_xnT, *prep["t_ready"]], mms, ev, eng="vector", sem="v")
    return ctx.tok("v")


def emit_l0_attn_b(p, io, x_in_ap, x_out_ap, tag):
    ctx = p.ctx
    u = tag
    sb = lambda n, s, d: ctx.sbuf(f"{u}_{n}", s, d)
    psm = lambda n, s, d: ctx.psum(f"{u}_{n}", s, d)
    NH = 16

    xv = x_in_ap.rearrange("(i p) d -> i p d", p=128)
    xov = x_out_ap.rearrange("(i p) d -> i p d", p=128)
    pev = io["pe_tok"].rearrange("(i p) d -> i p d", p=128)

    xfs = sb("xfs", [128, 2 * D], F32)
    peb = sb("peb", [128, D], F32)
    pescr = sb("pescr", [128, D], F32)

    ld_toks = []
    st_done = []
    for i in range(NTOK):
        def g_l(e, i=i):
            if i >= 2:
                e.wait(st_done[i - 2])
            ld_toks.append(ctx.inc(e.dma_start(out=xfs[:, (i % 2) * D:(i % 2 + 1) * D], in_=xv[i]), "qs", 16))
        on(ctx, "sync", g_l)
        def v_st(e, i=i):
            e.wait(ld_toks[i])
            st = p.ln_stats[:, i * 12:(i + 1) * 12]
            e.bn_stats(out=st[:, 0:6], in_=xfs[:, (i % 2) * D:(i % 2) * D + 512])
            inst = e.bn_stats(out=st[:, 6:12], in_=xfs[:, (i % 2) * D + 512:(i % 2 + 1) * D])
            st_done.append(ctx.inc(inst, "v"))
        on(ctx, "vector", v_st)
    def v_ag(e):
        e.wait(st_done[NTOK - 1])
        for i in range(NTOK):
            inst = e.bn_aggr(out=p.ln_mv[:, 2 * i:2 * i + 2], in_=p.ln_stats[:, i * 12:(i + 1) * 12])
        ctx.inc(inst, "v")
    on(ctx, "vector", v_ag)
    t_stats = ctx.tok("v")

    mvh = p.ln_mv[:].rearrange("p (i two) -> p i two", two=2)
    mu, var = mvh[:, :, 0], mvh[:, :, 1]
    def s_rstd(e):
        e.wait(t_stats)
        e.activation(out=p.ln_rstd[:], in_=var, func=AF.Ln, bias=p.fc(EPS1), scale=1.0)
        ctx.inc(e.activation(out=p.ln_rstd[:], in_=p.ln_rstd[:], func=AF.Exp, scale=-0.5), "s")
    on(ctx, "scalar", s_rstd)
    t_rstd = ctx.tok("s")
    def v_nmr(e):
        e.wait(t_rstd)
        ctx.inc(e.scalar_tensor_tensor(out=p.ln_nmr[:], in0=mu, scalar=-1.0,
                                       in1=p.ln_rstd[:], op0=ALU.mult, op1=ALU.mult), "v")
    on(ctx, "vector", v_nmr)
    t_nmr = ctx.tok("v")

    xa_toks, pa_toks = [], []
    V0 = ctx.cnt["v"]
    def producers(i):
        def g_l(e, i=i):
            e.wait(t_nmr)
            if i >= 1:
                e.wait(("v", V0 + i))
            xa_toks.append(ctx.inc(e.dma_start(out=xfs[:, (i % 2) * D:(i % 2 + 1) * D], in_=xv[i]), "qs", 16))
            pa_toks.append(ctx.inc(e.dma_start(out=peb[:], in_=pev[i]), "qs", 16))
        on(ctx, "sync", g_l)
        def fn(e, i=i):
            e.wait(xa_toks[i], pa_toks[i], t_nmr)
            e.tensor_scalar(out=pescr[:], in0=peb[:],
                            scalar1=p.ln_nmr[:, i:i + 1], scalar2=None, op0=ALU.add)
            return e.scalar_tensor_tensor(out=p.xstg[:, (i % 2) * D:(i % 2 + 1) * D],
                                          in0=xfs[:, (i % 2) * D:(i % 2 + 1) * D],
                                          scalar=p.ln_rstd[:, i:i + 1], in1=pescr[:],
                                          op0=ALU.mult, op1=ALU.add)
        return ("vector", "v", fn)
    t_xnT = emit_xnT(p, producers)

    wbuf = sb("wbuf", [128, NDT * D], BF16)
    stage = sb("stage", [128, D], F32)
    bcols = sb("bcols", [128, 2 * NDT], F32)
    rows_f = sb("rows_f", [1, D], F32)
    rows_b = sb("rows_b", [1, 2 * D], BF16)
    gam = sb("gam", [128, D], F32)
    bet = sb("bet", [128, D], F32)

    def g_small(e):
        ctx.inc(e.dma_start(out=bcols[:, 0:NDT], in_=io["bq_c"]), "qg", 16)
        ctx.inc(e.dma_start(out=bcols[:, NDT:2 * NDT], in_=io["bk_c"]), "qg", 16)
        ctx.inc(e.dma_start(out=rows_f[0:1, 0:D], in_=io["bv_row"]), "qg", 16)
        ctx.inc(e.dma_start(out=gam[:], in_=io["gamma_bc"]), "qg", 16)
        ctx.inc(e.dma_start(out=bet[:], in_=io["beta_bc"]), "qg", 16)
    on(ctx, "gpsimd", g_small)
    t_small = ctx.tok("qg")

    def s_rows(e):
        e.wait(t_small)
        ctx.inc(e.copy(out=rows_b[0:1, 0:D], in_=rows_f[:]), "s")
    on(ctx, "scalar", s_rows)
    t_rows0 = ctx.tok("s")

    def g_small2(e):
        e.wait(t_rows0)
        ctx.inc(e.dma_start(out=rows_f[0:1, 0:D], in_=io["bo_row"]), "qg", 16)
    on(ctx, "gpsimd", g_small2)
    t_small2 = ctx.tok("qg")

    def s_rows2(e):
        e.wait(t_small2)
        ctx.inc(e.copy(out=rows_b[0:1, D:2 * D], in_=rows_f[:]), "s")
    on(ctx, "scalar", s_rows2)
    t_rows = ctx.tok("s")

    def load_weight(wap, dep):
        wvv = wap.rearrange("(j p) f -> j p f", p=128)
        cts = []
        for j in range(NDT):
            def g_w(e, j=j, dep=dep):
                e.wait(*dep)
                if j >= 1:
                    e.wait(cts[j - 1])
                ctx.inc(e.dma_start(out=stage[:], in_=wvv[j]), "qg", 16)
            on(ctx, "gpsimd", g_w)
            t_l = ctx.tok("qg")
            def s_c(e, j=j, t_l=t_l):
                e.wait(t_l)
                cts.append(ctx.inc(e.copy(out=wbuf[:, j * D:(j + 1) * D], in_=stage[:]), "s"))
            on(ctx, "scalar", s_c)
        return ctx.tok("s")

    qT = sb("qT", [128, NDT * T], BF16)
    kT = sb("kT", [128, NDT * T], BF16)
    vplus = sb("vplus", [128, NTOK * 1040], BF16)
    vplus_v = vplus[:].rearrange("p (i h c) -> p i h c", i=NTOK, h=16)
    mm = MMPool(ctx, u)

    def v_ones(e):
        ctx.inc(e.memset(vplus_v[:, :, :, 64:65], 1.0), "v")
    on(ctx, "vector", v_ones)
    t_vones = ctx.tok("v")

    t_proj = None
    for which, dstT, bc_off in [("q", qT, 0), ("k", kT, NDT)]:
        t_w = load_weight(io["w" + which], [t_proj] if t_proj else [])
        for ft in range(NDT):
            for tc in range(4):
                mms = [(wbuf[:, j * D + ft * 128:j * D + (ft + 1) * 128],
                        p.xnT[:, j * T + tc * 512:j * T + (tc + 1) * 512]) for j in range(NDT)]
                def ev(e, ps, dstT=dstT, ft=ft, tc=tc, bc_off=bc_off):
                    return e.activation(out=dstT[:, ft * T + tc * 512:ft * T + (tc + 1) * 512],
                                        in_=ps[:, 0:512], func=AF.Identity, scale=1.0,
                                        bias=bcols[:, bc_off + ft:bc_off + ft + 1])
                t_proj = mm.step([t_xnT, t_w, t_small], mms, ev)
    t_wv = load_weight(io["wv"], [t_proj])
    for ti in range(NTOK):
        for vc in range(2):
            mms = [(p.xnT[:, j * T + ti * 128:j * T + (ti + 1) * 128],
                    wbuf[:, j * D + vc * 512:j * D + (vc + 1) * 512]) for j in range(NDT)]
            mms.append((p.ones_b[0:1, 0:128], rows_b[0:1, vc * 512:(vc + 1) * 512]))
            def ev(e, ps, ti=ti, vc=vc):
                ps_v = ps[:, 0:512].rearrange("p (h c) -> p h c", h=8)
                return e.copy(out=vplus_v[:, ti, vc * 8:(vc + 1) * 8, 0:64], in_=ps_v)
            t_proj = mm.step([t_xnT, t_wv, t_vones, t_rows], mms, ev)
    t_projs = ctx.tok("s")

    ps_s = psm("pss", [128, 1024], F32)
    pv2 = psm("pv2", [128, 512], F32)
    pv3 = psm("pv3", [128, 512], F32)
    pv_banks = [mm.ps[0], mm.ps[1], pv2, pv3]
    PT = sb("PT", [128, 2048], BF16)
    attn_sb = sb("attn_sb", [128, 2048], F32)
    den_sb = sb("den_sb", [1, 2048], F32)

    t_lastdiv = None
    exp_tok = {}
    pv_tok = {}
    for h in range(NH):
        ft, su = h // 2, (h % 2) * 64
        for kt in range(NTOK):
            for half in range(2):
                prev_exp = exp_tok.get((h, kt, half - 1)) or exp_tok.get((h, kt - 1, 1))
                prev_div = t_lastdiv
                def pe_s(e, h=h, kt=kt, half=half, ft=ft, su=su, prev_exp=prev_exp, prev_div=prev_div):
                    e.wait(t_projs)
                    if prev_exp is not None:
                        e.wait(prev_exp)
                    if kt == 0 and half == 0 and prev_div is not None:
                        e.wait(prev_div)
                    for c in range(2):
                        q0 = half * 1024 + c * 512
                        inst = e.matmul(ps_s[:, c * 512:(c + 1) * 512],
                                        kT[su:su + 64, ft * T + kt * 128:ft * T + (kt + 1) * 128],
                                        qT[su:su + 64, ft * T + q0:ft * T + q0 + 512],
                                        start=True, stop=True)
                    ctx.inc(inst, "t")
                on(ctx, "tensor", pe_s)
                t_sc = ctx.tok("t")
                prev_pv = pv_tok.get((h, kt - 1))
                def s_exp(e, kt=kt, half=half, t_sc=t_sc, prev_pv=prev_pv):
                    e.wait(t_sc)
                    if prev_pv is not None:
                        e.wait(prev_pv)
                    inst = e.activation(out=PT[:, half * 1024:(half + 1) * 1024],
                                        in_=ps_s[:], func=AF.Exp, scale=0.125)
                    ctx.inc(inst, "s")
                on(ctx, "scalar", s_exp)
                exp_tok[(h, kt, half)] = ctx.tok("s")
            e_toks = (exp_tok[(h, kt, 0)], exp_tok[(h, kt, 1)])
            prev_div = t_lastdiv
            def pe_pv(e, h=h, kt=kt, e_toks=e_toks, prev_div=prev_div):
                e.wait(*e_toks)
                if kt == 0:
                    e.wait(t_projs)
                    if prev_div is not None:
                        e.wait(prev_div)
                for qc in range(4):
                    inst = e.matmul(pv_banks[qc][0:65, 0:512], vplus_v[:, kt, h, :],
                                    PT[:, qc * 512:(qc + 1) * 512],
                                    start=(kt == 0), stop=(kt == NTOK - 1))
                ctx.inc(inst, "t")
            on(ctx, "tensor", pe_pv)
            pv_tok[(h, kt)] = ctx.tok("t")
        t_pv = pv_tok[(h, NTOK - 1)]

        def s_ev(e, t_pv=t_pv):
            e.wait(t_pv)
            for qc in range(4):
                e.copy(out=attn_sb[0:64, qc * 512:(qc + 1) * 512], in_=pv_banks[qc][0:64, 0:512])
                e.copy(out=den_sb[0:1, qc * 512:(qc + 1) * 512], in_=pv_banks[qc][64:65, 0:512])
            e.activation(out=den_sb[:], in_=den_sb[:], func=AF.Ln, bias=0.0)
            inst = e.activation(out=den_sb[:], in_=den_sb[:], func=AF.Exp, scale=-1.0)
            ctx.inc(inst, "s")
        on(ctx, "scalar", s_ev)
        t_ev = ctx.tok("s")

        def pe_bc(e, t_ev=t_ev):
            e.wait(t_ev)
            for qc in range(4):
                inst = e.matmul(ps_s[(qc % 2) * 64:(qc % 2) * 64 + 64, (qc // 2) * 512:(qc // 2 + 1) * 512],
                                p.ones_f[0:1, 0:64],
                                den_sb[0:1, qc * 512:(qc + 1) * 512],
                                start=True, stop=True)
            ctx.inc(inst, "t")
        on(ctx, "tensor", pe_bc)
        t_bc = ctx.tok("t")

        def v_div(e, ft=ft, su=su, t_bc=t_bc):
            e.wait(t_bc)
            for qc in range(4):
                rb = ps_s[(qc % 2) * 64:(qc % 2) * 64 + 64, (qc // 2) * 512:(qc // 2 + 1) * 512]
                inst = e.tensor_tensor(out=qT[su:su + 64, ft * T + qc * 512:ft * T + (qc + 1) * 512],
                                       in0=attn_sb[0:64, qc * 512:(qc + 1) * 512], in1=rb, op=ALU.mult)
            ctx.inc(inst, "v")
        on(ctx, "vector", v_div)
        t_lastdiv = ctx.tok("v")

    t_wo = load_weight(io["wo"], [t_lastdiv])
    xpescr = pescr
    hbuf = stage
    st2buf = sb("st2buf", [128, 16], F32)
    st2 = st2buf[:, 0:12]
    mv2 = st2buf[:, 12:14]
    r2 = st2buf[:, 14:15]
    n2 = st2buf[:, 15:16]
    xc_toks, pc_toks, out_toks = [], [], []
    vh_toks = []
    for i in range(NTOK):
        def g_l(e, i=i):
            if i >= 2:
                e.wait(out_toks[i - 2])
            if i >= 1:
                e.wait(vh_toks[i - 1])
            e.wait(t_nmr, t_lastdiv)
            xc_toks.append(ctx.inc(e.dma_start(out=xfs[:, (i % 2) * D:(i % 2 + 1) * D], in_=xv[i]), "qs", 16))
            pc_toks.append(ctx.inc(e.dma_start(out=peb[:], in_=pev[i]), "qs", 16))
        on(ctx, "sync", g_l)
        mm_toks = []
        for dc in range(2):
            mms = [(qT[:, j * T + i * 128:j * T + (i + 1) * 128],
                    wbuf[:, j * D + dc * 512:j * D + (dc + 1) * 512]) for j in range(NDT)]
            mms.append((p.ones_b[0:1, 0:128], rows_b[0:1, D + dc * 512:D + (dc + 1) * 512]))
            def ev(e, ps, i=i, dc=dc):
                return e.tensor_copy(out=hbuf[:, dc * 512:(dc + 1) * 512], in_=ps[:, 0:512])
            mm_toks.append(mm.step([t_wo, t_lastdiv, t_rows], mms, ev, eng="vector", sem="v"))
        def v_h(e, i=i, mm_toks=mm_toks):
            e.wait(xc_toks[i], pc_toks[i], *mm_toks)
            e.tensor_scalar(out=xpescr[:], in0=peb[:],
                            scalar1=p.ln_nmr[:, i:i + 1], scalar2=None, op0=ALU.add)
            e.scalar_tensor_tensor(out=xpescr[:], in0=xfs[:, (i % 2) * D:(i % 2 + 1) * D],
                                   scalar=p.ln_rstd[:, i:i + 1], in1=xpescr[:],
                                   op0=ALU.mult, op1=ALU.add)
            inst = e.tensor_tensor(out=hbuf[:], in0=hbuf[:], in1=xpescr[:], op=ALU.add)
            tk = ctx.inc(inst, "v")
            e.wait(tk)
            e.bn_stats(out=st2[:, 0:6], in_=hbuf[:, 0:512])
            inst = e.bn_stats(out=st2[:, 6:12], in_=hbuf[:, 512:1024])
            tk = ctx.inc(inst, "v")
            e.wait(tk)
            ctx.inc(e.bn_aggr(out=mv2[:], in_=st2[:]), "v")
        on(ctx, "vector", v_h)
        t_hs = ctx.tok("v")
        vh_toks.append(t_hs)
        def s_r2(e, t_hs=t_hs):
            e.wait(t_hs)
            e.activation(out=r2, in_=st2buf[:, 13:14], func=AF.Ln, bias=p.fc(EPS1), scale=1.0)
            ctx.inc(e.activation(out=r2, in_=r2, func=AF.Exp, scale=-0.5), "s")
        on(ctx, "scalar", s_r2)
        t_r2 = ctx.tok("s")
        def v_f(e, i=i, t_r2=t_r2):
            e.wait(t_r2)
            ctx.inc(e.scalar_tensor_tensor(out=n2, in0=st2buf[:, 12:13], scalar=-1.0, in1=r2,
                                           op0=ALU.mult, op1=ALU.mult), "v")
        on(ctx, "vector", v_f)
        t_n2 = ctx.tok("v")
        def s_a(e, i=i, t_n2=t_n2):
            e.wait(t_n2)
            ctx.inc(e.activation(out=xpescr[:], in_=hbuf[:], func=AF.Identity,
                                 scale=r2, bias=n2), "s")
        on(ctx, "scalar", s_a)
        t_a = ctx.tok("s")
        def v_out(e, i=i, t_a=t_a):
            e.wait(t_a, t_small)
            e.scalar_tensor_tensor(out=xpescr[:], in0=xpescr[:], scalar=1.0, in1=gam[:],
                                   op0=ALU.mult, op1=ALU.mult)
            e.tensor_tensor(out=xpescr[:], in0=xpescr[:], in1=bet[:], op=ALU.add)
            inst = e.tensor_tensor(out=xfs[:, (i % 2) * D:(i % 2 + 1) * D],
                                   in0=xfs[:, (i % 2) * D:(i % 2 + 1) * D], in1=xpescr[:], op=ALU.add)
            ctx.inc(inst, "v")
        on(ctx, "vector", v_out)
        t_vo = ctx.tok("v")
        def g_o(e, i=i, t_vo=t_vo):
            e.wait(t_vo)
            out_toks.append(ctx.inc(e.dma_start(out=xov[i], in_=xfs[:, (i % 2) * D:(i % 2 + 1) * D]), "qg", 16))
        on(ctx, "gpsimd", g_o)
    def g_fin(e):
        e.wait(ctx.tok("qg"))
    on(ctx, "gpsimd", g_fin)
    return ctx.tok("qg")


def emit_l0_attn_a(p, io, x_in_ap, x_out_ap, tag, dbg_upto=None, dbg_aps=None):
    ctx = p.ctx
    u_ = tag
    sb = lambda n, s, d: ctx.sbuf(f"{u_}_{n}", s, d)
    psm = lambda n, s, d: ctx.psum(f"{u_}_{n}", s, d)

    xv = x_in_ap.rearrange("(i p) d -> i p d", p=128)
    xov = x_out_ap.rearrange("(i p) d -> i p d", p=128)
    xfs = sb("xfs", [128, 2 * D], F32)

    ld_toks, st_done = [], []
    for i in range(NTOK):
        def g_l(e, i=i):
            if i >= 2:
                e.wait(st_done[i - 2])
            ld_toks.append(ctx.inc(e.dma_start(out=xfs[:, (i % 2) * D:(i % 2 + 1) * D], in_=xv[i]), "qs", 16))
        on(ctx, "sync", g_l)
        def v_st(e, i=i):
            e.wait(ld_toks[i])
            st = p.ln_stats[:, i * 12:(i + 1) * 12]
            e.bn_stats(out=st[:, 0:6], in_=xfs[:, (i % 2) * D:(i % 2) * D + 512])
            inst = e.bn_stats(out=st[:, 6:12], in_=xfs[:, (i % 2) * D + 512:(i % 2 + 1) * D])
            st_done.append(ctx.inc(inst, "v"))
        on(ctx, "vector", v_st)
    def v_ag(e):
        e.wait(st_done[NTOK - 1])
        for i in range(NTOK):
            inst = e.bn_aggr(out=p.ln_mv[:, 2 * i:2 * i + 2], in_=p.ln_stats[:, i * 12:(i + 1) * 12])
        ctx.inc(inst, "v")
    on(ctx, "vector", v_ag)
    t_stats = ctx.tok("v")
    mvh = p.ln_mv[:].rearrange("p (i two) -> p i two", two=2)
    mu0, var0 = mvh[:, :, 0], mvh[:, :, 1]
    def s_rstd(e):
        e.wait(t_stats)
        e.activation(out=p.ln_rstd[:], in_=var0, func=AF.Ln, bias=p.fc(EPS1 * 1e-6), scale=1.0 + 1e-6)
        ctx.inc(e.activation(out=p.ln_rstd[:], in_=p.ln_rstd[:], func=AF.Exp, scale=-0.5), "s")
    on(ctx, "scalar", s_rstd)
    t_rstd0 = ctx.tok("s")
    def v_nmr0(e):
        e.wait(t_rstd0)
        ctx.inc(e.scalar_tensor_tensor(out=p.ln_nmr[:], in0=mu0, scalar=-1.0,
                                       in1=p.ln_rstd[:], op0=ALU.mult, op1=ALU.mult), "v")
    on(ctx, "vector", v_nmr0)
    t_nmr = ctx.tok("v")

    xa_toks = []
    V0 = ctx.cnt["v"]
    def producers(i):
        def g_l2(e, i=i):
            e.wait(t_nmr)
            if i >= 2:
                e.wait(("v", V0 + (i - 1)))
            xa_toks.append(ctx.inc(e.dma_start(out=xfs[:, (i % 2) * D:(i % 2 + 1) * D], in_=xv[i]), "qs", 16))
        on(ctx, "sync", g_l2)
        def fn(e, i=i):
            e.wait(xa_toks[i], t_nmr)
            return e.tensor_scalar(out=p.xstg[:, (i % 2) * D:(i % 2 + 1) * D],
                                   in0=xfs[:, (i % 2) * D:(i % 2 + 1) * D],
                                   scalar1=p.ln_rstd[:, i:i + 1], scalar2=p.ln_nmr[:, i:i + 1],
                                   op0=ALU.mult, op1=ALU.add)
        return ("vector", "v", fn)
    t_xnT = emit_xnT(p, producers)

    ubuf = sb("ubuf", [128, NDT * T], BF16)
    cxbuf = sb("cxbuf", [128, NDT * T], BF16)
    wbuf = sb("wbuf", [128, NDT * D], BF16)
    stage = sb("stage", [128, D], F32)
    cols = sb("cols", [128, 8 * NDT], F32)
    ubias = sb("ubias", [128, NDT], F32)
    ratio = sb("ratio", [128, 2 * NDT], BF16)
    dbc = sb("dbc", [128, T], F32)
    wsum_b = sb("wsumb", [1, D], BF16)
    bwo_b = sb("bwob", [1, D], BF16)
    nmr_b = sb("nmrb", [1, T], BF16)
    rinv_b = sb("rinvb", [1, T], BF16)
    ones512 = sb("ones512", [1, 512], BF16)
    rbc = sb("rbc", [128, T], BF16)
    rrow = sb("rrow", [1, T], F32)
    mm = MMPool(ctx, u_)
    ps_bwt = psm("psbwt", [128, NDT], F32)
    ps_row = psm("psrow", [1, T], F32)

    c_ln1w = cols[:, 0:8]; c_ln1b = cols[:, 8:16]
    c_lncw = cols[:, 16:24]; c_lncb = cols[:, 24:32]
    c_lngw = cols[:, 32:40]; c_lngb = cols[:, 40:48]
    c_dec = cols[:, 48:56]; c_beta = cols[:, 56:64]

    def g_cols(e):
        ctx.inc(e.dma_start(out=c_ln1w, in_=io["ln1w_c"]), "qg", 16)
        ctx.inc(e.dma_start(out=c_ln1b, in_=io["ln1b_c"]), "qg", 16)
        ctx.inc(e.dma_start(out=c_lncw, in_=io["lncw_c"]), "qg", 16)
        ctx.inc(e.dma_start(out=c_lncb, in_=io["lncb_c"]), "qg", 16)
        ctx.inc(e.dma_start(out=c_lngw, in_=io["lngw_c"]), "qg", 16)
        ctx.inc(e.dma_start(out=c_lngb, in_=io["lngb_c"]), "qg", 16)
        ctx.inc(e.dma_start(out=c_dec, in_=p.c_aps["c_decay"]), "qg", 16)
        ctx.inc(e.dma_start(out=c_beta, in_=p.c_aps["c_decay_beta"]), "qg", 16)
    on(ctx, "gpsimd", g_cols)
    t_cols = ctx.tok("qg")

    def v_ratio(e):
        e.wait(t_cols)
        tk = ctx.inc(e.reciprocal(out=ubias[:, 0:NDT], in_=c_ln1w), "v")
        e.wait(tk)
        tk = ctx.inc(e.tensor_tensor(out=ratio[:, 0:NDT], in0=c_ln1b, in1=ubias[:, 0:NDT], op=ALU.mult), "v")
        e.wait(tk)
        tk = ctx.inc(e.reciprocal(out=ubias[:, 0:NDT], in_=c_lncw), "v")
        e.wait(tk)
        ctx.inc(e.tensor_tensor(out=ratio[:, NDT:2 * NDT], in0=c_lncb, in1=ubias[:, 0:NDT], op=ALU.mult), "v")
    on(ctx, "vector", v_ratio)
    t_ratio = ctx.tok("v")

    wtv = io["Wt"].rearrange("(j p) h -> j p h", p=128)
    cast_toks = []
    for j in range(NDT):
        def g_w(e, j=j):
            if j >= 1:
                e.wait(cast_toks[j - 1])
            ctx.inc(e.dma_start(out=stage[:], in_=wtv[j]), "qg", 16)
        on(ctx, "gpsimd", g_w)
        t_l = ctx.tok("qg")
        def s_c(e, j=j, t_l=t_l):
            e.wait(t_l, t_cols)
            cast_toks.append(ctx.inc(e.activation(out=wbuf[:, j * D:(j + 1) * D], in_=stage[:],
                                                  func=AF.Identity, scale=c_ln1w[:, j:j + 1], bias=0.0), "s"))
        on(ctx, "scalar", s_c)
    t_wt = ctx.tok("s")

    def pe_b(e):
        e.wait(t_wt, t_ratio)
        for jt in range(NDT):
            for j in range(NDT):
                inst = e.matmul(ps_bwt[:, jt:jt + 1], wbuf[:, j * D + jt * 128:j * D + (jt + 1) * 128],
                                ratio[:, j:j + 1], start=(j == 0), stop=(j == NDT - 1))
        ctx.inc(inst, "t")
    on(ctx, "tensor", pe_b)

    def v_ub(e):
        e.wait(ctx.tok("t"), t_wt)
        ctx.inc(e.tensor_tensor(out=ubias[:], in0=ps_bwt[:], in1=c_beta, op=ALU.mult), "v")
    on(ctx, "vector", v_ub)
    t_ub = ctx.tok("v")

    for ht in range(NDT):
        for sc in range(4):
            mms = [(wbuf[:, j * D + ht * 128:j * D + (ht + 1) * 128],
                    p.xnT[:, j * T + sc * 512:j * T + (sc + 1) * 512]) for j in range(NDT)]
            def ev(e, ps, ht=ht, sc=sc):
                return e.activation(out=ubuf[:, ht * T + sc * 512:ht * T + (sc + 1) * 512],
                                    in_=ps[:, 0:512], func=AF.Identity,
                                    scale=c_beta[:, ht:ht + 1], bias=ubias[:, ht:ht + 1])
            mm.step([t_xnT, t_wt, t_ub], mms, ev)
    t_u = ctx.tok("s")

    scan_toks = []
    def v_init(e):
        ctx.inc(e.memset(dbc[:], 0.0), "v")
    on(ctx, "vector", v_init)
    for ht in range(NDT):
        def v_sc(e, ht=ht):
            e.wait(t_u, t_cols)
            e.tensor_scalar(out=dbc[:], in0=dbc[:], scalar1=0.0, scalar2=c_dec[:, ht:ht + 1],
                            op0=ALU.mult, op1=ALU.add)
            inst = e.tensor_tensor_scan(out=cxbuf[:, ht * T:(ht + 1) * T], data0=dbc[:],
                                        data1=ubuf[:, ht * T:(ht + 1) * T], initial=0.0,
                                        op0=ALU.mult, op1=ALU.add)
            scan_toks.append(ctx.inc(inst, "v"))
        on(ctx, "vector", v_sc)
    t_scan = ctx.tok("v")

    sqs = sb("sqs", [128, 512], BF16)
    for sc in range(4):
        def pe_s1(e, sc=sc):
            e.wait(t_scan, p.const_tok)
            for ht in range(NDT):
                inst = e.matmul(ps_row[0:1, sc * 512:(sc + 1) * 512], p.ones_b[:, 0:1],
                                cxbuf[:, ht * T + sc * 512:ht * T + (sc + 1) * 512],
                                start=(ht == 0), stop=(ht == NDT - 1))
            ctx.inc(inst, "t")
        on(ctx, "tensor", pe_s1)
    t_sum1 = ctx.tok("t")
    def s_ev1(e):
        e.wait(t_sum1)
        ctx.inc(e.copy(out=dbc[0:1, 0:T], in_=ps_row[:]), "s")
    on(ctx, "scalar", s_ev1)
    t_r1 = ctx.tok("s")
    sq_toks = {}
    for sc in range(4):
        for ht in range(NDT):
            def s_sq(e, ht=ht, sc=sc):
                e.wait(t_scan)
                if sq_toks:
                    pass
                e.wait(t_r1)
                ctx.inc(e.activation(out=sqs[:], in_=cxbuf[:, ht * T + sc * 512:ht * T + (sc + 1) * 512],
                                     func=AF.Square), "s")
            on(ctx, "scalar", s_sq)
            t_sq = ctx.tok("s")
            def pe_s2(e, ht=ht, sc=sc, t_sq=t_sq):
                e.wait(t_sq)
                ctx.inc(e.matmul(ps_row[0:1, sc * 512:(sc + 1) * 512], p.ones_b[:, 0:1], sqs[:],
                                 start=(ht == 0), stop=(ht == NDT - 1)), "t")
            on(ctx, "tensor", pe_s2)
            sq_toks[(sc, ht)] = ctx.tok("t")
            t_mm2 = ctx.tok("t")
            def s_gate(e, t_mm2=t_mm2):
                e.wait(t_mm2)
            on(ctx, "scalar", s_gate)
    t_sum2 = ctx.tok("t")
    def s_ev2(e):
        e.wait(t_sum2)
        ctx.inc(e.copy(out=rrow[0:1, 0:T], in_=ps_row[:]), "s")
    on(ctx, "scalar", s_ev2)
    t_r2tok = ctx.tok("s")

    mu_r = dbc[0:1, 0:T]
    r_r = rrow[0:1, 0:T]
    def v_rows_fixed(e):
        e.wait(t_r2tok, t_scan)
        e.tensor_scalar(out=mu_r, in0=mu_r, scalar1=1.0 / D, scalar2=None, op0=ALU.mult)
        e.tensor_scalar(out=r_r, in0=r_r, scalar1=1.0 / D, scalar2=None, op0=ALU.mult)
        e.tensor_tensor(out=nmr_b[0:1, 0:T], in0=mu_r, in1=mu_r, op=ALU.mult)
        inst = e.tensor_tensor(out=r_r, in0=r_r, in1=nmr_b[0:1, 0:T], op=ALU.subtract)
        ctx.inc(inst, "v")
    on(ctx, "vector", v_rows_fixed)
    t_var = ctx.tok("v")
    def s_rr(e):
        e.wait(t_var)
        e.activation(out=r_r, in_=r_r, func=AF.Ln, bias=p.fc1(1e-6), scale=1.0)
        e.activation(out=rinv_b[0:1, 0:T], in_=r_r, func=AF.Exp, scale=0.5)
        ctx.inc(e.activation(out=r_r, in_=r_r, func=AF.Exp, scale=-0.5), "s")
    on(ctx, "scalar", s_rr)
    t_rr = ctx.tok("s")
    def v_nm(e):
        e.wait(t_rr)
        e.tensor_scalar(out=nmr_b[0:1, 0:T], in0=mu_r, scalar1=-1.0, scalar2=None, op0=ALU.mult)
        ctx.inc(e.memset(ones512[:], 1.0), "v")
    on(ctx, "vector", v_nm)
    t_nm = ctx.tok("v")

    if dbg_upto == "scan":
        def g_d(e):
            e.wait(t_scan, t_xnT)
            ctx.inc(e.dma_start(out=dbg_aps["d_u"], in_=ubuf[:]), "qg", 16)
            ctx.inc(e.dma_start(out=dbg_aps["d_cx"], in_=cxbuf[:]), "qg", 16)
            ctx.inc(e.dma_start(out=dbg_aps["d_xnT"], in_=p.xnT[:]), "qg", 16)
            ctx.inc(e.dma_start(out=dbg_aps["d_ub"], in_=ubias[:]), "qg", 16)
            ctx.inc(e.dma_start(out=dbg_aps["d_ratio"], in_=ratio[:]), "qg", 16)
            e.wait(ctx.tok("qg"))
        on(ctx, "gpsimd", g_d)
        return ctx.tok("qg")
    wov = io["Wo"].rearrange("(j p) d2 -> j p d2", p=128)
    cast2 = []
    for j in range(NDT):
        def g_w(e, j=j):
            e.wait(t_u)
            if j >= 1:
                e.wait(cast2[j - 1])
            ctx.inc(e.dma_start(out=stage[:], in_=wov[j]), "qg", 16)
        on(ctx, "gpsimd", g_w)
        t_l = ctx.tok("qg")
        def s_c2(e, j=j, t_l=t_l):
            e.wait(t_l)
            cast2.append(ctx.inc(e.activation(out=wbuf[:, j * D:(j + 1) * D], in_=stage[:],
                                              func=AF.Identity, scale=c_lncw[:, j:j + 1], bias=0.0), "s"))
        on(ctx, "scalar", s_c2)
    t_wo = ctx.tok("s")

    def pe_bwo(e):
        e.wait(t_wo, t_ratio, t_r2tok)
        for c in range(2):
            for j in range(NDT):
                inst = e.matmul(ps_row[0:1, c * 512:(c + 1) * 512], ratio[:, NDT + j:NDT + j + 1],
                                wbuf[:, j * D + c * 512:j * D + (c + 1) * 512],
                                start=(j == 0), stop=(j == NDT - 1))
        ctx.inc(inst, "t")
    on(ctx, "tensor", pe_bwo)

    def s_bwo(e):
        e.wait(ctx.tok("t"))
        ctx.inc(e.copy(out=bwo_b[0:1, 0:D], in_=ps_row[0:1, 0:D]), "s")
    on(ctx, "scalar", s_bwo)
    t_bwo = ctx.tok("s")

    for c in range(2):
        def pe_ws(e, c=c):
            e.wait(t_wo, t_bwo)
            for j in range(NDT):
                inst = e.matmul(ps_row[0:1, c * 512:(c + 1) * 512], p.ones_b[:, 0:1],
                                wbuf[:, j * D + c * 512:j * D + (c + 1) * 512],
                                start=(j == 0), stop=(j == NDT - 1))
            ctx.inc(inst, "t")
        on(ctx, "tensor", pe_ws)
    t_ws = ctx.tok("t")
    def s_ws(e):
        e.wait(t_ws)
        ctx.inc(e.copy(out=wsum_b[0:1, 0:D], in_=ps_row[0:1, 0:D]), "s")
    on(ctx, "scalar", s_ws)
    t_ws2 = ctx.tok("s")

    for c in range(4):
        def ev(e, ps, c=c):
            return e.tensor_copy(out=rbc[:, c * 512:(c + 1) * 512], in_=ps[:, 0:512])
        mm.step([t_rr, t_nm], [(p.ones_f[0:1, :], rrow[0:1, c * 512:(c + 1) * 512])], ev,
                eng="vector", sem="v")
    t_rbc = ctx.tok("v")

    for dt in range(NDT):
        for sc in range(4):
            mms = [(wbuf[:, j * D + dt * 128:j * D + (dt + 1) * 128],
                    cxbuf[:, j * T + sc * 512:j * T + (sc + 1) * 512]) for j in range(NDT)]
            mms.append((wsum_b[0:1, dt * 128:(dt + 1) * 128], nmr_b[0:1, sc * 512:(sc + 1) * 512]))
            mms.append((bwo_b[0:1, dt * 128:(dt + 1) * 128], rinv_b[0:1, sc * 512:(sc + 1) * 512]))
            def ev(e, ps, dt=dt, sc=sc):
                return e.tensor_tensor(out=ubuf[:, dt * T + sc * 512:dt * T + (sc + 1) * 512],
                                       in0=ps[:, 0:512], in1=rbc[:, sc * 512:(sc + 1) * 512],
                                       op=ALU.mult)
            mm.step([t_wo, t_ws2, t_nm, t_rbc, t_scan], mms, ev, eng="vector", sem="v")
    t_om = ctx.tok("v")

    if dbg_upto == "om":
        def g_d(e):
            e.wait(t_om)
            ctx.inc(e.dma_start(out=dbg_aps["d_u"], in_=ubuf[:]), "qg", 16)
            ctx.inc(e.dma_start(out=dbg_aps["d_cx"], in_=cxbuf[:]), "qg", 16)
            ctx.inc(e.dma_start(out=dbg_aps["d_xnT"], in_=p.xnT[:]), "qg", 16)
            ctx.inc(e.dma_start(out=dbg_aps["d_ub"], in_=ubias[:]), "qg", 16)
            ctx.inc(e.dma_start(out=dbg_aps["d_ratio"], in_=ratio[:]), "qg", 16)
            ctx.inc(e.dma_start(out=dbg_aps["d_rrow"], in_=rrow[:]), "qg", 16)
            ctx.inc(e.dma_start(out=dbg_aps["d_murow"], in_=dbc[0:1, 0:T]), "qg", 16)
            e.wait(ctx.tok("qg"))
        on(ctx, "gpsimd", g_d)
        return ctx.tok("qg")
    def s_xt(e):
        e.wait(t_om)
        for j in range(NDT):
            inst = e.activation(out=p.xnT[:, j * T:(j + 1) * T], in_=p.xnT[:, j * T:(j + 1) * T],
                                func=AF.Identity, scale=c_ln1w[:, j:j + 1], bias=c_ln1b[:, j:j + 1])
        ctx.inc(inst, "s")
    on(ctx, "scalar", s_xt)
    t_xt = ctx.tok("s")

    wgv = io["Wg"].rearrange("(j p) g -> j p g", p=128)
    bg_row = sb("bgrow", [1, D], BF16)
    bg_f = sb("bgf", [1, D], F32)
    def g_bg(e):
        ctx.inc(e.dma_start(out=bg_f[:], in_=io["bg_row"]), "qg", 16)
    on(ctx, "gpsimd", g_bg)
    t_bgl = ctx.tok("qg")
    def s_bg(e):
        e.wait(t_bgl)
        ctx.inc(e.copy(out=bg_row[:], in_=bg_f[:]), "s")
    on(ctx, "scalar", s_bg)
    t_bg = ctx.tok("s")
    cast3 = []
    for j in range(16):
        def g_w(e, j=j):
            e.wait(t_om)
            if j >= 1:
                e.wait(cast3[j - 1])
            ctx.inc(e.dma_start(out=stage[:], in_=wgv[j]), "qg", 16)
        on(ctx, "gpsimd", g_w)
        t_l = ctx.tok("qg")
        def s_c3(e, j=j, t_l=t_l):
            e.wait(t_l)
            cast3.append(ctx.inc(e.copy(out=cxbuf[:, j * D:(j + 1) * D], in_=stage[:]), "s"))
        on(ctx, "scalar", s_c3)
    t_wg = ctx.tok("s")

    ystg = sb("ystg", [128, NDT * 512], BF16)
    substg = ystg
    onesr = sb("onesr", [1, 512], BF16)
    def v_or(e):
        ctx.inc(e.memset(onesr[:], 1.0), "v")
    on(ctx, "vector", v_or)
    t_or = ctx.tok("v")
    gstg = sb("gstg", [128, NDT * 512], BF16)
    r3bc = sb("r3bc", [128, 512], F32)
    n3bc = sb("n3bc", [128, 512], F32)
    xc_toks, out_toks = [], []
    last_tok = None
    for sc in range(4):
        y_toks = []
        for gt in range(NDT):
            mms = [(cxbuf[:, j * D + gt * 128:j * D + (gt + 1) * 128],
                    (p.xnT if j < NDT else ubuf)[:, (j % NDT) * T + sc * 512:(j % NDT) * T + (sc + 1) * 512])
                   for j in range(16)]
            mms.append((bg_row[0:1, gt * 128:(gt + 1) * 128], onesr[0:1, 0:512]))
            def ev(e, ps, gt=gt, sc=sc):
                return e.copy(out=ystg[:, gt * 512:(gt + 1) * 512], in_=ps[:, 0:512])
            y_toks.append(mm.step([t_xt, t_wg, t_bg, t_or], mms, ev))
        t_y = ctx.tok("s")
        def pe_sy(e, sc=sc, t_y=t_y):
            e.wait(t_y)
            for gt in range(NDT):
                inst = e.matmul(ps_row[0:1, 0:512], p.ones_b[:, 0:1], ystg[:, gt * 512:(gt + 1) * 512],
                                start=(gt == 0), stop=(gt == NDT - 1))
            ctx.inc(inst, "t")
        on(ctx, "tensor", pe_sy)
        t_sy = ctx.tok("t")
        def s_evy(e, t_sy=t_sy):
            e.wait(t_sy)
            ctx.inc(e.copy(out=dbc[0:1, 0:512], in_=ps_row[0:1, 0:512]), "s")
        on(ctx, "scalar", s_evy)
        t_ry = ctx.tok("s")
        sq2_toks = []
        for gt in range(NDT):
            def s_sq2(e, gt=gt, t_ry=t_ry):
                e.wait(t_ry)
                if sq2_toks:
                    e.wait(sq2_toks[-1])
                ctx.inc(e.activation(out=sqs[:], in_=ystg[:, gt * 512:(gt + 1) * 512], func=AF.Square), "s")
            on(ctx, "scalar", s_sq2)
            t_sq = ctx.tok("s")
            def pe_sq2(e, gt=gt, t_sq=t_sq):
                e.wait(t_sq)
                ctx.inc(e.matmul(ps_row[0:1, 0:512], p.ones_b[:, 0:1], sqs[:],
                                 start=(gt == 0), stop=(gt == NDT - 1)), "t")
            on(ctx, "tensor", pe_sq2)
            sq2_toks.append(ctx.tok("t"))
        t_sy2 = ctx.tok("t")
        def s_evy2(e, t_sy2=t_sy2):
            e.wait(t_sy2)
            ctx.inc(e.copy(out=dbc[0:1, 512:1024], in_=ps_row[0:1, 0:512]), "s")
        on(ctx, "scalar", s_evy2)
        t_ry2 = ctx.tok("s")
        def v_r3(e, t_ry2=t_ry2):
            e.wait(t_ry2)
            e.tensor_scalar(out=dbc[0:1, 0:512], in0=dbc[0:1, 0:512], scalar1=1.0 / D,
                            scalar2=None, op0=ALU.mult)
            e.tensor_scalar(out=dbc[0:1, 512:1024], in0=dbc[0:1, 512:1024], scalar1=1.0 / D,
                            scalar2=None, op0=ALU.mult)
            e.tensor_tensor(out=dbc[0:1, 1024:1536], in0=dbc[0:1, 0:512], in1=dbc[0:1, 0:512],
                            op=ALU.mult)
            inst = e.tensor_tensor(out=dbc[0:1, 512:1024], in0=dbc[0:1, 512:1024],
                                   in1=dbc[0:1, 1024:1536], op=ALU.subtract)
            ctx.inc(inst, "v")
        on(ctx, "vector", v_r3)
        t_v3 = ctx.tok("v")
        def s_r3(e, t_v3=t_v3):
            e.wait(t_v3)
            e.activation(out=dbc[0:1, 512:1024], in_=dbc[0:1, 512:1024], func=AF.Ln,
                         bias=p.fc1(1e-6), scale=1.0)
            ctx.inc(e.activation(out=dbc[0:1, 512:1024], in_=dbc[0:1, 512:1024],
                                 func=AF.Exp, scale=-0.5), "s")
        on(ctx, "scalar", s_r3)
        t_sr3 = ctx.tok("s")
        def v_n3(e, t_sr3=t_sr3):
            e.wait(t_sr3)
            ctx.inc(e.scalar_tensor_tensor(out=dbc[0:1, 0:512], in0=dbc[0:1, 0:512], scalar=-1.0,
                                           in1=dbc[0:1, 512:1024], op0=ALU.mult, op1=ALU.mult), "v")
        on(ctx, "vector", v_n3)
        t_n3 = ctx.tok("v")
        def ev_r(e, ps):
            return e.tensor_copy(out=r3bc[:], in_=ps[:, 0:512])
        mm.step([t_n3], [(p.ones_f[0:1, :], dbc[0:1, 512:1024])], ev_r, eng="vector", sem="v")
        def ev_n(e, ps):
            return e.tensor_copy(out=n3bc[:], in_=ps[:, 0:512])
        mm.step([t_n3], [(p.ones_f[0:1, :], dbc[0:1, 0:512])], ev_n, eng="vector", sem="v")
        t_bc3 = ctx.tok("v")
        g_toks = []
        for gt in range(NDT):
            def v_yh(e, gt=gt, t_bc3=t_bc3):
                e.wait(t_bc3)
                e.scalar_tensor_tensor(out=sqs[:], in0=ystg[:, gt * 512:(gt + 1) * 512], scalar=1.0,
                                       in1=r3bc[:], op0=ALU.mult, op1=ALU.mult)
                inst = e.tensor_tensor(out=sqs[:], in0=sqs[:], in1=n3bc[:], op=ALU.add)
                ctx.inc(inst, "v")
            on(ctx, "vector", v_yh)
            t_yh = ctx.tok("v")
            def s_sig(e, gt=gt, t_yh=t_yh):
                e.wait(t_yh)
                ctx.inc(e.activation(out=gstg[:, gt * 512:(gt + 1) * 512], in_=sqs[:], func=AF.Sigmoid,
                                     scale=c_lngw[:, gt:gt + 1], bias=c_lngb[:, gt:gt + 1]), "s")
            on(ctx, "scalar", s_sig)
            t_sg = ctx.tok("s")
            def v_gate2(e, t_sg=t_sg):
                e.wait(t_sg)
            on(ctx, "vector", v_gate2)
            g_toks.append(t_sg)
        comb_toks = []
        for dt2 in range(NDT):
            def v_cb(e, dt2=dt2, sc=sc, g_toks=g_toks):
                e.wait(g_toks[dt2])
                a = p.xnT[:, dt2 * T + sc * 512:dt2 * T + (sc + 1) * 512]
                o = ubuf[:, dt2 * T + sc * 512:dt2 * T + (sc + 1) * 512]
                d_ = substg[:, dt2 * 512:(dt2 + 1) * 512]
                e.tensor_tensor(out=d_, in0=o, in1=a, op=ALU.subtract)
                e.tensor_tensor(out=d_, in0=d_, in1=gstg[:, dt2 * 512:(dt2 + 1) * 512], op=ALU.mult)
                inst = e.tensor_tensor(out=d_, in0=d_, in1=a, op=ALU.add)
                ctx.inc(inst, "v")
            on(ctx, "vector", v_cb)
            comb_toks.append(ctx.tok("v"))
        for q in range(4):
            ti = sc * 4 + q
            def g_lx(e, ti=ti):
                if len(out_toks) >= 2:
                    e.wait(out_toks[-2])
                e.wait(t_xnT)
                xc_toks.append(ctx.inc(e.dma_start(out=xfs[:, (ti % 2) * D:(ti % 2 + 1) * D], in_=xv[ti]), "qs", 16))
            on(ctx, "sync", g_lx)
            base_v = ctx.tok("v")
            def pe_tr(e, q=q, comb_toks=comb_toks, base_v=base_v):
                e.wait(*comb_toks)
                e.wait(base_v)
                for j in range(NDT):
                    inst = e.transpose(p.pst[:, j * 128:(j + 1) * 128],
                                       substg[:, j * 512 + q * 128:j * 512 + (q + 1) * 128], p.ident[:])
                ctx.inc(inst, "t")
            on(ctx, "tensor", pe_tr)
            t_tr = ctx.tok("t")
            def v_acc(e, ti=ti, t_tr=t_tr):
                e.wait(t_tr, xc_toks[-1])
                sl = xfs[:, (ti % 2) * D:(ti % 2 + 1) * D]
                ctx.inc(e.scalar_tensor_tensor(out=sl, in0=p.pst[:], scalar=1.0, in1=sl,
                                               op0=ALU.mult, op1=ALU.add), "v")
            on(ctx, "vector", v_acc)
            t_va = ctx.tok("v")
            def g_ox(e, ti=ti, t_va=t_va):
                e.wait(t_va)
                out_toks.append(ctx.inc(e.dma_start(out=xov[ti], in_=xfs[:, (ti % 2) * D:(ti % 2 + 1) * D]), "qg", 16))
            on(ctx, "gpsimd", g_ox)
            last_tok = ctx.tok("qg")
    def g_fin(e):
        e.wait(ctx.tok("qg"))
    on(ctx, "gpsimd", g_fin)
    return ctx.tok("qg")


def emit_l1_attn_a(p, io, x_in_ap, x_out_ap, tag):
    ctx = p.ctx
    u_ = tag
    sb = lambda n, s, d: ctx.sbuf(f"{u_}_{n}", s, d)
    psm = lambda n, s, d: ctx.psum(f"{u_}_{n}", s, d)
    nc = ctx.nc

    xv = x_in_ap.rearrange("(i p) d -> i p d", p=128)
    xov = x_out_ap.rearrange("(i p) d -> i p d", p=128)
    xfs = sb("xfs", [128, 2 * D], F32)

    ld_toks, st_done = [], []
    for i in range(NTOK):
        def g_l(e, i=i):
            if i >= 2:
                e.wait(st_done[i - 2])
            ld_toks.append(ctx.inc(e.dma_start(out=xfs[:, (i % 2) * D:(i % 2 + 1) * D], in_=xv[i]), "qs", 16))
        on(ctx, "sync", g_l)
        def v_st(e, i=i):
            e.wait(ld_toks[i])
            st = p.ln_stats[:, i * 12:(i + 1) * 12]
            e.bn_stats(out=st[:, 0:6], in_=xfs[:, (i % 2) * D:(i % 2) * D + 512])
            inst = e.bn_stats(out=st[:, 6:12], in_=xfs[:, (i % 2) * D + 512:(i % 2 + 1) * D])
            st_done.append(ctx.inc(inst, "v"))
        on(ctx, "vector", v_st)
    def v_ag(e):
        e.wait(st_done[NTOK - 1])
        for i in range(NTOK):
            inst = e.bn_aggr(out=p.ln_mv[:, 2 * i:2 * i + 2], in_=p.ln_stats[:, i * 12:(i + 1) * 12])
        ctx.inc(inst, "v")
    on(ctx, "vector", v_ag)
    t_stats = ctx.tok("v")
    mvh = p.ln_mv[:].rearrange("p (i two) -> p i two", two=2)
    mu0, var0 = mvh[:, :, 0], mvh[:, :, 1]
    def s_rstd(e):
        e.wait(t_stats)
        e.activation(out=p.ln_rstd[:], in_=var0, func=AF.Ln, bias=p.fc(EPS1 * EPS1), scale=1.0 + EPS1)
        ctx.inc(e.activation(out=p.ln_rstd[:], in_=p.ln_rstd[:], func=AF.Exp, scale=-0.5), "s")
    on(ctx, "scalar", s_rstd)
    t_rstd0 = ctx.tok("s")
    def v_nmr0(e):
        e.wait(t_rstd0)
        ctx.inc(e.scalar_tensor_tensor(out=p.ln_nmr[:], in0=mu0, scalar=-1.0,
                                       in1=p.ln_rstd[:], op0=ALU.mult, op1=ALU.mult), "v")
    on(ctx, "vector", v_nmr0)
    t_nmr = ctx.tok("v")

    xa_toks = []
    V0 = ctx.cnt["v"]
    def producers(i):
        def g_l2(e, i=i):
            e.wait(t_nmr)
            if i >= 2:
                e.wait(("v", V0 + (i - 1)))
            xa_toks.append(ctx.inc(e.dma_start(out=xfs[:, (i % 2) * D:(i % 2 + 1) * D], in_=xv[i]), "qs", 16))
        on(ctx, "sync", g_l2)
        def fn(e, i=i):
            e.wait(xa_toks[i], t_nmr)
            return e.tensor_scalar(out=p.xstg[:, (i % 2) * D:(i % 2 + 1) * D],
                                   in0=xfs[:, (i % 2) * D:(i % 2 + 1) * D],
                                   scalar1=p.ln_rstd[:, i:i + 1], scalar2=p.ln_nmr[:, i:i + 1],
                                   op0=ALU.mult, op1=ALU.add)
        return ("vector", "v", fn)
    t_xnT = emit_xnT(p, producers)

    qT = sb("qT", [128, NDT * T], BF16)
    kT = sb("kT", [128, NDT * T], BF16)
    gT = sb("gT", [128, NDT * T], BF16)
    vtok = sb("vtok", [128, NTOK * D], BF16)
    Sm = sb("Sm", [128, NDT * D], BF16)
    wchunk = sb("wchunk", [128, NDT * 128], BF16)
    stage = sb("stage", [128, 2 * 128], F32)
    wqkv_p = io["w_qkv_p"].rearrange("p (j f) -> p j f", j=NDT)
    wgate_p = io["w_gate_p"].rearrange("p (j f) -> p j f", j=NDT)
    stage_v = stage[:].rearrange("p (j c) -> p j c", j=2)

    def chunk_load(wp_view, col0, dep_free, dep_cast_extra=()):
        toks = []
        for hh in range(4):
            def g_w(e, hh=hh):
                e.wait(*dep_free)
                if hh >= 1:
                    e.wait(toks[hh - 1])
                ctx.inc(e.dma_start(out=stage_v, in_=wp_view[:, hh * 2:(hh + 1) * 2, col0:col0 + 128]), "qg", 16)
            on(ctx, "gpsimd", g_w)
            t_l = ctx.tok("qg")
            def v_c(e, hh=hh, t_l=t_l):
                e.wait(t_l, t_small, *dep_cast_extra)
                for j2 in range(2):
                    j = hh * 2 + j2
                    inst = e.tensor_scalar(out=wchunk[:, j * 128:(j + 1) * 128],
                                           in0=stage_v[:, j2, :], scalar1=cols[:, j:j + 1],
                                           scalar2=None, op0=ALU.mult)
                toks.append(ctx.inc(inst, "v"))
            on(ctx, "vector", v_c)
        return ctx.tok("v")
    cols = sb("cols", [128, 2 * NDT], F32)
    ratio = sb("ratio", [128, NDT], BF16)
    rscr = sb("rscr", [128, NDT], F32)
    borig = sb("borig", [1, 128], F32)
    bchunk = sb("bchunk", [1, 128], BF16)
    onesr = sb("onesr", [1, 512], BF16)
    scr1 = sb("scr1", [128, T], BF16)
    scr2 = scr1
    escr = sb("escr", [128, 512], BF16)
    escr2 = sb("escr2", [128, 512], BF16)
    den_acc = sb("den_acc", [1, T], BF16)
    dcol = sb("dcol", [128, NTOK], BF16)
    kqm = sb("kqm", [128, 128], BF16)
    bpbc = sb("bpbc", [128, D], BF16)
    dram_den = nc.dram_tensor(f"{u_}_denb", [1, T], BF16)

    mm = MMPool(ctx, u_)
    ps_num = [psm("pnumA", [128, 512], F32), psm("pnumB", [128, 512], F32)]
    ps_row = psm("psrow", [1, 512], F32)

    def g_small(e):
        ctx.inc(e.dma_start(out=cols[:, 0:NDT], in_=io["lng_c"]), "qg", 16)
        ctx.inc(e.dma_start(out=cols[:, NDT:2 * NDT], in_=io["lnb_c"]), "qg", 16)

    on(ctx, "gpsimd", g_small)
    t_small = ctx.tok("qg")
    bp_cast = []
    for hh in range(8):
        def g_bp(e, hh=hh):
            if bp_cast:
                e.wait(bp_cast[-1])
            ctx.inc(e.dma_start(out=stage[:, 0:128], in_=io["b_proj_bc"][:, hh * 128:(hh + 1) * 128]), "qg", 16)
        on(ctx, "gpsimd", g_bp)
        t_lbp = ctx.tok("qg")
        def s_bp(e, hh=hh, t_lbp=t_lbp):
            e.wait(t_lbp)
            bp_cast.append(ctx.inc(e.copy(out=bpbc[:, hh * 128:(hh + 1) * 128], in_=stage[:, 0:128]), "s"))
        on(ctx, "scalar", s_bp)
    t_bpbc = ctx.tok("s")

    def v_ratio(e):
        e.wait(t_small)
        tk = ctx.inc(e.reciprocal(out=rscr[:], in_=cols[:, 0:NDT]), "v")
        e.wait(tk)
        e.tensor_tensor(out=ratio[:], in0=cols[:, NDT:2 * NDT], in1=rscr[:], op=ALU.mult)
        ctx.inc(e.memset(onesr[:], 1.0), "v")
    on(ctx, "vector", v_ratio)
    t_ratio = ctx.tok("v")

    def stream_chunks(wap_p, n_fs, fs_off_in_w, dst_kind, brow_off, extra_dep, evac_maker,
                      bias_src=None, brow_off0=0):
        last = None
        for fs in range(n_fs):
            col0 = fs_off_in_w + fs * 128
            t_wc = chunk_load(wap_p, col0, [last] if last else [])
            def pe_bias(e, t_wc=t_wc):
                e.wait(t_wc, t_ratio)
                for j in range(NDT):
                    inst = e.matmul(ps_row[0:1, 0:128], ratio[:, j:j + 1],
                                    wchunk[:, j * 128:(j + 1) * 128], start=(j == 0), stop=(j == NDT - 1))
                ctx.inc(inst, "t")
            on(ctx, "tensor", pe_bias)
            t_pb = ctx.tok("t")
            def g_bo(e, fs=fs, t_pb=t_pb):
                e.wait(t_pb)
                ctx.inc(e.dma_start(out=borig[:], in_=bias_src[0:1, (brow_off - brow_off0) + fs * 128:(brow_off - brow_off0) + (fs + 1) * 128]), "qg", 16)
            on(ctx, "gpsimd", g_bo)
            t_bo = ctx.tok("qg")
            def v_bias(e, fs=fs, t_pb=t_pb, t_bo=t_bo):
                e.wait(t_pb, t_bo)
                ctx.inc(e.tensor_tensor(out=bchunk[0:1, 0:128], in0=ps_row[0:1, 0:128],
                                        in1=borig[:], op=ALU.add), "v")
            on(ctx, "vector", v_bias)
            t_bc = ctx.tok("v")
            for tc in range(4):
                if dst_kind == "feat":
                    mms = [(wchunk[:, j * 128:(j + 1) * 128],
                            p.xnT[:, j * T + tc * 512:j * T + (tc + 1) * 512]) for j in range(NDT)]
                    mms.append((bchunk[0:1, 0:128], onesr[0:1, 0:512]))
                else:
                    ti = None
                ev = evac_maker(fs, tc)
                last = mm.step([t_xnT, t_wc, t_bc, t_ratio], mms, ev[0], eng=ev[1], sem=ev[2])
        return last

    def gate_ev(fs, tc):
        def f(e, ps, fs=fs, tc=tc):
            return e.activation(out=gT[:, fs * T + tc * 512:fs * T + (tc + 1) * 512],
                                in_=ps[:, 0:512], func=AF.Sigmoid)
        return (f, "scalar", "s")
    t_gate = stream_chunks(wgate_p, NDT, 0, "feat", 3 * D, [], gate_ev,
                           bias_src=io["b_gate_row"], brow_off0=3 * D)

    def q_ev(fs, tc):
        def f(e, ps, fs=fs, tc=tc):
            e.activation(out=escr[:], in_=ps[:, 0:512], func=AF.Relu)
            e.activation(out=escr2[:], in_=ps[:, 0:512], func=AF.Relu, scale=-1.0)
            return e.activation(out=escr2[:], in_=escr2[:], func=AF.Exp, scale=-1.0)
        return (f, "scalar", "s")
    qsave = []
    def q_ev2(fs, tc):
        def f(e, ps, fs=fs, tc=tc):
            e.activation(out=escr[:], in_=ps[:, 0:512], func=AF.Relu)
            e.activation(out=escr2[:], in_=ps[:, 0:512], func=AF.Relu, scale=-1.0)
            inst = e.activation(out=escr2[:], in_=escr2[:], func=AF.Exp, scale=-1.0)
            return inst
        return (f, "scalar", "s")
    def q_stream():
        last = None
        for fs in range(NDT):
            col0 = 0 + fs * 128
            t_wc = chunk_load(wqkv_p, col0, [last] if last else [])
            def pe_bias(e, t_wc=t_wc):
                e.wait(t_wc, t_ratio)
                for j in range(NDT):
                    inst = e.matmul(ps_row[0:1, 0:128], ratio[:, j:j + 1],
                                    wchunk[:, j * 128:(j + 1) * 128], start=(j == 0), stop=(j == NDT - 1))
                ctx.inc(inst, "t")
            on(ctx, "tensor", pe_bias)
            t_pb = ctx.tok("t")
            def g_bo(e, fs=fs, t_pb=t_pb):
                e.wait(t_pb)
                ctx.inc(e.dma_start(out=borig[:], in_=io["b_qkv_row"][0:1, fs * 128:(fs + 1) * 128]), "qg", 16)
            on(ctx, "gpsimd", g_bo)
            t_bo = ctx.tok("qg")
            def v_bias(e, fs=fs, t_pb=t_pb, t_bo=t_bo):
                e.wait(t_pb, t_bo)
                ctx.inc(e.tensor_tensor(out=bchunk[0:1, 0:128], in0=ps_row[0:1, 0:128],
                                        in1=borig[:], op=ALU.add), "v")
            on(ctx, "vector", v_bias)
            t_bc = ctx.tok("v")
            for tc in range(4):
                mms = [(wchunk[:, j * 128:(j + 1) * 128],
                        p.xnT[:, j * T + tc * 512:j * T + (tc + 1) * 512]) for j in range(NDT)]
                mms.append((bchunk[0:1, 0:128], onesr[0:1, 0:512]))
                tok_s = mm.step([t_xnT, t_bc], mms, q_ev2(fs, tc)[0], eng="scalar", sem="s")
                def v_add(e, fs=fs, tc=tc, tok_s=tok_s):
                    e.wait(tok_s)
                    ctx.inc(e.tensor_tensor(out=qT[:, fs * T + tc * 512:fs * T + (tc + 1) * 512],
                                            in0=escr[:], in1=escr2[:], op=ALU.add), "v")
                on(ctx, "vector", v_add)
                last = ctx.tok("v")
                def s_gate2(e, last=last):
                    e.wait(last)
                on(ctx, "scalar", s_gate2)
        return last
    t_q = q_stream()

    def k_stream():
        last = None
        for fs in range(NDT):
            col0 = D + fs * 128
            t_wc = chunk_load(wqkv_p, col0, [last] if last else [])
            def pe_bias(e, t_wc=t_wc):
                e.wait(t_wc, t_ratio)
                for j in range(NDT):
                    inst = e.matmul(ps_row[0:1, 0:128], ratio[:, j:j + 1],
                                    wchunk[:, j * 128:(j + 1) * 128], start=(j == 0), stop=(j == NDT - 1))
                ctx.inc(inst, "t")
            on(ctx, "tensor", pe_bias)
            t_pb = ctx.tok("t")
            def g_bo(e, fs=fs, t_pb=t_pb):
                e.wait(t_pb)
                ctx.inc(e.dma_start(out=borig[:], in_=io["b_qkv_row"][0:1, D + fs * 128:D + (fs + 1) * 128]), "qg", 16)
            on(ctx, "gpsimd", g_bo)
            t_bo = ctx.tok("qg")
            def v_bias(e, fs=fs, t_pb=t_pb, t_bo=t_bo):
                e.wait(t_pb, t_bo)
                ctx.inc(e.tensor_tensor(out=bchunk[0:1, 0:128], in0=ps_row[0:1, 0:128],
                                        in1=borig[:], op=ALU.add), "v")
            on(ctx, "vector", v_bias)
            t_bc = ctx.tok("v")
            for tc in range(4):
                mms = [(wchunk[:, j * 128:(j + 1) * 128],
                        p.xnT[:, j * T + tc * 512:j * T + (tc + 1) * 512]) for j in range(NDT)]
                mms.append((bchunk[0:1, 0:128], onesr[0:1, 0:512]))
                def kev(e, ps, fs=fs, tc=tc):
                    return e.tensor_tensor(out=escr[:], in0=ps[:, 0:512],
                                           in1=gT[:, fs * T + tc * 512:fs * T + (tc + 1) * 512],
                                           op=ALU.mult)
                tok_v = mm.step([t_xnT, t_bc, t_gate], mms, kev, eng="vector", sem="v")
                def s_elu(e, fs=fs, tc=tc, tok_v=tok_v):
                    e.wait(tok_v)
                    e.activation(out=escr2[:], in_=escr[:], func=AF.Relu, scale=-1.0)
                    e.activation(out=escr2[:], in_=escr2[:], func=AF.Exp, scale=-1.0)
                    ctx.inc(e.activation(out=escr[:], in_=escr[:], func=AF.Relu), "s")
                on(ctx, "scalar", s_elu)
                t_se = ctx.tok("s")
                def v_add(e, fs=fs, tc=tc, t_se=t_se):
                    e.wait(t_se)
                    ctx.inc(e.tensor_tensor(out=kT[:, fs * T + tc * 512:fs * T + (tc + 1) * 512],
                                            in0=escr[:], in1=escr2[:], op=ALU.add), "v")
                on(ctx, "vector", v_add)
                last = ctx.tok("v")
        return last
    t_k = k_stream()

    def v_stream():
        last = None
        for fs in range(NDT):
            col0 = 2 * D + fs * 128
            t_wc = chunk_load(wqkv_p, col0, [last] if last else [])
            def pe_bias(e, t_wc=t_wc):
                e.wait(t_wc, t_ratio)
                for j in range(NDT):
                    inst = e.matmul(ps_row[0:1, 0:128], ratio[:, j:j + 1],
                                    wchunk[:, j * 128:(j + 1) * 128], start=(j == 0), stop=(j == NDT - 1))
                ctx.inc(inst, "t")
            on(ctx, "tensor", pe_bias)
            t_pb = ctx.tok("t")
            def g_bo(e, fs=fs, t_pb=t_pb):
                e.wait(t_pb)
                ctx.inc(e.dma_start(out=borig[:], in_=io["b_qkv_row"][0:1, 2 * D + fs * 128:2 * D + (fs + 1) * 128]), "qg", 16)
            on(ctx, "gpsimd", g_bo)
            t_bo = ctx.tok("qg")
            def v_bias(e, fs=fs, t_pb=t_pb, t_bo=t_bo):
                e.wait(t_pb, t_bo)
                ctx.inc(e.tensor_tensor(out=bchunk[0:1, 0:128], in0=ps_row[0:1, 0:128],
                                        in1=borig[:], op=ALU.add), "v")
            on(ctx, "vector", v_bias)
            t_bc = ctx.tok("v")
            for tc in range(4):
                for q4 in range(4):
                    ti = tc * 4 + q4
                    mms = [(p.xnT[:, j * T + ti * 128:j * T + (i + 1) * 128] if False else p.xnT[:, j * T + ti * 128:j * T + (ti + 1) * 128],
                            wchunk[:, j * 128:(j + 1) * 128]) for j in range(NDT)]
                    mms.append((onesr[0:1, 0:128], bchunk[0:1, 0:128]))
                    def vev(e, ps, fs=fs, ti=ti):
                        return e.copy(out=vtok[:, ti * D + fs * 128:ti * D + (fs + 1) * 128],
                                      in_=ps[:, 0:128])
                    last = mm.step([t_xnT, t_bc], mms, vev, eng="scalar", sem="s", width=128)
        return last
    t_v = v_stream()

    den_toks = []
    def v_den0(e):
        ctx.inc(e.memset(den_acc[:], 1e-6), "v")
    on(ctx, "vector", v_den0)
    for kf in range(NDT):
        def v_scan(e, kf=kf):
            e.wait(t_k, t_q)
            tk = ctx.inc(e.tensor_tensor_scan(out=scr1[:], data0=p.ones_b[:, 0:1].to_broadcast([128, T]),
                                              data1=kT[:, kf * T:(kf + 1) * T], initial=0.0,
                                              op0=ALU.mult, op1=ALU.add), "v")
            e.wait(tk)
            ctx.inc(e.tensor_tensor(out=scr2[:], in0=qT[:, kf * T:(kf + 1) * T], in1=scr1[:],
                                    op=ALU.mult), "v")
        on(ctx, "vector", v_scan)
        t_qk = ctx.tok("v")
        for tc in range(4):
            def pe_d(e, kf=kf, tc=tc, t_qk=t_qk):
                e.wait(t_qk)
                if den_toks:
                    e.wait(den_toks[-1])
                ctx.inc(e.matmul(ps_row[0:1, 0:512], p.ones_b[:, 0:1],
                                 scr2[:, tc * 512:(tc + 1) * 512], start=True, stop=True), "t")
            on(ctx, "tensor", pe_d)
            t_pd = ctx.tok("t")
            def v_d(e, tc=tc, t_pd=t_pd):
                e.wait(t_pd)
                ctx.inc(e.tensor_tensor(out=den_acc[0:1, tc * 512:(tc + 1) * 512],
                                        in0=den_acc[0:1, tc * 512:(tc + 1) * 512],
                                        in1=ps_row[0:1, 0:512], op=ALU.add), "v")
            on(ctx, "vector", v_d)
            den_toks.append(ctx.tok("v"))
    t_den = ctx.tok("v")
    def s_den(e):
        e.wait(t_den)
        e.activation(out=den_acc[:], in_=den_acc[:], func=AF.Ln, bias=0.0)
        ctx.inc(e.activation(out=den_acc[:], in_=den_acc[:], func=AF.Exp, scale=-1.0), "s")
    on(ctx, "scalar", s_den)
    t_dr = ctx.tok("s")
    def pe_dc(e):
        e.wait(t_dr)
        for i in range(NTOK):
            inst = e.matmul(ps_num[0][:, i:i + 1], den_acc[0:1, i * 128:(i + 1) * 128],
                            p.ones_b[0:1, 0:1], start=True, stop=True)
        ctx.inc(inst, "t")
    on(ctx, "tensor", pe_dc)
    t_pdc = ctx.tok("t")
    def s_dc(e):
        e.wait(t_pdc)
        ctx.inc(e.copy(out=dcol[:], in_=ps_num[0][:, 0:NTOK]), "s")
    on(ctx, "scalar", s_dc)
    t_dcol = ctx.tok("s")

    ktok = gT
    ktok_done = []
    for ti in range(NTOK):
        base_s = ctx.tok("s")
        def pe_t(e, ti=ti, base_s=base_s):
            e.wait(t_k, t_gate)
            if ti > 0:
                e.wait(base_s)
            for j in range(NDT):
                inst = e.transpose(p.pst[:, j * 128:(j + 1) * 128],
                                   kT[:, j * T + ti * 128:j * T + (ti + 1) * 128], p.ident[:])
            ctx.inc(inst, "t")
        on(ctx, "tensor", pe_t)
        t_tr = ctx.tok("t")
        def s_ev(e, ti=ti, t_tr=t_tr):
            e.wait(t_tr)
            pv = p.pst[:].rearrange("p (j c) -> p j c", j=NDT)
            kv = ktok[:].rearrange("p (i c) -> p i c", i=NTOK)[:, ti, :].rearrange("p (j c) -> p j c", j=NDT)
            ctx.inc(e.copy(out=kv, in_=pv), "s")
        on(ctx, "scalar", s_ev)
        ktok_done.append(ctx.tok("s"))
    t_ktok = ctx.tok("s")

    numT = p.xnT
    def v_S0(e):
        e.wait(t_v)
        ctx.inc(e.memset(Sm[:], 0.0), "v")
    on(ctx, "vector", v_S0)
    t_S = ctx.tok("v")

    num_done = []
    for c in range(NTOK):
        def pe_kq(e, c=c):
            e.wait(t_k, t_q, t_ktok)
            if num_done:
                e.wait(num_done[-1])
            for j in range(NDT):
                inst = e.matmul(mm.ps[0][:, 0:128], kT[:, j * T + c * 128:j * T + (c + 1) * 128],
                                qT[:, j * T + c * 128:j * T + (c + 1) * 128],
                                start=(j == 0), stop=(j == NDT - 1))
            ctx.inc(inst, "t")
        on(ctx, "tensor", pe_kq)
        t_kq = ctx.tok("t")
        def v_mask(e, t_kq=t_kq):
            e.wait(t_kq, p.const_tok)
            ctx.inc(e.tensor_tensor(out=kqm[:], in0=mm.ps[0][:, 0:128], in1=p.triu[:], op=ALU.mult), "v")
        on(ctx, "vector", v_mask)
        t_km = ctx.tok("v")

        for half in range(2):
            def pe_n(e, c=c, half=half, t_km=t_km):
                e.wait(t_km, t_S)
                for vq in range(4):
                    vf = half * 4 + vq
                    for j in range(NDT):
                        e.matmul(ps_num[half][:, vq * 128:(vq + 1) * 128],
                                 Sm[:, j * D + vf * 128:j * D + (vf + 1) * 128],
                                 qT[:, j * T + c * 128:j * T + (c + 1) * 128],
                                 start=(j == 0), stop=False)
                    inst = e.matmul(ps_num[half][:, vq * 128:(vq + 1) * 128],
                                    vtok[:, c * D + vf * 128:c * D + (vf + 1) * 128],
                                    kqm[:], start=False, stop=True)
                ctx.inc(inst, "t")
            on(ctx, "tensor", pe_n)
            t_n = ctx.tok("t")
            def s_evn(e, c=c, half=half, t_n=t_n):
                e.wait(t_n)
                pv = ps_num[half][:, 0:512].rearrange("p (v c) -> p v c", v=4)
                nv = numT[:].rearrange("p (j c) -> p j c", j=NDT)[:, half * 4:(half + 1) * 4,
                                                                 c * 128:(c + 1) * 128]
                ctx.inc(e.copy(out=nv, in_=pv), "s")
            on(ctx, "scalar", s_evn)
            num_done.append(ctx.tok("s"))

        if c < NTOK - 1:
            for kf in range(NDT):
                def pe_su(e, c=c, kf=kf):
                    e.wait(num_done[-1])
                    inst = e.matmul(mm.ps[1][:, 0:512], ktok[:, c * D + kf * 128:c * D + (kf + 1) * 128],
                                    vtok[:, c * D:c * D + 512], start=True, stop=True)
                    ctx.inc(inst, "t")
                    inst = e.matmul(mm.ps[0][:, 0:512], ktok[:, c * D + kf * 128:c * D + (kf + 1) * 128],
                                    vtok[:, c * D + 512:(c + 1) * D], start=True, stop=True)
                    ctx.inc(inst, "t")
                on(ctx, "tensor", pe_su)
                t_su = ctx.tok("t")
                def v_su(e, kf=kf, t_su=t_su):
                    e.wait(t_su)
                    e.tensor_tensor(out=Sm[:, kf * D:kf * D + 512], in0=Sm[:, kf * D:kf * D + 512],
                                    in1=mm.ps[1][:, 0:512], op=ALU.add)
                    ctx.inc(e.tensor_tensor(out=Sm[:, kf * D + 512:(kf + 1) * D],
                                            in0=Sm[:, kf * D + 512:(kf + 1) * D],
                                            in1=mm.ps[0][:, 0:512], op=ALU.add), "v")
                on(ctx, "vector", v_su)
                t_S = ctx.tok("v")
                num_done.append(t_S)

    t_num = ctx.tok("s")

    wpb = Sm
    wvv = io["w_proj"].rearrange("(j p) d2 -> j p d2", p=128)
    wp_casts = []
    for j in range(NDT):
        for hh in range(8):
            def g_w(e, j=j, hh=hh):
                e.wait(t_num)
                if wp_casts:
                    e.wait(wp_casts[-1])
                ctx.inc(e.dma_start(out=stage[:, 0:128], in_=wvv[j][:, hh * 128:(hh + 1) * 128]), "qg", 16)
            on(ctx, "gpsimd", g_w)
            t_l = ctx.tok("qg")
            def s_c(e, j=j, hh=hh, t_l=t_l):
                e.wait(t_l)
                wp_casts.append(ctx.inc(e.copy(out=wpb[:, j * D + hh * 128:j * D + (hh + 1) * 128],
                                               in_=stage[:, 0:128]), "s"))
            on(ctx, "scalar", s_c)
    t_wp = ctx.tok("s")

    xc_toks, out_toks = [], []
    for ti in range(NTOK):
        def g_lx(e, ti=ti):
            if len(out_toks) >= 2:
                e.wait(out_toks[-2])
            e.wait(t_num)
            xc_toks.append(ctx.inc(e.dma_start(out=xfs[:, (ti % 2) * D:(ti % 2 + 1) * D], in_=xv[ti]), "qs", 16))
        on(ctx, "sync", g_lx)
        def v_bp(e, ti=ti):
            e.wait(xc_toks[ti], t_bpbc)
            ctx.inc(e.tensor_tensor(out=xfs[:, (ti % 2) * D:(ti % 2 + 1) * D],
                                    in0=xfs[:, (ti % 2) * D:(ti % 2 + 1) * D], in1=bpbc[:], op=ALU.add), "v")
        on(ctx, "vector", v_bp)
        t_vbp = ctx.tok("v")
        ev_toks = []
        for dc in range(2):
            mms = [(numT[:, j * T + ti * 128:j * T + (ti + 1) * 128],
                    wpb[:, j * D + dc * 512:j * D + (dc + 1) * 512]) for j in range(NDT)]
            def ev(e, ps, ti=ti, dc=dc):
                sl = xfs[:, (ti % 2) * D + dc * 512:(ti % 2) * D + dc * 512 + 512]
                return e.scalar_tensor_tensor(out=sl, in0=ps[:, 0:512], scalar=dcol[:, ti:ti + 1],
                                              in1=sl, op0=ALU.mult, op1=ALU.add)
            ev_toks.append(mm.step([t_num, t_wp, t_vbp, t_dcol], mms, ev, eng="vector", sem="v"))
        def g_ox(e, ti=ti, ev_toks=ev_toks):
            e.wait(*ev_toks)
            out_toks.append(ctx.inc(e.dma_start(out=xov[ti], in_=xfs[:, (ti % 2) * D:(ti % 2 + 1) * D]), "qg", 16))
        on(ctx, "gpsimd", g_ox)
    def g_fin(e):
        e.wait(ctx.tok("qg"))
    on(ctx, "gpsimd", g_fin)
    return ctx.tok("qg")


def emit_l1_ffn(p, io, tag, ncores=8):
    ctx = p.ctx
    u_ = tag
    sb = lambda n, s, d: ctx.sbuf(f"{u_}_{n}", s, d)
    psm = lambda n, s, d: ctx.psum(f"{u_}_{n}", s, d)
    nc = ctx.nc
    RD = 48

    t_nmr, t_xnT = emit_ln_t(p, p.ln_dep)

    qT = sb("qT", [128, T], BF16)
    rqT = sb("rqT", [48, T], BF16)
    rkT = sb("rkT", [48, 4096], BF16)
    sc_sb = sb("sc", [128, 4096], F32)
    w_sb = sb("w", [128, 4096], BF16)
    wTg = sb("wTg", [128, 3 * 4096], BF16)
    m8 = sb("m8", [128, 8], F32)
    vstg = sb("vstg", [128, D], F32)
    vb = sb("vb", [128, D], BF16)
    h1T = sb("h1T", [128, 2 * T], BF16)
    cols = sb("cols", [128, 16 * 6], F32)
    ccc = sb("ccc", [128, 4], F32)
    rowbuf = sb("rowbuf", [1, T], BF16)
    tiny = sb("tiny", [1, 8], F32)
    stg128 = sb("stg128", [128, 128], F32)
    aggscr = sb("aggscr", [128, 512], F32)
    mm = MMPool(ctx, u_)
    ps_agg = [psm(f"agg{k}", [128, 512], F32) for k in range(4)]
    wsums = sb("wsums", [128, NTOK], F32)
    dram_cc_in = nc.dram_tensor(f"{u_}_ccin", [1, 8], F32)
    dram_cc_out = nc.dram_tensor(f"{u_}_ccout", [1, 8], F32)

    c_lv = cols[:, 0:16]; c_nv = cols[:, 16:32]; c_sg = cols[:, 32:48]
    c_unc = cols[:, 48:64]; c_gate = cols[:, 64:80]; c_rp1 = cols[:, 80:96]

    def g_small(e):
        ctx.inc(e.dma_start(out=ccc[0:1, 0:1], in_=io["unc_b2"]), "qg", 16)
        ctx.inc(e.dma_start(out=ccc[0:1, 1:2], in_=io["gate_w1"]), "qg", 16)
        ctx.inc(e.dma_start(out=ccc[0:1, 2:3], in_=io["gate_bias"]), "qg", 16)
    on(ctx, "gpsimd", g_small)
    t_small = ctx.tok("qg")

    qpv = io["q_projT"].rearrange("(j p) f -> j p f", p=128)

    wq8 = sb("wq8", [128, NDT * 128], BF16)
    cast_t = []
    for j in range(NDT):
        def g_w(e, j=j):
            if j >= 1:
                e.wait(cast_t[j - 1])
            ctx.inc(e.dma_start(out=stg128[:], in_=qpv[j]), "qg", 16)
        on(ctx, "gpsimd", g_w)
        t_l = ctx.tok("qg")
        def s_c(e, j=j, t_l=t_l):
            e.wait(t_l)
            cast_t.append(ctx.inc(e.copy(out=wq8[:, j * 128:(j + 1) * 128], in_=stg128[:]), "s"))
        on(ctx, "scalar", s_c)
    t_wq = ctx.tok("s")
    for tc in range(4):
        mms = [(wq8[:, j * 128:(j + 1) * 128],
                p.xnT[:, j * T + tc * 512:j * T + (tc + 1) * 512]) for j in range(NDT)]
        def ev(e, ps, tc=tc):
            return e.copy(out=qT[:, tc * 512:(tc + 1) * 512], in_=ps[:, 0:512])
        mm.step([t_xnT, t_wq], mms, ev)
    t_q = ctx.tok("s")

    wr = sb("wr", [128, 48], BF16)
    def g_wr(e):
        ctx.inc(e.dma_start(out=stg128[:, 0:48], in_=io["router_wT"]), "qg", 16)
    on(ctx, "gpsimd", g_wr)
    t_lwr = ctx.tok("qg")
    def s_wr(e):
        e.wait(t_lwr)
        ctx.inc(e.copy(out=wr[:], in_=stg128[:, 0:48]), "s")
    on(ctx, "scalar", s_wr)
    t_wr = ctx.tok("s")

    import math
    for tc in range(4):
        mms = [(wr[:], qT[:, tc * 512:(tc + 1) * 512])]
        def ev(e, ps, tc=tc):
            return e.activation(out=rqT[0:48, tc * 512:(tc + 1) * 512], in_=ps[0:48, 0:512],
                                func=AF.Copy, scale=1.0 / math.sqrt(RD))
        mm.step([t_q, t_wr], mms, ev, out_p=48)
    t_rq = ctx.tok("s")

    akv = io["aux_keysT"]
    akstg = sb("akstg", [128, 512], F32)
    akb = sb("akb", [128, 512], BF16)
    for ck in range(8):
        def g_ak(e, ck=ck):
            if ck >= 1:
                e.wait(("s", ctx.cnt["s"]))
            ctx.inc(e.dma_start(out=akstg[:], in_=akv[:, ck * 512:(ck + 1) * 512]), "qg", 16)
        on(ctx, "gpsimd", g_ak)
        t_l = ctx.tok("qg")
        def s_ak(e, ck=ck, t_l=t_l):
            e.wait(t_l)
            ctx.inc(e.copy(out=akb[:], in_=akstg[:]), "s")
        on(ctx, "scalar", s_ak)
        t_cak = ctx.tok("s")
        def pe_rk(e, ck=ck, t_cak=t_cak):
            e.wait(t_cak, t_wr)
            if ck >= 2:
                e.wait(("s", ctx.cnt["s"]))
            ctx.inc(e.matmul(mm.ps[ck % 2][0:48, 0:512], wr[:], akb[:], start=True, stop=True), "t")
        on(ctx, "tensor", pe_rk)
        t_prk = ctx.tok("t")
        def s_erk(e, ck=ck, t_prk=t_prk):
            e.wait(t_prk)
            ctx.inc(e.copy(out=rkT[0:48, ck * 512:(ck + 1) * 512], in_=mm.ps[ck % 2][0:48, 0:512]), "s")
        on(ctx, "scalar", s_erk)
        mm.evac_tok[ck % 2] = ctx.tok("s")
    t_rk = ctx.tok("s")

    uw1 = sb("uw1", [128, NDT * 256], BF16)
    uwv = io["unc_w1T"].rearrange("(j p) f -> j p f", p=128)
    c2 = []
    for j in range(NDT):
        for hh in range(2):
            def g_w(e, j=j, hh=hh):
                if c2:
                    e.wait(c2[-1])
                ctx.inc(e.dma_start(out=stg128[:], in_=uwv[j][:, hh * 128:(hh + 1) * 128]), "qg", 16)
            on(ctx, "gpsimd", g_w)
            t_l = ctx.tok("qg")
            def s_c(e, j=j, hh=hh, t_l=t_l):
                e.wait(t_l)
                c2.append(ctx.inc(e.copy(out=uw1[:, j * 256 + hh * 128:j * 256 + (hh + 1) * 128],
                                         in_=stg128[:]), "s"))
            on(ctx, "scalar", s_c)
    t_uw1 = ctx.tok("s")
    ub1 = sb("ub1", [128, 2], F32)
    def g_b1(e):
        ctx.inc(e.dma_start(out=ub1[:], in_=io["unc_b1_c"]), "qg", 16)
    on(ctx, "gpsimd", g_b1)
    t_ub1 = ctx.tok("qg")
    for us in range(2):
        for tc in range(4):
            mms = [(uw1[:, j * 256 + us * 128:j * 256 + (us + 1) * 128],
                    p.xnT[:, j * T + tc * 512:j * T + (tc + 1) * 512]) for j in range(NDT)]
            def ev(e, ps, us=us, tc=tc):
                return e.activation(out=h1T[:, us * T + tc * 512:us * T + (tc + 1) * 512],
                                    in_=ps[:, 0:512], func=AF.Gelu, bias=ub1[:, us:us + 1])
            mm.step([t_xnT, t_uw1, t_ub1], mms, ev)
    t_h1 = ctx.tok("s")
    uw2 = sb("uw2", [128, 2], BF16)
    def g_w2(e):
        ctx.inc(e.dma_start(out=stg128[:, 0:2], in_=io["unc_w2_c"]), "qg", 16)
    on(ctx, "gpsimd", g_w2)
    t_lw2 = ctx.tok("qg")
    def s_w2(e):
        e.wait(t_lw2)
        ctx.inc(e.copy(out=uw2[:], in_=stg128[:, 0:2]), "s")
    on(ctx, "scalar", s_w2)
    t_w2 = ctx.tok("s")
    lrn_toks = []
    for tc in range(4):
        def pe_lr(e, tc=tc):
            e.wait(t_h1, t_w2)
            if mm.evac_tok[0] is not None:
                e.wait(mm.evac_tok[0])
            if lrn_toks:
                e.wait(lrn_toks[-1])
            for us in range(2):
                inst = e.matmul(mm.ps[0][0:1, 0:512], uw2[:, us:us + 1],
                                h1T[:, us * T + tc * 512:us * T + (tc + 1) * 512],
                                start=(us == 0), stop=(us == 1))
            ctx.inc(inst, "t")
        on(ctx, "tensor", pe_lr)
        t_plr = ctx.tok("t")
        def s_lr(e, tc=tc, t_plr=t_plr):
            e.wait(t_plr, mm.evac_tok[0] if mm.evac_tok[0] else None)
            lrn_toks.append(ctx.inc(e.copy(out=rowbuf[0:1, tc * 512:(tc + 1) * 512],
                                           in_=mm.ps[0][0:1, 0:512]), "s"))
        on(ctx, "scalar", s_lr)
    mm.evac_tok[0] = ctx.tok("s")
    t_learn = ctx.tok("s")
    lcol = sb("lcol", [128, 16], F32)
    def pe_lc(e):
        e.wait(t_learn)
        e.wait(mm.evac_tok[0])
        for i in range(NTOK):
            inst = e.matmul(mm.ps[0][:, i:i + 1], rowbuf[0:1, i * 128:(i + 1) * 128],
                            p.ones_b[0:1, 0:1], start=True, stop=True)
        ctx.inc(inst, "t")
    on(ctx, "tensor", pe_lc)
    t_plc = ctx.tok("t")
    def s_lc(e):
        e.wait(t_plc)
        ctx.inc(e.copy(out=lcol[:], in_=mm.ps[0][:, 0:NTOK]), "s")
    on(ctx, "scalar", s_lc)
    mm.evac_tok[0] = ctx.tok("s")
    t_lcol = ctx.tok("s")

    mvh = p.ln_mv[:].rearrange("p (i two) -> p i two", two=2)
    var0 = mvh[:, :, 1]
    def v_lv(e):
        e.wait(t_nmr)
        e.tensor_scalar(out=c_lv, in0=var0, scalar1=EPS1, scalar2=None, op0=ALU.add)
        tk = ctx.inc(e.reciprocal(out=c_nv, in_=c_lv), "v")
        e.wait(tk)
        ctx.inc(e.tensor_tensor(out=c_lv, in0=var0, in1=c_nv, op=ALU.mult), "v")
    on(ctx, "vector", v_lv)
    t_vr = ctx.tok("v")
    def s_lv(e):
        e.wait(t_vr)
        ctx.inc(e.activation(out=c_lv, in_=c_lv, func=AF.Ln, bias=p.fc(1.0)), "s")
    on(ctx, "scalar", s_lv)
    t_slv = ctx.tok("s")
    def v_sum(e):
        e.wait(t_slv)
        ctx.inc(e.reduce_sum(out=ccc[:, 3:4], in_=c_lv, axis=AX.X), "v")
    on(ctx, "vector", v_sum)
    t_vs = ctx.tok("v")
    def pe_sum(e):
        e.wait(t_vs)
        e.wait(mm.evac_tok[1])
        ctx.inc(e.matmul(mm.ps[1][0:1, 0:1], p.ones_f[:, 0:1], ccc[:, 3:4], start=True, stop=True), "t")
    on(ctx, "tensor", pe_sum)
    t_ps = ctx.tok("t")
    def s_sum(e):
        e.wait(t_ps)
        ctx.inc(e.copy(out=tiny[0:1, 0:1], in_=mm.ps[1][0:1, 0:1]), "s")
    on(ctx, "scalar", s_sum)
    mm.evac_tok[1] = ctx.tok("s")
    t_tin = ctx.tok("s")
    def g_cc(e):
        e.wait(t_tin)
        ctx.inc(e.dma_start(out=dram_cc_in.ap(), in_=tiny[:]), "qg", 16)
        e.wait(ctx.tok("qg"))
        ctx.inc(e.collective_compute(
            "AllReduce", mybir.AluOpType.add,
            replica_groups=[list(range(ncores))],
            ins=[dram_cc_in.ap().opt()],
            outs=[dram_cc_out.ap().opt()],
        ), "cc")
        e.wait(ctx.tok("cc"))
        ctx.inc(e.dma_start(out=tiny[:], in_=dram_cc_out.ap()), "qg", 16)
    on(ctx, "gpsimd", g_cc)
    t_cc = ctx.tok("qg")
    def s_gm(e):
        e.wait(t_cc)
        ctx.inc(e.activation(out=tiny[0:1, 1:2], in_=tiny[0:1, 0:1], func=AF.Copy,
                             scale=1.0 / (ncores * T)), "s")
    on(ctx, "scalar", s_gm)
    t_gm = ctx.tok("s")
    def v_gi(e):
        e.wait(t_gm)
        e.tensor_scalar(out=tiny[0:1, 1:2], in0=tiny[0:1, 1:2], scalar1=1e-6, scalar2=None, op0=ALU.add)
        tk = ctx.inc(e.reciprocal(out=tiny[0:1, 2:3], in_=tiny[0:1, 1:2]), "v")
        e.wait(tk)
    on(ctx, "vector", v_gi)
    t_gi = ctx.tok("v")
    def pe_bc(e):
        e.wait(t_gi, t_small)
        e.wait(mm.evac_tok[0])
        ctx.inc(e.matmul(mm.ps[0][:, 0:4], p.ones_f[0:1, :], ccc[0:1, 0:3].to_broadcast([1, 3]) if False else tiny[0:1, 0:4],
                         start=True, stop=True), "t")
    on(ctx, "tensor", pe_bc)
    t_pbc = ctx.tok("t")
    def s_bc(e):
        e.wait(t_pbc)
        ctx.inc(e.copy(out=ccc[:, 0:4] if False else stg128[:, 0:4], in_=mm.ps[0][:, 0:4]), "s")
    on(ctx, "scalar", s_bc)
    mm.evac_tok[0] = ctx.tok("s")
    t_sbc = ctx.tok("s")
    def pe_bc2(e):
        e.wait(t_small, t_sbc)
        e.wait(mm.evac_tok[1])
        ctx.inc(e.matmul(mm.ps[1][:, 0:3], p.ones_f[0:1, :], ccc[0:1, 0:3], start=True, stop=True), "t")
    on(ctx, "tensor", pe_bc2)
    t_pbc2 = ctx.tok("t")
    def s_bc2(e):
        e.wait(t_pbc2)
        ctx.inc(e.copy(out=stg128[:, 4:7], in_=mm.ps[1][:, 0:3]), "s")
    on(ctx, "scalar", s_bc2)
    mm.evac_tok[1] = ctx.tok("s")
    t_sbc2 = ctx.tok("s")
    def v_cols(e):
        e.wait(t_sbc, t_sbc2, t_lcol, t_slv)
        ctx.inc(e.tensor_scalar(out=c_nv, in0=c_lv, scalar1=stg128[:, 2:3], scalar2=None,
                                op0=ALU.mult), "v")
    on(ctx, "vector", v_cols)
    t_nv = ctx.tok("v")
    def s_sg(e):
        e.wait(t_nv)
        ctx.inc(e.activation(out=c_sg, in_=lcol[:], func=AF.Sigmoid, bias=stg128[:, 4:5]), "s")
    on(ctx, "scalar", s_sg)
    t_sg2 = ctx.tok("s")
    def v_unc(e):
        e.wait(t_sg2)
        e.tensor_scalar(out=c_unc, in0=c_sg, scalar1=2.5, scalar2=None, op0=ALU.mult)
        ctx.inc(e.scalar_tensor_tensor(out=c_unc, in0=c_nv, scalar=0.5, in1=c_unc,
                                       op0=ALU.mult, op1=ALU.add), "v")
    on(ctx, "vector", v_unc)
    t_unc1 = ctx.tok("v")
    def v_clip(e):
        e.wait(t_unc1)
        ctx.inc(e.tensor_scalar(out=c_unc, in0=c_unc, scalar1=5.0, scalar2=0.0,
                                op0=ALU.min, op1=ALU.max), "v")
    on(ctx, "vector", v_clip)
    t_unc = ctx.tok("v")
    def s_gate(e):
        e.wait(t_unc)
        ctx.inc(e.activation(out=c_gate, in_=c_unc, func=AF.Sigmoid,
                             scale=stg128[:, 5:6], bias=stg128[:, 6:7]), "s")
    on(ctx, "scalar", s_gate)
    t_gate = ctx.tok("s")
    def v_rp1(e):
        e.wait(t_gate)
        ctx.inc(e.tensor_scalar(out=c_rp1, in0=p.ln_rstd[:], scalar1=1.0, scalar2=None, op0=ALU.add), "v")
    on(ctx, "vector", v_rp1)
    t_rp1 = ctx.tok("v")

    relv = io["rel_row"]
    rel_sb = sb("rel", [1, 4096], BF16)
    def g_rel(e):
        ctx.inc(e.dma_start(out=sc_sb[0:1, :], in_=relv), "qg", 16)
    on(ctx, "gpsimd", g_rel)
    t_lrel = ctx.tok("qg")
    def s_rel(e):
        e.wait(t_lrel)
        ctx.inc(e.copy(out=rel_sb[:], in_=sc_sb[0:1, :]), "s")
    on(ctx, "scalar", s_rel)
    t_rel = ctx.tok("s")
    onesr5 = sb("onesr5", [1, 512], BF16)
    def v_o5(e):
        ctx.inc(e.memset(onesr5[:], 1.0), "v")
    on(ctx, "vector", v_o5)
    t_o5 = ctx.tok("v")

    groups = [list(range(g * 3, min(16, g * 3 + 3))) for g in range(6)]
    last_tok = None
    wT_v = wTg[:].rearrange("p (g n c) -> p g n c", g=3, n=32)
    for grp_id, grp in enumerate(groups):
        wt_done = []
        for slot, ti in enumerate(grp):
            for nch in range(8):
                mms = [(rqT[0:48, ti * 128:(ti + 1) * 128], rkT[0:48, nch * 512:(nch + 1) * 512])]
                mms.append((onesr5[0:1, 0:128] if False else rel_sb[0:1, 0:1], None))
                mms.pop()
                mms.append((p.ones_b[0:1, 0:128], rel_sb[0:1, nch * 512:(nch + 1) * 512]))
                def ev(e, ps, ti=ti, nch=nch):
                    return e.copy(out=sc_sb[:, nch * 512:(nch + 1) * 512], in_=ps[:, 0:512])
                mm.step([t_rq, t_rk, t_rel, t_o5], mms, ev)
            t_sc = ctx.tok("s")
            def v_m8(e, t_sc=t_sc):
                e.wait(t_sc)
                tk = ctx.inc(e.max(out=m8[:], in_=sc_sb[:]), "v")
                e.wait(tk)
                ctx.inc(e.engine_nop() if False else e.tensor_copy(out=m8[:, 0:1], in_=m8[:, 0:1]), "v")
            on(ctx, "vector", v_m8)
            t_m8 = ctx.tok("v")
            def s_exp(e, t_m8=t_m8):
                e.wait(t_m8)
                ctx.inc(e.activation(out=w_sb[:], in_=sc_sb[:], func=AF.Exp), "s")
            on(ctx, "scalar", s_exp)
            t_ex = ctx.tok("s")
            def v_w(e, ti=ti, t_ex=t_ex):
                e.wait(t_ex)
                ctx.inc(e.scalar_tensor_tensor(out=w_sb[:], in0=sc_sb[:], scalar=m8[:, 7:8],
                                               in1=w_sb[:], op0=ALU.is_ge, op1=ALU.mult,
                                               accum_out=wsums[:, ti:ti + 1]), "v")
            on(ctx, "vector", v_w)
            t_w = ctx.tok("v")
            for rnd in range(4):
                base_s = ctx.tok("s")
                def pe_t(e, rnd=rnd, t_w=t_w, base_s=base_s):
                    e.wait(t_w)
                    e.wait(base_s)
                    for b in range(8):
                        inst = e.transpose(p.pst[:, b * 128:(b + 1) * 128],
                                           w_sb[:, rnd * 1024 + b * 128:rnd * 1024 + (b + 1) * 128],
                                           p.ident[:])
                    ctx.inc(inst, "t")
                on(ctx, "tensor", pe_t)
                t_tr = ctx.tok("t")
                def s_ev(e, slot=slot, ti=ti, rnd=rnd, t_tr=t_tr):
                    e.wait(t_tr)
                    pv = p.pst[:].rearrange("p (b c) -> p b c", b=8)
                    ov = wT_v[:, slot, rnd * 8:(rnd + 1) * 8, :]
                    ctx.inc(e.copy(out=ov, in_=pv), "s")
                on(ctx, "scalar", s_ev)
            wt_done.append(ctx.tok("s"))
        t_grp = ctx.tok("s")
        agg_banks = {}
        for slot in range(len(grp)):
            agg_banks[slot] = (ps_agg[(slot * 2) % 4], ps_agg[(slot * 2 + 1) % 4] if slot < 2 else mm.ps[slot % 2])
        agg_banks = {0: (ps_agg[0], ps_agg[1]), 1: (ps_agg[2], ps_agg[3]), 2: (mm.ps[0], mm.ps[1])}
        vload = []
        for nt in range(32):
            def g_v(e, nt=nt, grp_id=grp_id):
                if vload:
                    e.wait(vload[-1])
                if nt == 0:
                    e.wait(t_grp)
                    if grp_id > 0 and last_tok is not None:
                        e.wait(last_tok)
                ctx.inc(e.dma_start(out=vstg[:], in_=io["aux_values"][nt * 128:(nt + 1) * 128, :]), "qg", 16)
            on(ctx, "gpsimd", g_v)
            t_lv2 = ctx.tok("qg")
            def s_v(e, t_lv2=t_lv2):
                e.wait(t_lv2)
                if vload:
                    pass
                vload.append(ctx.inc(e.copy(out=vb[:], in_=vstg[:]), "s"))
            on(ctx, "scalar", s_v)
            t_vb = ctx.tok("s")
            def pe_agg(e, nt=nt, grp=grp, t_vb=t_vb, grp_id=grp_id):
                e.wait(t_vb, t_grp)
                if grp_id > 0 and last_tok is not None and nt == 0:
                    e.wait(last_tok)
                for slot, ti in enumerate(grp):
                    lhs = wT_v[:, slot, nt, :]
                    bA, bB = agg_banks[slot]
                    e.matmul(bA[:, 0:512], lhs, vb[:, 0:512], start=(nt == 0), stop=(nt == 31))
                    inst = e.matmul(bB[:, 0:512], lhs, vb[:, 512:1024], start=(nt == 0), stop=(nt == 31))
                ctx.inc(inst, "t")
            on(ctx, "tensor", pe_agg)
            t_pa = ctx.tok("t")
            def s_gate2(e, t_pa=t_pa):
                e.wait(t_pa)
            on(ctx, "scalar", s_gate2)
        t_agg = ctx.tok("t")
        fin = []
        for slot, ti in enumerate(grp):
            def v_fin(e, slot=slot, ti=ti, t_agg=t_agg):
                e.wait(t_agg, t_rp1)
                bA, bB = agg_banks[slot]
                tk = ctx.inc(e.reciprocal(out=ccc[:, 3:4], in_=wsums[:, ti:ti + 1]), "v")
                e.wait(tk)
                for half, bank in ((0, bA), (1, bB)):
                    sl = p.xres[:, ti * D + half * 512:ti * D + half * 512 + 512]
                    e.tensor_scalar(out=aggscr[:], in0=bank[:, 0:512], scalar1=ccc[:, 3:4],
                                    scalar2=c_gate[:, ti:ti + 1], op0=ALU.mult, op1=ALU.mult)
                    e.scalar_tensor_tensor(out=sl, in0=sl, scalar=c_rp1[:, ti:ti + 1], in1=aggscr[:],
                                           op0=ALU.mult, op1=ALU.add)
                    inst = e.tensor_scalar(out=sl, in0=sl, scalar1=p.ln_nmr[:, ti:ti + 1],
                                           scalar2=None, op0=ALU.add)
                ctx.inc(inst, "v")
            on(ctx, "vector", v_fin)
            fin.append(ctx.tok("v"))
        last_tok = fin[-1]
        mm.evac_tok[0] = last_tok
        mm.evac_tok[1] = last_tok
    return last_tok


def emit_l1_mamba(p, io, x_in_ap, x_out_ap, tag, uid=""):
    ctx = p.ctx
    u_ = tag
    sb = lambda n, s, d: ctx.sbuf(f"{u_}_{n}", s, d)
    psm = lambda n, s, d: ctx.psum(f"{u_}_{n}", s, d)
    nc = ctx.nc
    DI = 2048
    NDI = 16

    xv = x_in_ap.rearrange("(i p) d -> i p d", p=128)
    xov = x_out_ap.rearrange("(i p) d -> i p d", p=128)
    xfs = sb("xfs", [128, 2 * D], F32)

    ld_toks, st_done = [], []
    for i in range(NTOK):
        def g_l(e, i=i):
            if i >= 2:
                e.wait(st_done[i - 2])
            ld_toks.append(ctx.inc(e.dma_start(out=xfs[:, (i % 2) * D:(i % 2 + 1) * D], in_=xv[i]), "qs", 16))
        on(ctx, "sync", g_l)
        def v_st(e, i=i):
            e.wait(ld_toks[i])
            st = p.ln_stats[:, i * 12:(i + 1) * 12]
            e.bn_stats(out=st[:, 0:6], in_=xfs[:, (i % 2) * D:(i % 2) * D + 512])
            inst = e.bn_stats(out=st[:, 6:12], in_=xfs[:, (i % 2) * D + 512:(i % 2 + 1) * D])
            st_done.append(ctx.inc(inst, "v"))
        on(ctx, "vector", v_st)
    def v_ag(e):
        e.wait(st_done[NTOK - 1])
        for i in range(NTOK):
            inst = e.bn_aggr(out=p.ln_mv[:, 2 * i:2 * i + 2], in_=p.ln_stats[:, i * 12:(i + 1) * 12])
        ctx.inc(inst, "v")
    on(ctx, "vector", v_ag)
    t_stats = ctx.tok("v")
    mvh = p.ln_mv[:].rearrange("p (i two) -> p i two", two=2)
    mu0, var0 = mvh[:, :, 0], mvh[:, :, 1]
    def s_rstd(e):
        e.wait(t_stats)
        e.activation(out=p.ln_rstd[:], in_=var0, func=AF.Ln, bias=p.fc(EPS1), scale=1.0)
        ctx.inc(e.activation(out=p.ln_rstd[:], in_=p.ln_rstd[:], func=AF.Exp, scale=-0.5), "s")
    on(ctx, "scalar", s_rstd)
    t_rstd0 = ctx.tok("s")
    def v_nmr0(e):
        e.wait(t_rstd0)
        ctx.inc(e.scalar_tensor_tensor(out=p.ln_nmr[:], in0=mu0, scalar=-1.0,
                                       in1=p.ln_rstd[:], op0=ALU.mult, op1=ALU.mult), "v")
    on(ctx, "vector", v_nmr0)
    t_nmr = ctx.tok("v")

    xa_toks = []
    V0 = ctx.cnt["v"]
    def producers(i):
        def g_l2(e, i=i):
            e.wait(t_nmr)
            if i >= 2:
                e.wait(("v", V0 + (i - 1)))
            xa_toks.append(ctx.inc(e.dma_start(out=xfs[:, (i % 2) * D:(i % 2 + 1) * D], in_=xv[i]), "qs", 16))
        on(ctx, "sync", g_l2)
        def fn(e, i=i):
            e.wait(xa_toks[i], t_nmr)
            return e.tensor_scalar(out=p.xstg[:, (i % 2) * D:(i % 2 + 1) * D],
                                   in0=xfs[:, (i % 2) * D:(i % 2 + 1) * D],
                                   scalar1=p.ln_rstd[:, i:i + 1], scalar2=p.ln_nmr[:, i:i + 1],
                                   op0=ALU.mult, op1=ALU.add)
        return ("vector", "v", fn)
    t_xnT = emit_xnT(p, producers)

    xs = sb("xs", [128, NDI * T], BF16)
    wstg = sb("wstg", [128, 512], F32)
    wchunk = sb("wchunk", [128, NDT * 128], BF16)
    cols = sb("cols", [128, 64 + 16 * 3 + 256], F32)
    xdbl = sb("xdbl", [64, T], BF16)
    bc_sb = sb("bc_sb", [32, T], BF16)
    dtw = sb("dtw", [64, DI], BF16)
    xpw = sb("xpw", [128, NDI * 96], BF16)
    delt = sb("delt", [128, T], BF16)
    dxs = sb("dxs", [128, T], BF16)
    dA = sb("dA", [128, T], F32)
    ubuf = sb("ubuf", [128, T], BF16)
    hbuf = sb("hbuf", [128, T], BF16)
    ybuf = sb("ybuf", [128, T], F32)
    yscr = sb("yscr", [128, T], BF16)
    convscr = sb("convscr", [128, T], F32)
    rowB = sb("rowB", [1, T], BF16)
    rowC = sb("rowC", [1, T], BF16)
    zb = sb("zb", [128, T], BF16)
    c_convw = cols[:, 0:64]
    c_convb = cols[:, 64:80]
    c_dtb = cols[:, 80:96]
    c_D = cols[:, 96:112]
    c_A = cols[:, 112:368]
    mm = MMPool(ctx, u_)
    ps_b = psm("psB", [128, T], F32)
    zdram = nc.dram_tensor(f"{u_}_zscr{uid}", [128, NDI * T], BF16)
    bcdram = nc.dram_tensor(f"{u_}_bcscr{uid}", [32, T], BF16)

    def g_cols(e):
        ctx.inc(e.dma_start(out=c_convw, in_=io["conv_w_c"]), "qg", 16)
        ctx.inc(e.dma_start(out=c_convb, in_=io["conv_b_c"]), "qg", 16)
        ctx.inc(e.dma_start(out=c_dtb, in_=io["dtb_c"]), "qg", 16)
        ctx.inc(e.dma_start(out=c_D, in_=io["D_c"]), "qg", 16)
        ctx.inc(e.dma_start(out=c_A, in_=io["A_log_c"]), "qg", 16)
    on(ctx, "gpsimd", g_cols)
    t_cols = ctx.tok("qg")
    def s_A(e):
        e.wait(t_cols)
        ctx.inc(e.activation(out=c_A, in_=c_A, func=AF.Exp), "s")
    on(ctx, "scalar", s_A)
    t_sA = ctx.tok("s")
    def v_A(e):
        e.wait(t_sA)
        ctx.inc(e.tensor_scalar(out=c_A, in0=c_A, scalar1=-1.0, scalar2=None, op0=ALU.mult), "v")
    on(ctx, "vector", v_A)
    t_A = ctx.tok("v")

    ipv_p = io["in_projT_p"].rearrange("p (j f) -> p j f", j=NDT)
    wstg_v = wstg[:, 0:256].rearrange("p (j c) -> p j c", j=2)
    conv_toks = []
    zout_toks = []
    for cs in range(32):
        casts = []
        for hh in range(4):
            def g_w(e, hh=hh, cs=cs):
                if hh == 0 and (conv_toks or zout_toks):
                    e.wait((conv_toks + zout_toks)[-1])
                if hh >= 1:
                    e.wait(casts[hh - 1])
                ctx.inc(e.dma_start(out=wstg_v, in_=ipv_p[:, hh * 2:(hh + 1) * 2, cs * 128:(cs + 1) * 128]), "qg", 16)
            on(ctx, "gpsimd", g_w)
            t_l = ctx.tok("qg")
            def v_c(e, hh=hh, t_l=t_l):
                e.wait(t_l)
                inst = e.tensor_copy(out=wchunk[:, hh * 256:(hh + 1) * 256], in_=wstg[:, 0:256])
                casts.append(ctx.inc(inst, "v"))
            on(ctx, "vector", v_c)
        t_wc = ctx.tok("v")
        ev_toks = []
        for tc in range(4):
            mms = [(wchunk[:, j * 128:(j + 1) * 128],
                    p.xnT[:, j * T + tc * 512:j * T + (tc + 1) * 512]) for j in range(NDT)]
            if cs < 16:
                def ev(e, ps, cs=cs, tc=tc):
                    return e.copy(out=xs[:, cs * T + tc * 512:cs * T + (tc + 1) * 512],
                                  in_=ps[:, 0:512])
            else:
                def ev(e, ps, cs=cs, tc=tc):
                    return e.copy(out=zb[:, tc * 512:(tc + 1) * 512], in_=ps[:, 0:512])
            ev_toks.append(mm.step([t_xnT, t_wc], mms, ev))
        if cs < 16:
            t_e = ev_toks[-1]
            def v_conv(e, cs=cs, t_e=t_e):
                e.wait(t_e, t_cols)
                x_ = xs[:, cs * T:(cs + 1) * T]
                e.tensor_scalar(out=convscr[:], in0=x_, scalar1=c_convw[:, cs * 4 + 3:cs * 4 + 4],
                                scalar2=None, op0=ALU.mult)
                for s_ in (1, 2, 3):
                    k = 3 - s_
                    e.scalar_tensor_tensor(out=convscr[:, s_:T], in0=x_[:, 0:T - s_],
                                           scalar=c_convw[:, cs * 4 + k:cs * 4 + k + 1],
                                           in1=convscr[:, s_:T], op0=ALU.mult, op1=ALU.add)
                ctx.inc(e.engine_nop(), "v")
            on(ctx, "vector", v_conv)
            t_cv = ctx.tok("v")
            def s_silu(e, cs=cs, t_cv=t_cv):
                e.wait(t_cv)
                ctx.inc(e.activation(out=xs[:, cs * T:(cs + 1) * T], in_=convscr[:],
                                     func=AF.Silu, bias=c_convb[:, cs:cs + 1]), "s")
            on(ctx, "scalar", s_silu)
            conv_toks.append(ctx.tok("s"))
        else:
            t_e = ev_toks[-1]
            def g_z(e, cs=cs, t_e=t_e):
                e.wait(t_e)
                zout_toks.append(ctx.inc(e.dma_start(out=zdram.ap()[:, (cs - 16) * T:(cs - 15) * T],
                                                     in_=zb[:]), "qg", 16))
            on(ctx, "gpsimd", g_z)
    t_conv = ctx.tok("s")
    t_zout = ctx.tok("qg")

    xpv = io["x_projT"].rearrange("(j p) f -> j p f", p=128)
    c3 = []
    for j in range(NDI):
        def g_w(e, j=j):
            if c3:
                e.wait(c3[-1])
            ctx.inc(e.dma_start(out=wstg[:, 0:96], in_=xpv[j]), "qg", 16)
        on(ctx, "gpsimd", g_w)
        t_l = ctx.tok("qg")
        def s_c(e, j=j, t_l=t_l):
            e.wait(t_l)
            c3.append(ctx.inc(e.copy(out=xpw[:, j * 96:(j + 1) * 96], in_=wstg[:, 0:96]), "s"))
        on(ctx, "scalar", s_c)
    t_xpw = ctx.tok("s")
    for tc in range(4):
        mms = [(xpw[:, j * 96:(j + 1) * 96], xs[:, j * T + tc * 512:j * T + (tc + 1) * 512])
               for j in range(NDI)]
        def ev(e, ps, tc=tc):
            e.copy(out=xdbl[0:64, tc * 512:(tc + 1) * 512], in_=ps[0:64, 0:512])
            return e.copy(out=bc_sb[0:32, tc * 512:(tc + 1) * 512], in_=ps[64:96, 0:512])
        mm.step([t_conv, t_xpw], mms, ev, out_p=96)
    t_xdbl = ctx.tok("s")
    def g_bc(e):
        e.wait(t_xdbl)
        ctx.inc(e.dma_start(out=bcdram.ap(), in_=bc_sb[:]), "qg", 16)
    on(ctx, "gpsimd", g_bc)
    t_bc = ctx.tok("qg")

    dtv = io["dt_projT"]
    c4 = []
    for hh in range(4):
        def g_w(e, hh=hh):
            if c4:
                e.wait(c4[-1])
            ctx.inc(e.dma_start(out=wstg[0:64, 0:512], in_=dtv[:, hh * 512:(hh + 1) * 512]), "qg", 16)
        on(ctx, "gpsimd", g_w)
        t_l = ctx.tok("qg")
        def s_c(e, hh=hh, t_l=t_l):
            e.wait(t_l)
            c4.append(ctx.inc(e.copy(out=dtw[0:64, hh * 512:(hh + 1) * 512], in_=wstg[0:64, 0:512]), "s"))
        on(ctx, "scalar", s_c)
    t_dtw = ctx.tok("s")

    opb = p.xnT
    opv = io["out_projT"].rearrange("(j p) d2 -> j p d2", p=128)
    c5 = []
    for j in range(NDI):
        for hh in range(2):
            def g_w(e, j=j, hh=hh):
                e.wait(t_xnT if False else ctx.tok("qs"))
                e.wait(t_conv, t_zout)
                if c5:
                    e.wait(c5[-1])
                ctx.inc(e.dma_start(out=wstg[:, 0:512], in_=opv[j][:, hh * 512:(hh + 1) * 512]), "qg", 16)
            on(ctx, "gpsimd", g_w)
            t_l = ctx.tok("qg")
            def s_c(e, j=j, hh=hh, t_l=t_l):
                e.wait(t_l)
                c5.append(ctx.inc(e.copy(out=opb[:, j * D + hh * 512:j * D + (hh + 1) * 512],
                                         in_=wstg[:, 0:512]), "s"))
            on(ctx, "scalar", s_c)
    t_opb = ctx.tok("s")

    yg_toks = []
    for dt in range(NDI):
        dl_toks = []
        for tc in range(4):
            mms = [(dtw[0:64, dt * 128:(dt + 1) * 128], xdbl[0:64, tc * 512:(tc + 1) * 512])]
            def ev(e, ps, dt=dt, tc=tc):
                sl = delt[:, tc * 512:(tc + 1) * 512]
                e.activation(out=convscr[:, 0:512], in_=ps[:, 0:512], func=AF.Abs,
                             bias=c_dtb[:, dt:dt + 1])
                e.activation(out=convscr[:, 0:512], in_=convscr[:, 0:512], func=AF.Exp, scale=-1.0)
                e.activation(out=convscr[:, 0:512], in_=convscr[:, 0:512], func=AF.Ln,
                             bias=p.fc(1.0))
                e.activation(out=convscr[:, 512:1024], in_=ps[:, 0:512], func=AF.Relu,
                             bias=c_dtb[:, dt:dt + 1])
                return e.activation(out=convscr[:, 512:1024], in_=convscr[:, 512:1024],
                                    func=AF.Copy, scale=1.0)
            dl_toks.append(mm.step([t_xdbl, t_dtw], mms, ev, out_p=128))
            def v_sp(e, tc=tc, dl=dl_toks[-1]):
                e.wait(dl)
                ctx.inc(e.tensor_tensor(out=delt[:, tc * 512:(tc + 1) * 512],
                                        in0=convscr[:, 0:512], in1=convscr[:, 512:1024],
                                        op=ALU.add), "v")
            on(ctx, "vector", v_sp)
            t_vsp = ctx.tok("v")
            def s_g(e, t_vsp=t_vsp):
                e.wait(t_vsp)
            on(ctx, "scalar", s_g)
        t_delt = ctx.tok("v")
        def v_dxs(e, dt=dt, t_delt=t_delt):
            e.wait(t_delt, t_conv, t_cols)
            x_ = xs[:, dt * T:(dt + 1) * T]
            e.tensor_tensor(out=dxs[:], in0=delt[:], in1=x_, op=ALU.mult)
            ctx.inc(e.tensor_scalar(out=ybuf[:], in0=x_, scalar1=c_D[:, dt:dt + 1],
                                    scalar2=None, op0=ALU.mult), "v")
        on(ctx, "vector", v_dxs)
        t_dxs = ctx.tok("v")
        for n in range(16):
            def g_row(e, n=n, t_bc=t_bc):
                e.wait(t_bc)
                e.wait(("t", ctx.cnt["t"]))
                ctx.inc(e.dma_start(out=rowB[:], in_=bcdram.ap()[n:n + 1, :]), "qg", 16)
                ctx.inc(e.dma_start(out=rowC[:], in_=bcdram.ap()[16 + n:17 + n, :]), "qg", 16)
            on(ctx, "gpsimd", g_row)
            t_row = ctx.tok("qg")
            prev_u = ctx.tok("v")
            def pe_bc2(e, n=n, t_row=t_row, prev_u=prev_u):
                e.wait(t_row)
                e.wait(prev_u)
                for tc in range(4):
                    e.matmul(ps_b[:, tc * 512:(tc + 1) * 512], p.ones_b[0:1, :],
                             rowB[0:1, tc * 512:(tc + 1) * 512], start=True, stop=True)
                for tc in range(4):
                    inst = e.matmul(mm.ps[tc % 2][:, 0:512], p.ones_b[0:1, :],
                                    rowC[0:1, tc * 512:(tc + 1) * 512], start=True, stop=True)
                    if tc < 2:
                        pass
                ctx.inc(inst, "t")
            def pe_bcB(e, n=n, t_row=t_row, prev_u=prev_u):
                e.wait(t_row, prev_u)
                for tc in range(4):
                    inst = e.matmul(ps_b[:, tc * 512:(tc + 1) * 512], p.ones_b[0:1, :],
                                    rowB[0:1, tc * 512:(tc + 1) * 512], start=True, stop=True)
                ctx.inc(inst, "t")
            on(ctx, "tensor", pe_bcB)
            t_pbB = ctx.tok("t")
            prev_v = ctx.tok("v")
            def s_dA(e, dt=dt, n=n, t_delt=t_delt, prev_v=prev_v):
                e.wait(t_delt, t_A, prev_v)
                ctx.inc(e.activation(out=dA[:], in_=delt[:], func=AF.Exp,
                                     scale=c_A[:, dt * 16 + n:dt * 16 + n + 1]), "s")
            on(ctx, "scalar", s_dA)
            t_dA = ctx.tok("s")
            def v_u(e, t_pbB=t_pbB, t_dxs=t_dxs):
                e.wait(t_pbB, t_dxs)
                ctx.inc(e.tensor_tensor(out=ubuf[:], in0=dxs[:], in1=ps_b[:], op=ALU.mult), "v")
            on(ctx, "vector", v_u)
            t_u = ctx.tok("v")
            def pe_bcC(e, n=n, t_row=t_row, t_u=t_u):
                e.wait(t_row, t_u)
                for tc in range(4):
                    inst = e.matmul(ps_b[:, tc * 512:(tc + 1) * 512], p.ones_b[0:1, :],
                                    rowC[0:1, tc * 512:(tc + 1) * 512], start=True, stop=True)
                ctx.inc(inst, "t")
            on(ctx, "tensor", pe_bcC)
            t_pbC = ctx.tok("t")
            def v_scan(e, t_dA=t_dA, t_u=t_u):
                e.wait(t_dA, t_u)
                tk = ctx.inc(e.tensor_tensor_scan(out=hbuf[:], data0=dA[:], data1=ubuf[:],
                                                  initial=0.0, op0=ALU.mult, op1=ALU.add), "v")
                e.wait(tk)
                ctx.inc(e.engine_nop(), "v")
            on(ctx, "vector", v_scan)
            t_h = ctx.tok("v")
            def v_y(e, t_h=t_h, t_pbC=t_pbC):
                e.wait(t_h, t_pbC)
                e.tensor_tensor(out=yscr[:], in0=hbuf[:], in1=ps_b[:], op=ALU.mult)
                ctx.inc(e.tensor_tensor(out=ybuf[:], in0=ybuf[:], in1=yscr[:], op=ALU.add), "v")
            on(ctx, "vector", v_y)
        t_y = ctx.tok("v")
        def g_z2(e, dt=dt, t_zout=t_zout):
            e.wait(t_zout)
            if yg_toks:
                e.wait(yg_toks[-1])
            ctx.inc(e.dma_start(out=zb[:], in_=zdram.ap()[:, dt * T:(dt + 1) * T]), "qg", 16)
        on(ctx, "gpsimd", g_z2)
        t_z = ctx.tok("qg")
        def s_sz(e, t_z=t_z, t_y=t_y):
            e.wait(t_z, t_y)
            ctx.inc(e.activation(out=zb[:], in_=zb[:], func=AF.Silu), "s")
        on(ctx, "scalar", s_sz)
        t_sz = ctx.tok("s")
        def v_g(e, dt=dt, t_sz=t_sz, t_y=t_y):
            e.wait(t_sz, t_y)
            ctx.inc(e.tensor_tensor(out=xs[:, dt * T:(dt + 1) * T], in0=ybuf[:], in1=zb[:],
                                    op=ALU.mult), "v")
        on(ctx, "vector", v_g)
        yg_toks.append(ctx.tok("v"))
    t_yg = ctx.tok("v")

    xc_toks, out_toks = [], []
    for ti in range(NTOK):
        def g_lx(e, ti=ti):
            if len(out_toks) >= 2:
                e.wait(out_toks[-2])
            e.wait(t_yg)
            xc_toks.append(ctx.inc(e.dma_start(out=xfs[:, (ti % 2) * D:(ti % 2 + 1) * D], in_=xv[ti]), "qs", 16))
        on(ctx, "sync", g_lx)
        ev_toks = []
        for dc in range(2):
            mms = [(xs[:, j * T + ti * 128:j * T + (ti + 1) * 128],
                    opb[:, j * D + dc * 512:j * D + (dc + 1) * 512]) for j in range(NDI)]
            def ev(e, ps, ti=ti, dc=dc):
                sl = xfs[:, (ti % 2) * D + dc * 512:(ti % 2) * D + dc * 512 + 512]
                return e.scalar_tensor_tensor(out=sl, in0=ps[:, 0:512], scalar=1.0,
                                              in1=sl, op0=ALU.mult, op1=ALU.add)
            def evw(e, ps, ti=ti, dc=dc):
                e.wait(xc_toks[ti])
                return ev(e, ps)
            ev_toks.append(mm.step([t_yg, t_opb], mms, evw, eng="vector", sem="v"))
        def g_ox(e, ti=ti, ev_toks=ev_toks):
            e.wait(*ev_toks)
            out_toks.append(ctx.inc(e.dma_start(out=xov[ti], in_=xfs[:, (ti % 2) * D:(ti % 2 + 1) * D]), "qg", 16))
        on(ctx, "gpsimd", g_ox)
    def g_fin(e):
        e.wait(ctx.tok("qg"))
    on(ctx, "gpsimd", g_fin)
    return ctx.tok("qg")



def _build_fused():
    nc = bass.Bass("TRN2", target_bir_lowering=False, debug=False)
    x_ext = nc.dram_tensor("x_ext", [T, D], F32, kind="ExternalInput").ap()
    x_out = nc.dram_tensor("x_out", [T, D], F32, kind="ExternalOutput").ap()
    xsA = nc.dram_tensor("xscrA", [T, D], F32).ap()
    xsB = nc.dram_tensor("xscrB", [T, D], F32).ap()

    def din(n, shp):
        return nc.dram_tensor(n, shp, F32, kind="ExternalInput").ap()

    io_ra = {n: din(n, shp) for n, shp in [
        ("Wt", [D, D]), ("Wo", [D, D]), ("Wg", [2 * D, D]), ("bg_row", [1, D]),
        ("ln1w_c", [128, 8]), ("ln1b_c", [128, 8]), ("lncw_c", [128, 8]),
        ("lncb_c", [128, 8]), ("lngw_c", [128, 8]), ("lngb_c", [128, 8])]}
    io_f0 = {n: din(n, shp) for n, shp in [
        ("wT", [D, D]), ("alpha", [1, 1]), ("brow", [1, D])]}
    io_ab = {n: din(n, shp) for n, shp in [
        ("wq", [D, D]), ("wk", [D, D]), ("wv", [D, D]), ("wo", [D, D]),
        ("bq_c", [128, 8]), ("bk_c", [128, 8]), ("bv_row", [1, D]), ("bo_row", [1, D]),
        ("gamma_bc", [128, D]), ("beta_bc", [128, D]), ("pe_tok", [T, D])]}
    io_la = {n: din(n, shp) for n, shp in [
        ("w_qkv_p", [128, 8 * 3 * D]), ("w_gate_p", [128, 8 * D]), ("w_proj", [D, D]),
        ("b_qkv_row", [1, 3 * D]), ("b_gate_row", [1, D]), ("b_proj_bc", [128, D]),
        ("lng_c", [128, 8]), ("lnb_c", [128, 8])]}
    io_lf = {n: din(n, shp) for n, shp in [
        ("q_projT", [D, 128]), ("router_wT", [128, 48]), ("aux_keysT", [128, 4096]),
        ("aux_values", [4096, D]), ("rel_row", [1, 4096]), ("unc_w1T", [D, 256]),
        ("unc_b1_c", [128, 2]), ("unc_w2_c", [128, 2]), ("unc_b2", [1, 1]),
        ("gate_w1", [1, 1]), ("gate_bias", [1, 1])]}
    io_mb = {n: din(n, shp) for n, shp in [
        ("in_projT_p", [128, 8 * 4096]), ("x_projT", [2048, 96]), ("dt_projT", [64, 2048]),
        ("out_projT", [2048, D]), ("conv_w_c", [128, 64]), ("conv_b_c", [128, 16]),
        ("dtb_c", [128, 16]), ("D_c", [128, 16]), ("A_log_c", [128, 256])]}

    cvals = host_constants()
    consts = const_dram(nc, cvals)
    es = contextlib.ExitStack()
    with es:
        block = es.enter_context(nc.Block())
        p = Prog(nc, es, block, keep_xres=False)
        p.c_aps = consts
        p.load_const(consts)
        ctx = p.ctx

        def scoped(fn):
            sc = contextlib.ExitStack()
            ctx.scope = sc
            with sc:
                r = fn()
            ctx.scope = None
            emit_barrier(p)
            return r

        scoped(lambda: emit_l0_attn_a(p, io_ra, x_ext, xsA, "ra"))
        def f_ffn1():
            p.xres = ctx.sbuf("xres_f0", [128, NTOK * D], F32)
            p.load_x(xsA)
            t_nmr, t_xnT = emit_ln_t(p, [p.x_tok])
            prep = emit_l0_ffn_prep(p, io_f0["wT"], io_f0["alpha"], io_f0["brow"], "f0")
            mm = MMPool(ctx, "f0c")
            tok = emit_l0_ffn_call(p, prep, mm, t_xnT)
            p.store_x(xsB, [tok])
        scoped(f_ffn1)
        scoped(lambda: emit_l0_attn_b(p, io_ab, xsB, xsA, "ab"))
        def f_ffn2():
            p.xres = ctx.sbuf("xres_f1", [128, NTOK * D], F32)
            p.load_x(xsA)
            t_nmr, t_xnT = emit_ln_t(p, [p.x_tok])
            prep = emit_l0_ffn_prep(p, io_f0["wT"], io_f0["alpha"], io_f0["brow"], "f1")
            mm = MMPool(ctx, "f1c")
            tok = emit_l0_ffn_call(p, prep, mm, t_xnT)
            p.store_x(xsB, [tok])
        scoped(f_ffn2)
        scoped(lambda: emit_l1_attn_a(p, io_la, xsB, xsA, "la"))
        def f_lf1():
            p.xres = ctx.sbuf("xres_lf1", [128, NTOK * D], F32)
            p.load_x(xsA)
            p.ln_dep = [p.x_tok]
            tok = emit_l1_ffn(p, io_lf, "lf1")
            p.store_x(xsB, [tok])
        scoped(f_lf1)
        scoped(lambda: emit_l1_mamba(p, io_mb, xsB, xsA, "mb"))
        def f_lf2():
            p.xres = ctx.sbuf("xres_lf2", [128, NTOK * D], F32)
            p.load_x(xsA)
            p.ln_dep = [p.x_tok]
            tok = emit_l1_ffn(p, io_lf, "lf2")
            p.store_x(x_out, [tok])
        scoped(f_lf2)
    return nc


def kernel(**inputs):
    f32 = lambda a: np.ascontiguousarray(np.asarray(a), dtype=np.float32)
    I = {k: np.asarray(v) for k, v in inputs.items()}
    cvals = host_constants()
    col8 = lambda v: f32(v).reshape(8, 128).T.copy()
    col16 = lambda v: f32(v).reshape(16, 128).T.copy()

    m = dict(cvals)
    m.update({
        "Wt": f32(I["L0_attn_a_Wt"]), "Wo": f32(I["L0_attn_a_Wo"]), "Wg": f32(I["L0_attn_a_Wg"]),
        "bg_row": f32(I["L0_attn_a_bg"]).reshape(1, D),
        "ln1w_c": col8(I["L0_attn_a_ln1_w"]), "ln1b_c": col8(I["L0_attn_a_ln1_b"]),
        "lncw_c": col8(I["L0_attn_a_lnc_w"]), "lncb_c": col8(I["L0_attn_a_lnc_b"]),
        "lngw_c": col8(I["L0_attn_a_lng_w"]), "lngb_c": col8(I["L0_attn_a_lng_b"]),
        "wT": f32(I["L0_ffn_weight"].T), "alpha": f32(I["L0_ffn_alpha"]).reshape(1, 1),
        "brow": f32(I["L0_ffn_bias"]).reshape(1, D),
        "wq": f32(I["L0_attn_b_wq"]), "wk": f32(I["L0_attn_b_wk"]),
        "wv": f32(I["L0_attn_b_wv"]), "wo": f32(I["L0_attn_b_wo"]),
        "bq_c": col8(I["L0_attn_b_bq"]), "bk_c": col8(I["L0_attn_b_bk"]),
        "bv_row": f32(I["L0_attn_b_bv"]).reshape(1, D), "bo_row": f32(I["L0_attn_b_bo"]).reshape(1, D),
        "gamma_bc": np.broadcast_to(f32(I["L0_attn_b_gamma"]), (128, D)).copy(),
        "beta_bc": np.broadcast_to(f32(I["L0_attn_b_beta"]), (128, D)).copy(),
        "pe_tok": f32(I["L0_attn_b_pe"][0, :T, :]),
        "w_qkv_p": f32(I["L1_attn_a_w_qkv"]).reshape(8, 128, 3 * D).transpose(1, 0, 2).reshape(128, 24 * D).copy(),
        "w_gate_p": f32(I["L1_attn_a_w_gate"]).reshape(8, 128, D).transpose(1, 0, 2).reshape(128, 8 * D).copy(),
        "w_proj": f32(I["L1_attn_a_w_proj"]),
        "b_qkv_row": f32(I["L1_attn_a_b_qkv"]).reshape(1, 3 * D),
        "b_gate_row": f32(I["L1_attn_a_b_gate"]).reshape(1, D),
        "b_proj_bc": np.broadcast_to(f32(I["L1_attn_a_b_proj"]), (128, D)).copy(),
        "lng_c": col8(I["L1_attn_a_ln_g"]), "lnb_c": col8(I["L1_attn_a_ln_b"]),
        "q_projT": f32(I["L1_ffn_q_proj_w"].T), "router_wT": f32(I["L1_ffn_router_w"].T),
        "aux_keysT": f32(I["L1_ffn_aux_keys"].T), "aux_values": f32(I["L1_ffn_aux_values"]),
        "rel_row": f32(I["L1_ffn_reliability_mask"]).reshape(1, 4096),
        "unc_w1T": f32(I["L1_ffn_unc_w1"].T),
        "unc_b1_c": f32(I["L1_ffn_unc_b1"]).reshape(2, 128).T.copy(),
        "unc_w2_c": f32(I["L1_ffn_unc_w2"]).reshape(2, 128).T.copy(),
        "unc_b2": f32(I["L1_ffn_unc_b2"]).reshape(1, 1),
        "gate_w1": f32(I["L1_ffn_gate_w1"]).reshape(1, 1),
        "gate_bias": f32(I["L1_ffn_gate_bias"]).reshape(1, 1),
        "in_projT_p": f32(I["L1_attn_b_in_proj_w"].T).reshape(8, 128, 4096).transpose(1, 0, 2).reshape(128, 8 * 4096).copy(),
        "x_projT": f32(I["L1_attn_b_x_proj_w"].T),
        "dt_projT": f32(I["L1_attn_b_dt_proj_w"].T), "out_projT": f32(I["L1_attn_b_out_proj_w"].T),
        "conv_w_c": f32(I["L1_attn_b_conv_w"]).reshape(16, 128, 4).transpose(1, 0, 2).reshape(128, 64).copy(),
        "conv_b_c": col16(I["L1_attn_b_conv_b"]), "dtb_c": col16(I["L1_attn_b_dt_proj_b"]),
        "D_c": col16(I["L1_attn_b_D"]),
        "A_log_c": f32(I["L1_attn_b_A_log"]).reshape(16, 128, 16).transpose(1, 0, 2).reshape(128, 256).copy(),
    })

    x = f32(I["L0_attn_a_x"])
    nc = _build_fused()
    in_maps = []
    for c in range(8):
        mc = dict(m)
        mc["x_ext"] = np.ascontiguousarray(x[c])
        in_maps.append(mc)
    res = run_bass_kernel_spmd(nc, in_maps, core_ids=list(range(8)))
    return np.stack([res.results[c]["x_out"] for c in range(8)]).astype(np.float32)
